# version 01
# baseline (speedup 1.0000x reference)
import numpy as np
import ml_dtypes
import concourse.bass as bass
import concourse.mybir as mybir
from concourse.bass import IndirectOffsetOnAxis
from concourse.bass_utils import run_bass_kernel_spmd

F32 = mybir.dt.float32
BF16 = mybir.dt.bfloat16
I32 = mybir.dt.int32
AF = mybir.ActivationFunctionType
ALU = mybir.AluOpType
AX = mybir.AxisListType
BF = ml_dtypes.bfloat16

T = 2048
D = 1024
NT = T // 128
ND = D // 128
QB = 512
NQB = T // QB

ENG = ("tensor", "vector", "scalar", "gpsimd", "sync")
DMAQ = ("dq_g", "dq_y")


class _Slot:
    __slots__ = ("idx", "mark")
    def __init__(self, idx):
        self.idx = idx
        self.mark = None


NDMAS = 8


class P:
    def __init__(self, nc):
        self.nc = nc
        self.q = {e: [] for e in ENG}
        self.nemit = {e: 0 for e in ENG}
        self.marked = {e: [] for e in ENG}
        self.sem = {e: nc.alloc_semaphore(f"sm_{e}") for e in ENG}
        self.dsem = {d: [nc.alloc_semaphore(f"sm_{d}{i}") for i in range(NDMAS)]
                     for d in DMAQ}
        self.dcnt = {d: [0] * NDMAS for d in DMAQ}
        self.drr = {d: 0 for d in DMAQ}
        self.insts = {e: {} for e in ENG}
        self.waited = {e: {} for e in ENG}
        self.res = {}

    def _push(self, e, fn):
        self.q[e].append(fn)

    def _mark(self, e, slot):
        ml = self.marked[e]
        for m in ml[::-1]:
            if m.idx < slot.idx:
                break
        for pos, m in enumerate(ml):
            if m.idx >= slot.idx:
                return pos + 1
        slot.mark = len(ml) + 1
        ml.append(slot)
        sem = self.sem[e]
        idx = slot.idx
        def attach(eng, e=e, idx=idx):
            self.insts[e][idx].then_inc(sem, 1)
        self._push(e, attach)
        return slot.mark

    def _wait(self, e, src, val):
        w = self.waited[e]
        if w.get(src, 0) >= val:
            return
        w[src] = val
        h = self.sem[src[1]] if src[0] == "eng" else self.dsem[src[1]][src[2]]
        self._push(e, lambda eng: eng.wait_ge(h, val))

    def _deps(self, e, r, w):
        need = {}
        cur = self.nemit[e]
        def add_dep(d):
            if d is None:
                return
            if d[0] == "eng":
                _, e2, slot = d
                if e2 == e:
                    if e2 != "tensor" and cur - slot.idx < 8:
                        need.setdefault(("eng", e2), []).append(slot)
                    return
                need.setdefault(("eng", e2), []).append(slot)
            else:
                _, qn, si, val = d
                key = ("dma", qn, si)
                need[key] = [max(need.get(key, [0])[0], val)]
        for k in r:
            rec = self.res.get(k)
            if rec:
                add_dep(rec["w"])
        for k in w:
            rec = self.res.get(k)
            if rec:
                add_dep(rec["w"])
                for rd in rec["r"]:
                    add_dep(rd)
        for src, items in need.items():
            if src[0] == "eng":
                val = max(self._mark(src[1], s) for s in items)
                self._wait(e, ("eng", src[1]), val)
            else:
                self._wait(e, src, items[0])

    def _record(self, owner, r, w):
        for k in r:
            rec = self.res.setdefault(k, {"w": None, "r": []})
            rec["r"].append(owner)
        for k in w:
            self.res[k] = {"w": owner, "r": []}

    def emit(self, e, fn, r=(), w=()):
        self._deps(e, r, w)
        idx = self.nemit[e]
        self.nemit[e] += 1
        slot = _Slot(idx)
        def run(eng, e=e, fn=fn, idx=idx):
            inst = fn(eng)
            assert inst is not None, f"emit fn on {e} returned None"
            self.insts[e][idx] = inst
        self._push(e, run)
        self._record(("eng", e, slot), r, w)
        return slot

    def t(self, fn, r=(), w=()): return self.emit("tensor", fn, r, w)
    def v(self, fn, r=(), w=()): return self.emit("vector", fn, r, w)
    def s(self, fn, r=(), w=()): return self.emit("scalar", fn, r, w)
    def g(self, fn, r=(), w=()): return self.emit("gpsimd", fn, r, w)

    def _dma_slot(self, e, qname):
        si = self.drr[qname]
        self.drr[qname] = (si + 1) % NDMAS
        if self.dcnt[qname][si] > 0:
            self._wait(e, ("dma", qname, si), self.dcnt[qname][si])
        self.dcnt[qname][si] += 16
        return si, self.dcnt[qname][si]

    def dma(self, qname, out_ap, in_ap, r=(), w=(), engine=None, **kw):
        e = engine or ("gpsimd" if qname == "dq_g" else "sync")
        self._deps(e, r, w)
        si, val = self._dma_slot(e, qname)
        h = self.dsem[qname][si]
        def run(eng):
            eng.dma_start(out=out_ap, in_=in_ap, **kw).then_inc(h, 16)
        self._push(e, run)
        self._record(("dma", qname, si, val), r, w)
        return (qname, si, val)

    def dma_wait(self, e, ref):
        qname, si, val = ref
        self._wait(e, ("dma", qname, si), val)

    def dma_wait_all(self, e, qname):
        for si in range(NDMAS):
            if self.dcnt[qname][si] > 0:
                self._wait(e, ("dma", qname, si), self.dcnt[qname][si])

    def indirect(self, out_ap, out_off, in_ap, in_off, r=(), w=(), **kw):
        e = "gpsimd"
        self._deps(e, r, w)
        si, val = self._dma_slot(e, "dq_g")
        h = self.dsem["dq_g"][si]
        def run(eng):
            eng.indirect_dma_start(out=out_ap, out_offset=out_off, in_=in_ap,
                                   in_offset=in_off, **kw).then_inc(h, 16)
        self._push(e, run)
        self._record(("dma", "dq_g", si, val), r, w)
        return ("dq_g", si, val)

    def finalize(self):
        nc = self.nc
        with nc.Block() as blk:
            @blk.tensor
            def _(eng):
                for f in self.q["tensor"]:
                    f(eng)
            @blk.vector
            def _(eng):
                for f in self.q["vector"]:
                    f(eng)
            @blk.scalar
            def _(eng):
                for f in self.q["scalar"]:
                    f(eng)
            @blk.gpsimd
            def _(eng):
                for f in self.q["gpsimd"]:
                    f(eng)
            @blk.sync
            def _(eng):
                for f in self.q["sync"]:
                    f(eng)
        return nc


def build_consts():
    c = {}
    c["c_ident"] = np.eye(128, dtype=BF)
    c["c_identf"] = np.eye(128, dtype=np.float32)
    c["c_ones512"] = np.ones((1, 512), dtype=BF)
    c["c_onesrowf"] = np.ones((1, 128), dtype=np.float32)
    c["c_onescol"] = np.ones((128, 1), dtype=BF)
    sel = np.zeros((8, 8 * 128), dtype=BF)
    for e in range(8):
        sel[e, e * 128:(e + 1) * 128] = 1.0
    c["c_sel8"] = sel
    for off in range(4):
        jj = np.arange(128)[:, None] + off * 128
        qq = np.arange(512)[None, :]
        c[f"c_cmask{off}"] = (qq >= jj).astype(BF)
    jj = np.arange(128)[:, None]
    qq = np.arange(512)[None, :]
    for i in range(8):
        dlt = qq - ((i - 4) * 128 + jj)
        c[f"c_wm{i}"] = ((dlt >= 0) & (dlt <= 511)).astype(BF)
    inv = 1.0 / (10000.0 ** (np.arange(0, D, 2, dtype=np.float64) / D))
    ang = np.arange(T, dtype=np.float64)[:, None] * inv[None, :]
    cT = np.empty((D, T), np.float32); sT = np.empty((D, T), np.float32)
    cT[0::2] = np.cos(ang).T; cT[1::2] = np.cos(ang).T
    sT[0::2] = -np.sin(ang).T; sT[1::2] = np.sin(ang).T
    c["c_ropeq_c"] = cT
    c["c_ropeq_s"] = sT
    angk = ang[:, :32]
    cTk = np.empty((64, T), np.float32); sTk = np.empty((64, T), np.float32)
    cTk[0::2] = np.cos(angk).T; cTk[1::2] = np.cos(angk).T
    sTk[0::2] = -np.sin(angk).T; sTk[1::2] = np.sin(angk).T
    c["c_ropek_c"] = np.tile(cTk, (2, 1)).copy()
    c["c_ropek_s"] = np.tile(sTk, (2, 1)).copy()
    return c


CONSTS = build_consts()


class St:
    pass


def alloc_common(nc):
    st = St()
    st.nc = nc
    st.x = nc.alloc_sbuf_tensor("x_sb", [128, NT * D], F32)
    st.ht = nc.alloc_sbuf_tensor("ht_sb", [128, ND * T], BF16)
    st.htok = nc.alloc_sbuf_tensor("htok_sb", [128, 2 * D], BF16)
    st.stats = nc.alloc_sbuf_tensor("st_sb", [128, 8 * NT], F32)
    st.sq = st.htok[:, 0:D]
    st.eps = nc.alloc_sbuf_tensor("eps_sb", [128, 1], F32)
    st.ident = nc.alloc_sbuf_tensor("ident_sb", [128, 128], BF16)
    st.identf = nc.alloc_sbuf_tensor("identf_sb", [128, 128], F32)
    st.ones512 = nc.alloc_sbuf_tensor("ones512_sb", [1, 512], BF16)
    st.onesrow_f = nc.alloc_sbuf_tensor("onesrf_sb", [1, 128], F32)
    st.onescol = nc.alloc_sbuf_tensor("onesc_sb", [128, 1], BF16)
    st.zslice = nc.alloc_sbuf_tensor("zsl_sb", [128, 2 * 512], F32)
    st.pb = [nc.alloc_psum_tensor(f"pb{i}", [128, 512], F32) for i in range(6)]
    st.pt = [nc.alloc_psum_tensor(f"pt{i}", [128, 512], BF16) for i in range(2)]
    st.hbm = {}
    return st


def x_t(st, tc):
    return st.x[:, tc * D:(tc + 1) * D]


def ht_c(st, dc):
    return st.ht[:, dc * T:(dc + 1) * T]


def lane(st, i, tc=None):
    if tc is None:
        return st.stats[:, i * NT:(i + 1) * NT]
    return st.stats[:, i * NT + tc:i * NT + tc + 1]


XK = [f"x{tc}" for tc in range(NT)]
HTK = [f"ht{dc}" for dc in range(ND)]


def emit_load_x(p, st, x_hbm):
    p.dma("dq_y", st.x[:].rearrange("p (a m) -> p a m", a=NT),
          x_hbm.rearrange("(a p) m -> p a m", p=128), r=(), w=XK)


def emit_store_x(p, st, x_hbm_out):
    ref = p.dma("dq_y", x_hbm_out.rearrange("(a p) m -> p a m", p=128),
                st.x[:].rearrange("p (a m) -> p a m", a=NT), r=XK, w=())
    p.dma_wait("sync", ref)


def emit_load_consts(p, st):
    hb = st.hbm
    p.dma("dq_y", st.ident[:], hb["c_ident"], w=["ident"])
    p.dma("dq_y", st.identf[:], hb["c_identf"], w=["identf"])
    p.dma("dq_y", st.ones512[:], hb["c_ones512"], w=["ones512"])
    p.dma("dq_y", st.onesrow_f[:], hb["c_onesrowf"], w=["onesrowf"])
    p.dma("dq_y", st.onescol[:], hb["c_onescol"], w=["onescol"])
    p.v(lambda e: e.memset(st.eps[:], 1e-5), w=["eps"])


def emit_ln_stats(p, st, src_t=None, src_k=None):
    src_t = src_t or (lambda tc: x_t(st, tc))
    src_k = src_k or (lambda tc: f"x{tc}")
    p.v(lambda e: e.memset(lane(st, 1), 0.0), w=["st1"])
    for tc in range(NT):
        p.s(lambda e, tc=tc: e.activation(out=st.sq[:], in_=src_t(tc),
                                          func=AF.Square, accum_out=lane(st, 1, tc)),
            r=[src_k(tc), "st1"], w=["htok0", "st1"])
        p.v(lambda e, tc=tc: e.reduce_sum(lane(st, 0, tc), src_t(tc), axis=AX.X),
            r=[src_k(tc)], w=["st0"])
    p.v(lambda e: e.tensor_scalar_mul(lane(st, 4), lane(st, 0), 1.0 / D),
        r=["st0"], w=["st4"])
    p.v(lambda e: e.tensor_tensor(lane(st, 5), lane(st, 4), lane(st, 4), ALU.mult),
        r=["st4"], w=["st5"])
    p.v(lambda e: e.scalar_tensor_tensor(lane(st, 6), lane(st, 1), 1.0 / D,
                                         lane(st, 5), ALU.mult, ALU.subtract),
        r=["st1", "st5"], w=["st6"])
    p.s(lambda e: e.activation(out=lane(st, 5), in_=lane(st, 6), func=AF.Sqrt,
                               bias=st.eps[:, 0:1], scale=1.0),
        r=["st6", "eps"], w=["st5"])
    p.v(lambda e: e.reciprocal(lane(st, 2), lane(st, 5)), r=["st5"], w=["st2"])
    p.v(lambda e: e.scalar_tensor_tensor(lane(st, 3), lane(st, 4), -1.0, lane(st, 2),
                                         ALU.mult, ALU.mult),
        r=["st4", "st2"], w=["st3"])


def emit_ln_to_ht(p, st, xupdate=False, rlane=2, blane=3,
                  gamma_col=None, beta_col=None, gb_res=(), dram_out=None):
    for tc in range(NT):
        bi = tc % 2
        buf = st.htok[:, bi * D:bi * D + D]
        p.s(lambda e, tc=tc, buf=buf: e.activation(
            out=buf, in_=x_t(st, tc), func=AF.Identity,
            scale=lane(st, rlane, tc), bias=lane(st, blane, tc)),
            r=[f"x{tc}", f"st{rlane}", f"st{blane}"], w=[f"htok{bi}"])
        for dc in range(ND):
            pti = (tc * ND + dc) % 2
            p.t(lambda e, buf=buf, dc=dc, pti=pti: e.transpose(
                st.pt[pti][:, 0:128], buf[:, dc * 128:dc * 128 + 128], st.ident[:]),
                r=[f"htok{bi}", "ident"], w=[f"pt{pti}"])
            dst = ht_c(st, dc)[:, tc * 128:tc * 128 + 128]
            if gamma_col is None:
                p.v(lambda e, dst=dst, pti=pti: e.tensor_copy(dst, st.pt[pti][:, 0:128]),
                    r=[f"pt{pti}"], w=[f"ht{dc}"])
            else:
                p.v(lambda e, dst=dst, pti=pti, dc=dc: e.tensor_scalar(
                    dst, st.pt[pti][:, 0:128], gamma_col(dc), beta_col(dc),
                    ALU.mult, ALU.add),
                    r=[f"pt{pti}"] + list(gb_res), w=[f"ht{dc}"])
        if dram_out is not None:
            p.dma("dq_y", dram_out[tc * 128:(tc + 1) * 128, :], buf,
                  r=[f"htok{bi}"], w=[f"hd{tc}"])
        if xupdate:
            p.v(lambda e, tc=tc: e.tensor_scalar(
                x_t(st, tc), x_t(st, tc), lane(st, 7, tc), lane(st, 3, tc),
                ALU.mult, ALU.add),
                r=[f"x{tc}", "st7", "st3", f"htok{bi}"], w=[f"x{tc}"])


def load_w(p, st, name, hbm_ap, K, M, dst=None, dst_off=0, key=None):
    nc = st.nc
    nch = K // 128
    if dst is None:
        dst = nc.alloc_sbuf_tensor(f"w_{name}", [128, nch * M], BF16)
        dst_off = 0
    key = key or f"w_{name}"
    p.dma("dq_g",
          dst[:, dst_off:dst_off + nch * M].rearrange("p (c m) -> p c m", c=nch),
          hbm_ap.rearrange("(c p) m -> p c m", p=128), w=[key])
    return dst, key


def load_row(p, st, name, hbm_ap_1d, n, dtype=BF16):
    nc = st.nc
    dst = nc.alloc_sbuf_tensor(f"r_{name}", [1, n], dtype)
    p.dma("dq_g", dst[:], hbm_ap_1d.rearrange("(o n) -> o n", o=1), w=[f"r_{name}"])
    return dst, f"r_{name}"


_back_i = [0]


def emit_back_tile(p, st, pb_key, psrc, mi, qb, scale=1.0, ptok=None,
                   ptok_res=(), dstk="x", dst_t=None, add=True):
    dst_t = dst_t or (lambda tc: x_t(st, tc))
    zi = _back_i[0] % 2
    _back_i[0] += 1
    zbuf = st.zslice[:, zi * 512:zi * 512 + 512]
    p.s(lambda e, psrc=psrc, zbuf=zbuf: e.copy(zbuf, psrc),
        r=[pb_key], w=[f"zs{zi}"])
    for j in range(4):
        tc = qb * 4 + j
        pbi = 4 + (zi * 4 + j) % 2
        p.t(lambda e, zbuf=zbuf, j=j, pbi=pbi: e.transpose(
            st.pb[pbi][:, 0:128], zbuf[:, j * 128:j * 128 + 128], st.identf[:]),
            r=[f"zs{zi}", "identf"], w=[f"pb{pbi}"])
        dst = dst_t(tc)[:, mi * 128:mi * 128 + 128]
        dk = f"{dstk}{tc}"
        if add:
            if ptok is None:
                p.v(lambda e, dst=dst, pbi=pbi, s=scale: e.scalar_tensor_tensor(
                    dst, st.pb[pbi][:, 0:128], s, dst, ALU.mult, ALU.add),
                    r=[f"pb{pbi}", dk], w=[dk])
            else:
                p.v(lambda e, dst=dst, pbi=pbi, tc=tc: e.scalar_tensor_tensor(
                    dst, st.pb[pbi][:, 0:128], ptok(tc), dst, ALU.mult, ALU.add),
                    r=[f"pb{pbi}", dk] + list(ptok_res), w=[dk])
        else:
            if ptok is None:
                p.v(lambda e, dst=dst, pbi=pbi, s=scale: e.tensor_scalar_mul(
                    dst, st.pb[pbi][:, 0:128], s),
                    r=[f"pb{pbi}"], w=[dk])
            else:
                p.v(lambda e, dst=dst, pbi=pbi, tc=tc: e.tensor_scalar(
                    dst, st.pb[pbi][:, 0:128], ptok(tc), None, ALU.mult),
                    r=[f"pb{pbi}"] + list(ptok_res), w=[dk])


def emit_l0_attn(p, st):
    nc, hb = st.nc, st.hbm
    emit_ln_stats(p, st)
    emit_ln_to_ht(p, st)

    NHG = 4
    qT = nc.alloc_sbuf_tensor("l0_qT", [128, 2 * T], BF16)
    kT = nc.alloc_sbuf_tensor("l0_kT", [128, 2 * T], BF16)
    VH = nc.alloc_sbuf_tensor("l0_VH", [128, NT * NHG * 65], BF16)
    atT = nc.alloc_sbuf_tensor("l0_atT", [128, 4 * 512], BF16)
    vtmp = atT
    aoT = nc.alloc_sbuf_tensor("l0_aoT", [128, NQB * ND * 512], BF16)
    wq = nc.alloc_sbuf_tensor("l0_wq", [128, 2 * ND * 128], BF16)
    wk = nc.alloc_sbuf_tensor("l0_wk", [128, 2 * ND * 128], BF16)
    wv = nc.alloc_sbuf_tensor("l0_wv", [128, 2 * ND * 128], BF16)
    rrow = nc.alloc_sbuf_tensor("l0_rr", [1, 512], F32)
    rb = nc.alloc_sbuf_tensor("l0_rb", [64, 512], F32)
    nrm = nc.alloc_sbuf_tensor("l0_nrm", [1, 64], F32)
    negM = nc.alloc_sbuf_tensor("l0_negM", [128, NHG], F32)

    wqkv = hb["L0_attn_W_qkv"]
    bqkv = nc.alloc_sbuf_tensor("l0_bqkv", [1, 3 * 16 * 64], BF16)
    for wi, (c0, c1) in enumerate(((0, 64), (64, 128), (128, 192))):
        p.dma("dq_g", bqkv[0:1, wi * 1024:(wi + 1) * 1024]
              .rearrange("o (h m) -> o h m", h=16),
              hb["L0_attn_b_qkv"][:, c0:c1], w=["r_l0b"])
    bo_row, bk_bo = load_row(p, st, "l0bo", hb["L0_attn_b_o"], D)

    for grp in range(16 // NHG):
        h0 = grp * NHG
        for j in range(NHG // 2):
            ha = h0 + 2 * j
            for (wt, wkey, c0, c1) in ((wq, f"wq{j}", 0, 64), (wk, f"wk{j}", 64, 128),
                                       (wv, f"wv{j}", 128, 192)):
                for hio in range(2):
                    p.dma("dq_g",
                          wt[:, j * ND * 128:(j + 1) * ND * 128]
                          .rearrange("p (c h m) -> p c h m", c=ND, h=2)[:, :, hio, :],
                          wqkv[ha + hio, :, c0:c1]
                          .rearrange("(c p) m -> p c m", p=128),
                          w=[wkey])

        for j in range(NHG // 2):
            ha = h0 + 2 * j
            for wi, (wt, wkey, c0, c1) in enumerate(
                    ((wq, f"wq{j}", 0, 64), (wk, f"wk{j}", 64, 128),
                     (wv, f"wv{j}", 128, 192))):
                for qb in range(NQB):
                    pbm = (wi * NQB + qb) % 2
                    for kc in range(ND):
                        p.t(lambda e, wt=wt, j=j, kc=kc, qb=qb, pbm=pbm: e.matmul(
                            st.pb[pbm][:, 0:512],
                            wt[:, j * ND * 128 + kc * 128: j * ND * 128 + kc * 128 + 128],
                            ht_c(st, kc)[:, qb * QB:qb * QB + QB],
                            start=(kc == 0), stop=False),
                            r=[wkey, f"ht{kc}"], w=[f"pb{pbm}"])
                    p.t(lambda e, ha=ha, wi=wi, pbm=pbm: e.matmul(
                        st.pb[pbm][:, 0:512],
                        bqkv[0:1, wi * 1024 + ha * 64: wi * 1024 + ha * 64 + 128],
                        st.ones512[0:1, 0:512], start=False, stop=True),
                        r=["r_l0b", "ones512"], w=[f"pb{pbm}"])
                    if wi == 0:
                        dst, dk = qT[:, j * T + qb * QB: j * T + qb * QB + QB], f"qT{j}"
                    elif wi == 1:
                        dst, dk = kT[:, j * T + qb * QB: j * T + qb * QB + QB], f"kT{j}"
                    else:
                        dst, dk = vtmp[:, qb * QB: qb * QB + QB], f"atT{qb}"
                    p.s(lambda e, dst=dst, pbm=pbm: e.copy(dst, st.pb[pbm][:, 0:512]),
                        r=[f"pb{pbm}"], w=[dk])
            for kc in range(NT):
                pti = kc % 2
                p.t(lambda e, kc=kc, pti=pti: e.transpose(
                    st.pt[pti][:, 0:128], vtmp[:, kc * 128:kc * 128 + 128],
                    st.ident[:]),
                    r=[f"atT{kc // 4}", "ident"], w=[f"pt{pti}"])
                base = kc * NHG * 65 + (2 * j) * 65
                dstv = VH[:, base:base + 130].rearrange("p (h m) -> p h m", h=2)[:, :, 0:64]
                p.v(lambda e, dstv=dstv, pti=pti: e.tensor_copy(
                    dstv, st.pt[pti][:, 0:128].rearrange("p (h m) -> p h m", h=2)),
                    r=[f"pt{pti}"], w=[f"VH{kc}_{j}"])
        if grp == 0:
            for kc in range(NT):
                on = VH[:, kc * NHG * 65:(kc + 1) * NHG * 65].rearrange(
                    "p (h m) -> p h m", h=NHG)[:, :, 64:65]
                p.v(lambda e, on=on: e.memset(on, 1.0), w=[f"VHo{kc}"])

        p.v(lambda e: e.memset(nrm[:, 0:32], 0.0), w=["nrm"])
        for j in range(NHG // 2):
            for wi, (src, srck) in enumerate(((qT, "qT"), (kT, "kT"))):
                for qb in range(NQB):
                    p.s(lambda e, src=src, j=j, qb=qb: e.activation(
                        out=st.sq[:, 0:512],
                        in_=src[:, j * T + qb * QB: j * T + qb * QB + QB],
                        func=AF.Square),
                        r=[f"{srck}{j}"], w=["htok0"])
                    for hio in range(2):
                        base = hio * 64
                        idx = wi * 16 + 2 * j + hio
                        p.t(lambda e, base=base: e.matmul(
                            st.pb[2][0:1, 0:512], st.onescol[base:base + 64, 0:1],
                            st.sq[base:base + 64, 0:512], start=True, stop=True),
                            r=["htok0", "onescol"], w=["pb2"])
                        p.v(lambda e: e.reduce_max(nrm[0:1, 40:41],
                                                   st.pb[2][0:1, 0:512], axis=AX.X),
                            r=["pb2"], w=["nrms"])
                        p.v(lambda e, idx=idx: e.tensor_max(
                            nrm[0:1, idx:idx + 1], nrm[0:1, idx:idx + 1],
                            nrm[0:1, 40:41]),
                            r=["nrms", "nrm"], w=["nrm"])
        p.v(lambda e: e.tensor_tensor(nrm[0:1, 32:32 + NHG], nrm[0:1, 0:NHG],
                                      nrm[0:1, 16:16 + NHG], ALU.mult),
            r=["nrm"], w=["nrmm"])
        p.s(lambda e: e.activation(out=nrm[0:1, 48:48 + NHG], in_=nrm[0:1, 32:32 + NHG],
                                   func=AF.Sqrt),
            r=["nrmm"], w=["nrmsq"])
        p.v(lambda e: e.tensor_scalar_mul(nrm[0:1, 56:56 + NHG], nrm[0:1, 48:48 + NHG],
                                          -0.125),
            r=["nrmsq"], w=["nrmneg"])
        p.t(lambda e: e.matmul(st.pb[2][0:128, 0:NHG], st.onesrow_f[0:1, 0:128],
                               nrm[0:1, 56:56 + NHG], start=True, stop=True),
            r=["nrmneg", "onesrowf"], w=["pb2"])
        p.v(lambda e: e.tensor_copy(negM[:, 0:NHG], st.pb[2][0:128, 0:NHG]),
            r=["pb2"], w=["negM"])

        for hidx in range(NHG):
            j, hio = hidx // 2, hidx % 2
            base = hio * 64
            h = h0 + hidx
            dc, rrow_off = h // 2, (h % 2) * 64
            for qb in range(NQB):
                for kg in range(4):
                    for ki in range(4):
                        kc = kg * 4 + ki
                        pbs = ki % 2
                        p.t(lambda e, j=j, base=base, kc=kc, qb=qb, pbs=pbs: e.matmul(
                            st.pb[pbs][:, 0:512],
                            kT[base:base + 64, j * T + kc * 128: j * T + kc * 128 + 128],
                            qT[base:base + 64, j * T + qb * QB: j * T + qb * QB + QB],
                            start=True, stop=True),
                            r=[f"kT{j}", f"qT{j}"], w=[f"pb{pbs}"])
                        p.s(lambda e, ki=ki, pbs=pbs, hidx=hidx: e.activation(
                            out=atT[:, ki * 512:ki * 512 + 512], in_=st.pb[pbs][:, 0:512],
                            func=AF.Exp, scale=0.125, bias=negM[:, hidx:hidx + 1]),
                            r=[f"pb{pbs}", "negM"], w=[f"atT{ki}"])
                    for ki in range(4):
                        kc = kg * 4 + ki
                        p.t(lambda e, kc=kc, ki=ki, hidx=hidx,
                            s0=(kg == 0 and ki == 0), s1=(kg == 3 and ki == 3): e.matmul(
                            st.pb[3][0:65, 0:512],
                            VH[:, kc * NHG * 65 + hidx * 65: kc * NHG * 65 + hidx * 65 + 65],
                            atT[:, ki * 512:ki * 512 + 512],
                            start=s0, stop=s1),
                            r=[f"atT{ki}", f"VH{kc}_{hidx // 2}", f"VHo{kc}"],
                            w=["pb3"])
                p.v(lambda e: e.reciprocal(rrow[0:1, 0:512], st.pb[3][64:65, 0:512]),
                    r=["pb3"], w=["rrow"])
                p.t(lambda e: e.matmul(st.pb[2][0:64, 0:512], st.onesrow_f[0:1, 0:64],
                                       rrow[0:1, 0:512], start=True, stop=True),
                    r=["rrow", "onesrowf"], w=["pb2"])
                p.s(lambda e: e.copy(rb[0:64, 0:512], st.pb[2][0:64, 0:512]),
                    r=["pb2"], w=["rb"])
                dst = aoT[rrow_off:rrow_off + 64,
                          (qb * ND + dc) * 512:(qb * ND + dc) * 512 + 512]
                p.v(lambda e, dst=dst: e.tensor_tensor(
                    dst, st.pb[3][0:64, 0:512], rb[0:64, 0:512], ALU.mult),
                    r=["pb3", "rb"], w=[f"aoT{qb}_{dc}_{rrow_off}"])

    st.dbg_l0 = dict(qT=qT, kT=kT, VH=VH, aoT=aoT, negM=negM, nrm=nrm, atT=atT)
    wo, wok = load_w(p, st, "l0_wo", hb["L0_attn_W_o"], D, D)
    for mi in range(ND):
        for qb in range(NQB):
            pbm = (mi * NQB + qb) % 2
            for kc in range(ND):
                p.t(lambda e, mi=mi, qb=qb, kc=kc, pbm=pbm: e.matmul(
                    st.pb[pbm][:, 0:512],
                    wo[:, kc * D + mi * 128: kc * D + mi * 128 + 128],
                    aoT[:, (qb * ND + kc) * 512: (qb * ND + kc) * 512 + 512],
                    start=(kc == 0), stop=False),
                    r=[wok, f"aoT{qb}_{kc}_0", f"aoT{qb}_{kc}_64"], w=[f"pb{pbm}"])
            p.t(lambda e, mi=mi, pbm=pbm: e.matmul(
                st.pb[pbm][:, 0:512], bo_row[0:1, mi * 128:mi * 128 + 128],
                st.ones512[0:1, 0:512], start=False, stop=True),
                r=["r_l0bo", "ones512"], w=[f"pb{pbm}"])
            emit_back_tile(p, st, f"pb{pbm}", st.pb[pbm][:, 0:512], mi, qb)


def emit_l0_ffn(p, st):
    nc, hb = st.nc, st.hbm
    F = 4096
    emit_ln_stats(p, st)
    p.v(lambda e: e.tensor_scalar(lane(st, 5), lane(st, 6), 1e-5, None, ALU.add),
        r=["st6"], w=["st5"])
    p.v(lambda e: e.reciprocal(lane(st, 5), lane(st, 5)), r=["st5"], w=["st5"])
    p.v(lambda e: e.tensor_tensor(lane(st, 5), lane(st, 6), lane(st, 5), ALU.mult),
        r=["st5", "st6"], w=["st5"])
    p.s(lambda e: e.activation(out=lane(st, 5), in_=lane(st, 5), func=AF.Sqrt,
                               bias=st.eps[:, 0:1], scale=1.0),
        r=["st5", "eps"], w=["st5"])
    p.v(lambda e: e.reciprocal(lane(st, 6), lane(st, 5)), r=["st5"], w=["st6"])
    p.v(lambda e: e.tensor_tensor(lane(st, 5), lane(st, 2), lane(st, 6), ALU.mult),
        r=["st2", "st6"], w=["st5"])
    p.v(lambda e: e.scalar_tensor_tensor(lane(st, 6), lane(st, 4), -1.0, lane(st, 5),
                                         ALU.mult, ALU.mult),
        r=["st4", "st5"], w=["st6"])
    p.v(lambda e: e.tensor_scalar(lane(st, 7), lane(st, 2), 1.0, None, ALU.add),
        r=["st2"], w=["st7"])

    gcol = nc.alloc_sbuf_tensor("l0f_g", [128, 16], F32)
    for i, gk in enumerate(("L0_ffn_ln_gamma", "L0_ffn_ln_beta")):
        p.dma("dq_y", st.zslice[0:1, 0:D], hb[gk].rearrange("(o n) -> o n", o=1),
              w=["zs0", "zs1"])
        for dcg in range(ND):
            pti = dcg % 2
            p.t(lambda e, dcg=dcg, pti=pti: e.transpose(
                st.pb[4 + pti][0:128, 0:1],
                st.zslice[0:1, dcg * 128:dcg * 128 + 128], st.identf[0:1, 0:1]),
                r=["zs0", "zs1", "identf"], w=[f"pb{4 + pti}"])
            p.v(lambda e, i=i, dcg=dcg, pti=pti: e.tensor_copy(
                gcol[:, i * 8 + dcg:i * 8 + dcg + 1], st.pb[4 + pti][0:128, 0:1]),
                r=[f"pb{4 + pti}"], w=["gcol"])

    emit_ln_to_ht(p, st, xupdate=True, rlane=5, blane=6,
                  gamma_col=lambda dc: gcol[:, dc:dc + 1],
                  beta_col=lambda dc: gcol[:, 8 + dc:8 + dc + 1], gb_res=("gcol",))

    b2row, _ = load_row(p, st, "l0fb2", hb["L0_ffn_fc2_bias"], D)
    b1row, _ = load_row(p, st, "l0fb1", hb["L0_ffn_fc1_bias"], F)

    ybuf = nc.alloc_sbuf_tensor("l0f_y", [128, 32 * 512], BF16)
    w1d = nc.alloc_sbuf_tensor("l0f_w1d", [128, 2 * D], BF16)
    w1q = nc.alloc_sbuf_tensor("l0f_w1q", [128, 2 * D], BF16)
    s1b = nc.alloc_sbuf_tensor("l0f_s1b", [128, 2 * D], BF16)
    w2d = nc.alloc_sbuf_tensor("l0f_w2d", [128, 2 * D], BF16)
    w2q = nc.alloc_sbuf_tensor("l0f_w2q", [128, 2 * D], BF16)
    s2b = nc.alloc_sbuf_tensor("l0f_s2b", [128, 2 * D], BF16)

    for qb in range(NQB):
        for oc in range(32):
            sl = oc % 2
            p.dma("dq_g", w1q[:, sl * D:sl * D + D]
                  .rearrange("p (c m) -> p c m", c=8),
                  hb["L0_ffn_fc1_q"][:, oc * 128:(oc + 1) * 128]
                  .rearrange("(c p) m -> p c m", p=128),
                  w=[f"w1q{sl}"])
            p.dma("dq_g", s1b[:, sl * D:sl * D + D]
                  .rearrange("p (g m) -> p g m", g=8),
                  hb["L0_ffn_fc1_scales"][:, oc * 128:(oc + 1) * 128]
                  .rearrange("(o g) m -> o g m", o=1).broadcast_to((128, 8, 128)),
                  w=[f"s1b{sl}"])
            p.v(lambda e, sl=sl: e.scalar_tensor_tensor(
                w1d[:, sl * D:sl * D + D], w1q[:, sl * D:sl * D + D], -128.0,
                s1b[:, sl * D:sl * D + D], ALU.add, ALU.mult),
                r=[f"w1q{sl}", f"s1b{sl}"], w=[f"w1d{sl}"])
            pbm = oc % 2
            for kc in range(ND):
                p.t(lambda e, sl=sl, kc=kc, qb=qb, pbm=pbm: e.matmul(
                    st.pb[pbm][:, 0:512], w1d[:, sl * D + kc * 128:sl * D + kc * 128 + 128],
                    ht_c(st, kc)[:, qb * QB:qb * QB + QB],
                    start=(kc == 0), stop=False),
                    r=[f"w1d{sl}", f"ht{kc}"], w=[f"pb{pbm}"])
            p.t(lambda e, oc=oc, pbm=pbm: e.matmul(
                st.pb[pbm][:, 0:512], b1row[0:1, oc * 128:oc * 128 + 128],
                st.ones512[0:1, 0:512], start=False, stop=True),
                r=["r_l0fb1", "ones512"], w=[f"pb{pbm}"])
            p.s(lambda e, oc=oc, pbm=pbm: e.activation(
                out=ybuf[:, oc * 512:oc * 512 + 512], in_=st.pb[pbm][:, 0:512],
                func=AF.Silu),
                r=[f"pb{pbm}"], w=[f"yb{oc}"])
        pairs = [(tc, dh) for tc in range(4) for dh in range(2)]
        for wv in range(2):
            wave = pairs[wv * 4:(wv + 1) * 4]
            for kc in range(32):
                sl = kc % 2
                if True:
                    p.dma("dq_g", w2q[:, sl * D:sl * D + D],
                          hb["L0_ffn_fc2_q"][kc * 128:(kc + 1) * 128, :],
                          w=[f"w2q{sl}"])
                    p.dma("dq_g", s2b[:, sl * D:sl * D + D],
                          hb["L0_ffn_fc2_scales"][kc:kc + 1, :].broadcast_to((128, D)),
                          w=[f"s2b{sl}"])
                    p.v(lambda e, sl=sl: e.scalar_tensor_tensor(
                        w2d[:, sl * D:sl * D + D], w2q[:, sl * D:sl * D + D], -128.0,
                        s2b[:, sl * D:sl * D + D], ALU.add, ALU.mult),
                        r=[f"w2q{sl}", f"s2b{sl}"], w=[f"w2d{sl}", f"w2dd{kc}"])
                for i, (tc, dh) in enumerate(wave):
                    p.t(lambda e, kc=kc, sl=sl, tc=tc, dh=dh, i=i: e.matmul(
                        st.pb[i][:, 0:512],
                        ybuf[:, kc * 512 + tc * 128: kc * 512 + tc * 128 + 128],
                        w2d[:, sl * D + dh * 512: sl * D + dh * 512 + 512],
                        start=(kc == 0), stop=False),
                        r=[f"w2d{sl}", f"w2dd{kc}", f"yb{kc}"], w=[f"pb{i}"])
            for i, (tc, dh) in enumerate(wave):
                p.t(lambda e, dh=dh, i=i: e.matmul(
                    st.pb[i][:, 0:512], st.ones512[0:1, 0:128],
                    b2row[0:1, dh * 512:dh * 512 + 512], start=False, stop=True),
                    r=["r_l0fb2", "ones512"], w=[f"pb{i}"])
                tt = qb * 4 + tc
                dst = x_t(st, tt)[:, dh * 512:dh * 512 + 512]
                p.v(lambda e, dst=dst, i=i: e.scalar_tensor_tensor(
                    dst, st.pb[i][:, 0:512], 0.5, dst, ALU.mult, ALU.add),
                    r=[f"pb{i}", f"x{tt}"], w=[f"x{tt}"])


def emit_l1_attn(p, st):
    nc, hb = st.nc, st.hbm
    emit_ln_stats(p, st)
    emit_ln_to_ht(p, st)

    qT = nc.alloc_sbuf_tensor("l1_qT", [128, ND * T], BF16)
    kT = nc.alloc_sbuf_tensor("l1_kT", [128, ND * T], BF16)
    VH = st.ht
    atT = nc.alloc_sbuf_tensor("l1_atT", [128, 16 * 512], BF16)
    wbuf = nc.alloc_sbuf_tensor("l1_wb", [128, 2 * ND * 128], BF16)
    cm = [nc.alloc_sbuf_tensor(f"l1_cm{i}", [128, 512], BF16) for i in range(4)]
    for i in range(4):
        p.dma("dq_y", cm[i][:], hb[f"c_cmask{i}"], w=[f"cm{i}"])
    vdram = nc.dram_tensor("l1_vdram", [D, T], BF16, kind="Internal").ap()
    rrow = nc.alloc_sbuf_tensor("l1_rr", [1, 512], F32)
    rcol = nc.alloc_sbuf_tensor("l1_rc", [128, 4], F32)
    nrm = nc.alloc_sbuf_tensor("l1_nrm", [1, 64], F32)
    negM = nc.alloc_sbuf_tensor("l1_negM", [128, 1], F32)

    for wi, (wname, dstT, dk) in enumerate((("L1_attn_Wq", qT, "l1q"),
                                            ("L1_attn_Wk", kT, "l1k"))):
        for mi in range(ND):
            sl = mi % 2
            p.dma("dq_g", wbuf[:, sl * ND * 128:(sl + 1) * ND * 128]
                  .rearrange("p (c m) -> p c m", c=ND),
                  hb[wname][:, mi * 128:(mi + 1) * 128]
                  .rearrange("(c p) m -> p c m", p=128),
                  w=[f"l1wb{sl}"])
            for qb in range(NQB):
                pbm = (mi * NQB + qb) % 2
                for kc in range(ND):
                    p.t(lambda e, sl=sl, kc=kc, qb=qb, pbm=pbm: e.matmul(
                        st.pb[pbm][:, 0:512],
                        wbuf[:, sl * ND * 128 + kc * 128: sl * ND * 128 + kc * 128 + 128],
                        ht_c(st, kc)[:, qb * QB:qb * QB + QB],
                        start=(kc == 0), stop=(kc == ND - 1)),
                        r=[f"l1wb{sl}", f"ht{kc}"], w=[f"pb{pbm}"])
                p.s(lambda e, dstT=dstT, mi=mi, qb=qb, pbm=pbm: e.copy(
                    dstT[:, mi * T + qb * QB: mi * T + qb * QB + QB],
                    st.pb[pbm][:, 0:512]),
                    r=[f"pb{pbm}"], w=[f"{dk}{mi}"])
    for mi in range(ND):
        sl = mi % 2
        p.dma("dq_g", wbuf[:, sl * ND * 128:(sl + 1) * ND * 128]
              .rearrange("p (c m) -> p c m", c=ND),
              hb["L1_attn_Wv"][:, mi * 128:(mi + 1) * 128]
              .rearrange("(c p) m -> p c m", p=128),
              w=[f"l1wb{sl}"])
        for qb in range(NQB):
            pbm = (mi * NQB + qb) % 2
            for kc in range(ND):
                p.t(lambda e, sl=sl, kc=kc, qb=qb, pbm=pbm: e.matmul(
                    st.pb[pbm][:, 0:512],
                    wbuf[:, sl * ND * 128 + kc * 128: sl * ND * 128 + kc * 128 + 128],
                    ht_c(st, kc)[:, qb * QB:qb * QB + QB],
                    start=(kc == 0), stop=(kc == ND - 1)),
                    r=[f"l1wb{sl}", f"ht{kc}"], w=[f"pb{pbm}"])
            vsl = (mi * NQB + qb) % 2
            p.s(lambda e, vsl=vsl, pbm=pbm: e.copy(
                st.zslice[:, vsl * 512:vsl * 512 + 512].bitcast(BF16)[:, 0:512]
                if False else atT[:, vsl * 512:vsl * 512 + 512],
                st.pb[pbm][:, 0:512]),
                r=[f"pb{pbm}"], w=[f"atT{vsl}"])
            p.dma("dq_y", vdram[mi * 128:(mi + 1) * 128, qb * QB:qb * QB + QB],
                  atT[:, vsl * 512:vsl * 512 + 512], r=[f"atT{vsl}"],
                  w=[f"vd{mi}_{qb}"])
    for kc in range(NT):
        p.dma("dq_y", st.ht[:, kc * D:(kc + 1) * D],
              vdram[:, kc * 128:(kc + 1) * 128],
              r=[f"vd{mi}_{qb}" for mi in range(ND) for qb in range(NQB)],
              w=[f"vT{kc}", f"ht{kc // 2}"], transpose=True)

    p.v(lambda e: e.memset(nrm[:, 0:2], 0.0), w=["nrm"])
    for wi, (src, srck) in enumerate(((qT, "l1q"), (kT, "l1k"))):
        for qb in range(NQB):
            for dc in range(ND):
                p.s(lambda e, src=src, dc=dc, qb=qb: e.activation(
                    out=st.sq[:, 0:512],
                    in_=src[:, dc * T + qb * QB: dc * T + qb * QB + QB],
                    func=AF.Square),
                    r=[f"{srck}{dc}"], w=["htok0"])
                p.t(lambda e, dc=dc: e.matmul(
                    st.pb[2][0:1, 0:512], st.onescol[0:128, 0:1],
                    st.sq[:, 0:512], start=(dc == 0), stop=(dc == ND - 1)),
                    r=["htok0", "onescol"], w=["pb2"])
            p.v(lambda e: e.reduce_max(nrm[0:1, 40:41], st.pb[2][0:1, 0:512],
                                       axis=AX.X),
                r=["pb2"], w=["nrms"])
            p.v(lambda e, wi=wi: e.tensor_max(nrm[0:1, wi:wi + 1],
                                              nrm[0:1, wi:wi + 1], nrm[0:1, 40:41]),
                r=["nrms", "nrm"], w=["nrm"])
    p.v(lambda e: e.tensor_tensor(nrm[0:1, 2:3], nrm[0:1, 0:1], nrm[0:1, 1:2],
                                  ALU.mult), r=["nrm"], w=["nrmm"])
    p.s(lambda e: e.activation(out=nrm[0:1, 3:4], in_=nrm[0:1, 2:3], func=AF.Sqrt),
        r=["nrmm"], w=["nrmsq"])
    p.v(lambda e: e.tensor_scalar_mul(nrm[0:1, 4:5], nrm[0:1, 3:4], -1.0 / 32.0),
        r=["nrmsq"], w=["nrmneg"])
    p.t(lambda e: e.matmul(st.pb[2][0:128, 0:1], st.onesrow_f[0:1, 0:128],
                           nrm[0:1, 4:5], start=True, stop=True),
        r=["nrmneg", "onesrowf"], w=["pb2"])
    p.v(lambda e: e.tensor_copy(negM[:, 0:1], st.pb[2][0:128, 0:1]),
        r=["pb2"], w=["negM"])

    for qb in range(NQB):
        nkc = qb * 4 + 4
        for kc in range(nkc):
            pbs = kc % 2
            for dc in range(ND):
                p.t(lambda e, dc=dc, kc=kc, qb=qb, pbs=pbs: e.matmul(
                    st.pb[pbs][:, 0:512],
                    kT[:, dc * T + kc * 128: dc * T + kc * 128 + 128],
                    qT[:, dc * T + qb * QB: dc * T + qb * QB + QB],
                    start=(dc == 0), stop=(dc == ND - 1)),
                    r=[f"l1k{dc}", f"l1q{dc}"], w=[f"pb{pbs}"])
            p.s(lambda e, kc=kc, pbs=pbs: e.activation(
                out=atT[:, kc * 512:kc * 512 + 512], in_=st.pb[pbs][:, 0:512],
                func=AF.Exp, scale=1.0 / 32.0, bias=negM[:, 0:1]),
                r=[f"pb{pbs}", "negM"], w=[f"atT{kc}"])
            if kc >= qb * 4:
                off = kc - qb * 4
                p.v(lambda e, kc=kc, off=off: e.tensor_tensor(
                    atT[:, kc * 512:kc * 512 + 512], atT[:, kc * 512:kc * 512 + 512],
                    cm[off][:], ALU.mult),
                    r=[f"atT{kc}", f"cm{off}"], w=[f"atT{kc}"])
        for kc in range(nkc):
            p.t(lambda e, kc=kc, s0=(kc == 0), s1=(kc == nkc - 1): e.matmul(
                st.pb[2][0:1, 0:512], st.onescol[0:128, 0:1],
                atT[:, kc * 512:kc * 512 + 512], start=s0, stop=s1),
                r=[f"atT{kc}", "onescol"], w=["pb2"])
        p.v(lambda e: e.reciprocal(rrow[0:1, 0:512], st.pb[2][0:1, 0:512]),
            r=["pb2"], w=["rrow"])
        for j in range(4):
            p.t(lambda e, j=j: e.transpose(
                st.pb[3][0:128, 0:1], rrow[0:1, j * 128:j * 128 + 128],
                st.identf[0:1, 0:1]),
                r=["rrow", "identf"], w=["pb3"])
            p.v(lambda e, j=j: e.tensor_copy(rcol[:, j:j + 1], st.pb[3][0:128, 0:1]),
                r=["pb3"], w=["rcol"])
        for mi in range(ND):
            pbm = mi % 2
            for kc in range(nkc):
                p.t(lambda e, kc=kc, mi=mi, s0=(kc == 0), s1=(kc == nkc - 1),
                    pbm=pbm: e.matmul(
                    st.pb[pbm][:, 0:512],
                    st.ht[:, kc * D + mi * 128: kc * D + mi * 128 + 128],
                    atT[:, kc * 512:kc * 512 + 512], start=s0, stop=s1),
                    r=[f"vT{kc}", f"atT{kc}"], w=[f"pb{pbm}"])
            emit_back_tile(p, st, f"pb{pbm}", st.pb[pbm][:, 0:512], mi, qb,
                           ptok=lambda tc, qb=qb: rcol[:, tc - qb * 4:tc - qb * 4 + 1],
                           ptok_res=("rcol",))


def emit_l1_ffn(p, st):
    nc, hb = st.nc, st.hbm
    E, HID = 8, 512
    emit_ln_stats(p, st)
    emit_ln_to_ht(p, st)

    rwn = nc.alloc_sbuf_tensor("l1f_rwn", [8, D], BF16)
    rwT = nc.alloc_sbuf_tensor("l1f_rwT", [128, ND * 8], BF16)
    p.dma("dq_g", rwn[:], hb["L1_ffn_routing_W"], w=["rwn"])
    for dc in range(ND):
        pti = dc % 2
        p.t(lambda e, dc=dc, pti=pti: e.transpose(
            st.pt[pti][0:128, 0:8], rwn[:, dc * 128:(dc + 1) * 128], st.ident[0:8, 0:8]),
            r=["rwn", "ident"], w=[f"pt{pti}"])
        p.v(lambda e, dc=dc, pti=pti: e.tensor_copy(
            rwT[:, dc * 8:(dc + 1) * 8], st.pt[pti][0:128, 0:8]),
            r=[f"pt{pti}"], w=["rwT"])
    ebrow, _ = load_row(p, st, "l1feb", hb["L1_ffn_expert_bias"], E)
    logT = nc.alloc_sbuf_tensor("l1f_logT", [8, T], BF16)
    actT = nc.alloc_sbuf_tensor("l1f_actT", [8, T], BF16)
    acts = nc.alloc_sbuf_tensor("l1f_acts", [128, NT * 8], F32)
    cft = nc.alloc_sbuf_tensor("l1f_cft", [128, NT * 8], BF16)
    cfT = nc.alloc_sbuf_tensor("l1f_cfT", [8, T], BF16)
    ones8 = nc.alloc_sbuf_tensor("l1f_o8", [8, 1], F32)
    p.v(lambda e: e.memset(ones8[:], 1.0), w=["ones8"])
    sel8 = nc.alloc_sbuf_tensor("l1f_sel8", [8, 8 * 128], BF16)
    p.dma("dq_y", sel8[:], hb["c_sel8"], w=["sel8"])

    for qb in range(NQB):
        pbm = qb % 2
        for dc in range(ND):
            p.t(lambda e, dc=dc, qb=qb, pbm=pbm: e.matmul(
                st.pb[pbm][0:8, 0:512], rwT[:, dc * 8:(dc + 1) * 8],
                ht_c(st, dc)[:, qb * QB:qb * QB + QB],
                start=(dc == 0), stop=False),
                r=["rwT", f"ht{dc}"], w=[f"pb{pbm}"])
        p.t(lambda e, qb=qb, pbm=pbm: e.matmul(
            st.pb[pbm][0:8, 0:512], ebrow[0:1, 0:8], st.ones512[0:1, 0:512],
            start=False, stop=True),
            r=["r_l1feb", "ones512"], w=[f"pb{pbm}"])
        p.s(lambda e, qb=qb, pbm=pbm: e.copy(
            logT[:, qb * QB:qb * QB + QB], st.pb[pbm][0:8, 0:512]),
            r=[f"pb{pbm}"], w=["logT"])
    p.s(lambda e: e.activation(out=actT[:], in_=logT[:], func=AF.Abs),
        r=["logT"], w=["actT"])
    p.s(lambda e: e.activation(out=actT[:], in_=actT[:], func=AF.Exp, scale=-1.0),
        r=["actT"], w=["actT"])
    p.s(lambda e: e.activation(out=actT[:], in_=actT[:], func=AF.Ln,
                               bias=ones8[:, 0:1], scale=1.0),
        r=["actT", "ones8"], w=["actT"])
    p.v(lambda e: e.tensor_scalar_max(logT[:], logT[:], 0.0),
        r=["logT"], w=["logT"])
    p.v(lambda e: e.tensor_add(actT[:], actT[:], logT[:]),
        r=["actT", "logT"], w=["actT"])
    p.s(lambda e: e.activation(out=actT[:], in_=actT[:], func=AF.Sqrt),
        r=["actT"], w=["actT"])
    for tc in range(NT):
        pti = tc % 2
        p.t(lambda e, tc=tc, pti=pti: e.transpose(
            st.pt[pti][0:128, 0:8], actT[:, tc * 128:(tc + 1) * 128],
            st.ident[0:8, 0:8]),
            r=["actT", "ident"], w=[f"pt{pti}"])
        p.v(lambda e, tc=tc, pti=pti: e.tensor_copy(
            acts[:, tc * 8:(tc + 1) * 8], st.pt[pti][0:128, 0:8]),
            r=[f"pt{pti}"], w=["acts"])
    for tc in range(NT):
        a = acts[:, tc * 8:(tc + 1) * 8]
        c = cft[:, tc * 8:(tc + 1) * 8]
        m1 = lane(st, 5, tc)
        m2 = lane(st, 6, tc)
        p.v(lambda e, a=a, m1=m1: e.reduce_max(m1, a, axis=AX.X),
            r=["acts"], w=["st5"])
        p.v(lambda e, a=a, c=c, m1=m1: e.tensor_scalar(c, a, m1, None, ALU.is_ge),
            r=["acts", "st5"], w=["cft"])
        p.v(lambda e, a=a, c=c: e.scalar_tensor_tensor(c, c, -30000.0, a,
                                                       ALU.mult, ALU.add),
            r=["cft", "acts"], w=["cft"])
        p.v(lambda e, c=c, m2=m2: e.reduce_max(m2, c, axis=AX.X),
            r=["cft"], w=["st6"])
        p.v(lambda e, a=a, c=c, m2=m2: e.tensor_scalar(c, a, m2, None, ALU.is_ge),
            r=["acts", "st6"], w=["cft"])
        p.v(lambda e, a=a, c=c: e.tensor_tensor(c, c, a, ALU.mult),
            r=["cft", "acts"], w=["cft"])
    for tc in range(NT):
        pti = tc % 2
        p.t(lambda e, tc=tc, pti=pti: e.transpose(
            st.pt[pti][0:8, 0:128], cft[:, tc * 8:(tc + 1) * 8], st.ident[:]),
            r=["cft", "ident"], w=[f"pt{pti}"])
        p.v(lambda e, tc=tc, pti=pti: e.tensor_copy(
            cfT[:, tc * 128:(tc + 1) * 128], st.pt[pti][0:8, 0:128]),
            r=[f"pt{pti}"], w=["cfT"])

    sw1 = nc.alloc_sbuf_tensor("l1f_sw1", [128, ND * HID], BF16)
    sw2 = nc.alloc_sbuf_tensor("l1f_sw2", [128, 4 * D], BF16)
    p.dma("dq_g", sw1[:].rearrange("p (c m) -> p c m", c=ND),
          hb["L1_ffn_sW1"].rearrange("(c p) m -> p c m", p=128), w=["sw1"])
    p.dma("dq_g", sw2[:].rearrange("p (c m) -> p c m", c=4),
          hb["L1_ffn_sW2"].rearrange("(c p) m -> p c m", p=128), w=["sw2"])
    sb1row, _ = load_row(p, st, "l1fsb1", hb["L1_ffn_sb1"], HID)
    sb2row, _ = load_row(p, st, "l1fsb2", hb["L1_ffn_sb2"], D)
    rb1row, _ = load_row(p, st, "l1frb1", hb["L1_ffn_rb1"].rearrange("e h -> (e h)"),
                         E * HID)
    rb2n = nc.alloc_sbuf_tensor("l1f_rb2", [8, D], BF16)
    p.dma("dq_g", rb2n[:], hb["L1_ffn_rb2"], w=["rb2n"])
    rw1 = nc.alloc_sbuf_tensor("l1f_rw1", [128, 2 * ND * HID], BF16)
    rw2 = nc.alloc_sbuf_tensor("l1f_rw2", [128, 2 * 4 * 128], BF16)

    hs = nc.alloc_sbuf_tensor("l1f_hs", [128, 4 * 512], BF16)
    hdn = nc.alloc_sbuf_tensor("l1f_hdn", [128, E * 4 * 512], BF16)
    cfb = nc.alloc_sbuf_tensor("l1f_cfb", [128, 512], BF16)

    for qb in range(NQB):
        for hc in range(4):
            pbm = hc % 2
            for kc in range(ND):
                p.t(lambda e, hc=hc, kc=kc, qb=qb, pbm=pbm: e.matmul(
                    st.pb[pbm][:, 0:512], sw1[:, kc * HID + hc * 128: kc * HID + hc * 128 + 128],
                    ht_c(st, kc)[:, qb * QB:qb * QB + QB],
                    start=(kc == 0), stop=False),
                    r=["sw1", f"ht{kc}"], w=[f"pb{pbm}"])
            p.t(lambda e, hc=hc, pbm=pbm: e.matmul(
                st.pb[pbm][:, 0:512], sb1row[0:1, hc * 128:hc * 128 + 128],
                st.ones512[0:1, 0:512], start=False, stop=True),
                r=["r_l1fsb1", "ones512"], w=[f"pb{pbm}"])
            p.s(lambda e, hc=hc, pbm=pbm: e.activation(
                out=hs[:, hc * 512:hc * 512 + 512], in_=st.pb[pbm][:, 0:512],
                func=AF.Silu),
                r=[f"pb{pbm}"], w=[f"hs{hc}"])
        for ei in range(E):
            sl = ei % 2
            p.dma("dq_g", rw1[:, sl * ND * HID:(sl + 1) * ND * HID]
                  .rearrange("p (c m) -> p c m", c=ND),
                  hb["L1_ffn_rW1"][ei].rearrange("(c p) m -> p c m", p=128),
                  w=[f"rw1{sl}"])
            p.t(lambda e, ei=ei, qb=qb: e.matmul(
                st.pb[4][0:128, 0:512], sel8[:, ei * 128:(ei + 1) * 128],
                cfT[0:8, qb * QB:qb * QB + QB],
                start=True, stop=True),
                r=["cfT", "sel8"], w=["pb4"])
            p.v(lambda e: e.tensor_copy(cfb[:], st.pb[4][0:128, 0:512]),
                r=["pb4"], w=["cfb"])
            for hc in range(4):
                pbm = 2 + hc % 2
                for kc in range(ND):
                    p.t(lambda e, sl=sl, hc=hc, kc=kc, qb=qb, pbm=pbm: e.matmul(
                        st.pb[pbm][:, 0:512],
                        rw1[:, sl * ND * HID + kc * HID + hc * 128:
                            sl * ND * HID + kc * HID + hc * 128 + 128],
                        ht_c(st, kc)[:, qb * QB:qb * QB + QB],
                        start=(kc == 0), stop=False),
                        r=[f"rw1{sl}", f"ht{kc}"], w=[f"pb{pbm}"])
                p.t(lambda e, ei=ei, hc=hc, pbm=pbm: e.matmul(
                    st.pb[pbm][:, 0:512],
                    rb1row[0:1, ei * HID + hc * 128: ei * HID + hc * 128 + 128],
                    st.ones512[0:1, 0:512], start=False, stop=True),
                    r=["r_l1frb1", "ones512"], w=[f"pb{pbm}"])
                p.s(lambda e, ei=ei, hc=hc, pbm=pbm: e.activation(
                    out=hdn[:, (ei * 4 + hc) * 512:(ei * 4 + hc) * 512 + 512],
                    in_=st.pb[pbm][:, 0:512], func=AF.Silu),
                    r=[f"pb{pbm}"], w=[f"hdn{ei}_{hc}"])
                p.v(lambda e, ei=ei, hc=hc: e.tensor_tensor(
                    hdn[:, (ei * 4 + hc) * 512:(ei * 4 + hc) * 512 + 512],
                    hdn[:, (ei * 4 + hc) * 512:(ei * 4 + hc) * 512 + 512],
                    cfb[:], ALU.mult),
                    r=[f"hdn{ei}_{hc}", "cfb"], w=[f"hdn{ei}_{hc}"])
        for mi in range(ND):
            pbm = mi % 2
            for kc in range(4):
                p.t(lambda e, kc=kc, mi=mi, qb=qb, pbm=pbm: e.matmul(
                    st.pb[pbm][:, 0:512], sw2[:, kc * D + mi * 128: kc * D + mi * 128 + 128],
                    hs[:, kc * 512:kc * 512 + 512],
                    start=(kc == 0), stop=False),
                    r=["sw2", f"hs{kc}"], w=[f"pb{pbm}"])
            p.t(lambda e, mi=mi, pbm=pbm: e.matmul(
                st.pb[pbm][:, 0:512], sb2row[0:1, mi * 128:mi * 128 + 128],
                st.ones512[0:1, 0:512], start=False, stop=False),
                r=["r_l1fsb2", "ones512"], w=[f"pb{pbm}"])
            p.t(lambda e, mi=mi, qb=qb, pbm=pbm: e.matmul(
                st.pb[pbm][:, 0:512], rb2n[:, mi * 128:mi * 128 + 128],
                cfT[:, qb * QB:qb * QB + QB], start=False, stop=False),
                r=["rb2n", "cfT"], w=[f"pb{pbm}"])
            for ei in range(E):
                sl2 = ei % 2
                p.dma("dq_g", rw2[:, sl2 * 512:(sl2 + 1) * 512]
                      .rearrange("p (c m) -> p c m", c=4),
                      hb["L1_ffn_rW2"][ei][:, mi * 128:(mi + 1) * 128]
                      .rearrange("(c p) m -> p c m", p=128),
                      w=[f"rw2{sl2}"])
                for kc in range(4):
                    p.t(lambda e, sl2=sl2, kc=kc, ei=ei, s1=(ei == E - 1 and kc == 3),
                        pbm=pbm: e.matmul(
                        st.pb[pbm][:, 0:512],
                        rw2[:, sl2 * 512 + kc * 128: sl2 * 512 + kc * 128 + 128],
                        hdn[:, (ei * 4 + kc) * 512:(ei * 4 + kc) * 512 + 512],
                        start=False, stop=s1),
                        r=[f"rw2{sl2}", f"hdn{ei}_{kc}"], w=[f"pb{pbm}"])
            emit_back_tile(p, st, f"pb{pbm}", st.pb[pbm][:, 0:512], mi, qb)


def emit_l2_attn(p, st):
    nc, hb = st.nc, st.hbm
    emit_ln_stats(p, st)
    emit_ln_to_ht(p, st)

    qT = nc.alloc_sbuf_tensor("l2_qT", [128, ND * T], BF16)
    kTd = nc.alloc_sbuf_tensor("l2_kTd", [128, 4 * T], BF16)
    VH = st.ht[:, 0:NT * 4 * 65]
    atT = st.ht[:, 8192:8192 + 4 * 512]
    aoT = nc.alloc_sbuf_tensor("l2_aoT", [128, NQB * ND * 512], BF16)
    wbuf = nc.alloc_sbuf_tensor("l2_wb", [128, 2 * ND * 128], BF16)
    ropc = nc.alloc_sbuf_tensor("l2_ropc", [128, 2 * 512], F32)
    rops = nc.alloc_sbuf_tensor("l2_rops", [128, 2 * 512], F32)
    rsc = nc.alloc_sbuf_tensor("l2_rsc", [128, 2 * 512], BF16)
    rrow = nc.alloc_sbuf_tensor("l2_rr", [1, 512], F32)
    rb = nc.alloc_sbuf_tensor("l2_rb", [64, 512], F32)
    nrm = nc.alloc_sbuf_tensor("l2_nrm", [1, 64], F32)
    negM = nc.alloc_sbuf_tensor("l2_negM", [128, 16], F32)
    SCALE = 1.0 / 32.0

    def vh_keys(kc):
        return [f"vT{kc}", f"ht{(kc * 4 * 65) // T}", f"ht{((kc + 1) * 4 * 65 - 1) // T}"]

    for mi in range(ND):
        sl = mi % 2
        p.dma("dq_g", wbuf[:, sl * ND * 128:(sl + 1) * ND * 128]
              .rearrange("p (c m) -> p c m", c=ND),
              hb["L2_attn_Wq"][:, mi * 128:(mi + 1) * 128]
              .rearrange("(c p) m -> p c m", p=128),
              w=[f"l2wb{sl}"])
        for qb in range(NQB):
            pbm = (mi * NQB + qb) % 2
            for kc in range(ND):
                p.t(lambda e, sl=sl, kc=kc, qb=qb, pbm=pbm: e.matmul(
                    st.pb[pbm][:, 0:512],
                    wbuf[:, sl * ND * 128 + kc * 128: sl * ND * 128 + kc * 128 + 128],
                    ht_c(st, kc)[:, qb * QB:qb * QB + QB],
                    start=(kc == 0), stop=(kc == ND - 1)),
                    r=[f"l2wb{sl}", f"ht{kc}"], w=[f"pb{pbm}"])
            p.s(lambda e, mi=mi, qb=qb, pbm=pbm: e.copy(
                qT[:, mi * T + qb * QB: mi * T + qb * QB + QB], st.pb[pbm][:, 0:512]),
                r=[f"pb{pbm}"], w=[f"l2q{mi}"])
    for kv in range(4):
        sl = kv % 2
        for half in range(2):
            p.dma("dq_g", wbuf[:, sl * ND * 128:(sl + 1) * ND * 128]
                  .rearrange("p (c m) -> p c m", c=ND)[:, :, half * 64:half * 64 + 64],
                  hb["L2_attn_Wk"][:, kv * 64:(kv + 1) * 64]
                  .rearrange("(c p) m -> p c m", p=128),
                  w=[f"l2wb{sl}"])
        for qb in range(NQB):
            pbm = (kv * NQB + qb) % 2
            for kc in range(ND):
                p.t(lambda e, sl=sl, kc=kc, qb=qb, pbm=pbm: e.matmul(
                    st.pb[pbm][:, 0:512],
                    wbuf[:, sl * ND * 128 + kc * 128: sl * ND * 128 + kc * 128 + 128],
                    ht_c(st, kc)[:, qb * QB:qb * QB + QB],
                    start=(kc == 0), stop=(kc == ND - 1)),
                    r=[f"l2wb{sl}", f"ht{kc}"], w=[f"pb{pbm}"])
            p.s(lambda e, kv=kv, qb=qb, pbm=pbm: e.copy(
                kTd[:, kv * T + qb * QB: kv * T + qb * QB + QB], st.pb[pbm][:, 0:512]),
                r=[f"pb{pbm}"], w=[f"l2k{kv}"])
    for vi in range(2):
        sl = vi % 2
        p.dma("dq_g", wbuf[:, sl * ND * 128:(sl + 1) * ND * 128]
              .rearrange("p (c m) -> p c m", c=ND),
              hb["L2_attn_Wv"][:, vi * 128:(vi + 1) * 128]
              .rearrange("(c p) m -> p c m", p=128),
              w=[f"l2wb{sl}"])
        for qb in range(NQB):
            pbm = (vi * NQB + qb) % 2
            for kc in range(ND):
                p.t(lambda e, sl=sl, kc=kc, qb=qb, pbm=pbm: e.matmul(
                    st.pb[pbm][:, 0:512],
                    wbuf[:, sl * ND * 128 + kc * 128: sl * ND * 128 + kc * 128 + 128],
                    ht_c(st, kc)[:, qb * QB:qb * QB + QB],
                    start=(kc == 0), stop=(kc == ND - 1)),
                    r=[f"l2wb{sl}", f"ht{kc}"], w=[f"pb{pbm}"])
            vsl = (vi * NQB + qb) % 4
            p.s(lambda e, vsl=vsl, pbm=pbm: e.copy(
                atT[:, vsl * 512:vsl * 512 + 512], st.pb[pbm][:, 0:512]),
                r=[f"pb{pbm}"], w=[f"atT{vsl}", "ht4"])
            for j in range(4):
                kc = qb * 4 + j
                pti = j % 2
                p.t(lambda e, vsl=vsl, j=j, pti=pti: e.transpose(
                    st.pt[pti][:, 0:128],
                    atT[:, vsl * 512 + j * 128: vsl * 512 + j * 128 + 128],
                    st.ident[:]),
                    r=[f"atT{vsl}", "ht4", "ident"], w=[f"pt{pti}"])
                base = kc * 4 * 65 + (2 * vi) * 65
                dstv = st.ht[:, base:base + 130].rearrange("p (h m) -> p h m", h=2)[:, :, 0:64]
                p.v(lambda e, dstv=dstv, pti=pti: e.tensor_copy(
                    dstv, st.pt[pti][:, 0:128].rearrange("p (h m) -> p h m", h=2)),
                    r=[f"pt{pti}"], w=vh_keys(kc))
    for kc in range(NT):
        on = st.ht[:, kc * 4 * 65:(kc + 1) * 4 * 65].rearrange(
            "p (h m) -> p h m", h=4)[:, :, 64:65]
        p.v(lambda e, on=on: e.memset(on, 1.0), w=vh_keys(kc))

    swapmask = [pp ^ 1 for pp in range(32)]
    for dc in range(ND):
        for qb in range(NQB):
            sl = (dc * NQB + qb) % 2
            p.dma("dq_y", ropc[:, sl * 512:sl * 512 + 512],
                  hb["c_ropeq_c"][dc * 128:(dc + 1) * 128, qb * QB:qb * QB + QB],
                  w=[f"ropc{sl}"])
            p.dma("dq_y", rops[:, sl * 512:sl * 512 + 512],
                  hb["c_ropeq_s"][dc * 128:(dc + 1) * 128, qb * QB:qb * QB + QB],
                  w=[f"rops{sl}"])
            qsl = qT[:, dc * T + qb * QB: dc * T + qb * QB + QB]
            p.v(lambda e, qsl=qsl, sl=sl: e.stream_shuffle(
                rsc[:, sl * 512:sl * 512 + 512], qsl, swapmask),
                r=[f"l2q{dc}"], w=[f"rsc{sl}"])
            p.v(lambda e, sl=sl: e.tensor_tensor(
                rsc[:, sl * 512:sl * 512 + 512], rsc[:, sl * 512:sl * 512 + 512],
                rops[:, sl * 512:sl * 512 + 512], ALU.mult),
                r=[f"rsc{sl}", f"rops{sl}"], w=[f"rsc{sl}"])
            p.v(lambda e, qsl=qsl, sl=sl: e.scalar_tensor_tensor(
                rsc[:, sl * 512:sl * 512 + 512], qsl, 1.0,
                rsc[:, sl * 512:sl * 512 + 512], ALU.bypass, ALU.bypass),
                r=[f"l2q{dc}", f"rsc{sl}"], w=[f"rsc{sl}"]) if False else None
            p.v(lambda e, qsl=qsl, sl=sl: e.tensor_tensor(
                qsl, qsl, ropc[:, sl * 512:sl * 512 + 512], ALU.mult),
                r=[f"l2q{dc}", f"ropc{sl}"], w=[f"l2q{dc}"])
            p.v(lambda e, qsl=qsl, sl=sl: e.tensor_tensor(
                qsl, qsl, rsc[:, sl * 512:sl * 512 + 512], ALU.add),
                r=[f"l2q{dc}", f"rsc{sl}"], w=[f"l2q{dc}"])
    for kv in range(4):
        for qb in range(NQB):
            sl = (kv * NQB + qb) % 2
            p.dma("dq_y", ropc[:, sl * 512:sl * 512 + 512],
                  hb["c_ropek_c"][:, qb * QB:qb * QB + QB], w=[f"ropc{sl}"])
            p.dma("dq_y", rops[:, sl * 512:sl * 512 + 512],
                  hb["c_ropek_s"][:, qb * QB:qb * QB + QB], w=[f"rops{sl}"])
            ksl = kTd[:, kv * T + qb * QB: kv * T + qb * QB + QB]
            p.v(lambda e, ksl=ksl, sl=sl: e.stream_shuffle(
                rsc[:, sl * 512:sl * 512 + 512], ksl, swapmask),
                r=[f"l2k{kv}"], w=[f"rsc{sl}"])
            p.v(lambda e, sl=sl: e.tensor_tensor(
                rsc[:, sl * 512:sl * 512 + 512], rsc[:, sl * 512:sl * 512 + 512],
                rops[:, sl * 512:sl * 512 + 512], ALU.mult),
                r=[f"rsc{sl}", f"rops{sl}"], w=[f"rsc{sl}"])
            p.v(lambda e, ksl=ksl, sl=sl: e.tensor_tensor(
                ksl, ksl, ropc[:, sl * 512:sl * 512 + 512], ALU.mult),
                r=[f"l2k{kv}", f"ropc{sl}"], w=[f"l2k{kv}"])
            p.v(lambda e, ksl=ksl, sl=sl: e.tensor_tensor(
                ksl, ksl, rsc[:, sl * 512:sl * 512 + 512], ALU.add),
                r=[f"l2k{kv}", f"rsc{sl}"], w=[f"l2k{kv}"])

    p.v(lambda e: e.memset(nrm[:, 0:32], 0.0), w=["nrm"])
    for wi, (src, srck, nh, cs) in enumerate(((qT, "l2q", 16, 0), (kTd, "l2k", 4, 16))):
        for hh in range(nh):
            dc = hh // 2 if wi == 0 else hh
            base = (hh % 2) * 64 if wi == 0 else 0
            for qb in range(NQB):
                p.s(lambda e, src=src, dc=dc, qb=qb: e.activation(
                    out=st.sq[:, 0:512],
                    in_=src[:, dc * T + qb * QB: dc * T + qb * QB + QB],
                    func=AF.Square),
                    r=[f"{srck}{dc}"], w=["htok0"])
                p.t(lambda e, base=base: e.matmul(
                    st.pb[2][0:1, 0:512], st.onescol[base:base + 64, 0:1],
                    st.sq[base:base + 64, 0:512], start=True, stop=True),
                    r=["htok0", "onescol"], w=["pb2"])
                p.v(lambda e: e.reduce_max(nrm[0:1, 40:41], st.pb[2][0:1, 0:512],
                                           axis=AX.X),
                    r=["pb2"], w=["nrms"])
                idx = cs + hh
                p.v(lambda e, idx=idx: e.tensor_max(
                    nrm[0:1, idx:idx + 1], nrm[0:1, idx:idx + 1], nrm[0:1, 40:41]),
                    r=["nrms", "nrm"], w=["nrm"])
    for h in range(16):
        p.v(lambda e, h=h: e.tensor_tensor(
            nrm[0:1, 32 + h:33 + h], nrm[0:1, h:h + 1], nrm[0:1, 16 + h // 4:17 + h // 4],
            ALU.mult), r=["nrm"], w=["nrmm"])
    p.s(lambda e: e.activation(out=nrm[0:1, 32:48], in_=nrm[0:1, 32:48], func=AF.Sqrt),
        r=["nrmm"], w=["nrmm"])
    p.v(lambda e: e.tensor_scalar_mul(nrm[0:1, 32:48], nrm[0:1, 32:48], -SCALE),
        r=["nrmm"], w=["nrmm"])
    p.t(lambda e: e.matmul(st.pb[2][0:128, 0:16], st.onesrow_f[0:1, 0:128],
                           nrm[0:1, 32:48], start=True, stop=True),
        r=["nrmm", "onesrowf"], w=["pb2"])
    p.v(lambda e: e.tensor_copy(negM[:, 0:16], st.pb[2][0:128, 0:16]),
        r=["pb2"], w=["negM"])

    for h in range(16):
        dc, base = h // 2, (h % 2) * 64
        kv = h // 4
        rrow_off = (h % 2) * 64
        for qb in range(NQB):
            for kg in range(4):
                for ki in range(4):
                    kc = kg * 4 + ki
                    pbs = ki % 2
                    p.t(lambda e, kv=kv, base=base, dc=dc, kc=kc, qb=qb, pbs=pbs: e.matmul(
                        st.pb[pbs][:, 0:512],
                        kTd[base:base + 64, kv * T + kc * 128: kv * T + kc * 128 + 128],
                        qT[base:base + 64, dc * T + qb * QB: dc * T + qb * QB + QB],
                        start=True, stop=True),
                        r=[f"l2k{kv}", f"l2q{dc}"], w=[f"pb{pbs}"])
                    p.s(lambda e, ki=ki, pbs=pbs, h=h: e.activation(
                        out=atT[:, ki * 512:ki * 512 + 512], in_=st.pb[pbs][:, 0:512],
                        func=AF.Exp, scale=SCALE, bias=negM[:, h:h + 1]),
                        r=[f"pb{pbs}", "negM"], w=[f"atT{ki}", "ht4"])
                for ki in range(4):
                    kc = kg * 4 + ki
                    p.t(lambda e, kc=kc, ki=ki, kv=kv,
                        s0=(kg == 0 and ki == 0), s1=(kg == 3 and ki == 3): e.matmul(
                        st.pb[3][0:65, 0:512],
                        st.ht[:, kc * 4 * 65 + kv * 65: kc * 4 * 65 + kv * 65 + 65],
                        atT[:, ki * 512:ki * 512 + 512],
                        start=s0, stop=s1),
                        r=[f"atT{ki}", "ht4"] + vh_keys(kc), w=["pb3"])
            p.v(lambda e: e.reciprocal(rrow[0:1, 0:512], st.pb[3][64:65, 0:512]),
                r=["pb3"], w=["rrow"])
            p.t(lambda e: e.matmul(st.pb[2][0:64, 0:512], st.onesrow_f[0:1, 0:64],
                                   rrow[0:1, 0:512], start=True, stop=True),
                r=["rrow", "onesrowf"], w=["pb2"])
            p.s(lambda e: e.copy(rb[0:64, 0:512], st.pb[2][0:64, 0:512]),
                r=["pb2"], w=["rb"])
            p.v(lambda e, rrow_off=rrow_off, qb=qb, dc=dc: e.tensor_tensor(
                aoT[rrow_off:rrow_off + 64, (qb * ND + dc) * 512:(qb * ND + dc) * 512 + 512],
                st.pb[3][0:64, 0:512], rb[0:64, 0:512], ALU.mult),
                r=["pb3", "rb"], w=[f"aoT{qb}_{dc}_{rrow_off}"])

    for mi in range(ND):
        sl = mi % 2
        p.dma("dq_g", wbuf[:, sl * ND * 128:(sl + 1) * ND * 128]
              .rearrange("p (c m) -> p c m", c=ND),
              hb["L2_attn_Wo"][:, mi * 128:(mi + 1) * 128]
              .rearrange("(c p) m -> p c m", p=128),
              w=[f"l2wb{sl}"])
        for qb in range(NQB):
            pbm = (mi * NQB + qb) % 2
            for kc in range(ND):
                p.t(lambda e, sl=sl, mi=mi, qb=qb, kc=kc, pbm=pbm: e.matmul(
                    st.pb[pbm][:, 0:512],
                    wbuf[:, sl * ND * 128 + kc * 128: sl * ND * 128 + kc * 128 + 128],
                    aoT[:, (qb * ND + kc) * 512: (qb * ND + kc) * 512 + 512],
                    start=(kc == 0), stop=(kc == ND - 1)),
                    r=[f"l2wb{sl}", f"aoT{qb}_{kc}_0", f"aoT{qb}_{kc}_64"], w=[f"pb{pbm}"])
            emit_back_tile(p, st, f"pb{pbm}", st.pb[pbm][:, 0:512], mi, qb)


def emit_l2_ffn(p, st):
    nc, hb = st.nc, st.hbm
    HID, NH = 3072, 24
    emit_ln_stats(p, st)
    emit_ln_to_ht(p, st)

    wgu = nc.alloc_sbuf_tensor("l2f_wgu", [128, 2 * D], BF16)
    wT = nc.alloc_sbuf_tensor("l2f_wT", [128, 2 * ND * 128], BF16)
    hbuf = nc.alloc_sbuf_tensor("l2f_h", [128, NH * 512], BF16)
    ubuf = nc.alloc_sbuf_tensor("l2f_u", [128, 512], BF16)
    wdT = nc.alloc_sbuf_tensor("l2f_wdT", [128, NH * D], BF16)

    for dmi in range(ND):
        sl = dmi % 2
        p.dma("dq_g", wgu[:, sl * D:sl * D + D].rearrange("p (a m) -> p a m", a=1)
              if False else wgu[:, sl * D:sl * D + D],
              hb["L2_ffn_w_down"][dmi * 128:(dmi + 1) * 128, 0:1024],
              w=[f"l2wgu{sl}"])
        p.dma("dq_g", wT[:, sl * 1024:sl * 1024 + 1024],
              hb["L2_ffn_w_down"][dmi * 128:(dmi + 1) * 128, 1024:2048],
              w=[f"l2wT{sl}"])
        p.dma("dq_g", ubuf[:, 0:512] if False else
              hbuf[:, (sl) * 512:(sl) * 512 + 512].rearrange("p m -> p m")
              if False else wgu[:, sl * D:sl * D + D],
              hb["L2_ffn_w_down"][dmi * 128:(dmi + 1) * 128, 2048:3072],
              w=[f"l2wgu{sl}"]) if False else None
        p.dma("dq_g", hbuf[:, 23 * 512:23 * 512 + 512], 
              hb["L2_ffn_w_down"][dmi * 128:(dmi + 1) * 128, 2048:2560],
              w=["yb23"])
        p.dma("dq_g", hbuf[:, 22 * 512:22 * 512 + 512],
              hb["L2_ffn_w_down"][dmi * 128:(dmi + 1) * 128, 2560:3072],
              w=["yb22"])
        for hc in range(NH):
            if hc < 8:
                src, srk = wgu[:, sl * D + hc * 128: sl * D + hc * 128 + 128], f"l2wgu{sl}"
            elif hc < 16:
                src, srk = wT[:, sl * 1024 + (hc - 8) * 128: sl * 1024 + (hc - 8) * 128 + 128], f"l2wT{sl}"
            elif hc < 20:
                src, srk = hbuf[:, 23 * 512 + (hc - 16) * 128: 23 * 512 + (hc - 16) * 128 + 128], "yb23"
            else:
                src, srk = hbuf[:, 22 * 512 + (hc - 20) * 128: 22 * 512 + (hc - 20) * 128 + 128], "yb22"
            pti = hc % 2
            p.t(lambda e, src=src, pti=pti: e.transpose(
                st.pt[pti][:, 0:128], src, st.ident[:]),
                r=[srk, "ident"], w=[f"pt{pti}"])
            p.v(lambda e, hc=hc, dmi=dmi, pti=pti: e.tensor_copy(
                wdT[:, hc * D + dmi * 128: hc * D + dmi * 128 + 128],
                st.pt[pti][:, 0:128]),
                r=[f"pt{pti}"], w=[f"wdT{hc}"])

    for qb in range(NQB):
        for hc in range(NH):
            sl = hc % 2
            for which, (wname, dk) in enumerate((("L2_ffn_w_gate", "g"),
                                                 ("L2_ffn_w_up", "u"))):
                p.dma("dq_g", wgu[:, sl * D:sl * D + D],
                      hb[wname][hc * 128:(hc + 1) * 128, :],
                      w=[f"l2wgu{sl}"])
                for dc in range(ND):
                    pti = dc % 2
                    p.t(lambda e, sl=sl, dc=dc, pti=pti: e.transpose(
                        st.pt[pti][:, 0:128],
                        wgu[:, sl * D + dc * 128: sl * D + dc * 128 + 128],
                        st.ident[:]),
                        r=[f"l2wgu{sl}", "ident"], w=[f"pt{pti}"])
                    p.v(lambda e, sl=sl, dc=dc, pti=pti: e.tensor_copy(
                        wT[:, sl * 1024 + dc * 128: sl * 1024 + dc * 128 + 128],
                        st.pt[pti][:, 0:128]),
                        r=[f"pt{pti}"], w=[f"l2wT{sl}"])
                pbm = 2 + which
                for dc in range(ND):
                    p.t(lambda e, sl=sl, dc=dc, qb=qb, pbm=pbm: e.matmul(
                        st.pb[pbm][:, 0:512],
                        wT[:, sl * 1024 + dc * 128: sl * 1024 + dc * 128 + 128],
                        ht_c(st, dc)[:, qb * QB:qb * QB + QB],
                        start=(dc == 0), stop=(dc == ND - 1)),
                        r=[f"l2wT{sl}", f"ht{dc}"], w=[f"pb{pbm}"])
                if which == 0:
                    p.s(lambda e, hc=hc, pbm=pbm: e.activation(
                        out=hbuf[:, hc * 512:hc * 512 + 512], in_=st.pb[pbm][:, 0:512],
                        func=AF.Silu, scale=1.0 / 127.0),
                        r=[f"pb{pbm}"], w=[f"yb{hc}"])
                else:
                    p.s(lambda e, pbm=pbm: e.mul(ubuf[:, 0:512], st.pb[pbm][:, 0:512],
                                                 1.0 / 127.0),
                        r=[f"pb{pbm}"], w=["ubuf"])
                    p.v(lambda e, hc=hc: e.tensor_tensor(
                        hbuf[:, hc * 512:hc * 512 + 512],
                        hbuf[:, hc * 512:hc * 512 + 512], ubuf[:, 0:512], ALU.mult),
                        r=[f"yb{hc}", "ubuf"], w=[f"yb{hc}"])
        for mi in range(ND):
            pbm = mi % 2
            for hc in range(NH):
                p.t(lambda e, hc=hc, mi=mi, pbm=pbm, s0=(hc == 0), s1=(hc == NH - 1): e.matmul(
                    st.pb[pbm][:, 0:512],
                    wdT[:, hc * D + mi * 128: hc * D + mi * 128 + 128],
                    hbuf[:, hc * 512:hc * 512 + 512],
                    start=s0, stop=s1),
                    r=[f"wdT{hc}", f"yb{hc}"], w=[f"pb{pbm}"])
            emit_back_tile(p, st, f"pb{pbm}", st.pb[pbm][:, 0:512], mi, qb,
                           scale=1.0 / 127.0)


def emit_l3_attn(p, st):
    nc, hb = st.nc, st.hbm
    NHG = 4
    xold = nc.dram_tensor("l3_xold", [T, D], F32, kind="Internal").ap()
    vdram = nc.dram_tensor("l3_vdram", [16 * 68, T], BF16, kind="Internal").ap()
    w1tile = nc.dram_tensor("l3_w1t", [32 * ND * 128, 128], BF16, kind="Internal").ap()
    w2td = nc.dram_tensor("l3_w2t", [4096, D], BF16, kind="Internal").ap()

    p.dma("dq_y", xold.rearrange("(a p) m -> p a m", p=128),
          st.x[:].rearrange("p (a m) -> p a m", a=NT), r=XK, w=["xold"])

    emit_ln_stats(p, st)
    p.v(lambda e: e.tensor_scalar(lane(st, 5), lane(st, 6), 1e-5, None, ALU.add),
        r=["st6"], w=["st5"])
    p.v(lambda e: e.reciprocal(lane(st, 5), lane(st, 5)), r=["st5"], w=["st5"])
    p.v(lambda e: e.tensor_tensor(lane(st, 5), lane(st, 6), lane(st, 5), ALU.mult),
        r=["st5", "st6"], w=["st5"])
    p.s(lambda e: e.activation(out=lane(st, 5), in_=lane(st, 5), func=AF.Sqrt,
                               bias=st.eps[:, 0:1], scale=1.0),
        r=["st5", "eps"], w=["st5"])
    p.v(lambda e: e.reciprocal(lane(st, 6), lane(st, 5)), r=["st5"], w=["st6"])
    p.v(lambda e: e.tensor_tensor(lane(st, 5), lane(st, 2), lane(st, 6), ALU.mult),
        r=["st2", "st6"], w=["st5"])
    p.v(lambda e: e.scalar_tensor_tensor(lane(st, 6), lane(st, 4), -1.0, lane(st, 5),
                                         ALU.mult, ALU.mult),
        r=["st4", "st5"], w=["st6"])

    gcol = nc.alloc_sbuf_tensor("l3_g", [128, 32], F32)
    for i, gk in enumerate(("L3_attn_ln1_g", "L3_attn_ln1_b",
                            "L3_attn_ln2_g", "L3_attn_ln2_b")):
        p.dma("dq_y", st.zslice[0:1, 0:D], hb[gk].rearrange("(o n) -> o n", o=1),
              w=["zs0", "zs1"])
        for dcg in range(ND):
            pti = dcg % 2
            p.t(lambda e, dcg=dcg, pti=pti: e.transpose(
                st.pb[4 + pti][0:128, 0:1],
                st.zslice[0:1, dcg * 128:dcg * 128 + 128], st.identf[0:1, 0:1]),
                r=["zs0", "zs1", "identf"], w=[f"pb{4 + pti}"])
            p.v(lambda e, i=i, dcg=dcg, pti=pti: e.tensor_copy(
                gcol[:, i * 8 + dcg:i * 8 + dcg + 1], st.pb[4 + pti][0:128, 0:1]),
                r=[f"pb{4 + pti}"], w=["gcol"])

    emit_ln_to_ht(p, st, rlane=5, blane=6,
                  gamma_col=lambda dc: gcol[:, dc:dc + 1],
                  beta_col=lambda dc: gcol[:, 8 + dc:8 + dc + 1], gb_res=("gcol",))
    for tc in range(NT):
        p.v(lambda e, tc=tc: e.tensor_scalar(
            x_t(st, tc), x_t(st, tc), lane(st, 2, tc), lane(st, 3, tc),
            ALU.mult, ALU.add),
            r=[f"x{tc}", "st2", "st3"], w=[f"x{tc}"])

    qT = nc.alloc_sbuf_tensor("l3_qT", [128, 2 * T], BF16)
    kT = nc.alloc_sbuf_tensor("l3_kT", [128, 2 * T], BF16)
    aoT = nc.alloc_sbuf_tensor("l3_aoT", [128, NQB * ND * 512], BF16)
    wpr = nc.alloc_sbuf_tensor("l3_wpr", [128, 2 * D], BF16)
    wpT = nc.alloc_sbuf_tensor("l3_wpT", [128, 2 * ND * 128], BF16)
    masks = nc.alloc_sbuf_tensor("l3_mask", [128, 8 * 512], BF16)
    rrow = nc.alloc_sbuf_tensor("l3_rr", [1, 512], F32)
    rb = nc.alloc_sbuf_tensor("l3_rb", [64, 512], F32)
    nrm = nc.alloc_sbuf_tensor("l3_nrm", [1, 64], F32)
    negM = nc.alloc_sbuf_tensor("l3_negM", [128, NHG], F32)
    VH = st.ht[:, 0:NT * NHG * 68]
    atT = st.ht[:, 8192:8192 + 8 * 512]
    ones16 = st.zslice[:, 0:512]

    for i in range(8):
        p.dma("dq_y", masks[:, i * 512:(i + 1) * 512], hb[f"c_wm{i}"],
              w=[f"wm{i}"])

    bqkv_missing = None
    for grp in range(4):
        h0 = grp * NHG
        for j in range(2):
            ha = h0 + 2 * j
            for wi, (dstT, dk, roff) in enumerate(((qT, "l3q", 0), (kT, "l3k", D))):
                sl = (grp * 4 + j * 2 + wi) % 2
                p.dma("dq_g", wpr[:, sl * D:sl * D + D],
                      hb["L3_attn_w_qkv"][roff + ha * 64: roff + ha * 64 + 128, :],
                      w=[f"l3wpr{sl}"])
                for dc in range(ND):
                    pti = dc % 2
                    p.t(lambda e, sl=sl, dc=dc, pti=pti: e.transpose(
                        st.pt[pti][:, 0:128],
                        wpr[:, sl * D + dc * 128: sl * D + dc * 128 + 128],
                        st.ident[:]),
                        r=[f"l3wpr{sl}", "ident"], w=[f"pt{pti}"])
                    p.v(lambda e, sl=sl, dc=dc, pti=pti: e.tensor_copy(
                        wpT[:, sl * ND * 128 + dc * 128: sl * ND * 128 + dc * 128 + 128],
                        st.pt[pti][:, 0:128]),
                        r=[f"pt{pti}"], w=[f"l3wpT{sl}"])
                for qb in range(NQB):
                    pbm = qb % 2
                    for kc in range(ND):
                        p.t(lambda e, sl=sl, kc=kc, qb=qb, pbm=pbm: e.matmul(
                            st.pb[pbm][:, 0:512],
                            wpT[:, sl * ND * 128 + kc * 128: sl * ND * 128 + kc * 128 + 128],
                            ht_c(st, kc)[:, qb * QB:qb * QB + QB],
                            start=(kc == 0), stop=(kc == ND - 1)),
                            r=[f"l3wpT{sl}", f"ht{kc}"], w=[f"pb{pbm}"])
                    p.s(lambda e, dstT=dstT, j=j, qb=qb, pbm=pbm: e.copy(
                        dstT[:, j * T + qb * QB: j * T + qb * QB + QB],
                        st.pb[pbm][:, 0:512]),
                        r=[f"pb{pbm}"], w=[f"{dk}{j}_{grp}"])
            sl = (j) % 2
            p.dma("dq_g", wpr[:, sl * D:sl * D + D],
                  hb["L3_attn_w_qkv"][2 * D + ha * 64: 2 * D + ha * 64 + 128, :],
                  w=[f"l3wpr{sl}"])
            for dc in range(ND):
                pti = dc % 2
                p.t(lambda e, sl=sl, dc=dc, pti=pti: e.transpose(
                    st.pt[pti][:, 0:128],
                    wpr[:, sl * D + dc * 128: sl * D + dc * 128 + 128],
                    st.ident[:]),
                    r=[f"l3wpr{sl}", "ident"], w=[f"pt{pti}"])
                p.v(lambda e, sl=sl, dc=dc, pti=pti: e.tensor_copy(
                    wpT[:, sl * ND * 128 + dc * 128: sl * ND * 128 + dc * 128 + 128],
                    st.pt[pti][:, 0:128]),
                    r=[f"pt{pti}"], w=[f"l3wpT{sl}"])
            for qb in range(NQB):
                pbm = qb % 2
                for kc in range(ND):
                    p.t(lambda e, sl=sl, kc=kc, qb=qb, pbm=pbm: e.matmul(
                        st.pb[pbm][:, 0:512],
                        wpT[:, sl * ND * 128 + kc * 128: sl * ND * 128 + kc * 128 + 128],
                        ht_c(st, kc)[:, qb * QB:qb * QB + QB],
                        start=(kc == 0), stop=(kc == ND - 1)),
                        r=[f"l3wpT{sl}", f"ht{kc}"], w=[f"pb{pbm}"])
                vsl = qb % 2
                p.s(lambda e, vsl=vsl, pbm=pbm: e.copy(
                    st.zslice[:, vsl * 512:vsl * 512 + 512].bitcast(BF16)[:, 0:512]
                    if False else st.htok[:, vsl * 512:vsl * 512 + 512],
                    st.pb[pbm][:, 0:512]),
                    r=[f"pb{pbm}"], w=[f"htok{vsl}"])
                p.dma("dq_y", vdram[(h0 + 2 * j) * 68:(h0 + 2 * j) * 68 + 2 * 68,
                                    qb * QB:qb * QB + QB]
                      .rearrange("(h m) t -> h m t", m=68)[:, 0:64, :],
                      st.htok[:, vsl * 512:vsl * 512 + 512],
                      r=[f"htok{vsl}"], w=[f"vd{grp}_{j}_{qb}"])
    p.v(lambda e: e.memset(st.htok[0:16, 0:512], 1.0), w=["htok0", "htok1"])
    for h in range(16):
        for qb in range(NQB):
            p.dma("dq_y", vdram[h * 68 + 64:h * 68 + 68, qb * QB:qb * QB + QB],
                  st.htok[0:4, 0:512], r=["htok0"], w=[f"vdo{h}_{qb}"])

    SCALE = 0.125
    for grp in range(4):
        h0 = grp * NHG
        for kc in range(NT):
            deps = ([f"vd{grp}_{j}_{qb}" for j in range(2) for qb in range(NQB)]
                    + [f"vdo{h}_{qb}" for h in range(h0, h0 + 4) for qb in range(NQB)])
            if grp == 0:
                deps += HTK
            p.dma("dq_y", st.ht[:, kc * NHG * 68:(kc + 1) * NHG * 68],
                  vdram[h0 * 68:(h0 + 4) * 68, kc * 128:(kc + 1) * 128],
                  r=deps, w=[f"VH{kc}_{grp}", f"ht{(kc * NHG * 68) // T}",
                             f"ht{((kc + 1) * NHG * 68 - 1) // T}"],
                  transpose=True)
        p.v(lambda e: e.memset(nrm[:, 0:32], 0.0), w=["nrm"])
        for j in range(2):
            for wi, (src, srck) in enumerate(((qT, "l3q"), (kT, "l3k"))):
                for qb in range(NQB):
                    p.s(lambda e, src=src, j=j, qb=qb: e.activation(
                        out=st.sq[:, 0:512],
                        in_=src[:, j * T + qb * QB: j * T + qb * QB + QB],
                        func=AF.Square),
                        r=[f"{srck}{j}_{grp}"], w=["htok0"])
                    for hio in range(2):
                        base = hio * 64
                        idx = wi * 16 + 2 * j + hio
                        p.t(lambda e, base=base: e.matmul(
                            st.pb[2][0:1, 0:512], st.onescol[base:base + 64, 0:1],
                            st.sq[base:base + 64, 0:512], start=True, stop=True),
                            r=["htok0", "onescol"], w=["pb2"])
                        p.v(lambda e: e.reduce_max(nrm[0:1, 40:41],
                                                   st.pb[2][0:1, 0:512], axis=AX.X),
                            r=["pb2"], w=["nrms"])
                        p.v(lambda e, idx=idx: e.tensor_max(
                            nrm[0:1, idx:idx + 1], nrm[0:1, idx:idx + 1],
                            nrm[0:1, 40:41]),
                            r=["nrms", "nrm"], w=["nrm"])
        p.v(lambda e: e.tensor_tensor(nrm[0:1, 32:32 + NHG], nrm[0:1, 0:NHG],
                                      nrm[0:1, 16:16 + NHG], ALU.mult),
            r=["nrm"], w=["nrmm"])
        p.s(lambda e: e.activation(out=nrm[0:1, 48:48 + NHG],
                                   in_=nrm[0:1, 32:32 + NHG], func=AF.Sqrt),
            r=["nrmm"], w=["nrmsq"])
        p.v(lambda e: e.tensor_scalar_mul(nrm[0:1, 56:56 + NHG],
                                          nrm[0:1, 48:48 + NHG], -SCALE),
            r=["nrmsq"], w=["nrmneg"])
        p.t(lambda e: e.matmul(st.pb[2][0:128, 0:NHG], st.onesrow_f[0:1, 0:128],
                               nrm[0:1, 56:56 + NHG], start=True, stop=True),
            r=["nrmneg", "onesrowf"], w=["pb2"])
        p.v(lambda e: e.tensor_copy(negM[:, 0:NHG], st.pb[2][0:128, 0:NHG]),
            r=["pb2"], w=["negM"])

        for hidx in range(NHG):
            j, hio = hidx // 2, hidx % 2
            base = hio * 64
            h = h0 + hidx
            dc_o, rrow_off = h // 2, (h % 2) * 64
            for qc in range(NQB):
                first = True
                nvalid = [i for i in range(8) if 0 <= qc * 4 - 4 + i < NT]
                for i in nvalid:
                    kc = qc * 4 - 4 + i
                    pbs = i % 2
                    p.t(lambda e, j=j, base=base, kc=kc, qc=qc, pbs=pbs: e.matmul(
                        st.pb[pbs][:, 0:512],
                        kT[base:base + 64, j * T + kc * 128: j * T + kc * 128 + 128],
                        qT[base:base + 64, j * T + qc * QB: j * T + qc * QB + QB],
                        start=True, stop=True),
                        r=[f"l3k{j}_{grp}", f"l3q{j}_{grp}"], w=[f"pb{pbs}"])
                    p.s(lambda e, i=i, pbs=pbs, hidx=hidx: e.activation(
                        out=atT[:, i * 512:i * 512 + 512], in_=st.pb[pbs][:, 0:512],
                        func=AF.Exp, scale=SCALE, bias=negM[:, hidx:hidx + 1]),
                        r=[f"pb{pbs}", "negM"], w=[f"atT{i}", "ht4", "ht5"])
                    p.v(lambda e, i=i: e.tensor_tensor(
                        atT[:, i * 512:i * 512 + 512], atT[:, i * 512:i * 512 + 512],
                        masks[:, i * 512:(i + 1) * 512], ALU.mult),
                        r=[f"atT{i}", f"wm{i}", "ht5"], w=[f"atT{i}", "ht4", "ht5"])
                for i in nvalid:
                    kc = qc * 4 - 4 + i
                    p.t(lambda e, kc=kc, i=i, hidx=hidx,
                        s0=(i == nvalid[0]), s1=(i == nvalid[-1]): e.matmul(
                        st.pb[3][0:68, 0:512],
                        st.ht[:, kc * NHG * 68 + hidx * 68: kc * NHG * 68 + hidx * 68 + 68],
                        atT[:, i * 512:i * 512 + 512],
                        start=s0, stop=s1),
                        r=[f"atT{i}", f"VH{kc}_{grp}",
                           f"ht{(kc * NHG * 68) // T}"], w=["pb3"])
                p.v(lambda e: e.reciprocal(rrow[0:1, 0:512], st.pb[3][64:65, 0:512]),
                    r=["pb3"], w=["rrow"])
                p.t(lambda e: e.matmul(st.pb[2][0:64, 0:512], st.onesrow_f[0:1, 0:64],
                                       rrow[0:1, 0:512], start=True, stop=True),
                    r=["rrow", "onesrowf"], w=["pb2"])
                p.s(lambda e: e.copy(rb[0:64, 0:512], st.pb[2][0:64, 0:512]),
                    r=["pb2"], w=["rb"])
                p.v(lambda e, rrow_off=rrow_off, qc=qc, dc_o=dc_o: e.tensor_tensor(
                    aoT[rrow_off:rrow_off + 64,
                        (qc * ND + dc_o) * 512:(qc * ND + dc_o) * 512 + 512],
                    st.pb[3][0:64, 0:512], rb[0:64, 0:512], ALU.mult),
                    r=["pb3", "rb"], w=[f"aoT{qc}_{dc_o}_{rrow_off}"])

    for mi in range(ND):
        sl = mi % 2
        p.dma("dq_g", wpr[:, sl * D:sl * D + D],
              hb["L3_attn_w_o"][mi * 128:(mi + 1) * 128, :],
              w=[f"l3wpr{sl}"])
        for dc in range(ND):
            pti = dc % 2
            p.t(lambda e, sl=sl, dc=dc, pti=pti: e.transpose(
                st.pt[pti][:, 0:128],
                wpr[:, sl * D + dc * 128: sl * D + dc * 128 + 128], st.ident[:]),
                r=[f"l3wpr{sl}", "ident"], w=[f"pt{pti}"])
            p.v(lambda e, sl=sl, dc=dc, pti=pti: e.tensor_copy(
                wpT[:, sl * ND * 128 + dc * 128: sl * ND * 128 + dc * 128 + 128],
                st.pt[pti][:, 0:128]),
                r=[f"pt{pti}"], w=[f"l3wpT{sl}"])
        for qb in range(NQB):
            pbm = qb % 2
            for kc in range(ND):
                p.t(lambda e, sl=sl, kc=kc, qb=qb, pbm=pbm: e.matmul(
                    st.pb[pbm][:, 0:512],
                    wpT[:, sl * ND * 128 + kc * 128: sl * ND * 128 + kc * 128 + 128],
                    aoT[:, (qb * ND + kc) * 512: (qb * ND + kc) * 512 + 512],
                    start=(kc == 0), stop=(kc == ND - 1)),
                    r=[f"l3wpT{sl}", f"aoT{qb}_{kc}_0", f"aoT{qb}_{kc}_64"],
                    w=[f"pb{pbm}"])
            emit_back_tile(p, st, f"pb{pbm}", st.pb[pbm][:, 0:512], mi, qb)

    emit_ln_stats(p, st)
    emit_ln_to_ht(p, st, gamma_col=lambda dc: gcol[:, 16 + dc:16 + dc + 1],
                  beta_col=lambda dc: gcol[:, 24 + dc:24 + dc + 1], gb_res=("gcol",))

    b1row, _ = load_row(p, st, "l3b1", hb["L3_attn_mlp_b1"], 4096)
    b2row, _ = load_row(p, st, "l3b2", hb["L3_attn_mlp_b2"], D)
    for oc in range(32):
        sl = oc % 2
        p.dma("dq_g", wpr[:, sl * D:sl * D + D],
              hb["L3_attn_mlp_w1"][oc * 128:(oc + 1) * 128, :],
              w=[f"l3wpr{sl}"])
        for dc in range(ND):
            pti = dc % 2
            p.t(lambda e, sl=sl, dc=dc, pti=pti: e.transpose(
                st.pt[pti][:, 0:128],
                wpr[:, sl * D + dc * 128: sl * D + dc * 128 + 128], st.ident[:]),
                r=[f"l3wpr{sl}", "ident"], w=[f"pt{pti}"])
            p.v(lambda e, sl=sl, dc=dc, pti=pti: e.tensor_copy(
                wpT[:, sl * ND * 128 + dc * 128: sl * ND * 128 + dc * 128 + 128],
                st.pt[pti][:, 0:128]),
                r=[f"pt{pti}"], w=[f"l3wpT{sl}"])
        p.dma("dq_y", w1tile[oc * ND * 128:(oc + 1) * ND * 128, :]
              .rearrange("(c p) m -> p c m", p=128),
              wpT[:, sl * ND * 128:(sl + 1) * ND * 128]
              .rearrange("p (c m) -> p c m", c=ND),
              r=[f"l3wpT{sl}"], w=[f"w1t{oc}"])
    for dmi in range(ND):
        sl = dmi % 2
        p.dma("dq_g", wpr[:, sl * D:sl * D + D]
              .rearrange("p (a m) -> p a m", a=2)
              if False else wpr[:, sl * D:sl * D + D],
              hb["L3_attn_mlp_w2"][dmi * 128:(dmi + 1) * 128, 0:1024],
              w=[f"l3wpr{sl}"])
        p.dma("dq_g", wpT[:, sl * 1024:sl * 1024 + 1024],
              hb["L3_attn_mlp_w2"][dmi * 128:(dmi + 1) * 128, 1024:2048],
              w=[f"l3wpT{sl}"])
        p.dma("dq_g", st.htok[:, 0:1024],
              hb["L3_attn_mlp_w2"][dmi * 128:(dmi + 1) * 128, 2048:3072],
              w=["htok0"])
        p.dma("dq_g", st.htok[:, 1024:2048],
              hb["L3_attn_mlp_w2"][dmi * 128:(dmi + 1) * 128, 3072:4096],
              w=["htok1"])
        for hc in range(32):
            if hc < 8:
                src, srk = wpr[:, sl * D + hc * 128: sl * D + hc * 128 + 128], f"l3wpr{sl}"
            elif hc < 16:
                src, srk = wpT[:, sl * 1024 + (hc - 8) * 128: sl * 1024 + (hc - 8) * 128 + 128], f"l3wpT{sl}"
            elif hc < 24:
                src, srk = st.htok[:, (hc - 16) * 128:(hc - 16) * 128 + 128], "htok0"
            else:
                src, srk = st.htok[:, 1024 + (hc - 24) * 128:1024 + (hc - 24) * 128 + 128], "htok1"
            pti = hc % 2
            p.t(lambda e, src=src, pti=pti: e.transpose(
                st.pt[pti][:, 0:128], src, st.ident[:]),
                r=[srk, "ident"], w=[f"pt{pti}"])
            p.v(lambda e, hc=hc, pti=pti: e.tensor_copy(
                st.zslice[:, 0:512].bitcast(BF16)[:, 0:128] if False else
                st.sq[:, 0:128], st.pt[pti][:, 0:128]),
                r=[f"pt{pti}"], w=["htok0"]) if False else None
            p.s(lambda e, hc=hc, pti=pti: e.copy(st.sq[:, 0:128], st.pt[pti][:, 0:128]),
                r=[f"pt{pti}"], w=["sqx"])
            p.dma("dq_y", w2td[hc * 128:(hc + 1) * 128, dmi * 128:(dmi + 1) * 128],
                  st.sq[:, 0:128], r=["sqx"], w=[f"w2t{hc}_{dmi}"])

    ybuf = aoT
    w2s = nc.alloc_sbuf_tensor("l3_w2s", [128, 2 * D], BF16)
    w1s = nc.alloc_sbuf_tensor("l3_w1s", [128, 2 * ND * 128], BF16)
    for qb in range(NQB):
        for oc in range(32):
            sl = oc % 2
            p.dma("dq_g", w1s[:, sl * ND * 128:(sl + 1) * ND * 128]
                  .rearrange("p (c m) -> p c m", c=ND),
                  w1tile[oc * ND * 128:(oc + 1) * ND * 128, :]
                  .rearrange("(c p) m -> p c m", p=128),
                  r=[f"w1t{oc}"], w=[f"w1s{sl}"])
            pbm = oc % 2
            for kc in range(ND):
                p.t(lambda e, sl=sl, kc=kc, qb=qb, pbm=pbm: e.matmul(
                    st.pb[pbm][:, 0:512],
                    w1s[:, sl * ND * 128 + kc * 128: sl * ND * 128 + kc * 128 + 128],
                    ht_c(st, kc)[:, qb * QB:qb * QB + QB],
                    start=(kc == 0), stop=False),
                    r=[f"w1s{sl}", f"ht{kc}"], w=[f"pb{pbm}"])
            p.t(lambda e, oc=oc, pbm=pbm: e.matmul(
                st.pb[pbm][:, 0:512], b1row[0:1, oc * 128:oc * 128 + 128],
                st.ones512[0:1, 0:512], start=False, stop=True),
                r=["r_l3b1", "ones512"], w=[f"pb{pbm}"])
            p.s(lambda e, oc=oc, pbm=pbm: e.activation(
                out=ybuf[:, oc * 512:oc * 512 + 512], in_=st.pb[pbm][:, 0:512],
                func=AF.Gelu),
                r=[f"pb{pbm}"],
                w=[f"yb{oc}", f"aoT{oc // 8}_{oc % 8}_0", f"aoT{oc // 8}_{oc % 8}_64"])
        pairs = [(tc, dh) for tc in range(4) for dh in range(2)]
        for wv in range(2):
            wave = pairs[wv * 4:(wv + 1) * 4]
            for kc in range(32):
                sl = kc % 2
                p.dma("dq_g", w2s[:, sl * D:sl * D + D],
                      w2td[kc * 128:(kc + 1) * 128, :],
                      r=[f"w2t{kc}_{dmi}" for dmi in range(ND)], w=[f"w2s{sl}"])
                for i, (tc, dh) in enumerate(wave):
                    p.t(lambda e, kc=kc, sl=sl, tc=tc, dh=dh, i=i: e.matmul(
                        st.pb[i][:, 0:512],
                        ybuf[:, kc * 512 + tc * 128: kc * 512 + tc * 128 + 128],
                        w2s[:, sl * D + dh * 512: sl * D + dh * 512 + 512],
                        start=(kc == 0), stop=False),
                        r=[f"w2s{sl}", f"yb{kc}"], w=[f"pb{i}"])
            for i, (tc, dh) in enumerate(wave):
                p.t(lambda e, dh=dh, i=i: e.matmul(
                    st.pb[i][:, 0:512], st.ones512[0:1, 0:128],
                    b2row[0:1, dh * 512:dh * 512 + 512], start=False, stop=True),
                    r=["r_l3b2", "ones512"], w=[f"pb{i}"])
                tt = qb * 4 + tc
                dst = x_t(st, tt)[:, dh * 512:dh * 512 + 512]
                p.v(lambda e, dst=dst, i=i: e.scalar_tensor_tensor(
                    dst, st.pb[i][:, 0:512], 1.0, dst, ALU.mult, ALU.add),
                    r=[f"pb{i}", f"x{tt}"], w=[f"x{tt}"])

    for tc in range(NT):
        sl = tc % 2
        p.dma("dq_y", st.zslice[:, sl * 512:sl * 512 + 512]
              .rearrange("p (a m) -> p a m", a=1) if False else
              st.zslice[:, sl * 512:sl * 512 + 512],
              xold[tc * 128:(tc + 1) * 128, 0:512], r=["xold"], w=[f"zs{sl}"])
        p.v(lambda e, tc=tc, sl=sl: e.tensor_tensor(
            x_t(st, tc)[:, 0:512], x_t(st, tc)[:, 0:512],
            st.zslice[:, sl * 512:sl * 512 + 512], ALU.add),
            r=[f"x{tc}", f"zs{sl}"], w=[f"x{tc}"])
        p.dma("dq_y", st.zslice[:, sl * 512:sl * 512 + 512],
              xold[tc * 128:(tc + 1) * 128, 512:1024], r=["xold"], w=[f"zs{sl}"])
        p.v(lambda e, tc=tc, sl=sl: e.tensor_tensor(
            x_t(st, tc)[:, 512:1024], x_t(st, tc)[:, 512:1024],
            st.zslice[:, sl * 512:sl * 512 + 512], ALU.add),
            r=[f"x{tc}", f"zs{sl}"], w=[f"x{tc}"])


def emit_l3_ffn(p, st):
    nc, hb = st.nc, st.hbm
    E, CAP, F = 8, 512, 4096
    hdram = nc.dram_tensor("l3f_h", [T, D], BF16, kind="Internal").ap()
    ydram = nc.dram_tensor("l3f_y", [T, D], F32, kind="Internal").ap()

    emit_ln_stats(p, st)
    emit_ln_to_ht(p, st, dram_out=hdram)

    p.v(lambda e: e.memset(st.zslice[:], 0.0), w=["zs0", "zs1"])
    for tc in range(NT):
        p.dma("dq_y", ydram[tc * 128:(tc + 1) * 128, 0:256], st.zslice[:, 0:256],
              r=["zs0"], w=[f"yd{tc}"])
        p.dma("dq_y", ydram[tc * 128:(tc + 1) * 128, 256:512], st.zslice[:, 256:512],
              r=["zs0"], w=[f"yd{tc}"])
        p.dma("dq_y", ydram[tc * 128:(tc + 1) * 128, 512:768], st.zslice[:, 512:768],
              r=["zs1"], w=[f"yd{tc}"])
        p.dma("dq_y", ydram[tc * 128:(tc + 1) * 128, 768:1024], st.zslice[:, 768:1024],
              r=["zs1"], w=[f"yd{tc}"])

    gwT = nc.alloc_sbuf_tensor("l3f_gwT", [128, ND * 8], BF16)
    gwn = nc.alloc_sbuf_tensor("l3f_gwn", [8, D], BF16)
    p.dma("dq_g", gwn[:], hb["L3_ffn_gate_w"], w=["gwn"])
    for dc in range(ND):
        pti = dc % 2
        p.t(lambda e, dc=dc, pti=pti: e.transpose(
            st.pt[pti][0:128, 0:8], gwn[:, dc * 128:(dc + 1) * 128], st.ident[0:8, 0:8]),
            r=["gwn", "ident"], w=[f"pt{pti}"])
        p.v(lambda e, dc=dc, pti=pti: e.tensor_copy(
            gwT[:, dc * 8:(dc + 1) * 8], st.pt[pti][0:128, 0:8]),
            r=[f"pt{pti}"], w=["gwT"])
    logT = nc.alloc_sbuf_tensor("l3f_logT", [8, T], F32)
    probsT = nc.alloc_sbuf_tensor("l3f_probsT", [8, T], F32)
    probs = nc.alloc_sbuf_tensor("l3f_probs", [128, NT * 8], F32)
    for qb in range(NQB):
        pbm = qb % 2
        for dc in range(ND):
            p.t(lambda e, dc=dc, qb=qb, pbm=pbm: e.matmul(
                st.pb[pbm][0:8, 0:512], gwT[:, dc * 8:(dc + 1) * 8],
                ht_c(st, dc)[:, qb * QB:qb * QB + QB],
                start=(dc == 0), stop=(dc == ND - 1)),
                r=["gwT", f"ht{dc}"], w=[f"pb{pbm}"])
        p.s(lambda e, qb=qb, pbm=pbm: e.copy(
            logT[:, qb * QB:qb * QB + QB], st.pb[pbm][0:8, 0:512]),
            r=[f"pb{pbm}"], w=["logT"])
    p.v(lambda e: e.memset(lane(st, 6), 0.0), w=["st6"])
    for tc in range(NT):
        pti = tc % 2
        p.t(lambda e, tc=tc, pti=pti: e.transpose(
            st.pb[4 + pti][0:128, 0:8], logT[:, tc * 128:(tc + 1) * 128],
            st.identf[0:8, 0:8]),
            r=["logT", "identf"], w=[f"pb{4 + pti}"])
        pr = probs[:, tc * 8:(tc + 1) * 8]
        m1 = lane(st, 5, tc)
        p.v(lambda e, pr=pr, pti=pti: e.tensor_copy(pr, st.pb[4 + pti][0:128, 0:8]),
            r=[f"pb{4 + pti}"], w=["probs"])
        p.v(lambda e, pr=pr, m1=m1: e.reduce_max(m1, pr, axis=AX.X),
            r=["probs"], w=["st5"])
        p.v(lambda e, m1=m1: e.tensor_scalar_mul(m1, m1, -1.0), r=["st5"], w=["st5"])
        p.s(lambda e, pr=pr, m1=m1, tc=tc: e.activation(
            out=pr, in_=pr, func=AF.Exp, bias=m1, scale=1.0,
            accum_out=lane(st, 6, tc)),
            r=["probs", "st5", "st6"], w=["probs", "st6"])
        p.v(lambda e, tc=tc: e.reciprocal(lane(st, 6, tc), lane(st, 6, tc)),
            r=["st6"], w=["st6"])
        p.v(lambda e, pr=pr, tc=tc: e.tensor_scalar(pr, pr, lane(st, 6, tc), None,
                                                    ALU.mult),
            r=["probs", "st6"], w=["probs"])
    for tc in range(NT):
        pti = tc % 2
        p.t(lambda e, tc=tc, pti=pti: e.transpose(
            st.pb[4 + pti][0:8, 0:128], probs[:, tc * 8:(tc + 1) * 8], st.identf[:]),
            r=["probs", "identf"], w=[f"pb{4 + pti}"])
        p.v(lambda e, tc=tc, pti=pti: e.tensor_copy(
            probsT[:, tc * 128:(tc + 1) * 128], st.pb[4 + pti][0:8, 0:128]),
            r=[f"pb{4 + pti}"], w=["probsT"])

    thr = nc.alloc_sbuf_tensor("l3f_thr", [8, 8], F32)
    ge = logT
    p.v(lambda e: e.memset(thr[:, 0:1], 0.5), w=["thr"])
    step = 0.25
    for it in range(24):
        p.v(lambda e: e.tensor_scalar(ge[:], probsT[:], thr[:, 0:1], None, ALU.is_ge),
            r=["probsT", "thr"], w=["logT"])
        p.v(lambda e: e.reduce_sum(thr[:, 1:2], ge[:], axis=AX.X),
            r=["logT"], w=["thrc"])
        p.v(lambda e: e.tensor_scalar(thr[:, 2:3], thr[:, 1:2], 512.0, None, ALU.is_gt),
            r=["thrc"], w=["thrd"])
        p.v(lambda e: e.tensor_scalar(thr[:, 2:3], thr[:, 2:3], 2.0, -1.0,
                                      ALU.mult, ALU.add),
            r=["thrd"], w=["thrd"])
        p.v(lambda e, stp=step: e.scalar_tensor_tensor(
            thr[:, 0:1], thr[:, 2:3], stp, thr[:, 0:1], ALU.mult, ALU.add),
            r=["thrd", "thr"], w=["thr"])
        step *= 0.5
    for rnd in range(4):
        p.v(lambda e: e.tensor_scalar(ge[:], probsT[:], thr[:, 0:1], None, ALU.is_ge),
            r=["probsT", "thr"], w=["logT"])
        p.v(lambda e: e.reduce_sum(thr[:, 1:2], ge[:], axis=AX.X),
            r=["logT"], w=["thrc"])
        p.v(lambda e: e.scalar_tensor_tensor(probs[:, 0:1] if False else ge[:],
                                             ge[:], 1.0, probsT[:], ALU.bypass,
                                             ALU.mult),
            r=["logT", "probsT"], w=["logT"]) if False else None
        p.v(lambda e: e.tensor_scalar(probsT[:], probsT[:], -2.0, None, ALU.add),
            r=["probsT"], w=["probsT"])
        p.v(lambda e: e.tensor_tensor(ge[:], ge[:], probsT[:], ALU.mult),
            r=["logT", "probsT"], w=["logT"])
        p.v(lambda e: e.tensor_scalar(probsT[:], probsT[:], 2.0, None, ALU.add),
            r=["probsT"], w=["probsT"])
        p.v(lambda e: e.tensor_scalar(ge[:], ge[:], 2.0, None, ALU.add),
            r=["logT"], w=["logT"])
        p.v(lambda e: e.tensor_reduce(thr[:, 3:4], ge[:], axis=AX.X, op=ALU.min),
            r=["logT"], w=["thrmin"])
        p.v(lambda e: e.tensor_scalar(ge[:], probsT[:], thr[:, 0:1], None, ALU.is_lt),
            r=["probsT", "thr"], w=["logT"])
        p.v(lambda e: e.tensor_tensor(ge[:], ge[:], probsT[:], ALU.mult),
            r=["logT", "probsT"], w=["logT"])
        p.v(lambda e: e.tensor_reduce(thr[:, 4:5], ge[:], axis=AX.X, op=ALU.max),
            r=["logT"], w=["thrmax"])
        p.v(lambda e: e.tensor_scalar(thr[:, 5:6], thr[:, 1:2], 512.0, None, ALU.is_gt),
            r=["thrc"], w=["thrp"])
        p.v(lambda e: e.tensor_scalar(thr[:, 6:7], thr[:, 1:2], 512.0, None, ALU.is_lt),
            r=["thrc"], w=["thrn"])
        p.v(lambda e: e.tensor_scalar(thr[:, 3:4], thr[:, 3:4], 1.0 + 2.0 ** -22,
                                      None, ALU.mult),
            r=["thrmin"], w=["thrmin"])
        p.v(lambda e: e.tensor_tensor(thr[:, 3:4], thr[:, 3:4], thr[:, 0:1],
                                      ALU.subtract), r=["thrmin", "thr"], w=["thrmin"])
        p.v(lambda e: e.tensor_tensor(thr[:, 3:4], thr[:, 3:4], thr[:, 5:6],
                                      ALU.mult), r=["thrmin", "thrp"], w=["thrmin"])
        p.v(lambda e: e.tensor_tensor(thr[:, 0:1], thr[:, 0:1], thr[:, 3:4],
                                      ALU.add), r=["thrmin", "thr"], w=["thr"])
        p.v(lambda e: e.tensor_tensor(thr[:, 4:5], thr[:, 4:5], thr[:, 0:1],
                                      ALU.subtract), r=["thrmax", "thr"], w=["thrmax"])
        p.v(lambda e: e.tensor_tensor(thr[:, 4:5], thr[:, 4:5], thr[:, 6:7],
                                      ALU.mult), r=["thrmax", "thrn"], w=["thrmax"])
        p.v(lambda e: e.tensor_tensor(thr[:, 0:1], thr[:, 0:1], thr[:, 4:5],
                                      ALU.add), r=["thrmax", "thr"], w=["thr"])

    slotT = nc.alloc_sbuf_tensor("l3f_slotT", [8, T], F32)
    zer8 = nc.alloc_sbuf_tensor("l3f_z8", [8, T], F32)
    p.v(lambda e: e.memset(zer8[:], 0.0), w=["zer8"])
    p.v(lambda e: e.tensor_scalar(ge[:], probsT[:], thr[:, 0:1], None, ALU.is_ge),
        r=["probsT", "thr"], w=["logT"])
    p.v(lambda e: e.tensor_tensor_scan(slotT[:], ge[:], zer8[:], 0.0,
                                       ALU.add, ALU.bypass),
        r=["logT", "zer8"], w=["slotT"])
    p.v(lambda e: e.tensor_tensor(slotT[:], slotT[:], ge[:], ALU.mult),
        r=["slotT", "logT"], w=["slotT"])
    p.v(lambda e: e.tensor_scalar(slotT[:], slotT[:], -1.0, None, ALU.add),
        r=["slotT"], w=["slotT"])
    slottok = nc.alloc_sbuf_tensor("l3f_slott", [128, NT * 8], F32)
    for tc in range(NT):
        pti = tc % 2
        p.t(lambda e, tc=tc, pti=pti: e.transpose(
            st.pb[4 + pti][0:128, 0:8], slotT[:, tc * 128:(tc + 1) * 128],
            st.identf[0:8, 0:8]),
            r=["slotT", "identf"], w=[f"pb{4 + pti}"])
        p.v(lambda e, tc=tc, pti=pti: e.tensor_copy(
            slottok[:, tc * 8:(tc + 1) * 8], st.pb[4 + pti][0:128, 0:8]),
            r=[f"pb{4 + pti}"], w=["slottok"])
    iot = nc.alloc_sbuf_tensor("l3f_iot", [128, 512], F32)
    tokcol = nc.alloc_sbuf_tensor("l3f_tokc", [128, 16], F32)
    p.g(lambda e: e.iota(iot[:], pattern=[[1, 512]], base=0, channel_multiplier=0,
                         allow_small_or_imprecise_dtypes=True), w=["iot"])
    p.g(lambda e: e.iota(tokcol[:], pattern=[[128, 16]], base=0, channel_multiplier=1,
                         allow_small_or_imprecise_dtypes=True), w=["tokcol"])
    PT = nc.alloc_sbuf_tensor("l3f_PT", [128, 512], BF16)
    rhs2 = nc.alloc_sbuf_tensor("l3f_rhs2", [128, NT * 2], BF16)
    mg = nc.alloc_sbuf_tensor("l3f_mg", [128, 2 * E * 4], F32)
    mint = nc.alloc_sbuf_tensor("l3f_mint", [128, E * 4], I32)
    grow = nc.alloc_sbuf_tensor("l3f_grow", [128, 4], F32)
    p.v(lambda e: e.tensor_copy(rhs2[:].rearrange("p (t two) -> p t two", two=2)[:, :, 0:1],
                                tokcol[:].rearrange("p (t o) -> p t o", o=1)),
        r=["tokcol"], w=["rhs2"])
    for e_i in range(E):
        p.v(lambda e, e_i=e_i: e.tensor_copy(
            rhs2[:].rearrange("p (t two) -> p t two", two=2)[:, :, 1:2],
            probs[:].rearrange("p (t ee) -> p t ee", ee=8)[:, :, e_i:e_i + 1]),
            r=["probs", "rhs2"], w=["rhs2"])
        for tc in range(NT):
            p.v(lambda e, e_i=e_i, tc=tc: e.tensor_scalar(
                PT[:], iot[:], slottok[:, tc * 8 + e_i:tc * 8 + e_i + 1], None,
                ALU.is_equal),
                r=["iot", "slottok"], w=["PT"])
            for sc in range(4):
                p.t(lambda e, tc=tc, sc=sc, s0=(tc == 0): e.matmul(
                    st.pb[sc][0:128, 0:2], PT[:, sc * 128:(sc + 1) * 128],
                    rhs2[:, tc * 2:tc * 2 + 2], start=s0, stop=(tc == NT - 1)),
                    r=["PT", "rhs2"], w=[f"pb{sc}"])
        for sc in range(4):
            p.v(lambda e, e_i=e_i, sc=sc: e.tensor_copy(
                mg[:, e_i * 4 + sc:e_i * 4 + sc + 1], st.pb[sc][0:128, 0:1]),
                r=[f"pb{sc}"], w=["mg"])
            p.v(lambda e, e_i=e_i, sc=sc: e.tensor_copy(
                mg[:, 32 + e_i * 4 + sc:32 + e_i * 4 + sc + 1], st.pb[sc][0:128, 1:2]),
                r=[f"pb{sc}"], w=["mg"])
    p.v(lambda e: e.tensor_copy(mint[:], mg[:, 0:32]), r=["mg"], w=["mint"])

    xg = nc.alloc_sbuf_tensor("l3f_xg", [128, 4 * D], BF16)
    xgT = nc.alloc_sbuf_tensor("l3f_xgT", [128, ND * 512], BF16)
    w1b = nc.alloc_sbuf_tensor("l3f_w1b", [128, 2 * ND * 128], BF16)
    yb1 = st.ht
    w2b = nc.alloc_sbuf_tensor("l3f_w2b", [128, 2 * 32 * 128], BF16)
    y2t = nc.alloc_sbuf_tensor("l3f_y2t", [128, 4 * D], F32)

    for e_i in range(E):
        for sc in range(4):
            p.indirect(xg[:, sc * D:(sc + 1) * D], None,
                       hdram, IndirectOffsetOnAxis(mint[:, e_i * 4 + sc:e_i * 4 + sc + 1], 0),
                       r=["mint"] + [f"hd{tc}" for tc in range(NT)],
                       w=[f"xg{sc}"])
            for dc in range(ND):
                pti = dc % 2
                p.t(lambda e, sc=sc, dc=dc, pti=pti: e.transpose(
                    st.pt[pti][:, 0:128], xg[:, sc * D + dc * 128: sc * D + dc * 128 + 128],
                    st.ident[:]),
                    r=[f"xg{sc}", "ident"], w=[f"pt{pti}"])
                p.v(lambda e, sc=sc, dc=dc, pti=pti: e.tensor_copy(
                    xgT[:, dc * 512 + sc * 128: dc * 512 + sc * 128 + 128],
                    st.pt[pti][:, 0:128]),
                    r=[f"pt{pti}"], w=[f"xgT{dc}"])
        for oc in range(32):
            sl = oc % 2
            p.dma("dq_g", w1b[:, sl * ND * 128:(sl + 1) * ND * 128]
                  .rearrange("p (c m) -> p c m", c=ND),
                  hb["L3_ffn_w1"][e_i][:, oc * 128:(oc + 1) * 128]
                  .rearrange("(c p) m -> p c m", p=128),
                  w=[f"w1b{sl}"])
            pbm = oc % 2
            for kc in range(ND):
                p.t(lambda e, sl=sl, kc=kc, pbm=pbm: e.matmul(
                    st.pb[pbm][:, 0:512],
                    w1b[:, sl * ND * 128 + kc * 128: sl * ND * 128 + kc * 128 + 128],
                    xgT[:, kc * 512:(kc + 1) * 512],
                    start=(kc == 0), stop=(kc == ND - 1)),
                    r=[f"w1b{sl}", f"xgT{kc}"], w=[f"pb{pbm}"])
            p.s(lambda e, oc=oc, pbm=pbm: e.activation(
                out=yb1[:, oc * 512:oc * 512 + 512], in_=st.pb[pbm][:, 0:512],
                func=AF.Gelu),
                r=[f"pb{pbm}"], w=[f"yb{oc}", f"ht{oc // 4}"])
        for mi in range(ND):
            sl = mi % 2
            p.dma("dq_g", w2b[:, sl * 32 * 128:(sl + 1) * 32 * 128]
                  .rearrange("p (c m) -> p c m", c=32),
                  hb["L3_ffn_w2"][e_i][:, mi * 128:(mi + 1) * 128]
                  .rearrange("(c p) m -> p c m", p=128),
                  w=[f"w2b{sl}"])
            pbm = mi % 2
            for kc in range(32):
                p.t(lambda e, sl=sl, kc=kc, pbm=pbm: e.matmul(
                    st.pb[pbm][:, 0:512],
                    w2b[:, sl * 32 * 128 + kc * 128: sl * 32 * 128 + kc * 128 + 128],
                    yb1[:, kc * 512:(kc + 1) * 512],
                    start=(kc == 0), stop=(kc == 31)),
                    r=[f"w2b{sl}", f"yb{kc}", f"ht{kc // 4}"], w=[f"pb{pbm}"])
            zi = mi % 2
            p.s(lambda e, pbm=pbm, zi=zi: e.copy(
                st.zslice[:, zi * 512:zi * 512 + 512], st.pb[pbm][:, 0:512]),
                r=[f"pb{pbm}"], w=[f"zs{zi}"])
            for sc in range(4):
                pbi = 4 + sc % 2
                p.t(lambda e, zi=zi, sc=sc, pbi=pbi: e.transpose(
                    st.pb[pbi][:, 0:128],
                    st.zslice[:, zi * 512 + sc * 128: zi * 512 + sc * 128 + 128],
                    st.identf[:]),
                    r=[f"zs{zi}", "identf"], w=[f"pb{pbi}"])
                p.v(lambda e, e_i=e_i, sc=sc, mi=mi, pbi=pbi: e.tensor_scalar(
                    y2t[:, sc * D + mi * 128: sc * D + mi * 128 + 128],
                    st.pb[pbi][:, 0:128],
                    mg[:, 32 + e_i * 4 + sc:32 + e_i * 4 + sc + 1], None, ALU.mult),
                    r=[f"pb{pbi}", "mg"], w=[f"y2t{sc}"])
        for sc in range(4):
            p.indirect(ydram, IndirectOffsetOnAxis(mint[:, e_i * 4 + sc:e_i * 4 + sc + 1], 0),
                       y2t[:, sc * D:(sc + 1) * D], None,
                       compute_op=ALU.add,
                       r=["mint", f"y2t{sc}"] + [f"yd{tc}" for tc in range(NT)],
                       w=[f"yd{tc}" for tc in range(NT)])

    for tc in range(NT):
        for half in range(2):
            sl = (tc * 2 + half) % 2
            p.dma("dq_y", st.zslice[:, sl * 512:sl * 512 + 512],
                  ydram[tc * 128:(tc + 1) * 128, half * 512:half * 512 + 512],
                  r=[f"yd{tcc}" for tcc in range(NT)], w=[f"zs{sl}"])
            p.v(lambda e, tc=tc, half=half, sl=sl: e.tensor_tensor(
                x_t(st, tc)[:, half * 512:half * 512 + 512],
                x_t(st, tc)[:, half * 512:half * 512 + 512],
                st.zslice[:, sl * 512:sl * 512 + 512], ALU.add),
                r=[f"x{tc}", f"zs{sl}"], w=[f"x{tc}"])


SUBLAYERS = {
    0: ("l0_attn", emit_l0_attn,
        ["L0_attn_W_qkv", "L0_attn_b_qkv", "L0_attn_W_o", "L0_attn_b_o"]),
    1: ("l0_ffn", emit_l0_ffn,
        ["L0_ffn_ln_gamma", "L0_ffn_ln_beta", "L0_ffn_fc1_q", "L0_ffn_fc1_scales",
         "L0_ffn_fc1_bias", "L0_ffn_fc2_q", "L0_ffn_fc2_scales", "L0_ffn_fc2_bias"]),
    2: ("l1_attn", emit_l1_attn, ["L1_attn_Wk", "L1_attn_Wq", "L1_attn_Wv"]),
    3: ("l1_ffn", emit_l1_ffn,
        ["L1_ffn_routing_W", "L1_ffn_expert_bias", "L1_ffn_sW1", "L1_ffn_sb1",
         "L1_ffn_sW2", "L1_ffn_sb2", "L1_ffn_rW1", "L1_ffn_rb1", "L1_ffn_rW2",
         "L1_ffn_rb2"]),
    4: ("l2_attn", emit_l2_attn,
        ["L2_attn_Wq", "L2_attn_Wk", "L2_attn_Wv", "L2_attn_Wo"]),
    5: ("l2_ffn", emit_l2_ffn, ["L2_ffn_w_gate", "L2_ffn_w_up", "L2_ffn_w_down"]),
    6: ("l3_attn", emit_l3_attn,
        ["L3_attn_w_qkv", "L3_attn_w_o", "L3_attn_ln1_g", "L3_attn_ln1_b",
         "L3_attn_ln2_g", "L3_attn_ln2_b", "L3_attn_mlp_w1", "L3_attn_mlp_b1",
         "L3_attn_mlp_w2", "L3_attn_mlp_b2"]),
    7: ("l3_ffn", emit_l3_ffn, ["L3_ffn_gate_w", "L3_ffn_w1", "L3_ffn_w2"]),
}

CONST_KEYS = list(CONSTS.keys())


def build_sublayer_program(k, weight_shapes):
    name, emitter, wkeys = SUBLAYERS[k]
    nc = bass.Bass("TRN2", target_bir_lowering=False)
    st = alloc_common(nc)
    x_in = nc.dram_tensor("x_in", [T, D], F32, kind="ExternalInput").ap()
    x_out = nc.dram_tensor("x_out", [T, D], F32, kind="ExternalOutput").ap()
    for ck in CONST_KEYS:
        a = CONSTS[ck]
        dt = BF16 if a.dtype == BF else (F32 if a.dtype == np.float32 else I32)
        st.hbm[ck] = nc.dram_tensor(ck, list(a.shape), dt, kind="ExternalInput").ap()
    for wk in wkeys:
        shape, dtype = weight_shapes[wk]
        dt = {"float32": F32, "int32": I32}[dtype]
        st.hbm[wk] = nc.dram_tensor(wk, list(shape), dt, kind="ExternalInput").ap()
    p = P(nc)
    emit_load_consts(p, st)
    emit_load_x(p, st, x_in)
    emitter(p, st)
    emit_store_x(p, st, x_out)
    return p.finalize()


def kernel(**inputs):
    weight_shapes = {k: (v.shape, str(v.dtype)) for k, v in inputs.items()}
    x = np.ascontiguousarray(inputs["L0_attn_residual_stream"], np.float32)
    B = x.shape[0]
    ks = sorted(SUBLAYERS.keys())
    for k in ks:
        name, emitter, wkeys = SUBLAYERS[k]
        nc = build_sublayer_program(k, weight_shapes)
        in_maps = []
        for b in range(B):
            m = {"x_in": np.ascontiguousarray(x[b])}
            for ck in CONST_KEYS:
                m[ck] = CONSTS[ck]
            for wk in wkeys:
                m[wk] = np.ascontiguousarray(inputs[wk])
            in_maps.append(m)
        res = run_bass_kernel_spmd(nc, in_maps, core_ids=list(range(B)))
        x = np.stack([res.results[b]["x_out"] for b in range(B)], 0)
    return x


# version 03
# speedup vs baseline: 1.1914x; 1.1914x over previous
import numpy as np
import ml_dtypes
import concourse.bass as bass
import concourse.mybir as mybir
from concourse.bass import IndirectOffsetOnAxis
from concourse.bass_utils import run_bass_kernel_spmd

F32 = mybir.dt.float32
BF16 = mybir.dt.bfloat16
I32 = mybir.dt.int32
AF = mybir.ActivationFunctionType
ALU = mybir.AluOpType
AX = mybir.AxisListType
BF = ml_dtypes.bfloat16

T = 2048
D = 1024
NT = T // 128
ND = D // 128
QB = 512
NQB = T // QB

ENG = ("tensor", "vector", "scalar", "gpsimd", "sync")
DMAQ = ("dq_g", "dq_y", "dq_a")


class _Slot:
    __slots__ = ("idx", "mark")
    def __init__(self, idx):
        self.idx = idx
        self.mark = None


NDMAS = 8


class P:
    def __init__(self, nc):
        self.nc = nc
        self.q = {e: [] for e in ENG}
        self.nemit = {e: 0 for e in ENG}
        self.marked = {e: [] for e in ENG}
        self.sem = {e: nc.alloc_semaphore(f"sm_{e}") for e in ENG}
        self.dsem = {d: [nc.alloc_semaphore(f"sm_{d}{i}") for i in range(NDMAS)]
                     for d in DMAQ}
        self.dcnt = {d: [0] * NDMAS for d in DMAQ}
        self.drr = {d: 0 for d in DMAQ}
        self.insts = {e: {} for e in ENG}
        self.waited = {e: {} for e in ENG}
        self.res = {}

    def _push(self, e, fn):
        self.q[e].append(fn)

    def _mark(self, e, slot):
        ml = self.marked[e]
        for m in ml[::-1]:
            if m.idx < slot.idx:
                break
        for pos, m in enumerate(ml):
            if m.idx >= slot.idx:
                return pos + 1
        slot.mark = len(ml) + 1
        ml.append(slot)
        sem = self.sem[e]
        idx = slot.idx
        def attach(eng, e=e, idx=idx):
            self.insts[e][idx].then_inc(sem, 1)
        self._push(e, attach)
        return slot.mark

    def _wait(self, e, src, val):
        w = self.waited[e]
        if w.get(src, 0) >= val:
            return
        w[src] = val
        h = self.sem[src[1]] if src[0] == "eng" else self.dsem[src[1]][src[2]]
        self._push(e, lambda eng: eng.wait_ge(h, val))

    def _deps(self, e, r, w):
        need = {}
        cur = self.nemit[e]
        def add_dep(d):
            if d is None:
                return
            if d[0] == "eng":
                _, e2, slot = d
                if e2 == e:
                    if e2 != "tensor" and cur - slot.idx < 8:
                        need.setdefault(("eng", e2), []).append(slot)
                    return
                need.setdefault(("eng", e2), []).append(slot)
            else:
                _, qn, si, val = d
                key = ("dma", qn, si)
                need[key] = [max(need.get(key, [0])[0], val)]
        for k in r:
            rec = self.res.get(k)
            if rec:
                add_dep(rec["w"])
        for k in w:
            rec = self.res.get(k)
            if rec:
                add_dep(rec["w"])
                for rd in rec["r"]:
                    add_dep(rd)
        for src, items in need.items():
            if src[0] == "eng":
                val = max(self._mark(src[1], s) for s in items)
                self._wait(e, ("eng", src[1]), val)
            else:
                self._wait(e, src, items[0])

    def _record(self, owner, r, w):
        for k in r:
            rec = self.res.setdefault(k, {"w": None, "r": []})
            rec["r"].append(owner)
        for k in w:
            self.res[k] = {"w": owner, "r": []}

    def emit(self, e, fn, r=(), w=()):
        self._deps(e, r, w)
        idx = self.nemit[e]
        self.nemit[e] += 1
        slot = _Slot(idx)
        self.last_slot = getattr(self, "last_slot", {})
        self.last_slot[e] = slot
        def run(eng, e=e, fn=fn, idx=idx):
            inst = fn(eng)
            assert inst is not None, f"emit fn on {e} returned None"
            self.insts[e][idx] = inst
        self._push(e, run)
        self._record(("eng", e, slot), r, w)
        return slot

    def t(self, fn, r=(), w=()): return self.emit("tensor", fn, r, w)
    def v(self, fn, r=(), w=()): return self.emit("vector", fn, r, w)
    def s(self, fn, r=(), w=()): return self.emit("scalar", fn, r, w)
    def g(self, fn, r=(), w=()): return self.emit("gpsimd", fn, r, w)

    def _dma_slot(self, e, qname):
        si = self.drr[qname]
        self.drr[qname] = (si + 1) % NDMAS
        if self.dcnt[qname][si] > 0:
            self._wait(e, ("dma", qname, si), self.dcnt[qname][si])
        self.dcnt[qname][si] += 16
        return si, self.dcnt[qname][si]

    def dma(self, qname, out_ap, in_ap, r=(), w=(), engine=None, **kw):
        e = engine or {"dq_g": "gpsimd", "dq_y": "sync", "dq_a": "scalar"}[qname]
        self._deps(e, r, w)
        si, val = self._dma_slot(e, qname)
        h = self.dsem[qname][si]
        def run(eng):
            eng.dma_start(out=out_ap, in_=in_ap, **kw).then_inc(h, 16)
        self._push(e, run)
        self._record(("dma", qname, si, val), r, w)
        return (qname, si, val)

    def dma_wait(self, e, ref):
        qname, si, val = ref
        self._wait(e, ("dma", qname, si), val)

    def dma_wait_all(self, e, qname):
        for si in range(NDMAS):
            if self.dcnt[qname][si] > 0:
                self._wait(e, ("dma", qname, si), self.dcnt[qname][si])

    def indirect(self, out_ap, out_off, in_ap, in_off, r=(), w=(), **kw):
        e = "gpsimd"
        self._deps(e, r, w)
        si, val = self._dma_slot(e, "dq_g")
        h = self.dsem["dq_g"][si]
        def run(eng):
            eng.indirect_dma_start(out=out_ap, out_offset=out_off, in_=in_ap,
                                   in_offset=in_off, **kw).then_inc(h, 16)
        self._push(e, run)
        self._record(("dma", "dq_g", si, val), r, w)
        return ("dq_g", si, val)

    def barrier_all(self):
        ls = getattr(self, "last_slot", {})
        marks = {e: self._mark(e, sl) for e, sl in ls.items()}
        for e in ENG:
            for e2, v in marks.items():
                if e2 != e:
                    self._wait(e, ("eng", e2), v)
            for qn in DMAQ:
                for si in range(NDMAS):
                    if self.dcnt[qn][si] > 0:
                        self._wait(e, ("dma", qn, si), self.dcnt[qn][si])
        self.res = {}

    def finalize(self):
        nc = self.nc
        with nc.Block() as blk:
            @blk.tensor
            def _(eng):
                for f in self.q["tensor"]:
                    f(eng)
            @blk.vector
            def _(eng):
                for f in self.q["vector"]:
                    f(eng)
            @blk.scalar
            def _(eng):
                for f in self.q["scalar"]:
                    f(eng)
            @blk.gpsimd
            def _(eng):
                for f in self.q["gpsimd"]:
                    f(eng)
            @blk.sync
            def _(eng):
                for f in self.q["sync"]:
                    f(eng)
        return nc


def build_consts():
    c = {}
    c["c_ident"] = np.eye(128, dtype=BF)
    c["c_identf"] = np.eye(128, dtype=np.float32)
    c["c_ones512"] = np.ones((1, 512), dtype=BF)
    c["c_onesrowf"] = np.ones((1, 128), dtype=np.float32)
    c["c_onescol"] = np.ones((128, 1), dtype=BF)
    sel = np.zeros((8, 8 * 128), dtype=BF)
    for e in range(8):
        sel[e, e * 128:(e + 1) * 128] = 1.0
    c["c_sel8"] = sel
    for off in range(4):
        jj = np.arange(128)[:, None] + off * 128
        qq = np.arange(512)[None, :]
        c[f"c_cmask{off}"] = (qq >= jj).astype(BF)
    jj = np.arange(128)[:, None]
    qq = np.arange(512)[None, :]
    for i in range(8):
        dlt = qq - ((i - 4) * 128 + jj)
        c[f"c_wm{i}"] = ((dlt >= 0) & (dlt <= 511)).astype(BF)
    inv = 1.0 / (10000.0 ** (np.arange(0, D, 2, dtype=np.float64) / D))
    ang = np.arange(T, dtype=np.float64)[:, None] * inv[None, :]
    cT = np.empty((D, T), np.float32); sT = np.empty((D, T), np.float32)
    cT[0::2] = np.cos(ang).T; cT[1::2] = np.cos(ang).T
    sT[0::2] = -np.sin(ang).T; sT[1::2] = np.sin(ang).T
    c["c_ropeq_c"] = cT
    c["c_ropeq_s"] = sT
    angk = ang[:, :32]
    cTk = np.empty((64, T), np.float32); sTk = np.empty((64, T), np.float32)
    cTk[0::2] = np.cos(angk).T; cTk[1::2] = np.cos(angk).T
    sTk[0::2] = -np.sin(angk).T; sTk[1::2] = np.sin(angk).T
    c["c_ropek_c"] = np.tile(cTk, (2, 1)).copy()
    c["c_ropek_s"] = np.tile(sTk, (2, 1)).copy()
    return c


CONSTS = build_consts()


class St:
    fused = False
    sub = 0


def salloc(st, name, shape, dtype):
    nc = st.nc
    if not st.fused:
        return nc.alloc_sbuf_tensor(name, shape, dtype)
    esz = {F32: 4, BF16: 2, I32: 4}[dtype]
    nbytes = shape[1] * esz
    nbytes = (nbytes + 31) // 32 * 32
    off = st.arena_base + st.arena_pos
    st.arena_pos += nbytes
    assert off + nbytes <= 229344, f"arena overflow at {name}: {off + nbytes}"
    return nc.alloc_sbuf_tensor_at(f"{name}_s{st.sub}", shape, dtype, offset=off)


def alloc_common(nc):
    st = St()
    st.nc = nc
    st.x = nc.alloc_sbuf_tensor("x_sb", [128, NT * D], F32)
    st.ht = nc.alloc_sbuf_tensor("ht_sb", [128, ND * T], BF16)
    st.htok = nc.alloc_sbuf_tensor("htok_sb", [128, 2 * D], BF16)
    st.stats = nc.alloc_sbuf_tensor("st_sb", [128, 8 * NT], F32)
    st.sq = st.htok[:, 0:D]
    st.eps = nc.alloc_sbuf_tensor("eps_sb", [128, 1], F32)
    st.ident = nc.alloc_sbuf_tensor("ident_sb", [128, 128], BF16)
    st.identf = nc.alloc_sbuf_tensor("identf_sb", [128, 128], F32)
    st.ones512 = nc.alloc_sbuf_tensor("ones512_sb", [1, 512], BF16)
    st.onesrow_f = nc.alloc_sbuf_tensor("onesrf_sb", [1, 128], F32)
    st.onescol = nc.alloc_sbuf_tensor("onesc_sb", [128, 1], BF16)
    st.zslice = nc.alloc_sbuf_tensor("zsl_sb", [128, 2 * 512], F32)
    st.pb = [nc.alloc_psum_tensor(f"pb{i}", [128, 512], F32) for i in range(6)]
    st.pt = [nc.alloc_psum_tensor(f"pt{i}", [128, 512], BF16) for i in range(2)]
    st.hbm = {}
    return st


def x_t(st, tc):
    return st.x[:, tc * D:(tc + 1) * D]


def ht_c(st, dc):
    return st.ht[:, dc * T:(dc + 1) * T]


def lane(st, i, tc=None):
    if tc is None:
        return st.stats[:, i * NT:(i + 1) * NT]
    return st.stats[:, i * NT + tc:i * NT + tc + 1]


XK = [f"x{tc}" for tc in range(NT)]
HTK = [f"ht{dc}" for dc in range(ND)]


def emit_load_x(p, st, x_hbm):
    p.dma("dq_y", st.x[:].rearrange("p (a m) -> p a m", a=NT),
          x_hbm.rearrange("(a p) m -> p a m", p=128), r=(), w=XK)


def emit_store_x(p, st, x_hbm_out):
    ref = p.dma("dq_y", x_hbm_out.rearrange("(a p) m -> p a m", p=128),
                st.x[:].rearrange("p (a m) -> p a m", a=NT), r=XK, w=())
    p.dma_wait("sync", ref)


def emit_load_consts(p, st):
    hb = st.hbm
    p.dma("dq_y", st.ident[:], hb["c_ident"], w=["ident"])
    p.dma("dq_y", st.identf[:], hb["c_identf"], w=["identf"])
    p.dma("dq_y", st.ones512[:], hb["c_ones512"], w=["ones512"])
    p.dma("dq_y", st.onesrow_f[:], hb["c_onesrowf"], w=["onesrowf"])
    p.dma("dq_y", st.onescol[:], hb["c_onescol"], w=["onescol"])
    p.v(lambda e: e.memset(st.eps[:], 1e-5), w=["eps"])


def emit_ln_stats(p, st, src_t=None, src_k=None):
    src_t = src_t or (lambda tc: x_t(st, tc))
    src_k = src_k or (lambda tc: f"x{tc}")
    p.v(lambda e: e.memset(lane(st, 1), 0.0), w=["st1"])
    for tc in range(NT):
        p.s(lambda e, tc=tc: e.activation(out=st.sq[:], in_=src_t(tc),
                                          func=AF.Square, accum_out=lane(st, 1, tc)),
            r=[src_k(tc), "st1"], w=["htok0", "st1"])
        p.v(lambda e, tc=tc: e.reduce_sum(lane(st, 0, tc), src_t(tc), axis=AX.X),
            r=[src_k(tc)], w=["st0"])
    p.v(lambda e: e.tensor_scalar_mul(lane(st, 4), lane(st, 0), 1.0 / D),
        r=["st0"], w=["st4"])
    p.v(lambda e: e.tensor_tensor(lane(st, 5), lane(st, 4), lane(st, 4), ALU.mult),
        r=["st4"], w=["st5"])
    p.v(lambda e: e.scalar_tensor_tensor(lane(st, 6), lane(st, 1), 1.0 / D,
                                         lane(st, 5), ALU.mult, ALU.subtract),
        r=["st1", "st5"], w=["st6"])
    p.s(lambda e: e.activation(out=lane(st, 5), in_=lane(st, 6), func=AF.Sqrt,
                               bias=st.eps[:, 0:1], scale=1.0),
        r=["st6", "eps"], w=["st5"])
    p.v(lambda e: e.reciprocal(lane(st, 2), lane(st, 5)), r=["st5"], w=["st2"])
    p.v(lambda e: e.scalar_tensor_tensor(lane(st, 3), lane(st, 4), -1.0, lane(st, 2),
                                         ALU.mult, ALU.mult),
        r=["st4", "st2"], w=["st3"])


def emit_ln_to_ht(p, st, xupdate=False, rlane=2, blane=3,
                  gamma_col=None, beta_col=None, gb_res=(), dram_out=None):
    for tc in range(NT):
        bi = tc % 2
        buf = st.htok[:, bi * D:bi * D + D]
        p.s(lambda e, tc=tc, buf=buf: e.activation(
            out=buf, in_=x_t(st, tc), func=AF.Identity,
            scale=lane(st, rlane, tc), bias=lane(st, blane, tc)),
            r=[f"x{tc}", f"st{rlane}", f"st{blane}"], w=[f"htok{bi}"])
        for dc in range(ND):
            pti = (tc * ND + dc) % 2
            p.t(lambda e, buf=buf, dc=dc, pti=pti: e.transpose(
                st.pt[pti][:, 0:128], buf[:, dc * 128:dc * 128 + 128], st.ident[:]),
                r=[f"htok{bi}", "ident"], w=[f"pt{pti}"])
            dst = ht_c(st, dc)[:, tc * 128:tc * 128 + 128]
            if gamma_col is None:
                p.v(lambda e, dst=dst, pti=pti: e.tensor_copy(dst, st.pt[pti][:, 0:128]),
                    r=[f"pt{pti}"], w=[f"ht{dc}"])
            else:
                p.v(lambda e, dst=dst, pti=pti, dc=dc: e.tensor_scalar(
                    dst, st.pt[pti][:, 0:128], gamma_col(dc), beta_col(dc),
                    ALU.mult, ALU.add),
                    r=[f"pt{pti}"] + list(gb_res), w=[f"ht{dc}"])
        if dram_out is not None:
            p.dma("dq_y", dram_out[tc * 128:(tc + 1) * 128, :], buf,
                  r=[f"htok{bi}"], w=[f"hd{tc}"])
        if xupdate:
            p.v(lambda e, tc=tc: e.tensor_scalar(
                x_t(st, tc), x_t(st, tc), lane(st, 7, tc), lane(st, 3, tc),
                ALU.mult, ALU.add),
                r=[f"x{tc}", "st7", "st3", f"htok{bi}"], w=[f"x{tc}"])


def load_w(p, st, name, hbm_ap, K, M, dst=None, dst_off=0, key=None):
    nc = st.nc
    nch = K // 128
    if dst is None:
        dst = salloc(st, f"w_{name}", [128, nch * M], BF16)
        dst_off = 0
    key = key or f"w_{name}"
    p.dma("dq_g",
          dst[:, dst_off:dst_off + nch * M].rearrange("p (c m) -> p c m", c=nch),
          hbm_ap.rearrange("(c p) m -> p c m", p=128), w=[key])
    return dst, key


def load_row(p, st, name, hbm_ap_1d, n, dtype=BF16):
    nc = st.nc
    dst = salloc(st, f"r_{name}", [1, n], dtype)
    p.dma("dq_g", dst[:], hbm_ap_1d.rearrange("(o n) -> o n", o=1), w=[f"r_{name}"])
    return dst, f"r_{name}"


_back_i = [0]


def emit_back_tile(p, st, pb_key, psrc, mi, qb, scale=1.0, ptok=None,
                   ptok_res=(), dstk="x", dst_t=None, add=True):
    dst_t = dst_t or (lambda tc: x_t(st, tc))
    zi = _back_i[0] % 2
    _back_i[0] += 1
    zbuf = st.zslice[:, zi * 512:zi * 512 + 512]
    p.s(lambda e, psrc=psrc, zbuf=zbuf: e.copy(zbuf, psrc),
        r=[pb_key], w=[f"zs{zi}"])
    for j in range(4):
        tc = qb * 4 + j
        pbi = 4 + (zi * 4 + j) % 2
        p.t(lambda e, zbuf=zbuf, j=j, pbi=pbi: e.transpose(
            st.pb[pbi][:, 0:128], zbuf[:, j * 128:j * 128 + 128], st.identf[:]),
            r=[f"zs{zi}", "identf"], w=[f"pb{pbi}"])
        dst = dst_t(tc)[:, mi * 128:mi * 128 + 128]
        dk = f"{dstk}{tc}"
        if add:
            if ptok is None:
                p.v(lambda e, dst=dst, pbi=pbi, s=scale: e.scalar_tensor_tensor(
                    dst, st.pb[pbi][:, 0:128], s, dst, ALU.mult, ALU.add),
                    r=[f"pb{pbi}", dk], w=[dk])
            else:
                p.v(lambda e, dst=dst, pbi=pbi, tc=tc: e.scalar_tensor_tensor(
                    dst, st.pb[pbi][:, 0:128], ptok(tc), dst, ALU.mult, ALU.add),
                    r=[f"pb{pbi}", dk] + list(ptok_res), w=[dk])
        else:
            if ptok is None:
                p.v(lambda e, dst=dst, pbi=pbi, s=scale: e.tensor_scalar_mul(
                    dst, st.pb[pbi][:, 0:128], s),
                    r=[f"pb{pbi}"], w=[dk])
            else:
                p.v(lambda e, dst=dst, pbi=pbi, tc=tc: e.tensor_scalar(
                    dst, st.pb[pbi][:, 0:128], ptok(tc), None, ALU.mult),
                    r=[f"pb{pbi}"] + list(ptok_res), w=[dk])


def emit_l0_attn(p, st):
    nc, hb = st.nc, st.hbm
    emit_ln_stats(p, st)
    emit_ln_to_ht(p, st)

    NHG = 4
    qT = salloc(st, "l0_qT", [128, 2 * T], BF16)
    kT = salloc(st, "l0_kT", [128, 2 * T], BF16)
    VH = salloc(st, "l0_VH", [128, NT * NHG * 65], BF16)
    atT = salloc(st, "l0_atT", [128, 4 * 512], BF16)
    vtmp = atT
    aoT = salloc(st, "l0_aoT", [128, NQB * ND * 512], BF16)
    wq = salloc(st, "l0_wq", [128, 2 * ND * 128], BF16)
    wk = salloc(st, "l0_wk", [128, 2 * ND * 128], BF16)
    wv = salloc(st, "l0_wv", [128, 2 * ND * 128], BF16)
    rrow = salloc(st, "l0_rr", [1, 512], F32)
    rb = salloc(st, "l0_rb", [64, 512], F32)
    nrm = salloc(st, "l0_nrm", [1, 64], F32)
    negM = salloc(st, "l0_negM", [128, NHG], F32)

    wqkv = hb["L0_attn_W_qkv"]
    bqkv = salloc(st, "l0_bqkv", [1, 3 * 16 * 64], BF16)
    for wi, (c0, c1) in enumerate(((0, 64), (64, 128), (128, 192))):
        p.dma("dq_g", bqkv[0:1, wi * 1024:(wi + 1) * 1024]
              .rearrange("o (h m) -> o h m", h=16),
              hb["L0_attn_b_qkv"][:, c0:c1], w=["r_l0b"])
    bo_row, bk_bo = load_row(p, st, "l0bo", hb["L0_attn_b_o"], D)

    for grp in range(16 // NHG):
        h0 = grp * NHG
        for j in range(NHG // 2):
            ha = h0 + 2 * j
            for (wt, wkey, c0, c1) in ((wq, f"wq{j}", 0, 64), (wk, f"wk{j}", 64, 128),
                                       (wv, f"wv{j}", 128, 192)):
                for hio in range(2):
                    p.dma("dq_g",
                          wt[:, j * ND * 128:(j + 1) * ND * 128]
                          .rearrange("p (c h m) -> p c h m", c=ND, h=2)[:, :, hio, :],
                          wqkv[ha + hio, :, c0:c1]
                          .rearrange("(c p) m -> p c m", p=128),
                          w=[wkey])

        for j in range(NHG // 2):
            ha = h0 + 2 * j
            for wi, (wt, wkey, c0, c1) in enumerate(
                    ((wq, f"wq{j}", 0, 64), (wk, f"wk{j}", 64, 128),
                     (wv, f"wv{j}", 128, 192))):
                for qb in range(NQB):
                    pbm = (wi * NQB + qb) % 2
                    for kc in range(ND):
                        p.t(lambda e, wt=wt, j=j, kc=kc, qb=qb, pbm=pbm: e.matmul(
                            st.pb[pbm][:, 0:512],
                            wt[:, j * ND * 128 + kc * 128: j * ND * 128 + kc * 128 + 128],
                            ht_c(st, kc)[:, qb * QB:qb * QB + QB],
                            start=(kc == 0), stop=False),
                            r=[wkey, f"ht{kc}"], w=[f"pb{pbm}"])
                    p.t(lambda e, ha=ha, wi=wi, pbm=pbm: e.matmul(
                        st.pb[pbm][:, 0:512],
                        bqkv[0:1, wi * 1024 + ha * 64: wi * 1024 + ha * 64 + 128],
                        st.ones512[0:1, 0:512], start=False, stop=True),
                        r=["r_l0b", "ones512"], w=[f"pb{pbm}"])
                    if wi == 0:
                        dst, dk = qT[:, j * T + qb * QB: j * T + qb * QB + QB], f"qT{j}"
                    elif wi == 1:
                        dst, dk = kT[:, j * T + qb * QB: j * T + qb * QB + QB], f"kT{j}"
                    else:
                        dst, dk = vtmp[:, qb * QB: qb * QB + QB], f"atT{qb}"
                    p.s(lambda e, dst=dst, pbm=pbm: e.copy(dst, st.pb[pbm][:, 0:512]),
                        r=[f"pb{pbm}"], w=[dk])
            for kc in range(NT):
                pti = kc % 2
                p.t(lambda e, kc=kc, pti=pti: e.transpose(
                    st.pt[pti][:, 0:128], vtmp[:, kc * 128:kc * 128 + 128],
                    st.ident[:]),
                    r=[f"atT{kc // 4}", "ident"], w=[f"pt{pti}"])
                base = kc * NHG * 65 + (2 * j) * 65
                dstv = VH[:, base:base + 130].rearrange("p (h m) -> p h m", h=2)[:, :, 0:64]
                p.v(lambda e, dstv=dstv, pti=pti: e.tensor_copy(
                    dstv, st.pt[pti][:, 0:128].rearrange("p (h m) -> p h m", h=2)),
                    r=[f"pt{pti}"], w=[f"VH{kc}_{j}"])
        if grp == 0:
            for kc in range(NT):
                on = VH[:, kc * NHG * 65:(kc + 1) * NHG * 65].rearrange(
                    "p (h m) -> p h m", h=NHG)[:, :, 64:65]
                p.v(lambda e, on=on: e.memset(on, 1.0), w=[f"VHo{kc}"])

        p.v(lambda e: e.memset(nrm[:, 0:32], 0.0), w=["nrm"])
        for j in range(NHG // 2):
            for wi, (src, srck) in enumerate(((qT, "qT"), (kT, "kT"))):
                for qb in range(NQB):
                    p.s(lambda e, src=src, j=j, qb=qb: e.activation(
                        out=st.sq[:, 0:512],
                        in_=src[:, j * T + qb * QB: j * T + qb * QB + QB],
                        func=AF.Square),
                        r=[f"{srck}{j}"], w=["htok0"])
                    for hio in range(2):
                        base = hio * 64
                        idx = wi * 16 + 2 * j + hio
                        p.t(lambda e, base=base: e.matmul(
                            st.pb[2][0:1, 0:512], st.onescol[base:base + 64, 0:1],
                            st.sq[base:base + 64, 0:512], start=True, stop=True),
                            r=["htok0", "onescol"], w=["pb2"])
                        p.v(lambda e: e.reduce_max(nrm[0:1, 40:41],
                                                   st.pb[2][0:1, 0:512], axis=AX.X),
                            r=["pb2"], w=["nrms"])
                        p.v(lambda e, idx=idx: e.tensor_max(
                            nrm[0:1, idx:idx + 1], nrm[0:1, idx:idx + 1],
                            nrm[0:1, 40:41]),
                            r=["nrms", "nrm"], w=["nrm"])
        p.v(lambda e: e.tensor_tensor(nrm[0:1, 32:32 + NHG], nrm[0:1, 0:NHG],
                                      nrm[0:1, 16:16 + NHG], ALU.mult),
            r=["nrm"], w=["nrmm"])
        p.s(lambda e: e.activation(out=nrm[0:1, 48:48 + NHG], in_=nrm[0:1, 32:32 + NHG],
                                   func=AF.Sqrt),
            r=["nrmm"], w=["nrmsq"])
        p.v(lambda e: e.tensor_scalar_mul(nrm[0:1, 56:56 + NHG], nrm[0:1, 48:48 + NHG],
                                          -0.125),
            r=["nrmsq"], w=["nrmneg"])
        p.t(lambda e: e.matmul(st.pb[2][0:128, 0:NHG], st.onesrow_f[0:1, 0:128],
                               nrm[0:1, 56:56 + NHG], start=True, stop=True),
            r=["nrmneg", "onesrowf"], w=["pb2"])
        p.v(lambda e: e.tensor_copy(negM[:, 0:NHG], st.pb[2][0:128, 0:NHG]),
            r=["pb2"], w=["negM"])

        for hidx in range(NHG):
            j, hio = hidx // 2, hidx % 2
            base = hio * 64
            h = h0 + hidx
            dc, rrow_off = h // 2, (h % 2) * 64
            for qb in range(NQB):
                for kg in range(4):
                    for ki in range(4):
                        kc = kg * 4 + ki
                        pbs = ki % 2
                        p.t(lambda e, j=j, base=base, kc=kc, qb=qb, pbs=pbs: e.matmul(
                            st.pb[pbs][:, 0:512],
                            kT[base:base + 64, j * T + kc * 128: j * T + kc * 128 + 128],
                            qT[base:base + 64, j * T + qb * QB: j * T + qb * QB + QB],
                            start=True, stop=True),
                            r=[f"kT{j}", f"qT{j}"], w=[f"pb{pbs}"])
                        p.s(lambda e, ki=ki, pbs=pbs, hidx=hidx: e.activation(
                            out=atT[:, ki * 512:ki * 512 + 512], in_=st.pb[pbs][:, 0:512],
                            func=AF.Exp, scale=0.125, bias=negM[:, hidx:hidx + 1]),
                            r=[f"pb{pbs}", "negM"], w=[f"atT{ki}"])
                    for ki in range(4):
                        kc = kg * 4 + ki
                        p.t(lambda e, kc=kc, ki=ki, hidx=hidx,
                            s0=(kg == 0 and ki == 0), s1=(kg == 3 and ki == 3): e.matmul(
                            st.pb[3][0:65, 0:512],
                            VH[:, kc * NHG * 65 + hidx * 65: kc * NHG * 65 + hidx * 65 + 65],
                            atT[:, ki * 512:ki * 512 + 512],
                            start=s0, stop=s1),
                            r=[f"atT{ki}", f"VH{kc}_{hidx // 2}", f"VHo{kc}"],
                            w=["pb3"])
                p.v(lambda e: e.reciprocal(rrow[0:1, 0:512], st.pb[3][64:65, 0:512]),
                    r=["pb3"], w=["rrow"])
                p.t(lambda e: e.matmul(st.pb[2][0:64, 0:512], st.onesrow_f[0:1, 0:64],
                                       rrow[0:1, 0:512], start=True, stop=True),
                    r=["rrow", "onesrowf"], w=["pb2"])
                p.s(lambda e: e.copy(rb[0:64, 0:512], st.pb[2][0:64, 0:512]),
                    r=["pb2"], w=["rb"])
                dst = aoT[rrow_off:rrow_off + 64,
                          (qb * ND + dc) * 512:(qb * ND + dc) * 512 + 512]
                p.v(lambda e, dst=dst: e.tensor_tensor(
                    dst, st.pb[3][0:64, 0:512], rb[0:64, 0:512], ALU.mult),
                    r=["pb3", "rb"], w=[f"aoT{qb}_{dc}_{rrow_off}"])

    st.dbg_l0 = dict(qT=qT, kT=kT, VH=VH, aoT=aoT, negM=negM, nrm=nrm, atT=atT)
    wo, wok = load_w(p, st, "l0_wo", hb["L0_attn_W_o"], D, D)
    for mi in range(ND):
        for qb in range(NQB):
            pbm = (mi * NQB + qb) % 2
            for kc in range(ND):
                p.t(lambda e, mi=mi, qb=qb, kc=kc, pbm=pbm: e.matmul(
                    st.pb[pbm][:, 0:512],
                    wo[:, kc * D + mi * 128: kc * D + mi * 128 + 128],
                    aoT[:, (qb * ND + kc) * 512: (qb * ND + kc) * 512 + 512],
                    start=(kc == 0), stop=False),
                    r=[wok, f"aoT{qb}_{kc}_0", f"aoT{qb}_{kc}_64"], w=[f"pb{pbm}"])
            p.t(lambda e, mi=mi, pbm=pbm: e.matmul(
                st.pb[pbm][:, 0:512], bo_row[0:1, mi * 128:mi * 128 + 128],
                st.ones512[0:1, 0:512], start=False, stop=True),
                r=["r_l0bo", "ones512"], w=[f"pb{pbm}"])
            emit_back_tile(p, st, f"pb{pbm}", st.pb[pbm][:, 0:512], mi, qb)


def emit_l0_ffn(p, st):
    nc, hb = st.nc, st.hbm
    F = 4096
    emit_ln_stats(p, st)
    p.v(lambda e: e.tensor_scalar(lane(st, 5), lane(st, 6), 1e-5, None, ALU.add),
        r=["st6"], w=["st5"])
    p.v(lambda e: e.reciprocal(lane(st, 5), lane(st, 5)), r=["st5"], w=["st5"])
    p.v(lambda e: e.tensor_tensor(lane(st, 5), lane(st, 6), lane(st, 5), ALU.mult),
        r=["st5", "st6"], w=["st5"])
    p.s(lambda e: e.activation(out=lane(st, 5), in_=lane(st, 5), func=AF.Sqrt,
                               bias=st.eps[:, 0:1], scale=1.0),
        r=["st5", "eps"], w=["st5"])
    p.v(lambda e: e.reciprocal(lane(st, 6), lane(st, 5)), r=["st5"], w=["st6"])
    p.v(lambda e: e.tensor_tensor(lane(st, 5), lane(st, 2), lane(st, 6), ALU.mult),
        r=["st2", "st6"], w=["st5"])
    p.v(lambda e: e.scalar_tensor_tensor(lane(st, 6), lane(st, 4), -1.0, lane(st, 5),
                                         ALU.mult, ALU.mult),
        r=["st4", "st5"], w=["st6"])
    p.v(lambda e: e.tensor_scalar(lane(st, 7), lane(st, 2), 1.0, None, ALU.add),
        r=["st2"], w=["st7"])

    gcol = salloc(st, "l0f_g", [128, 16], F32)
    for i, gk in enumerate(("L0_ffn_ln_gamma", "L0_ffn_ln_beta")):
        p.dma("dq_y", st.zslice[0:1, 0:D], hb[gk].rearrange("(o n) -> o n", o=1),
              w=["zs0", "zs1"])
        for dcg in range(ND):
            pti = dcg % 2
            p.t(lambda e, dcg=dcg, pti=pti: e.transpose(
                st.pb[4 + pti][0:128, 0:1],
                st.zslice[0:1, dcg * 128:dcg * 128 + 128], st.identf[0:1, 0:1]),
                r=["zs0", "zs1", "identf"], w=[f"pb{4 + pti}"])
            p.v(lambda e, i=i, dcg=dcg, pti=pti: e.tensor_copy(
                gcol[:, i * 8 + dcg:i * 8 + dcg + 1], st.pb[4 + pti][0:128, 0:1]),
                r=[f"pb{4 + pti}"], w=["gcol"])

    emit_ln_to_ht(p, st, xupdate=True, rlane=5, blane=6,
                  gamma_col=lambda dc: gcol[:, dc:dc + 1],
                  beta_col=lambda dc: gcol[:, 8 + dc:8 + dc + 1], gb_res=("gcol",))

    b2row, _ = load_row(p, st, "l0fb2", hb["L0_ffn_fc2_bias"], D)
    b1row, _ = load_row(p, st, "l0fb1", hb["L0_ffn_fc1_bias"], F)

    ybuf = salloc(st, "l0f_y", [128, 32 * 512], BF16)
    w1q = salloc(st, "l0f_w1q", [128, 2 * 4 * D], BF16)
    s1b = salloc(st, "l0f_s1b", [128, 4 * D], BF16)
    w2q = salloc(st, "l0f_w2q", [128, 2 * 4 * D], BF16)
    s2b = salloc(st, "l0f_s2b", [128, 4 * D], BF16)
    w1d, w2d = w1q, w2q
    B4 = 4 * D

    for qb in range(NQB):
        for oc in range(32):
            sl = (oc // 4) % 2
            if oc % 4 == 0:
                p.dma("dq_g", w1q[:, sl * B4:sl * B4 + B4]
                      .rearrange("p (c m) -> p c m", c=8),
                      hb["L0_ffn_fc1_q"][:, oc * 128:(oc + 4) * 128]
                      .rearrange("(c p) m -> p c m", p=128),
                      w=[f"w1q{sl}"])
                p.dma("dq_g", s1b[:, 0:B4]
                      .rearrange("p (g m) -> p g m", g=8),
                      hb["L0_ffn_fc1_scales"][:, oc * 128:(oc + 4) * 128]
                      .rearrange("(o g) m -> o g m", o=1).broadcast_to((128, 8, 512)),
                      w=["s1b"])
                p.v(lambda e, sl=sl: e.scalar_tensor_tensor(
                    w1d[:, sl * B4:sl * B4 + B4], w1q[:, sl * B4:sl * B4 + B4], -128.0,
                    s1b[:, 0:B4], ALU.add, ALU.mult),
                    r=[f"w1q{sl}", "s1b"], w=[f"w1q{sl}"])
            ob = oc % 4
            pbm = oc % 2
            for kc in range(ND):
                p.t(lambda e, sl=sl, ob=ob, kc=kc, qb=qb, pbm=pbm: e.matmul(
                    st.pb[pbm][:, 0:512],
                    w1d[:, sl * B4 + kc * 512 + ob * 128:sl * B4 + kc * 512 + ob * 128 + 128],
                    ht_c(st, kc)[:, qb * QB:qb * QB + QB],
                    start=(kc == 0), stop=False),
                    r=[f"w1q{sl}", f"ht{kc}"], w=[f"pb{pbm}"])
            p.t(lambda e, oc=oc, pbm=pbm: e.matmul(
                st.pb[pbm][:, 0:512], b1row[0:1, oc * 128:oc * 128 + 128],
                st.ones512[0:1, 0:512], start=False, stop=True),
                r=["r_l0fb1", "ones512"], w=[f"pb{pbm}"])
            p.s(lambda e, oc=oc, pbm=pbm: e.activation(
                out=ybuf[:, oc * 512:oc * 512 + 512], in_=st.pb[pbm][:, 0:512],
                func=AF.Silu),
                r=[f"pb{pbm}"], w=[f"yb{oc}"])
        pairs = [(tc, dh) for tc in range(4) for dh in range(2)]
        for wv in range(2):
            wave = pairs[wv * 4:(wv + 1) * 4]
            for kc in range(32):
                sl = (kc // 4) % 2
                if kc % 4 == 0:
                    p.dma("dq_g", w2q[:, sl * B4:sl * B4 + B4]
                          .rearrange("p (c m) -> p c m", c=4),
                          hb["L0_ffn_fc2_q"][kc * 128:(kc + 4) * 128, :]
                          .rearrange("(c p) m -> p c m", p=128),
                          w=[f"w2q{sl}"])
                    p.dma("dq_g", s2b[:, 0:B4]
                          .rearrange("p (g m) -> p g m", g=4),
                          hb["L0_ffn_fc2_scales"][kc:kc + 4, :]
                          .rearrange("(o g) m -> o g m", o=1)
                          .broadcast_to((128, 4, 1024)),
                          w=["s2b"])
                    p.v(lambda e, sl=sl: e.scalar_tensor_tensor(
                        w2d[:, sl * B4:sl * B4 + B4], w2q[:, sl * B4:sl * B4 + B4],
                        -128.0, s2b[:, 0:B4], ALU.add, ALU.mult),
                        r=[f"w2q{sl}", "s2b"], w=[f"w2q{sl}"])
                kb = kc % 4
                for i, (tc, dh) in enumerate(wave):
                    p.t(lambda e, kc=kc, sl=sl, kb=kb, tc=tc, dh=dh, i=i: e.matmul(
                        st.pb[i][:, 0:512],
                        ybuf[:, kc * 512 + tc * 128: kc * 512 + tc * 128 + 128],
                        w2d[:, sl * B4 + kb * D + dh * 512: sl * B4 + kb * D + dh * 512 + 512],
                        start=(kc == 0), stop=False),
                        r=[f"w2q{sl}", f"yb{kc}"], w=[f"pb{i}"])
            for i, (tc, dh) in enumerate(wave):
                p.t(lambda e, dh=dh, i=i: e.matmul(
                    st.pb[i][:, 0:512], st.ones512[0:1, 0:128],
                    b2row[0:1, dh * 512:dh * 512 + 512], start=False, stop=True),
                    r=["r_l0fb2", "ones512"], w=[f"pb{i}"])
                tt = qb * 4 + tc
                dst = x_t(st, tt)[:, dh * 512:dh * 512 + 512]
                p.v(lambda e, dst=dst, i=i: e.scalar_tensor_tensor(
                    dst, st.pb[i][:, 0:512], 0.5, dst, ALU.mult, ALU.add),
                    r=[f"pb{i}", f"x{tt}"], w=[f"x{tt}"])


def emit_l1_attn(p, st):
    nc, hb = st.nc, st.hbm
    emit_ln_stats(p, st)
    emit_ln_to_ht(p, st)

    qT = salloc(st, "l1_qT", [128, ND * T], BF16)
    kT = salloc(st, "l1_kT", [128, ND * T], BF16)
    VH = st.ht
    atT = salloc(st, "l1_atT", [128, 16 * 512], BF16)
    wbuf = salloc(st, "l1_wb", [128, 2 * ND * 128], BF16)
    cm = [salloc(st, f"l1_cm{i}", [128, 512], BF16) for i in range(4)]
    for i in range(4):
        p.dma("dq_y", cm[i][:], hb[f"c_cmask{i}"], w=[f"cm{i}"])
    vdram = nc.dram_tensor("l1_vdram", [D, T], BF16, kind="Internal").ap()
    rrow = salloc(st, "l1_rr", [1, 512], F32)
    rcol = salloc(st, "l1_rc", [128, 4], F32)
    nrm = salloc(st, "l1_nrm", [1, 64], F32)
    negM = salloc(st, "l1_negM", [128, 1], F32)

    for wi, (wname, dstT, dk) in enumerate((("L1_attn_Wq", qT, "l1q"),
                                            ("L1_attn_Wk", kT, "l1k"))):
        for mi in range(ND):
            sl = mi % 2
            p.dma("dq_g", wbuf[:, sl * ND * 128:(sl + 1) * ND * 128]
                  .rearrange("p (c m) -> p c m", c=ND),
                  hb[wname][:, mi * 128:(mi + 1) * 128]
                  .rearrange("(c p) m -> p c m", p=128),
                  w=[f"l1wb{sl}"])
            for qb in range(NQB):
                pbm = (mi * NQB + qb) % 2
                for kc in range(ND):
                    p.t(lambda e, sl=sl, kc=kc, qb=qb, pbm=pbm: e.matmul(
                        st.pb[pbm][:, 0:512],
                        wbuf[:, sl * ND * 128 + kc * 128: sl * ND * 128 + kc * 128 + 128],
                        ht_c(st, kc)[:, qb * QB:qb * QB + QB],
                        start=(kc == 0), stop=(kc == ND - 1)),
                        r=[f"l1wb{sl}", f"ht{kc}"], w=[f"pb{pbm}"])
                p.s(lambda e, dstT=dstT, mi=mi, qb=qb, pbm=pbm: e.copy(
                    dstT[:, mi * T + qb * QB: mi * T + qb * QB + QB],
                    st.pb[pbm][:, 0:512]),
                    r=[f"pb{pbm}"], w=[f"{dk}{mi}"])
    for mi in range(ND):
        sl = mi % 2
        p.dma("dq_g", wbuf[:, sl * ND * 128:(sl + 1) * ND * 128]
              .rearrange("p (c m) -> p c m", c=ND),
              hb["L1_attn_Wv"][:, mi * 128:(mi + 1) * 128]
              .rearrange("(c p) m -> p c m", p=128),
              w=[f"l1wb{sl}"])
        for qb in range(NQB):
            pbm = (mi * NQB + qb) % 2
            for kc in range(ND):
                p.t(lambda e, sl=sl, kc=kc, qb=qb, pbm=pbm: e.matmul(
                    st.pb[pbm][:, 0:512],
                    wbuf[:, sl * ND * 128 + kc * 128: sl * ND * 128 + kc * 128 + 128],
                    ht_c(st, kc)[:, qb * QB:qb * QB + QB],
                    start=(kc == 0), stop=(kc == ND - 1)),
                    r=[f"l1wb{sl}", f"ht{kc}"], w=[f"pb{pbm}"])
            vsl = (mi * NQB + qb) % 2
            p.s(lambda e, vsl=vsl, pbm=pbm: e.copy(
                st.zslice[:, vsl * 512:vsl * 512 + 512].bitcast(BF16)[:, 0:512]
                if False else atT[:, vsl * 512:vsl * 512 + 512],
                st.pb[pbm][:, 0:512]),
                r=[f"pb{pbm}"], w=[f"atT{vsl}"])
            p.dma("dq_y", vdram[mi * 128:(mi + 1) * 128, qb * QB:qb * QB + QB],
                  atT[:, vsl * 512:vsl * 512 + 512], r=[f"atT{vsl}"],
                  w=[f"vd{mi}_{qb}"])
    for kc in range(NT):
        p.dma("dq_y", st.ht[:, kc * D:(kc + 1) * D],
              vdram[:, kc * 128:(kc + 1) * 128],
              r=[f"vd{mi}_{qb}" for mi in range(ND) for qb in range(NQB)],
              w=[f"vT{kc}", f"ht{kc // 2}"], transpose=True)

    p.v(lambda e: e.memset(nrm[:, 0:2], 0.0), w=["nrm"])
    for wi, (src, srck) in enumerate(((qT, "l1q"), (kT, "l1k"))):
        for qb in range(NQB):
            for dc in range(ND):
                p.s(lambda e, src=src, dc=dc, qb=qb: e.activation(
                    out=st.sq[:, 0:512],
                    in_=src[:, dc * T + qb * QB: dc * T + qb * QB + QB],
                    func=AF.Square),
                    r=[f"{srck}{dc}"], w=["htok0"])
                p.t(lambda e, dc=dc: e.matmul(
                    st.pb[2][0:1, 0:512], st.onescol[0:128, 0:1],
                    st.sq[:, 0:512], start=(dc == 0), stop=(dc == ND - 1)),
                    r=["htok0", "onescol"], w=["pb2"])
            p.v(lambda e: e.reduce_max(nrm[0:1, 40:41], st.pb[2][0:1, 0:512],
                                       axis=AX.X),
                r=["pb2"], w=["nrms"])
            p.v(lambda e, wi=wi: e.tensor_max(nrm[0:1, wi:wi + 1],
                                              nrm[0:1, wi:wi + 1], nrm[0:1, 40:41]),
                r=["nrms", "nrm"], w=["nrm"])
    p.v(lambda e: e.tensor_tensor(nrm[0:1, 2:3], nrm[0:1, 0:1], nrm[0:1, 1:2],
                                  ALU.mult), r=["nrm"], w=["nrmm"])
    p.s(lambda e: e.activation(out=nrm[0:1, 3:4], in_=nrm[0:1, 2:3], func=AF.Sqrt),
        r=["nrmm"], w=["nrmsq"])
    p.v(lambda e: e.tensor_scalar_mul(nrm[0:1, 4:5], nrm[0:1, 3:4], -1.0 / 32.0),
        r=["nrmsq"], w=["nrmneg"])
    p.t(lambda e: e.matmul(st.pb[2][0:128, 0:1], st.onesrow_f[0:1, 0:128],
                           nrm[0:1, 4:5], start=True, stop=True),
        r=["nrmneg", "onesrowf"], w=["pb2"])
    p.v(lambda e: e.tensor_copy(negM[:, 0:1], st.pb[2][0:128, 0:1]),
        r=["pb2"], w=["negM"])

    for qb in range(NQB):
        nkc = qb * 4 + 4
        for kc in range(nkc):
            pbs = kc % 2
            for dc in range(ND):
                p.t(lambda e, dc=dc, kc=kc, qb=qb, pbs=pbs: e.matmul(
                    st.pb[pbs][:, 0:512],
                    kT[:, dc * T + kc * 128: dc * T + kc * 128 + 128],
                    qT[:, dc * T + qb * QB: dc * T + qb * QB + QB],
                    start=(dc == 0), stop=(dc == ND - 1)),
                    r=[f"l1k{dc}", f"l1q{dc}"], w=[f"pb{pbs}"])
            p.s(lambda e, kc=kc, pbs=pbs: e.activation(
                out=atT[:, kc * 512:kc * 512 + 512], in_=st.pb[pbs][:, 0:512],
                func=AF.Exp, scale=1.0 / 32.0, bias=negM[:, 0:1]),
                r=[f"pb{pbs}", "negM"], w=[f"atT{kc}"])
            if kc >= qb * 4:
                off = kc - qb * 4
                p.v(lambda e, kc=kc, off=off: e.tensor_tensor(
                    atT[:, kc * 512:kc * 512 + 512], atT[:, kc * 512:kc * 512 + 512],
                    cm[off][:], ALU.mult),
                    r=[f"atT{kc}", f"cm{off}"], w=[f"atT{kc}"])
        for kc in range(nkc):
            p.t(lambda e, kc=kc, s0=(kc == 0), s1=(kc == nkc - 1): e.matmul(
                st.pb[2][0:1, 0:512], st.onescol[0:128, 0:1],
                atT[:, kc * 512:kc * 512 + 512], start=s0, stop=s1),
                r=[f"atT{kc}", "onescol"], w=["pb2"])
        p.v(lambda e: e.reciprocal(rrow[0:1, 0:512], st.pb[2][0:1, 0:512]),
            r=["pb2"], w=["rrow"])
        for j in range(4):
            p.t(lambda e, j=j: e.transpose(
                st.pb[3][0:128, 0:1], rrow[0:1, j * 128:j * 128 + 128],
                st.identf[0:1, 0:1]),
                r=["rrow", "identf"], w=["pb3"])
            p.v(lambda e, j=j: e.tensor_copy(rcol[:, j:j + 1], st.pb[3][0:128, 0:1]),
                r=["pb3"], w=["rcol"])
        for mi in range(ND):
            pbm = mi % 2
            for kc in range(nkc):
                p.t(lambda e, kc=kc, mi=mi, s0=(kc == 0), s1=(kc == nkc - 1),
                    pbm=pbm: e.matmul(
                    st.pb[pbm][:, 0:512],
                    st.ht[:, kc * D + mi * 128: kc * D + mi * 128 + 128],
                    atT[:, kc * 512:kc * 512 + 512], start=s0, stop=s1),
                    r=[f"vT{kc}", f"atT{kc}"], w=[f"pb{pbm}"])
            emit_back_tile(p, st, f"pb{pbm}", st.pb[pbm][:, 0:512], mi, qb,
                           ptok=lambda tc, qb=qb: rcol[:, tc - qb * 4:tc - qb * 4 + 1],
                           ptok_res=("rcol",))


def emit_l1_ffn(p, st):
    nc, hb = st.nc, st.hbm
    E, HID = 8, 512
    emit_ln_stats(p, st)
    emit_ln_to_ht(p, st)

    rwn = None
    rwT = salloc(st, "l1f_rwT", [128, ND * 8], BF16)
    hs = salloc(st, "l1f_hs", [128, 4 * 512], BF16)
    rwn = hs[0:8, 0:D]
    p.dma("dq_g", rwn, hb["L1_ffn_routing_W"], w=["rwn", "hs0", "hs1"])
    for dc in range(ND):
        pti = dc % 2
        p.t(lambda e, dc=dc, pti=pti: e.transpose(
            st.pt[pti][0:128, 0:8], rwn[:, dc * 128:(dc + 1) * 128], st.ident[0:8, 0:8]),
            r=["rwn", "hs0", "hs1", "ident"], w=[f"pt{pti}"])
        p.v(lambda e, dc=dc, pti=pti: e.tensor_copy(
            rwT[:, dc * 8:(dc + 1) * 8], st.pt[pti][0:128, 0:8]),
            r=[f"pt{pti}"], w=["rwT"])
    ebrow, _ = load_row(p, st, "l1feb", hb["L1_ffn_expert_bias"], E)
    logT = salloc(st, "l1f_logT", [8, T], BF16)
    actT = salloc(st, "l1f_actT", [8, T], BF16)
    acts = salloc(st, "l1f_acts", [128, NT * 8], F32)
    cft = salloc(st, "l1f_cft", [128, NT * 8], BF16)
    cfT = actT
    ones8 = salloc(st, "l1f_o8", [8, 1], F32)
    p.v(lambda e: e.memset(ones8[:], 1.0), w=["ones8"])
    sel8 = salloc(st, "l1f_sel8", [8, 8 * 128], BF16)
    p.dma("dq_y", sel8[:], hb["c_sel8"], w=["sel8"])

    for qb in range(NQB):
        pbm = qb % 2
        for dc in range(ND):
            p.t(lambda e, dc=dc, qb=qb, pbm=pbm: e.matmul(
                st.pb[pbm][0:8, 0:512], rwT[:, dc * 8:(dc + 1) * 8],
                ht_c(st, dc)[:, qb * QB:qb * QB + QB],
                start=(dc == 0), stop=False),
                r=["rwT", f"ht{dc}"], w=[f"pb{pbm}"])
        p.t(lambda e, qb=qb, pbm=pbm: e.matmul(
            st.pb[pbm][0:8, 0:512], ebrow[0:1, 0:8], st.ones512[0:1, 0:512],
            start=False, stop=True),
            r=["r_l1feb", "ones512"], w=[f"pb{pbm}"])
        p.s(lambda e, qb=qb, pbm=pbm: e.copy(
            logT[:, qb * QB:qb * QB + QB], st.pb[pbm][0:8, 0:512]),
            r=[f"pb{pbm}"], w=["logT"])
    p.s(lambda e: e.activation(out=actT[:], in_=logT[:], func=AF.Abs),
        r=["logT"], w=["actT"])
    p.s(lambda e: e.activation(out=actT[:], in_=actT[:], func=AF.Exp, scale=-1.0),
        r=["actT"], w=["actT"])
    p.s(lambda e: e.activation(out=actT[:], in_=actT[:], func=AF.Ln,
                               bias=ones8[:, 0:1], scale=1.0),
        r=["actT", "ones8"], w=["actT"])
    p.v(lambda e: e.tensor_scalar_max(logT[:], logT[:], 0.0),
        r=["logT"], w=["logT"])
    p.v(lambda e: e.tensor_add(actT[:], actT[:], logT[:]),
        r=["actT", "logT"], w=["actT"])
    p.s(lambda e: e.activation(out=actT[:], in_=actT[:], func=AF.Sqrt),
        r=["actT"], w=["actT"])
    for tc in range(NT):
        pti = tc % 2
        p.t(lambda e, tc=tc, pti=pti: e.transpose(
            st.pt[pti][0:128, 0:8], actT[:, tc * 128:(tc + 1) * 128],
            st.ident[0:8, 0:8]),
            r=["actT", "ident"], w=[f"pt{pti}"])
        p.v(lambda e, tc=tc, pti=pti: e.tensor_copy(
            acts[:, tc * 8:(tc + 1) * 8], st.pt[pti][0:128, 0:8]),
            r=[f"pt{pti}"], w=["acts"])
    for tc in range(NT):
        a = acts[:, tc * 8:(tc + 1) * 8]
        c = cft[:, tc * 8:(tc + 1) * 8]
        m1 = lane(st, 5, tc)
        m2 = lane(st, 6, tc)
        p.v(lambda e, a=a, m1=m1: e.reduce_max(m1, a, axis=AX.X),
            r=["acts"], w=["st5"])
        p.v(lambda e, a=a, c=c, m1=m1: e.tensor_scalar(c, a, m1, None, ALU.is_ge),
            r=["acts", "st5"], w=["cft"])
        p.v(lambda e, a=a, c=c: e.scalar_tensor_tensor(c, c, -30000.0, a,
                                                       ALU.mult, ALU.add),
            r=["cft", "acts"], w=["cft"])
        p.v(lambda e, c=c, m2=m2: e.reduce_max(m2, c, axis=AX.X),
            r=["cft"], w=["st6"])
        p.v(lambda e, a=a, c=c, m2=m2: e.tensor_scalar(c, a, m2, None, ALU.is_ge),
            r=["acts", "st6"], w=["cft"])
        p.v(lambda e, a=a, c=c: e.tensor_tensor(c, c, a, ALU.mult),
            r=["cft", "acts"], w=["cft"])
    for tc in range(NT):
        pti = tc % 2
        p.t(lambda e, tc=tc, pti=pti: e.transpose(
            st.pt[pti][0:8, 0:128], cft[:, tc * 8:(tc + 1) * 8], st.ident[:]),
            r=["cft", "ident"], w=[f"pt{pti}"])
        p.v(lambda e, tc=tc, pti=pti: e.tensor_copy(
            cfT[:, tc * 128:(tc + 1) * 128], st.pt[pti][0:8, 0:128]),
            r=[f"pt{pti}"], w=["cfT", "actT"])

    sw1 = salloc(st, "l1f_sw1", [128, ND * HID], BF16)
    sw2 = salloc(st, "l1f_sw2", [128, 4 * D], BF16)
    p.dma("dq_g", sw1[:].rearrange("p (c m) -> p c m", c=ND),
          hb["L1_ffn_sW1"].rearrange("(c p) m -> p c m", p=128), w=["sw1"])
    p.dma("dq_g", sw2[:].rearrange("p (c m) -> p c m", c=4),
          hb["L1_ffn_sW2"].rearrange("(c p) m -> p c m", p=128), w=["sw2"])
    sb1row, _ = load_row(p, st, "l1fsb1", hb["L1_ffn_sb1"], HID)
    sb2row, _ = load_row(p, st, "l1fsb2", hb["L1_ffn_sb2"], D)
    rb1row, _ = load_row(p, st, "l1frb1", hb["L1_ffn_rb1"].rearrange("e h -> (e h)"),
                         E * HID)
    rb2n = salloc(st, "l1f_rb2", [8, D], BF16)
    p.dma("dq_g", rb2n[:], hb["L1_ffn_rb2"], w=["rb2n"])
    rw1 = salloc(st, "l1f_rw1", [128, 2 * ND * HID], BF16)
    rw2 = salloc(st, "l1f_rw2", [128, 8 * 512], BF16)

    hdn = salloc(st, "l1f_hdn", [128, E * 4 * 512], BF16)
    cfb = salloc(st, "l1f_cfb", [128, 512], BF16)

    for qb in range(NQB):
        for hc in range(4):
            pbm = hc % 2
            for kc in range(ND):
                p.t(lambda e, hc=hc, kc=kc, qb=qb, pbm=pbm: e.matmul(
                    st.pb[pbm][:, 0:512], sw1[:, kc * HID + hc * 128: kc * HID + hc * 128 + 128],
                    ht_c(st, kc)[:, qb * QB:qb * QB + QB],
                    start=(kc == 0), stop=False),
                    r=["sw1", f"ht{kc}"], w=[f"pb{pbm}"])
            p.t(lambda e, hc=hc, pbm=pbm: e.matmul(
                st.pb[pbm][:, 0:512], sb1row[0:1, hc * 128:hc * 128 + 128],
                st.ones512[0:1, 0:512], start=False, stop=True),
                r=["r_l1fsb1", "ones512"], w=[f"pb{pbm}"])
            p.s(lambda e, hc=hc, pbm=pbm: e.activation(
                out=hs[:, hc * 512:hc * 512 + 512], in_=st.pb[pbm][:, 0:512],
                func=AF.Silu),
                r=[f"pb{pbm}"], w=[f"hs{hc}"])
        for ei in range(E):
            sl = ei % 2
            p.dma("dq_g", rw1[:, sl * ND * HID:(sl + 1) * ND * HID]
                  .rearrange("p (c m) -> p c m", c=ND),
                  hb["L1_ffn_rW1"][ei].rearrange("(c p) m -> p c m", p=128),
                  w=[f"rw1{sl}"])
            p.t(lambda e, ei=ei, qb=qb: e.matmul(
                st.pb[4][0:128, 0:512], sel8[:, ei * 128:(ei + 1) * 128],
                cfT[0:8, qb * QB:qb * QB + QB],
                start=True, stop=True),
                r=["cfT", "sel8"], w=["pb4"])
            p.v(lambda e: e.tensor_copy(cfb[:], st.pb[4][0:128, 0:512]),
                r=["pb4"], w=["cfb"])
            for hc in range(4):
                pbm = 2 + hc % 2
                for kc in range(ND):
                    p.t(lambda e, sl=sl, hc=hc, kc=kc, qb=qb, pbm=pbm: e.matmul(
                        st.pb[pbm][:, 0:512],
                        rw1[:, sl * ND * HID + kc * HID + hc * 128:
                            sl * ND * HID + kc * HID + hc * 128 + 128],
                        ht_c(st, kc)[:, qb * QB:qb * QB + QB],
                        start=(kc == 0), stop=False),
                        r=[f"rw1{sl}", f"ht{kc}"], w=[f"pb{pbm}"])
                p.t(lambda e, ei=ei, hc=hc, pbm=pbm: e.matmul(
                    st.pb[pbm][:, 0:512],
                    rb1row[0:1, ei * HID + hc * 128: ei * HID + hc * 128 + 128],
                    st.ones512[0:1, 0:512], start=False, stop=True),
                    r=["r_l1frb1", "ones512"], w=[f"pb{pbm}"])
                p.s(lambda e, ei=ei, hc=hc, pbm=pbm: e.activation(
                    out=hdn[:, (ei * 4 + hc) * 512:(ei * 4 + hc) * 512 + 512],
                    in_=st.pb[pbm][:, 0:512], func=AF.Silu),
                    r=[f"pb{pbm}"], w=[f"hdn{ei}_{hc}"])
                p.v(lambda e, ei=ei, hc=hc: e.tensor_tensor(
                    hdn[:, (ei * 4 + hc) * 512:(ei * 4 + hc) * 512 + 512],
                    hdn[:, (ei * 4 + hc) * 512:(ei * 4 + hc) * 512 + 512],
                    cfb[:], ALU.mult),
                    r=[f"hdn{ei}_{hc}", "cfb"], w=[f"hdn{ei}_{hc}"])
        for mi in range(ND):
            pbm = mi % 2
            for kc in range(4):
                p.t(lambda e, kc=kc, mi=mi, qb=qb, pbm=pbm: e.matmul(
                    st.pb[pbm][:, 0:512], sw2[:, kc * D + mi * 128: kc * D + mi * 128 + 128],
                    hs[:, kc * 512:kc * 512 + 512],
                    start=(kc == 0), stop=False),
                    r=["sw2", f"hs{kc}"], w=[f"pb{pbm}"])
            p.t(lambda e, mi=mi, pbm=pbm: e.matmul(
                st.pb[pbm][:, 0:512], sb2row[0:1, mi * 128:mi * 128 + 128],
                st.ones512[0:1, 0:512], start=False, stop=False),
                r=["r_l1fsb2", "ones512"], w=[f"pb{pbm}"])
            p.t(lambda e, mi=mi, qb=qb, pbm=pbm: e.matmul(
                st.pb[pbm][:, 0:512], rb2n[:, mi * 128:mi * 128 + 128],
                cfT[:, qb * QB:qb * QB + QB], start=False, stop=False),
                r=["rb2n", "cfT"], w=[f"pb{pbm}"])
            sl2 = 0
            p.dma("dq_g", rw2[:, sl2 * E * 512:(sl2 + 1) * E * 512]
                  .rearrange("p (ee c m) -> p ee c m", ee=E, c=4),
                  hb["L1_ffn_rW2"][:, :, mi * 128:(mi + 1) * 128]
                  .rearrange("ee (c p) m -> p ee c m", p=128),
                  w=[f"rw2{sl2}"])
            for ei in range(E):
                for kc in range(4):
                    p.t(lambda e, sl2=sl2, kc=kc, ei=ei, s1=(ei == E - 1 and kc == 3),
                        pbm=pbm: e.matmul(
                        st.pb[pbm][:, 0:512],
                        rw2[:, sl2 * E * 512 + (ei * 4 + kc) * 128:
                            sl2 * E * 512 + (ei * 4 + kc) * 128 + 128],
                        hdn[:, (ei * 4 + kc) * 512:(ei * 4 + kc) * 512 + 512],
                        start=False, stop=s1),
                        r=[f"rw2{sl2}", f"hdn{ei}_{kc}"], w=[f"pb{pbm}"])
            emit_back_tile(p, st, f"pb{pbm}", st.pb[pbm][:, 0:512], mi, qb)


def emit_l2_attn(p, st):
    nc, hb = st.nc, st.hbm
    emit_ln_stats(p, st)
    emit_ln_to_ht(p, st)

    qT = salloc(st, "l2_qT", [128, ND * T], BF16)
    kTd = salloc(st, "l2_kTd", [128, 4 * T], BF16)
    VH = st.ht[:, 0:NT * 4 * 65]
    atT = st.ht[:, 8192:8192 + 4 * 512]
    aoT = salloc(st, "l2_aoT", [128, NQB * ND * 512], BF16)
    wbuf = salloc(st, "l2_wb", [128, 2 * ND * 128], BF16)
    ropc = salloc(st, "l2_ropc", [128, 2 * 512], F32)
    rops = salloc(st, "l2_rops", [128, 2 * 512], F32)
    rsc = salloc(st, "l2_rsc", [128, 2 * 512], BF16)
    rrow = salloc(st, "l2_rr", [1, 512], F32)
    rb = salloc(st, "l2_rb", [64, 512], F32)
    nrm = salloc(st, "l2_nrm", [1, 64], F32)
    negM = salloc(st, "l2_negM", [128, 16], F32)
    SCALE = 1.0 / 32.0

    def vh_keys(kc):
        return [f"vT{kc}", f"ht{(kc * 4 * 65) // T}", f"ht{((kc + 1) * 4 * 65 - 1) // T}"]

    for mi in range(ND):
        sl = mi % 2
        p.dma("dq_g", wbuf[:, sl * ND * 128:(sl + 1) * ND * 128]
              .rearrange("p (c m) -> p c m", c=ND),
              hb["L2_attn_Wq"][:, mi * 128:(mi + 1) * 128]
              .rearrange("(c p) m -> p c m", p=128),
              w=[f"l2wb{sl}"])
        for qb in range(NQB):
            pbm = (mi * NQB + qb) % 2
            for kc in range(ND):
                p.t(lambda e, sl=sl, kc=kc, qb=qb, pbm=pbm: e.matmul(
                    st.pb[pbm][:, 0:512],
                    wbuf[:, sl * ND * 128 + kc * 128: sl * ND * 128 + kc * 128 + 128],
                    ht_c(st, kc)[:, qb * QB:qb * QB + QB],
                    start=(kc == 0), stop=(kc == ND - 1)),
                    r=[f"l2wb{sl}", f"ht{kc}"], w=[f"pb{pbm}"])
            p.s(lambda e, mi=mi, qb=qb, pbm=pbm: e.copy(
                qT[:, mi * T + qb * QB: mi * T + qb * QB + QB], st.pb[pbm][:, 0:512]),
                r=[f"pb{pbm}"], w=[f"l2q{mi}"])
    for kv in range(4):
        sl = kv % 2
        for half in range(2):
            p.dma("dq_g", wbuf[:, sl * ND * 128:(sl + 1) * ND * 128]
                  .rearrange("p (c m) -> p c m", c=ND)[:, :, half * 64:half * 64 + 64],
                  hb["L2_attn_Wk"][:, kv * 64:(kv + 1) * 64]
                  .rearrange("(c p) m -> p c m", p=128),
                  w=[f"l2wb{sl}"])
        for qb in range(NQB):
            pbm = (kv * NQB + qb) % 2
            for kc in range(ND):
                p.t(lambda e, sl=sl, kc=kc, qb=qb, pbm=pbm: e.matmul(
                    st.pb[pbm][:, 0:512],
                    wbuf[:, sl * ND * 128 + kc * 128: sl * ND * 128 + kc * 128 + 128],
                    ht_c(st, kc)[:, qb * QB:qb * QB + QB],
                    start=(kc == 0), stop=(kc == ND - 1)),
                    r=[f"l2wb{sl}", f"ht{kc}"], w=[f"pb{pbm}"])
            p.s(lambda e, kv=kv, qb=qb, pbm=pbm: e.copy(
                kTd[:, kv * T + qb * QB: kv * T + qb * QB + QB], st.pb[pbm][:, 0:512]),
                r=[f"pb{pbm}"], w=[f"l2k{kv}"])
    for vi in range(2):
        sl = vi % 2
        p.dma("dq_g", wbuf[:, sl * ND * 128:(sl + 1) * ND * 128]
              .rearrange("p (c m) -> p c m", c=ND),
              hb["L2_attn_Wv"][:, vi * 128:(vi + 1) * 128]
              .rearrange("(c p) m -> p c m", p=128),
              w=[f"l2wb{sl}"])
        for qb in range(NQB):
            pbm = (vi * NQB + qb) % 2
            for kc in range(ND):
                p.t(lambda e, sl=sl, kc=kc, qb=qb, pbm=pbm: e.matmul(
                    st.pb[pbm][:, 0:512],
                    wbuf[:, sl * ND * 128 + kc * 128: sl * ND * 128 + kc * 128 + 128],
                    ht_c(st, kc)[:, qb * QB:qb * QB + QB],
                    start=(kc == 0), stop=(kc == ND - 1)),
                    r=[f"l2wb{sl}", f"ht{kc}"], w=[f"pb{pbm}"])
            vsl = (vi * NQB + qb) % 4
            p.s(lambda e, vsl=vsl, pbm=pbm: e.copy(
                atT[:, vsl * 512:vsl * 512 + 512], st.pb[pbm][:, 0:512]),
                r=[f"pb{pbm}"], w=[f"atT{vsl}", "ht4"])
            for j in range(4):
                kc = qb * 4 + j
                pti = j % 2
                p.t(lambda e, vsl=vsl, j=j, pti=pti: e.transpose(
                    st.pt[pti][:, 0:128],
                    atT[:, vsl * 512 + j * 128: vsl * 512 + j * 128 + 128],
                    st.ident[:]),
                    r=[f"atT{vsl}", "ht4", "ident"], w=[f"pt{pti}"])
                base = kc * 4 * 65 + (2 * vi) * 65
                dstv = st.ht[:, base:base + 130].rearrange("p (h m) -> p h m", h=2)[:, :, 0:64]
                p.v(lambda e, dstv=dstv, pti=pti: e.tensor_copy(
                    dstv, st.pt[pti][:, 0:128].rearrange("p (h m) -> p h m", h=2)),
                    r=[f"pt{pti}"], w=vh_keys(kc))
    for kc in range(NT):
        on = st.ht[:, kc * 4 * 65:(kc + 1) * 4 * 65].rearrange(
            "p (h m) -> p h m", h=4)[:, :, 64:65]
        p.v(lambda e, on=on: e.memset(on, 1.0), w=vh_keys(kc))

    swapmask = [pp ^ 1 for pp in range(32)]
    for dc in range(ND):
        for qb in range(NQB):
            sl = (dc * NQB + qb) % 2
            p.dma("dq_y", ropc[:, sl * 512:sl * 512 + 512],
                  hb["c_ropeq_c"][dc * 128:(dc + 1) * 128, qb * QB:qb * QB + QB],
                  w=[f"ropc{sl}"])
            p.dma("dq_y", rops[:, sl * 512:sl * 512 + 512],
                  hb["c_ropeq_s"][dc * 128:(dc + 1) * 128, qb * QB:qb * QB + QB],
                  w=[f"rops{sl}"])
            qsl = qT[:, dc * T + qb * QB: dc * T + qb * QB + QB]
            p.v(lambda e, qsl=qsl, sl=sl: e.stream_shuffle(
                rsc[:, sl * 512:sl * 512 + 512], qsl, swapmask),
                r=[f"l2q{dc}"], w=[f"rsc{sl}"])
            p.v(lambda e, sl=sl: e.tensor_tensor(
                rsc[:, sl * 512:sl * 512 + 512], rsc[:, sl * 512:sl * 512 + 512],
                rops[:, sl * 512:sl * 512 + 512], ALU.mult),
                r=[f"rsc{sl}", f"rops{sl}"], w=[f"rsc{sl}"])
            p.v(lambda e, qsl=qsl, sl=sl: e.scalar_tensor_tensor(
                rsc[:, sl * 512:sl * 512 + 512], qsl, 1.0,
                rsc[:, sl * 512:sl * 512 + 512], ALU.bypass, ALU.bypass),
                r=[f"l2q{dc}", f"rsc{sl}"], w=[f"rsc{sl}"]) if False else None
            p.v(lambda e, qsl=qsl, sl=sl: e.tensor_tensor(
                qsl, qsl, ropc[:, sl * 512:sl * 512 + 512], ALU.mult),
                r=[f"l2q{dc}", f"ropc{sl}"], w=[f"l2q{dc}"])
            p.v(lambda e, qsl=qsl, sl=sl: e.tensor_tensor(
                qsl, qsl, rsc[:, sl * 512:sl * 512 + 512], ALU.add),
                r=[f"l2q{dc}", f"rsc{sl}"], w=[f"l2q{dc}"])
    for kv in range(4):
        for qb in range(NQB):
            sl = (kv * NQB + qb) % 2
            p.dma("dq_y", ropc[:, sl * 512:sl * 512 + 512],
                  hb["c_ropek_c"][:, qb * QB:qb * QB + QB], w=[f"ropc{sl}"])
            p.dma("dq_y", rops[:, sl * 512:sl * 512 + 512],
                  hb["c_ropek_s"][:, qb * QB:qb * QB + QB], w=[f"rops{sl}"])
            ksl = kTd[:, kv * T + qb * QB: kv * T + qb * QB + QB]
            p.v(lambda e, ksl=ksl, sl=sl: e.stream_shuffle(
                rsc[:, sl * 512:sl * 512 + 512], ksl, swapmask),
                r=[f"l2k{kv}"], w=[f"rsc{sl}"])
            p.v(lambda e, sl=sl: e.tensor_tensor(
                rsc[:, sl * 512:sl * 512 + 512], rsc[:, sl * 512:sl * 512 + 512],
                rops[:, sl * 512:sl * 512 + 512], ALU.mult),
                r=[f"rsc{sl}", f"rops{sl}"], w=[f"rsc{sl}"])
            p.v(lambda e, ksl=ksl, sl=sl: e.tensor_tensor(
                ksl, ksl, ropc[:, sl * 512:sl * 512 + 512], ALU.mult),
                r=[f"l2k{kv}", f"ropc{sl}"], w=[f"l2k{kv}"])
            p.v(lambda e, ksl=ksl, sl=sl: e.tensor_tensor(
                ksl, ksl, rsc[:, sl * 512:sl * 512 + 512], ALU.add),
                r=[f"l2k{kv}", f"rsc{sl}"], w=[f"l2k{kv}"])

    p.v(lambda e: e.memset(nrm[:, 0:32], 0.0), w=["nrm"])
    for wi, (src, srck, nh, cs) in enumerate(((qT, "l2q", 16, 0), (kTd, "l2k", 4, 16))):
        for hh in range(nh):
            dc = hh // 2 if wi == 0 else hh
            base = (hh % 2) * 64 if wi == 0 else 0
            for qb in range(NQB):
                p.s(lambda e, src=src, dc=dc, qb=qb: e.activation(
                    out=st.sq[:, 0:512],
                    in_=src[:, dc * T + qb * QB: dc * T + qb * QB + QB],
                    func=AF.Square),
                    r=[f"{srck}{dc}"], w=["htok0"])
                p.t(lambda e, base=base: e.matmul(
                    st.pb[2][0:1, 0:512], st.onescol[base:base + 64, 0:1],
                    st.sq[base:base + 64, 0:512], start=True, stop=True),
                    r=["htok0", "onescol"], w=["pb2"])
                p.v(lambda e: e.reduce_max(nrm[0:1, 40:41], st.pb[2][0:1, 0:512],
                                           axis=AX.X),
                    r=["pb2"], w=["nrms"])
                idx = cs + hh
                p.v(lambda e, idx=idx: e.tensor_max(
                    nrm[0:1, idx:idx + 1], nrm[0:1, idx:idx + 1], nrm[0:1, 40:41]),
                    r=["nrms", "nrm"], w=["nrm"])
    for h in range(16):
        p.v(lambda e, h=h: e.tensor_tensor(
            nrm[0:1, 32 + h:33 + h], nrm[0:1, h:h + 1], nrm[0:1, 16 + h // 4:17 + h // 4],
            ALU.mult), r=["nrm"], w=["nrmm"])
    p.s(lambda e: e.activation(out=nrm[0:1, 32:48], in_=nrm[0:1, 32:48], func=AF.Sqrt),
        r=["nrmm"], w=["nrmm"])
    p.v(lambda e: e.tensor_scalar_mul(nrm[0:1, 32:48], nrm[0:1, 32:48], -SCALE),
        r=["nrmm"], w=["nrmm"])
    p.t(lambda e: e.matmul(st.pb[2][0:128, 0:16], st.onesrow_f[0:1, 0:128],
                           nrm[0:1, 32:48], start=True, stop=True),
        r=["nrmm", "onesrowf"], w=["pb2"])
    p.v(lambda e: e.tensor_copy(negM[:, 0:16], st.pb[2][0:128, 0:16]),
        r=["pb2"], w=["negM"])

    for h in range(16):
        dc, base = h // 2, (h % 2) * 64
        kv = h // 4
        rrow_off = (h % 2) * 64
        for qb in range(NQB):
            for kg in range(4):
                for ki in range(4):
                    kc = kg * 4 + ki
                    pbs = ki % 2
                    p.t(lambda e, kv=kv, base=base, dc=dc, kc=kc, qb=qb, pbs=pbs: e.matmul(
                        st.pb[pbs][:, 0:512],
                        kTd[base:base + 64, kv * T + kc * 128: kv * T + kc * 128 + 128],
                        qT[base:base + 64, dc * T + qb * QB: dc * T + qb * QB + QB],
                        start=True, stop=True),
                        r=[f"l2k{kv}", f"l2q{dc}"], w=[f"pb{pbs}"])
                    p.s(lambda e, ki=ki, pbs=pbs, h=h: e.activation(
                        out=atT[:, ki * 512:ki * 512 + 512], in_=st.pb[pbs][:, 0:512],
                        func=AF.Exp, scale=SCALE, bias=negM[:, h:h + 1]),
                        r=[f"pb{pbs}", "negM"], w=[f"atT{ki}", "ht4"])
                for ki in range(4):
                    kc = kg * 4 + ki
                    p.t(lambda e, kc=kc, ki=ki, kv=kv,
                        s0=(kg == 0 and ki == 0), s1=(kg == 3 and ki == 3): e.matmul(
                        st.pb[3][0:65, 0:512],
                        st.ht[:, kc * 4 * 65 + kv * 65: kc * 4 * 65 + kv * 65 + 65],
                        atT[:, ki * 512:ki * 512 + 512],
                        start=s0, stop=s1),
                        r=[f"atT{ki}", "ht4"] + vh_keys(kc), w=["pb3"])
            p.v(lambda e: e.reciprocal(rrow[0:1, 0:512], st.pb[3][64:65, 0:512]),
                r=["pb3"], w=["rrow"])
            p.t(lambda e: e.matmul(st.pb[2][0:64, 0:512], st.onesrow_f[0:1, 0:64],
                                   rrow[0:1, 0:512], start=True, stop=True),
                r=["rrow", "onesrowf"], w=["pb2"])
            p.s(lambda e: e.copy(rb[0:64, 0:512], st.pb[2][0:64, 0:512]),
                r=["pb2"], w=["rb"])
            p.v(lambda e, rrow_off=rrow_off, qb=qb, dc=dc: e.tensor_tensor(
                aoT[rrow_off:rrow_off + 64, (qb * ND + dc) * 512:(qb * ND + dc) * 512 + 512],
                st.pb[3][0:64, 0:512], rb[0:64, 0:512], ALU.mult),
                r=["pb3", "rb"], w=[f"aoT{qb}_{dc}_{rrow_off}"])

    for mi in range(ND):
        sl = mi % 2
        p.dma("dq_g", wbuf[:, sl * ND * 128:(sl + 1) * ND * 128]
              .rearrange("p (c m) -> p c m", c=ND),
              hb["L2_attn_Wo"][:, mi * 128:(mi + 1) * 128]
              .rearrange("(c p) m -> p c m", p=128),
              w=[f"l2wb{sl}"])
        for qb in range(NQB):
            pbm = (mi * NQB + qb) % 2
            for kc in range(ND):
                p.t(lambda e, sl=sl, mi=mi, qb=qb, kc=kc, pbm=pbm: e.matmul(
                    st.pb[pbm][:, 0:512],
                    wbuf[:, sl * ND * 128 + kc * 128: sl * ND * 128 + kc * 128 + 128],
                    aoT[:, (qb * ND + kc) * 512: (qb * ND + kc) * 512 + 512],
                    start=(kc == 0), stop=(kc == ND - 1)),
                    r=[f"l2wb{sl}", f"aoT{qb}_{kc}_0", f"aoT{qb}_{kc}_64"], w=[f"pb{pbm}"])
            emit_back_tile(p, st, f"pb{pbm}", st.pb[pbm][:, 0:512], mi, qb)


def emit_l2_ffn(p, st):
    nc, hb = st.nc, st.hbm
    HID, NH = 3072, 24
    emit_ln_stats(p, st)
    emit_ln_to_ht(p, st)

    wgu = salloc(st, "l2f_wgu", [128, 2 * D], BF16)
    wT = salloc(st, "l2f_wT", [128, 2 * ND * 128], BF16)
    hbuf = salloc(st, "l2f_h", [128, NH * 512], BF16)
    ubuf = salloc(st, "l2f_u", [128, 512], BF16)
    wdT = salloc(st, "l2f_wdT", [128, NH * D], BF16)
    wgut = salloc(st, "l2f_wgut", [128, 2 * 2048], BF16)

    gut = nc.dram_tensor("l2f_gut", [2 * NH * ND * 128, 128], BF16,
                         kind="Internal").ap()
    for which, wname in enumerate(("L2_ffn_w_gate", "L2_ffn_w_up")):
        for hc in range(NH):
            sl = hc % 2
            p.dma("dq_g", wgu[:, sl * D:sl * D + D],
                  hb[wname][hc * 128:(hc + 1) * 128, :],
                  w=[f"l2wgu{sl}"])
            for dc in range(ND):
                pti = dc % 2
                p.t(lambda e, sl=sl, dc=dc, pti=pti: e.transpose(
                    st.pt[pti][:, 0:128],
                    wgu[:, sl * D + dc * 128: sl * D + dc * 128 + 128],
                    st.ident[:]),
                    r=[f"l2wgu{sl}", "ident"], w=[f"pt{pti}"])
                p.v(lambda e, sl=sl, dc=dc, pti=pti: e.tensor_copy(
                    wT[:, sl * 1024 + dc * 128: sl * 1024 + dc * 128 + 128],
                    st.pt[pti][:, 0:128]),
                    r=[f"pt{pti}"], w=[f"l2wT{sl}"])
            p.dma("dq_y", gut[(which * NH + hc) * ND * 128:
                              (which * NH + hc + 1) * ND * 128, :]
                  .rearrange("(c p) m -> p c m", p=128),
                  wT[:, sl * 1024:(sl + 1) * 1024]
                  .rearrange("p (c m) -> p c m", c=ND),
                  r=[f"l2wT{sl}"], w=[f"gut{which}_{hc}"])

    for dmi in range(ND):
        sl = dmi % 2
        p.dma("dq_g", wgu[:, sl * D:sl * D + D],
              hb["L2_ffn_w_down"][dmi * 128:(dmi + 1) * 128, 0:1024],
              w=[f"l2wgu{sl}"])
        p.dma("dq_g", wT[:, sl * 1024:sl * 1024 + 1024],
              hb["L2_ffn_w_down"][dmi * 128:(dmi + 1) * 128, 1024:2048],
              w=[f"l2wT{sl}"])
        p.dma("dq_g", hbuf[:, 23 * 512:23 * 512 + 512],
              hb["L2_ffn_w_down"][dmi * 128:(dmi + 1) * 128, 2048:2560],
              w=["yb23"])
        p.dma("dq_g", hbuf[:, 22 * 512:22 * 512 + 512],
              hb["L2_ffn_w_down"][dmi * 128:(dmi + 1) * 128, 2560:3072],
              w=["yb22"])
        for hc in range(NH):
            if hc < 8:
                src, srk = wgu[:, sl * D + hc * 128: sl * D + hc * 128 + 128], f"l2wgu{sl}"
            elif hc < 16:
                src, srk = wT[:, sl * 1024 + (hc - 8) * 128: sl * 1024 + (hc - 8) * 128 + 128], f"l2wT{sl}"
            elif hc < 20:
                src, srk = hbuf[:, 23 * 512 + (hc - 16) * 128: 23 * 512 + (hc - 16) * 128 + 128], "yb23"
            else:
                src, srk = hbuf[:, 22 * 512 + (hc - 20) * 128: 22 * 512 + (hc - 20) * 128 + 128], "yb22"
            pti = hc % 2
            p.t(lambda e, src=src, pti=pti: e.transpose(
                st.pt[pti][:, 0:128], src, st.ident[:]),
                r=[srk, "ident"], w=[f"pt{pti}"])
            p.v(lambda e, hc=hc, dmi=dmi, pti=pti: e.tensor_copy(
                wdT[:, hc * D + dmi * 128: hc * D + dmi * 128 + 128],
                st.pt[pti][:, 0:128]),
                r=[f"pt{pti}"], w=[f"wdT{hc}"])

    for qb in range(NQB):
        for hc in range(NH):
            sl = hc % 2
            if hc % 2 == 0:
                p.dma("dq_y", wgut[:, 0:2048]
                      .rearrange("p (a c m) -> p a c m", a=2, c=ND),
                      gut[hc * ND * 128:(hc + 2) * ND * 128, :]
                      .rearrange("(a c p) m -> p a c m", p=128, c=ND),
                      r=[f"gut0_{hc}", f"gut0_{hc + 1}"], w=["wgt0"])
                p.dma("dq_a", wgut[:, 2048:4096]
                      .rearrange("p (a c m) -> p a c m", a=2, c=ND),
                      gut[(NH + hc) * ND * 128:(NH + hc + 2) * ND * 128, :]
                      .rearrange("(a c p) m -> p a c m", p=128, c=ND),
                      r=[f"gut1_{hc}", f"gut1_{hc + 1}"], w=["wgt1"])
            for which in range(2):
                base = which * 2048 + (hc % 2) * 1024
                pbm = 2 + which
                for dc in range(ND):
                    p.t(lambda e, base=base, dc=dc, qb=qb, pbm=pbm, which=which: e.matmul(
                        st.pb[pbm][:, 0:512],
                        wgut[:, base + dc * 128: base + dc * 128 + 128],
                        ht_c(st, dc)[:, qb * QB:qb * QB + QB],
                        start=(dc == 0), stop=(dc == ND - 1)),
                        r=[f"wgt{which}", f"ht{dc}"], w=[f"pb{pbm}"])
                if which == 0:
                    p.s(lambda e, hc=hc, pbm=pbm: e.activation(
                        out=hbuf[:, hc * 512:hc * 512 + 512], in_=st.pb[pbm][:, 0:512],
                        func=AF.Silu, scale=1.0 / 127.0),
                        r=[f"pb{pbm}"], w=[f"yb{hc}"])
                else:
                    p.s(lambda e, pbm=pbm: e.mul(ubuf[:, 0:512], st.pb[pbm][:, 0:512],
                                                 1.0 / 127.0),
                        r=[f"pb{pbm}"], w=["ubuf"])
                    p.v(lambda e, hc=hc: e.tensor_tensor(
                        hbuf[:, hc * 512:hc * 512 + 512],
                        hbuf[:, hc * 512:hc * 512 + 512], ubuf[:, 0:512], ALU.mult),
                        r=[f"yb{hc}", "ubuf"], w=[f"yb{hc}"])
        for mi in range(ND):
            pbm = mi % 2
            for hc in range(NH):
                p.t(lambda e, hc=hc, mi=mi, pbm=pbm, s0=(hc == 0), s1=(hc == NH - 1): e.matmul(
                    st.pb[pbm][:, 0:512],
                    wdT[:, hc * D + mi * 128: hc * D + mi * 128 + 128],
                    hbuf[:, hc * 512:hc * 512 + 512],
                    start=s0, stop=s1),
                    r=[f"wdT{hc}", f"yb{hc}"], w=[f"pb{pbm}"])
            emit_back_tile(p, st, f"pb{pbm}", st.pb[pbm][:, 0:512], mi, qb,
                           scale=1.0 / 127.0)


def emit_l3_attn(p, st):
    nc, hb = st.nc, st.hbm
    NHG = 4
    xold = nc.dram_tensor("l3_xold", [T, D], F32, kind="Internal").ap()
    vdram = nc.dram_tensor("l3_vdram", [16 * 68, T], BF16, kind="Internal").ap()
    w1tile = nc.dram_tensor("l3_w1t", [32 * ND * 128, 128], BF16, kind="Internal").ap()
    w2td = nc.dram_tensor("l3_w2t", [4096, D], BF16, kind="Internal").ap()

    p.dma("dq_y", xold.rearrange("(a p) m -> p a m", p=128),
          st.x[:].rearrange("p (a m) -> p a m", a=NT), r=XK, w=["xold"])

    emit_ln_stats(p, st)
    p.v(lambda e: e.tensor_scalar(lane(st, 5), lane(st, 6), 1e-5, None, ALU.add),
        r=["st6"], w=["st5"])
    p.v(lambda e: e.reciprocal(lane(st, 5), lane(st, 5)), r=["st5"], w=["st5"])
    p.v(lambda e: e.tensor_tensor(lane(st, 5), lane(st, 6), lane(st, 5), ALU.mult),
        r=["st5", "st6"], w=["st5"])
    p.s(lambda e: e.activation(out=lane(st, 5), in_=lane(st, 5), func=AF.Sqrt,
                               bias=st.eps[:, 0:1], scale=1.0),
        r=["st5", "eps"], w=["st5"])
    p.v(lambda e: e.reciprocal(lane(st, 6), lane(st, 5)), r=["st5"], w=["st6"])
    p.v(lambda e: e.tensor_tensor(lane(st, 5), lane(st, 2), lane(st, 6), ALU.mult),
        r=["st2", "st6"], w=["st5"])
    p.v(lambda e: e.scalar_tensor_tensor(lane(st, 6), lane(st, 4), -1.0, lane(st, 5),
                                         ALU.mult, ALU.mult),
        r=["st4", "st5"], w=["st6"])

    gcol = salloc(st, "l3_g", [128, 32], F32)
    for i, gk in enumerate(("L3_attn_ln1_g", "L3_attn_ln1_b",
                            "L3_attn_ln2_g", "L3_attn_ln2_b")):
        p.dma("dq_y", st.zslice[0:1, 0:D], hb[gk].rearrange("(o n) -> o n", o=1),
              w=["zs0", "zs1"])
        for dcg in range(ND):
            pti = dcg % 2
            p.t(lambda e, dcg=dcg, pti=pti: e.transpose(
                st.pb[4 + pti][0:128, 0:1],
                st.zslice[0:1, dcg * 128:dcg * 128 + 128], st.identf[0:1, 0:1]),
                r=["zs0", "zs1", "identf"], w=[f"pb{4 + pti}"])
            p.v(lambda e, i=i, dcg=dcg, pti=pti: e.tensor_copy(
                gcol[:, i * 8 + dcg:i * 8 + dcg + 1], st.pb[4 + pti][0:128, 0:1]),
                r=[f"pb{4 + pti}"], w=["gcol"])

    emit_ln_to_ht(p, st, rlane=5, blane=6,
                  gamma_col=lambda dc: gcol[:, dc:dc + 1],
                  beta_col=lambda dc: gcol[:, 8 + dc:8 + dc + 1], gb_res=("gcol",))
    for tc in range(NT):
        p.v(lambda e, tc=tc: e.tensor_scalar(
            x_t(st, tc), x_t(st, tc), lane(st, 2, tc), lane(st, 3, tc),
            ALU.mult, ALU.add),
            r=[f"x{tc}", "st2", "st3"], w=[f"x{tc}"])

    qT = salloc(st, "l3_qT", [128, 2 * T], BF16)
    kT = salloc(st, "l3_kT", [128, 2 * T], BF16)
    aoT = salloc(st, "l3_aoT", [128, NQB * ND * 512], BF16)
    wpr = salloc(st, "l3_wpr", [128, 2 * D], BF16)
    wpT = salloc(st, "l3_wpT", [128, 2 * ND * 128], BF16)
    masks = salloc(st, "l3_mask", [128, 8 * 512], BF16)
    rrow = salloc(st, "l3_rr", [1, 512], F32)
    rb = salloc(st, "l3_rb", [64, 512], F32)
    nrm = salloc(st, "l3_nrm", [1, 64], F32)
    negM = salloc(st, "l3_negM", [128, NHG], F32)
    VH = st.ht[:, 0:NT * NHG * 68]
    atT = st.ht[:, 8192:8192 + 8 * 512]
    ones16 = st.zslice[:, 0:512]

    for i in range(8):
        p.dma("dq_y", masks[:, i * 512:(i + 1) * 512], hb[f"c_wm{i}"],
              w=[f"wm{i}"])

    bqkv_missing = None
    for grp in range(4):
        h0 = grp * NHG
        for j in range(2):
            ha = h0 + 2 * j
            for wi, (dstT, dk, roff) in enumerate(((qT, "l3q", 0), (kT, "l3k", D))):
                sl = (grp * 4 + j * 2 + wi) % 2
                p.dma("dq_g", wpr[:, sl * D:sl * D + D],
                      hb["L3_attn_w_qkv"][roff + ha * 64: roff + ha * 64 + 128, :],
                      w=[f"l3wpr{sl}"])
                for dc in range(ND):
                    pti = dc % 2
                    p.t(lambda e, sl=sl, dc=dc, pti=pti: e.transpose(
                        st.pt[pti][:, 0:128],
                        wpr[:, sl * D + dc * 128: sl * D + dc * 128 + 128],
                        st.ident[:]),
                        r=[f"l3wpr{sl}", "ident"], w=[f"pt{pti}"])
                    p.v(lambda e, sl=sl, dc=dc, pti=pti: e.tensor_copy(
                        wpT[:, sl * ND * 128 + dc * 128: sl * ND * 128 + dc * 128 + 128],
                        st.pt[pti][:, 0:128]),
                        r=[f"pt{pti}"], w=[f"l3wpT{sl}"])
                for qb in range(NQB):
                    pbm = qb % 2
                    for kc in range(ND):
                        p.t(lambda e, sl=sl, kc=kc, qb=qb, pbm=pbm: e.matmul(
                            st.pb[pbm][:, 0:512],
                            wpT[:, sl * ND * 128 + kc * 128: sl * ND * 128 + kc * 128 + 128],
                            ht_c(st, kc)[:, qb * QB:qb * QB + QB],
                            start=(kc == 0), stop=(kc == ND - 1)),
                            r=[f"l3wpT{sl}", f"ht{kc}"], w=[f"pb{pbm}"])
                    p.s(lambda e, dstT=dstT, j=j, qb=qb, pbm=pbm: e.copy(
                        dstT[:, j * T + qb * QB: j * T + qb * QB + QB],
                        st.pb[pbm][:, 0:512]),
                        r=[f"pb{pbm}"], w=[f"{dk}{j}_{grp}"])
            sl = (j) % 2
            p.dma("dq_g", wpr[:, sl * D:sl * D + D],
                  hb["L3_attn_w_qkv"][2 * D + ha * 64: 2 * D + ha * 64 + 128, :],
                  w=[f"l3wpr{sl}"])
            for dc in range(ND):
                pti = dc % 2
                p.t(lambda e, sl=sl, dc=dc, pti=pti: e.transpose(
                    st.pt[pti][:, 0:128],
                    wpr[:, sl * D + dc * 128: sl * D + dc * 128 + 128],
                    st.ident[:]),
                    r=[f"l3wpr{sl}", "ident"], w=[f"pt{pti}"])
                p.v(lambda e, sl=sl, dc=dc, pti=pti: e.tensor_copy(
                    wpT[:, sl * ND * 128 + dc * 128: sl * ND * 128 + dc * 128 + 128],
                    st.pt[pti][:, 0:128]),
                    r=[f"pt{pti}"], w=[f"l3wpT{sl}"])
            for qb in range(NQB):
                pbm = qb % 2
                for kc in range(ND):
                    p.t(lambda e, sl=sl, kc=kc, qb=qb, pbm=pbm: e.matmul(
                        st.pb[pbm][:, 0:512],
                        wpT[:, sl * ND * 128 + kc * 128: sl * ND * 128 + kc * 128 + 128],
                        ht_c(st, kc)[:, qb * QB:qb * QB + QB],
                        start=(kc == 0), stop=(kc == ND - 1)),
                        r=[f"l3wpT{sl}", f"ht{kc}"], w=[f"pb{pbm}"])
                vsl = qb % 2
                p.s(lambda e, vsl=vsl, pbm=pbm: e.copy(
                    st.zslice[:, vsl * 512:vsl * 512 + 512].bitcast(BF16)[:, 0:512]
                    if False else st.htok[:, vsl * 512:vsl * 512 + 512],
                    st.pb[pbm][:, 0:512]),
                    r=[f"pb{pbm}"], w=[f"htok{vsl}"])
                p.dma("dq_y", vdram[(h0 + 2 * j) * 68:(h0 + 2 * j) * 68 + 2 * 68,
                                    qb * QB:qb * QB + QB]
                      .rearrange("(h m) t -> h m t", m=68)[:, 0:64, :],
                      st.htok[:, vsl * 512:vsl * 512 + 512],
                      r=[f"htok{vsl}"], w=[f"vd{grp}_{j}_{qb}"])
    p.v(lambda e: e.memset(st.htok[0:16, 0:512], 1.0), w=["htok0", "htok1"])
    for h in range(16):
        for qb in range(NQB):
            p.dma("dq_y", vdram[h * 68 + 64:h * 68 + 68, qb * QB:qb * QB + QB],
                  st.htok[0:4, 0:512], r=["htok0"], w=[f"vdo{h}_{qb}"])

    SCALE = 0.125
    for grp in range(4):
        h0 = grp * NHG
        for kc in range(NT):
            deps = ([f"vd{grp}_{j}_{qb}" for j in range(2) for qb in range(NQB)]
                    + [f"vdo{h}_{qb}" for h in range(h0, h0 + 4) for qb in range(NQB)])
            if grp == 0:
                deps += HTK
            p.dma("dq_y", st.ht[:, kc * NHG * 68:(kc + 1) * NHG * 68],
                  vdram[h0 * 68:(h0 + 4) * 68, kc * 128:(kc + 1) * 128],
                  r=deps, w=[f"VH{kc}_{grp}", f"ht{(kc * NHG * 68) // T}",
                             f"ht{((kc + 1) * NHG * 68 - 1) // T}"],
                  transpose=True)
        p.v(lambda e: e.memset(nrm[:, 0:32], 0.0), w=["nrm"])
        for j in range(2):
            for wi, (src, srck) in enumerate(((qT, "l3q"), (kT, "l3k"))):
                for qb in range(NQB):
                    p.s(lambda e, src=src, j=j, qb=qb: e.activation(
                        out=st.sq[:, 0:512],
                        in_=src[:, j * T + qb * QB: j * T + qb * QB + QB],
                        func=AF.Square),
                        r=[f"{srck}{j}_{grp}"], w=["htok0"])
                    for hio in range(2):
                        base = hio * 64
                        idx = wi * 16 + 2 * j + hio
                        p.t(lambda e, base=base: e.matmul(
                            st.pb[2][0:1, 0:512], st.onescol[base:base + 64, 0:1],
                            st.sq[base:base + 64, 0:512], start=True, stop=True),
                            r=["htok0", "onescol"], w=["pb2"])
                        p.v(lambda e: e.reduce_max(nrm[0:1, 40:41],
                                                   st.pb[2][0:1, 0:512], axis=AX.X),
                            r=["pb2"], w=["nrms"])
                        p.v(lambda e, idx=idx: e.tensor_max(
                            nrm[0:1, idx:idx + 1], nrm[0:1, idx:idx + 1],
                            nrm[0:1, 40:41]),
                            r=["nrms", "nrm"], w=["nrm"])
        p.v(lambda e: e.tensor_tensor(nrm[0:1, 32:32 + NHG], nrm[0:1, 0:NHG],
                                      nrm[0:1, 16:16 + NHG], ALU.mult),
            r=["nrm"], w=["nrmm"])
        p.s(lambda e: e.activation(out=nrm[0:1, 48:48 + NHG],
                                   in_=nrm[0:1, 32:32 + NHG], func=AF.Sqrt),
            r=["nrmm"], w=["nrmsq"])
        p.v(lambda e: e.tensor_scalar_mul(nrm[0:1, 56:56 + NHG],
                                          nrm[0:1, 48:48 + NHG], -SCALE),
            r=["nrmsq"], w=["nrmneg"])
        p.t(lambda e: e.matmul(st.pb[2][0:128, 0:NHG], st.onesrow_f[0:1, 0:128],
                               nrm[0:1, 56:56 + NHG], start=True, stop=True),
            r=["nrmneg", "onesrowf"], w=["pb2"])
        p.v(lambda e: e.tensor_copy(negM[:, 0:NHG], st.pb[2][0:128, 0:NHG]),
            r=["pb2"], w=["negM"])

        for hidx in range(NHG):
            j, hio = hidx // 2, hidx % 2
            base = hio * 64
            h = h0 + hidx
            dc_o, rrow_off = h // 2, (h % 2) * 64
            for qc in range(NQB):
                first = True
                nvalid = [i for i in range(8) if 0 <= qc * 4 - 4 + i < NT]
                for i in nvalid:
                    kc = qc * 4 - 4 + i
                    pbs = i % 2
                    p.t(lambda e, j=j, base=base, kc=kc, qc=qc, pbs=pbs: e.matmul(
                        st.pb[pbs][:, 0:512],
                        kT[base:base + 64, j * T + kc * 128: j * T + kc * 128 + 128],
                        qT[base:base + 64, j * T + qc * QB: j * T + qc * QB + QB],
                        start=True, stop=True),
                        r=[f"l3k{j}_{grp}", f"l3q{j}_{grp}"], w=[f"pb{pbs}"])
                    p.s(lambda e, i=i, pbs=pbs, hidx=hidx: e.activation(
                        out=atT[:, i * 512:i * 512 + 512], in_=st.pb[pbs][:, 0:512],
                        func=AF.Exp, scale=SCALE, bias=negM[:, hidx:hidx + 1]),
                        r=[f"pb{pbs}", "negM"], w=[f"atT{i}", "ht4", "ht5"])
                    p.v(lambda e, i=i: e.tensor_tensor(
                        atT[:, i * 512:i * 512 + 512], atT[:, i * 512:i * 512 + 512],
                        masks[:, i * 512:(i + 1) * 512], ALU.mult),
                        r=[f"atT{i}", f"wm{i}", "ht5"], w=[f"atT{i}", "ht4", "ht5"])
                for i in nvalid:
                    kc = qc * 4 - 4 + i
                    p.t(lambda e, kc=kc, i=i, hidx=hidx,
                        s0=(i == nvalid[0]), s1=(i == nvalid[-1]): e.matmul(
                        st.pb[3][0:68, 0:512],
                        st.ht[:, kc * NHG * 68 + hidx * 68: kc * NHG * 68 + hidx * 68 + 68],
                        atT[:, i * 512:i * 512 + 512],
                        start=s0, stop=s1),
                        r=[f"atT{i}", f"VH{kc}_{grp}",
                           f"ht{(kc * NHG * 68) // T}"], w=["pb3"])
                p.v(lambda e: e.reciprocal(rrow[0:1, 0:512], st.pb[3][64:65, 0:512]),
                    r=["pb3"], w=["rrow"])
                p.t(lambda e: e.matmul(st.pb[2][0:64, 0:512], st.onesrow_f[0:1, 0:64],
                                       rrow[0:1, 0:512], start=True, stop=True),
                    r=["rrow", "onesrowf"], w=["pb2"])
                p.s(lambda e: e.copy(rb[0:64, 0:512], st.pb[2][0:64, 0:512]),
                    r=["pb2"], w=["rb"])
                p.v(lambda e, rrow_off=rrow_off, qc=qc, dc_o=dc_o: e.tensor_tensor(
                    aoT[rrow_off:rrow_off + 64,
                        (qc * ND + dc_o) * 512:(qc * ND + dc_o) * 512 + 512],
                    st.pb[3][0:64, 0:512], rb[0:64, 0:512], ALU.mult),
                    r=["pb3", "rb"], w=[f"aoT{qc}_{dc_o}_{rrow_off}"])

    for mi in range(ND):
        sl = mi % 2
        p.dma("dq_g", wpr[:, sl * D:sl * D + D],
              hb["L3_attn_w_o"][mi * 128:(mi + 1) * 128, :],
              w=[f"l3wpr{sl}"])
        for dc in range(ND):
            pti = dc % 2
            p.t(lambda e, sl=sl, dc=dc, pti=pti: e.transpose(
                st.pt[pti][:, 0:128],
                wpr[:, sl * D + dc * 128: sl * D + dc * 128 + 128], st.ident[:]),
                r=[f"l3wpr{sl}", "ident"], w=[f"pt{pti}"])
            p.v(lambda e, sl=sl, dc=dc, pti=pti: e.tensor_copy(
                wpT[:, sl * ND * 128 + dc * 128: sl * ND * 128 + dc * 128 + 128],
                st.pt[pti][:, 0:128]),
                r=[f"pt{pti}"], w=[f"l3wpT{sl}"])
        for qb in range(NQB):
            pbm = qb % 2
            for kc in range(ND):
                p.t(lambda e, sl=sl, kc=kc, qb=qb, pbm=pbm: e.matmul(
                    st.pb[pbm][:, 0:512],
                    wpT[:, sl * ND * 128 + kc * 128: sl * ND * 128 + kc * 128 + 128],
                    aoT[:, (qb * ND + kc) * 512: (qb * ND + kc) * 512 + 512],
                    start=(kc == 0), stop=(kc == ND - 1)),
                    r=[f"l3wpT{sl}", f"aoT{qb}_{kc}_0", f"aoT{qb}_{kc}_64"],
                    w=[f"pb{pbm}"])
            emit_back_tile(p, st, f"pb{pbm}", st.pb[pbm][:, 0:512], mi, qb)

    emit_ln_stats(p, st)
    emit_ln_to_ht(p, st, gamma_col=lambda dc: gcol[:, 16 + dc:16 + dc + 1],
                  beta_col=lambda dc: gcol[:, 24 + dc:24 + dc + 1], gb_res=("gcol",))

    b1row, _ = load_row(p, st, "l3b1", hb["L3_attn_mlp_b1"], 4096)
    b2row, _ = load_row(p, st, "l3b2", hb["L3_attn_mlp_b2"], D)
    for oc in range(32):
        sl = oc % 2
        p.dma("dq_g", wpr[:, sl * D:sl * D + D],
              hb["L3_attn_mlp_w1"][oc * 128:(oc + 1) * 128, :],
              w=[f"l3wpr{sl}"])
        for dc in range(ND):
            pti = dc % 2
            p.t(lambda e, sl=sl, dc=dc, pti=pti: e.transpose(
                st.pt[pti][:, 0:128],
                wpr[:, sl * D + dc * 128: sl * D + dc * 128 + 128], st.ident[:]),
                r=[f"l3wpr{sl}", "ident"], w=[f"pt{pti}"])
            p.v(lambda e, sl=sl, dc=dc, pti=pti: e.tensor_copy(
                wpT[:, sl * ND * 128 + dc * 128: sl * ND * 128 + dc * 128 + 128],
                st.pt[pti][:, 0:128]),
                r=[f"pt{pti}"], w=[f"l3wpT{sl}"])
        p.dma("dq_y", w1tile[oc * ND * 128:(oc + 1) * ND * 128, :]
              .rearrange("(c p) m -> p c m", p=128),
              wpT[:, sl * ND * 128:(sl + 1) * ND * 128]
              .rearrange("p (c m) -> p c m", c=ND),
              r=[f"l3wpT{sl}"], w=[f"w1t{oc}"])
    for dmi in range(ND):
        sl = dmi % 2
        p.dma("dq_g", wpr[:, sl * D:sl * D + D]
              .rearrange("p (a m) -> p a m", a=2)
              if False else wpr[:, sl * D:sl * D + D],
              hb["L3_attn_mlp_w2"][dmi * 128:(dmi + 1) * 128, 0:1024],
              w=[f"l3wpr{sl}"])
        p.dma("dq_g", wpT[:, sl * 1024:sl * 1024 + 1024],
              hb["L3_attn_mlp_w2"][dmi * 128:(dmi + 1) * 128, 1024:2048],
              w=[f"l3wpT{sl}"])
        p.dma("dq_g", st.htok[:, 0:1024],
              hb["L3_attn_mlp_w2"][dmi * 128:(dmi + 1) * 128, 2048:3072],
              w=["htok0"])
        p.dma("dq_g", st.htok[:, 1024:2048],
              hb["L3_attn_mlp_w2"][dmi * 128:(dmi + 1) * 128, 3072:4096],
              w=["htok1"])
        for hc in range(32):
            if hc < 8:
                src, srk = wpr[:, sl * D + hc * 128: sl * D + hc * 128 + 128], f"l3wpr{sl}"
            elif hc < 16:
                src, srk = wpT[:, sl * 1024 + (hc - 8) * 128: sl * 1024 + (hc - 8) * 128 + 128], f"l3wpT{sl}"
            elif hc < 24:
                src, srk = st.htok[:, (hc - 16) * 128:(hc - 16) * 128 + 128], "htok0"
            else:
                src, srk = st.htok[:, 1024 + (hc - 24) * 128:1024 + (hc - 24) * 128 + 128], "htok1"
            pti = hc % 2
            p.t(lambda e, src=src, pti=pti: e.transpose(
                st.pt[pti][:, 0:128], src, st.ident[:]),
                r=[srk, "ident"], w=[f"pt{pti}"])
            p.v(lambda e, hc=hc, pti=pti: e.tensor_copy(
                st.zslice[:, 0:512].bitcast(BF16)[:, 0:128] if False else
                st.sq[:, 0:128], st.pt[pti][:, 0:128]),
                r=[f"pt{pti}"], w=["htok0"]) if False else None
            p.s(lambda e, hc=hc, pti=pti: e.copy(st.sq[:, 0:128], st.pt[pti][:, 0:128]),
                r=[f"pt{pti}"], w=["sqx"])
            p.dma("dq_y", w2td[hc * 128:(hc + 1) * 128, dmi * 128:(dmi + 1) * 128],
                  st.sq[:, 0:128], r=["sqx"], w=[f"w2t{hc}_{dmi}"])

    ybuf = aoT
    w2s = salloc(st, "l3_w2s", [128, 2 * 2 * D], BF16)
    w1s = salloc(st, "l3_w1s", [128, 2 * 2 * ND * 128], BF16)
    for qb in range(NQB):
        for oc in range(32):
            sl = (oc // 2) % 2
            if oc % 2 == 0:
                p.dma("dq_y", w1s[:, sl * 2048:(sl + 1) * 2048]
                      .rearrange("p (c m) -> p c m", c=2 * ND),
                      w1tile[oc * ND * 128:(oc + 2) * ND * 128, :]
                      .rearrange("(c p) m -> p c m", p=128),
                      r=[f"w1t{oc}", f"w1t{oc + 1}"],
                      w=[f"w1s{sl}"])
            ob = oc % 2
            pbm = oc % 2
            for kc in range(ND):
                p.t(lambda e, sl=sl, ob=ob, kc=kc, qb=qb, pbm=pbm: e.matmul(
                    st.pb[pbm][:, 0:512],
                    w1s[:, sl * 2048 + (ob * ND + kc) * 128:
                        sl * 2048 + (ob * ND + kc) * 128 + 128],
                    ht_c(st, kc)[:, qb * QB:qb * QB + QB],
                    start=(kc == 0), stop=False),
                    r=[f"w1s{sl}", f"ht{kc}"], w=[f"pb{pbm}"])
            p.t(lambda e, oc=oc, pbm=pbm: e.matmul(
                st.pb[pbm][:, 0:512], b1row[0:1, oc * 128:oc * 128 + 128],
                st.ones512[0:1, 0:512], start=False, stop=True),
                r=["r_l3b1", "ones512"], w=[f"pb{pbm}"])
            p.s(lambda e, oc=oc, pbm=pbm: e.activation(
                out=ybuf[:, oc * 512:oc * 512 + 512], in_=st.pb[pbm][:, 0:512],
                func=AF.Gelu),
                r=[f"pb{pbm}"],
                w=[f"yb{oc}", f"aoT{oc // 8}_{oc % 8}_0", f"aoT{oc // 8}_{oc % 8}_64"])
        pairs = [(tc, dh) for tc in range(4) for dh in range(2)]
        for wv in range(2):
            wave = pairs[wv * 4:(wv + 1) * 4]
            for kc in range(32):
                sl = (kc // 2) % 2
                if kc % 2 == 0:
                    p.dma("dq_y", w2s[:, sl * 2 * D:(sl + 1) * 2 * D]
                          .rearrange("p (c m) -> p c m", c=2),
                          w2td[kc * 128:(kc + 2) * 128, :]
                          .rearrange("(c p) m -> p c m", p=128),
                          r=[f"w2t{kc + q}_{dmi}" for q in range(2)
                             for dmi in range(ND)],
                          w=[f"w2s{sl}"])
                kb = kc % 2
                for i, (tc, dh) in enumerate(wave):
                    p.t(lambda e, kc=kc, sl=sl, kb=kb, tc=tc, dh=dh, i=i: e.matmul(
                        st.pb[i][:, 0:512],
                        ybuf[:, kc * 512 + tc * 128: kc * 512 + tc * 128 + 128],
                        w2s[:, sl * 2 * D + kb * D + dh * 512:
                            sl * 2 * D + kb * D + dh * 512 + 512],
                        start=(kc == 0), stop=False),
                        r=[f"w2s{sl}", f"yb{kc}"], w=[f"pb{i}"])
            for i, (tc, dh) in enumerate(wave):
                p.t(lambda e, dh=dh, i=i: e.matmul(
                    st.pb[i][:, 0:512], st.ones512[0:1, 0:128],
                    b2row[0:1, dh * 512:dh * 512 + 512], start=False, stop=True),
                    r=["r_l3b2", "ones512"], w=[f"pb{i}"])
                tt = qb * 4 + tc
                dst = x_t(st, tt)[:, dh * 512:dh * 512 + 512]
                p.v(lambda e, dst=dst, i=i: e.scalar_tensor_tensor(
                    dst, st.pb[i][:, 0:512], 1.0, dst, ALU.mult, ALU.add),
                    r=[f"pb{i}", f"x{tt}"], w=[f"x{tt}"])

    for tc in range(NT):
        sl = tc % 2
        p.dma("dq_y", st.zslice[:, sl * 512:sl * 512 + 512]
              .rearrange("p (a m) -> p a m", a=1) if False else
              st.zslice[:, sl * 512:sl * 512 + 512],
              xold[tc * 128:(tc + 1) * 128, 0:512], r=["xold"], w=[f"zs{sl}"])
        p.v(lambda e, tc=tc, sl=sl: e.tensor_tensor(
            x_t(st, tc)[:, 0:512], x_t(st, tc)[:, 0:512],
            st.zslice[:, sl * 512:sl * 512 + 512], ALU.add),
            r=[f"x{tc}", f"zs{sl}"], w=[f"x{tc}"])
        p.dma("dq_y", st.zslice[:, sl * 512:sl * 512 + 512],
              xold[tc * 128:(tc + 1) * 128, 512:1024], r=["xold"], w=[f"zs{sl}"])
        p.v(lambda e, tc=tc, sl=sl: e.tensor_tensor(
            x_t(st, tc)[:, 512:1024], x_t(st, tc)[:, 512:1024],
            st.zslice[:, sl * 512:sl * 512 + 512], ALU.add),
            r=[f"x{tc}", f"zs{sl}"], w=[f"x{tc}"])


def emit_l3_ffn(p, st):
    nc, hb = st.nc, st.hbm
    E, CAP, F = 8, 512, 4096
    hdram = nc.dram_tensor("l3f_h", [T, D], BF16, kind="Internal").ap()
    ydram = nc.dram_tensor("l3f_y", [T, D], F32, kind="Internal").ap()

    emit_ln_stats(p, st)
    emit_ln_to_ht(p, st, dram_out=hdram)

    p.v(lambda e: e.memset(st.zslice[:], 0.0), w=["zs0", "zs1"])
    for tc in range(NT):
        p.dma("dq_y", ydram[tc * 128:(tc + 1) * 128, 0:256], st.zslice[:, 0:256],
              r=["zs0"], w=[f"yd{tc}"])
        p.dma("dq_y", ydram[tc * 128:(tc + 1) * 128, 256:512], st.zslice[:, 256:512],
              r=["zs0"], w=[f"yd{tc}"])
        p.dma("dq_y", ydram[tc * 128:(tc + 1) * 128, 512:768], st.zslice[:, 512:768],
              r=["zs1"], w=[f"yd{tc}"])
        p.dma("dq_y", ydram[tc * 128:(tc + 1) * 128, 768:1024], st.zslice[:, 768:1024],
              r=["zs1"], w=[f"yd{tc}"])

    gwT = salloc(st, "l3f_gwT", [128, ND * 8], BF16)
    gwn = salloc(st, "l3f_gwn", [8, D], BF16)
    p.dma("dq_g", gwn[:], hb["L3_ffn_gate_w"], w=["gwn"])
    for dc in range(ND):
        pti = dc % 2
        p.t(lambda e, dc=dc, pti=pti: e.transpose(
            st.pt[pti][0:128, 0:8], gwn[:, dc * 128:(dc + 1) * 128], st.ident[0:8, 0:8]),
            r=["gwn", "ident"], w=[f"pt{pti}"])
        p.v(lambda e, dc=dc, pti=pti: e.tensor_copy(
            gwT[:, dc * 8:(dc + 1) * 8], st.pt[pti][0:128, 0:8]),
            r=[f"pt{pti}"], w=["gwT"])
    logT = salloc(st, "l3f_logT", [8, T], F32)
    probsT = salloc(st, "l3f_probsT", [8, T], F32)
    probs = salloc(st, "l3f_probs", [128, NT * 8], F32)
    for qb in range(NQB):
        pbm = qb % 2
        for dc in range(ND):
            p.t(lambda e, dc=dc, qb=qb, pbm=pbm: e.matmul(
                st.pb[pbm][0:8, 0:512], gwT[:, dc * 8:(dc + 1) * 8],
                ht_c(st, dc)[:, qb * QB:qb * QB + QB],
                start=(dc == 0), stop=(dc == ND - 1)),
                r=["gwT", f"ht{dc}"], w=[f"pb{pbm}"])
        p.s(lambda e, qb=qb, pbm=pbm: e.copy(
            logT[:, qb * QB:qb * QB + QB], st.pb[pbm][0:8, 0:512]),
            r=[f"pb{pbm}"], w=["logT"])
    p.v(lambda e: e.memset(lane(st, 6), 0.0), w=["st6"])
    for tc in range(NT):
        pti = tc % 2
        p.t(lambda e, tc=tc, pti=pti: e.transpose(
            st.pb[4 + pti][0:128, 0:8], logT[:, tc * 128:(tc + 1) * 128],
            st.identf[0:8, 0:8]),
            r=["logT", "identf"], w=[f"pb{4 + pti}"])
        pr = probs[:, tc * 8:(tc + 1) * 8]
        m1 = lane(st, 5, tc)
        p.v(lambda e, pr=pr, pti=pti: e.tensor_copy(pr, st.pb[4 + pti][0:128, 0:8]),
            r=[f"pb{4 + pti}"], w=["probs"])
        p.v(lambda e, pr=pr, m1=m1: e.reduce_max(m1, pr, axis=AX.X),
            r=["probs"], w=["st5"])
        p.v(lambda e, m1=m1: e.tensor_scalar_mul(m1, m1, -1.0), r=["st5"], w=["st5"])
        p.s(lambda e, pr=pr, m1=m1, tc=tc: e.activation(
            out=pr, in_=pr, func=AF.Exp, bias=m1, scale=1.0,
            accum_out=lane(st, 6, tc)),
            r=["probs", "st5", "st6"], w=["probs", "st6"])
        p.v(lambda e, tc=tc: e.reciprocal(lane(st, 6, tc), lane(st, 6, tc)),
            r=["st6"], w=["st6"])
        p.v(lambda e, pr=pr, tc=tc: e.tensor_scalar(pr, pr, lane(st, 6, tc), None,
                                                    ALU.mult),
            r=["probs", "st6"], w=["probs"])
    for tc in range(NT):
        pti = tc % 2
        p.t(lambda e, tc=tc, pti=pti: e.transpose(
            st.pb[4 + pti][0:8, 0:128], probs[:, tc * 8:(tc + 1) * 8], st.identf[:]),
            r=["probs", "identf"], w=[f"pb{4 + pti}"])
        p.v(lambda e, tc=tc, pti=pti: e.tensor_copy(
            probsT[:, tc * 128:(tc + 1) * 128], st.pb[4 + pti][0:8, 0:128]),
            r=[f"pb{4 + pti}"], w=["probsT"])

    thr = salloc(st, "l3f_thr", [8, 8], F32)
    ge = logT
    p.v(lambda e: e.memset(thr[:, 0:1], 0.5), w=["thr"])
    step = 0.25
    for it in range(24):
        p.v(lambda e: e.tensor_scalar(ge[:], probsT[:], thr[:, 0:1], None, ALU.is_ge),
            r=["probsT", "thr"], w=["logT"])
        p.v(lambda e: e.reduce_sum(thr[:, 1:2], ge[:], axis=AX.X),
            r=["logT"], w=["thrc"])
        p.v(lambda e: e.tensor_scalar(thr[:, 2:3], thr[:, 1:2], 512.0, None, ALU.is_gt),
            r=["thrc"], w=["thrd"])
        p.v(lambda e: e.tensor_scalar(thr[:, 2:3], thr[:, 2:3], 2.0, -1.0,
                                      ALU.mult, ALU.add),
            r=["thrd"], w=["thrd"])
        p.v(lambda e, stp=step: e.scalar_tensor_tensor(
            thr[:, 0:1], thr[:, 2:3], stp, thr[:, 0:1], ALU.mult, ALU.add),
            r=["thrd", "thr"], w=["thr"])
        step *= 0.5
    for rnd in range(4):
        p.v(lambda e: e.tensor_scalar(ge[:], probsT[:], thr[:, 0:1], None, ALU.is_ge),
            r=["probsT", "thr"], w=["logT"])
        p.v(lambda e: e.reduce_sum(thr[:, 1:2], ge[:], axis=AX.X),
            r=["logT"], w=["thrc"])
        p.v(lambda e: e.scalar_tensor_tensor(probs[:, 0:1] if False else ge[:],
                                             ge[:], 1.0, probsT[:], ALU.bypass,
                                             ALU.mult),
            r=["logT", "probsT"], w=["logT"]) if False else None
        p.v(lambda e: e.tensor_scalar(probsT[:], probsT[:], -2.0, None, ALU.add),
            r=["probsT"], w=["probsT"])
        p.v(lambda e: e.tensor_tensor(ge[:], ge[:], probsT[:], ALU.mult),
            r=["logT", "probsT"], w=["logT"])
        p.v(lambda e: e.tensor_scalar(probsT[:], probsT[:], 2.0, None, ALU.add),
            r=["probsT"], w=["probsT"])
        p.v(lambda e: e.tensor_scalar(ge[:], ge[:], 2.0, None, ALU.add),
            r=["logT"], w=["logT"])
        p.v(lambda e: e.tensor_reduce(thr[:, 3:4], ge[:], axis=AX.X, op=ALU.min),
            r=["logT"], w=["thrmin"])
        p.v(lambda e: e.tensor_scalar(ge[:], probsT[:], thr[:, 0:1], None, ALU.is_lt),
            r=["probsT", "thr"], w=["logT"])
        p.v(lambda e: e.tensor_tensor(ge[:], ge[:], probsT[:], ALU.mult),
            r=["logT", "probsT"], w=["logT"])
        p.v(lambda e: e.tensor_reduce(thr[:, 4:5], ge[:], axis=AX.X, op=ALU.max),
            r=["logT"], w=["thrmax"])
        p.v(lambda e: e.tensor_scalar(thr[:, 5:6], thr[:, 1:2], 512.0, None, ALU.is_gt),
            r=["thrc"], w=["thrp"])
        p.v(lambda e: e.tensor_scalar(thr[:, 6:7], thr[:, 1:2], 512.0, None, ALU.is_lt),
            r=["thrc"], w=["thrn"])
        p.v(lambda e: e.tensor_scalar(thr[:, 3:4], thr[:, 3:4], 1.0 + 2.0 ** -22,
                                      None, ALU.mult),
            r=["thrmin"], w=["thrmin"])
        p.v(lambda e: e.tensor_tensor(thr[:, 3:4], thr[:, 3:4], thr[:, 0:1],
                                      ALU.subtract), r=["thrmin", "thr"], w=["thrmin"])
        p.v(lambda e: e.tensor_tensor(thr[:, 3:4], thr[:, 3:4], thr[:, 5:6],
                                      ALU.mult), r=["thrmin", "thrp"], w=["thrmin"])
        p.v(lambda e: e.tensor_tensor(thr[:, 0:1], thr[:, 0:1], thr[:, 3:4],
                                      ALU.add), r=["thrmin", "thr"], w=["thr"])
        p.v(lambda e: e.tensor_tensor(thr[:, 4:5], thr[:, 4:5], thr[:, 0:1],
                                      ALU.subtract), r=["thrmax", "thr"], w=["thrmax"])
        p.v(lambda e: e.tensor_tensor(thr[:, 4:5], thr[:, 4:5], thr[:, 6:7],
                                      ALU.mult), r=["thrmax", "thrn"], w=["thrmax"])
        p.v(lambda e: e.tensor_tensor(thr[:, 0:1], thr[:, 0:1], thr[:, 4:5],
                                      ALU.add), r=["thrmax", "thr"], w=["thr"])

    slotT = salloc(st, "l3f_slotT", [8, T], F32)
    zer8 = salloc(st, "l3f_z8", [8, T], F32)
    p.v(lambda e: e.memset(zer8[:], 0.0), w=["zer8"])
    p.v(lambda e: e.tensor_scalar(ge[:], probsT[:], thr[:, 0:1], None, ALU.is_ge),
        r=["probsT", "thr"], w=["logT"])
    p.v(lambda e: e.tensor_tensor_scan(slotT[:], ge[:], zer8[:], 0.0,
                                       ALU.add, ALU.bypass),
        r=["logT", "zer8"], w=["slotT"])
    p.v(lambda e: e.tensor_tensor(slotT[:], slotT[:], ge[:], ALU.mult),
        r=["slotT", "logT"], w=["slotT"])
    p.v(lambda e: e.tensor_scalar(slotT[:], slotT[:], -1.0, None, ALU.add),
        r=["slotT"], w=["slotT"])
    slottok = salloc(st, "l3f_slott", [128, NT * 8], F32)
    for tc in range(NT):
        pti = tc % 2
        p.t(lambda e, tc=tc, pti=pti: e.transpose(
            st.pb[4 + pti][0:128, 0:8], slotT[:, tc * 128:(tc + 1) * 128],
            st.identf[0:8, 0:8]),
            r=["slotT", "identf"], w=[f"pb{4 + pti}"])
        p.v(lambda e, tc=tc, pti=pti: e.tensor_copy(
            slottok[:, tc * 8:(tc + 1) * 8], st.pb[4 + pti][0:128, 0:8]),
            r=[f"pb{4 + pti}"], w=["slottok"])
    iot = salloc(st, "l3f_iot", [128, 512], F32)
    tokcol = salloc(st, "l3f_tokc", [128, 16], F32)
    p.g(lambda e: e.iota(iot[:], pattern=[[1, 512]], base=0, channel_multiplier=0,
                         allow_small_or_imprecise_dtypes=True), w=["iot"])
    p.g(lambda e: e.iota(tokcol[:], pattern=[[128, 16]], base=0, channel_multiplier=1,
                         allow_small_or_imprecise_dtypes=True), w=["tokcol"])
    PT = salloc(st, "l3f_PT", [128, 512], BF16)
    rhs2 = salloc(st, "l3f_rhs2", [128, NT * 2], BF16)
    mg = salloc(st, "l3f_mg", [128, 2 * E * 4], F32)
    mint = salloc(st, "l3f_mint", [128, E * 4], I32)
    grow = salloc(st, "l3f_grow", [128, 4], F32)
    p.v(lambda e: e.tensor_copy(rhs2[:].rearrange("p (t two) -> p t two", two=2)[:, :, 0:1],
                                tokcol[:].rearrange("p (t o) -> p t o", o=1)),
        r=["tokcol"], w=["rhs2"])
    for e_i in range(E):
        p.v(lambda e, e_i=e_i: e.tensor_copy(
            rhs2[:].rearrange("p (t two) -> p t two", two=2)[:, :, 1:2],
            probs[:].rearrange("p (t ee) -> p t ee", ee=8)[:, :, e_i:e_i + 1]),
            r=["probs", "rhs2"], w=["rhs2"])
        for tc in range(NT):
            p.v(lambda e, e_i=e_i, tc=tc: e.tensor_scalar(
                PT[:], iot[:], slottok[:, tc * 8 + e_i:tc * 8 + e_i + 1], None,
                ALU.is_equal),
                r=["iot", "slottok"], w=["PT"])
            for sc in range(4):
                p.t(lambda e, tc=tc, sc=sc, s0=(tc == 0): e.matmul(
                    st.pb[sc][0:128, 0:2], PT[:, sc * 128:(sc + 1) * 128],
                    rhs2[:, tc * 2:tc * 2 + 2], start=s0, stop=(tc == NT - 1)),
                    r=["PT", "rhs2"], w=[f"pb{sc}"])
        for sc in range(4):
            p.v(lambda e, e_i=e_i, sc=sc: e.tensor_copy(
                mg[:, e_i * 4 + sc:e_i * 4 + sc + 1], st.pb[sc][0:128, 0:1]),
                r=[f"pb{sc}"], w=["mg"])
            p.v(lambda e, e_i=e_i, sc=sc: e.tensor_copy(
                mg[:, 32 + e_i * 4 + sc:32 + e_i * 4 + sc + 1], st.pb[sc][0:128, 1:2]),
                r=[f"pb{sc}"], w=["mg"])
    p.v(lambda e: e.tensor_copy(mint[:], mg[:, 0:32]), r=["mg"], w=["mint"])

    xg = salloc(st, "l3f_xg", [128, 4 * D], BF16)
    xgT = salloc(st, "l3f_xgT", [128, ND * 512], BF16)
    w1b = salloc(st, "l3f_w1b", [128, 2 * 2048], BF16)
    yb1 = st.ht
    w2b = salloc(st, "l3f_w2b", [128, 2 * 32 * 128], BF16)
    y2t = salloc(st, "l3f_y2t", [128, 4 * D], F32)

    for e_i in range(E):
        for sc in range(4):
            p.indirect(xg[:, sc * D:(sc + 1) * D], None,
                       hdram, IndirectOffsetOnAxis(mint[:, e_i * 4 + sc:e_i * 4 + sc + 1], 0),
                       r=["mint"] + [f"hd{tc}" for tc in range(NT)],
                       w=[f"xg{sc}"])
            for dc in range(ND):
                pti = dc % 2
                p.t(lambda e, sc=sc, dc=dc, pti=pti: e.transpose(
                    st.pt[pti][:, 0:128], xg[:, sc * D + dc * 128: sc * D + dc * 128 + 128],
                    st.ident[:]),
                    r=[f"xg{sc}", "ident"], w=[f"pt{pti}"])
                p.v(lambda e, sc=sc, dc=dc, pti=pti: e.tensor_copy(
                    xgT[:, dc * 512 + sc * 128: dc * 512 + sc * 128 + 128],
                    st.pt[pti][:, 0:128]),
                    r=[f"pt{pti}"], w=[f"xgT{dc}"])
        for oc in range(32):
            sl = (oc // 2) % 2
            if oc % 2 == 0:
                p.dma("dq_g", w1b[:, sl * 2048:(sl + 1) * 2048]
                      .rearrange("p (c m) -> p c m", c=ND),
                      hb["L3_ffn_w1"][e_i][:, oc * 128:(oc + 2) * 128]
                      .rearrange("(c p) m -> p c m", p=128),
                      w=[f"w1b{sl}"])
            ob = oc % 2
            pbm = oc % 2
            for kc in range(ND):
                p.t(lambda e, sl=sl, ob=ob, kc=kc, pbm=pbm: e.matmul(
                    st.pb[pbm][:, 0:512],
                    w1b[:, sl * 2048 + kc * 256 + ob * 128:
                        sl * 2048 + kc * 256 + ob * 128 + 128],
                    xgT[:, kc * 512:(kc + 1) * 512],
                    start=(kc == 0), stop=(kc == ND - 1)),
                    r=[f"w1b{sl}", f"xgT{kc}"], w=[f"pb{pbm}"])
            p.s(lambda e, oc=oc, pbm=pbm: e.activation(
                out=yb1[:, oc * 512:oc * 512 + 512], in_=st.pb[pbm][:, 0:512],
                func=AF.Gelu),
                r=[f"pb{pbm}"], w=[f"yb{oc}", f"ht{oc // 4}"])
        for mi in range(ND):
            sl = mi % 2
            p.dma("dq_g", w2b[:, sl * 32 * 128:(sl + 1) * 32 * 128]
                  .rearrange("p (c m) -> p c m", c=32),
                  hb["L3_ffn_w2"][e_i][:, mi * 128:(mi + 1) * 128]
                  .rearrange("(c p) m -> p c m", p=128),
                  w=[f"w2b{sl}"])
            pbm = mi % 2
            for kc in range(32):
                p.t(lambda e, sl=sl, kc=kc, pbm=pbm: e.matmul(
                    st.pb[pbm][:, 0:512],
                    w2b[:, sl * 32 * 128 + kc * 128: sl * 32 * 128 + kc * 128 + 128],
                    yb1[:, kc * 512:(kc + 1) * 512],
                    start=(kc == 0), stop=(kc == 31)),
                    r=[f"w2b{sl}", f"yb{kc}", f"ht{kc // 4}"], w=[f"pb{pbm}"])
            zi = mi % 2
            p.s(lambda e, pbm=pbm, zi=zi: e.copy(
                st.zslice[:, zi * 512:zi * 512 + 512], st.pb[pbm][:, 0:512]),
                r=[f"pb{pbm}"], w=[f"zs{zi}"])
            for sc in range(4):
                pbi = 4 + sc % 2
                p.t(lambda e, zi=zi, sc=sc, pbi=pbi: e.transpose(
                    st.pb[pbi][:, 0:128],
                    st.zslice[:, zi * 512 + sc * 128: zi * 512 + sc * 128 + 128],
                    st.identf[:]),
                    r=[f"zs{zi}", "identf"], w=[f"pb{pbi}"])
                p.v(lambda e, e_i=e_i, sc=sc, mi=mi, pbi=pbi: e.tensor_scalar(
                    y2t[:, sc * D + mi * 128: sc * D + mi * 128 + 128],
                    st.pb[pbi][:, 0:128],
                    mg[:, 32 + e_i * 4 + sc:32 + e_i * 4 + sc + 1], None, ALU.mult),
                    r=[f"pb{pbi}", "mg"], w=[f"y2t{sc}"])
        for sc in range(4):
            p.indirect(ydram, IndirectOffsetOnAxis(mint[:, e_i * 4 + sc:e_i * 4 + sc + 1], 0),
                       y2t[:, sc * D:(sc + 1) * D], None,
                       compute_op=ALU.add,
                       r=["mint", f"y2t{sc}"] + [f"yd{tc}" for tc in range(NT)],
                       w=[f"yd{tc}" for tc in range(NT)])

    for tc in range(NT):
        for half in range(2):
            sl = (tc * 2 + half) % 2
            p.dma("dq_y", st.zslice[:, sl * 512:sl * 512 + 512],
                  ydram[tc * 128:(tc + 1) * 128, half * 512:half * 512 + 512],
                  r=[f"yd{tcc}" for tcc in range(NT)], w=[f"zs{sl}"])
            p.v(lambda e, tc=tc, half=half, sl=sl: e.tensor_tensor(
                x_t(st, tc)[:, half * 512:half * 512 + 512],
                x_t(st, tc)[:, half * 512:half * 512 + 512],
                st.zslice[:, sl * 512:sl * 512 + 512], ALU.add),
                r=[f"x{tc}", f"zs{sl}"], w=[f"x{tc}"])


SUBLAYERS = {
    0: ("l0_attn", emit_l0_attn,
        ["L0_attn_W_qkv", "L0_attn_b_qkv", "L0_attn_W_o", "L0_attn_b_o"]),
    1: ("l0_ffn", emit_l0_ffn,
        ["L0_ffn_ln_gamma", "L0_ffn_ln_beta", "L0_ffn_fc1_q", "L0_ffn_fc1_scales",
         "L0_ffn_fc1_bias", "L0_ffn_fc2_q", "L0_ffn_fc2_scales", "L0_ffn_fc2_bias"]),
    2: ("l1_attn", emit_l1_attn, ["L1_attn_Wk", "L1_attn_Wq", "L1_attn_Wv"]),
    3: ("l1_ffn", emit_l1_ffn,
        ["L1_ffn_routing_W", "L1_ffn_expert_bias", "L1_ffn_sW1", "L1_ffn_sb1",
         "L1_ffn_sW2", "L1_ffn_sb2", "L1_ffn_rW1", "L1_ffn_rb1", "L1_ffn_rW2",
         "L1_ffn_rb2"]),
    4: ("l2_attn", emit_l2_attn,
        ["L2_attn_Wq", "L2_attn_Wk", "L2_attn_Wv", "L2_attn_Wo"]),
    5: ("l2_ffn", emit_l2_ffn, ["L2_ffn_w_gate", "L2_ffn_w_up", "L2_ffn_w_down"]),
    6: ("l3_attn", emit_l3_attn,
        ["L3_attn_w_qkv", "L3_attn_w_o", "L3_attn_ln1_g", "L3_attn_ln1_b",
         "L3_attn_ln2_g", "L3_attn_ln2_b", "L3_attn_mlp_w1", "L3_attn_mlp_b1",
         "L3_attn_mlp_w2", "L3_attn_mlp_b2"]),
    7: ("l3_ffn", emit_l3_ffn, ["L3_ffn_gate_w", "L3_ffn_w1", "L3_ffn_w2"]),
}

CONST_KEYS = list(CONSTS.keys())


def build_sublayer_program(k, weight_shapes):
    name, emitter, wkeys = SUBLAYERS[k]
    nc = bass.Bass("TRN2", target_bir_lowering=False)
    st = alloc_common(nc)
    x_in = nc.dram_tensor("x_in", [T, D], F32, kind="ExternalInput").ap()
    x_out = nc.dram_tensor("x_out", [T, D], F32, kind="ExternalOutput").ap()
    for ck in CONST_KEYS:
        a = CONSTS[ck]
        dt = BF16 if a.dtype == BF else (F32 if a.dtype == np.float32 else I32)
        st.hbm[ck] = nc.dram_tensor(ck, list(a.shape), dt, kind="ExternalInput").ap()
    for wk in wkeys:
        shape, dtype = weight_shapes[wk]
        dt = {"float32": F32, "int32": I32}[dtype]
        st.hbm[wk] = nc.dram_tensor(wk, list(shape), dt, kind="ExternalInput").ap()
    p = P(nc)
    emit_load_consts(p, st)
    emit_load_x(p, st, x_in)
    emitter(p, st)
    emit_store_x(p, st, x_out)
    return p.finalize()


def build_fused_program(weight_shapes):
    nc = bass.Bass("TRN2", target_bir_lowering=False)
    st = alloc_common(nc)
    st.fused = True
    st.arena_base = nc.sbuf_base
    x_in = nc.dram_tensor("x_in", [T, D], F32, kind="ExternalInput").ap()
    x_out = nc.dram_tensor("x_out", [T, D], F32, kind="ExternalOutput").ap()
    for ck in CONST_KEYS:
        a = CONSTS[ck]
        dt = BF16 if a.dtype == BF else (F32 if a.dtype == np.float32 else I32)
        st.hbm[ck] = nc.dram_tensor(ck, list(a.shape), dt, kind="ExternalInput").ap()
    for k in sorted(SUBLAYERS):
        for wk in SUBLAYERS[k][2]:
            shape, dtype = weight_shapes[wk]
            dt = {"float32": F32, "int32": I32}[dtype]
            st.hbm[wk] = nc.dram_tensor(wk, list(shape), dt, kind="ExternalInput").ap()
    p = P(nc)
    emit_load_consts(p, st)
    emit_load_x(p, st, x_in)
    for k in sorted(SUBLAYERS):
        st.sub = k
        st.arena_pos = 0
        SUBLAYERS[k][1](p, st)
        p.barrier_all()
    emit_store_x(p, st, x_out)
    return p.finalize()


def kernel(**inputs):
    weight_shapes = {k: (v.shape, str(v.dtype)) for k, v in inputs.items()}
    x = np.ascontiguousarray(inputs["L0_attn_residual_stream"], np.float32)
    B = x.shape[0]
    nc = build_fused_program(weight_shapes)
    wall = [wk for k in sorted(SUBLAYERS) for wk in SUBLAYERS[k][2]]
    in_maps = []
    for b in range(B):
        m = {"x_in": np.ascontiguousarray(x[b])}
        for ck in CONST_KEYS:
            m[ck] = CONSTS[ck]
        for wk in wall:
            m[wk] = np.ascontiguousarray(inputs[wk])
        in_maps.append(m)
    res = run_bass_kernel_spmd(nc, in_maps, core_ids=list(range(B)))
    return np.stack([res.results[b]["x_out"] for b in range(B)], 0)


# version 04
# speedup vs baseline: 1.2873x; 1.0805x over previous
import numpy as np
import ml_dtypes
import concourse.bass as bass
import concourse.mybir as mybir
from concourse.bass import IndirectOffsetOnAxis
from concourse.bass_utils import run_bass_kernel_spmd

F32 = mybir.dt.float32
BF16 = mybir.dt.bfloat16
I32 = mybir.dt.int32
AF = mybir.ActivationFunctionType
ALU = mybir.AluOpType
AX = mybir.AxisListType
BF = ml_dtypes.bfloat16

T = 2048
D = 1024
NT = T // 128
ND = D // 128
QB = 512
NQB = T // QB

ENG = ("tensor", "vector", "scalar", "gpsimd", "sync")
DMAQ = ("dq_g", "dq_y", "dq_a")


class _Slot:
    __slots__ = ("idx", "mark")
    def __init__(self, idx):
        self.idx = idx
        self.mark = None


NDMAS = 8


class P:
    def __init__(self, nc):
        self.nc = nc
        self.q = {e: [] for e in ENG}
        self.nemit = {e: 0 for e in ENG}
        self.marked = {e: [] for e in ENG}
        self.sem = {e: nc.alloc_semaphore(f"sm_{e}") for e in ENG}
        self.dsem = {d: [nc.alloc_semaphore(f"sm_{d}{i}") for i in range(NDMAS)]
                     for d in DMAQ}
        self.dcnt = {d: [0] * NDMAS for d in DMAQ}
        self.drr = {d: 0 for d in DMAQ}
        self.insts = {e: {} for e in ENG}
        self.waited = {e: {} for e in ENG}
        self.res = {}

    def _push(self, e, fn):
        self.q[e].append(fn)

    def _mark(self, e, slot):
        ml = self.marked[e]
        for m in ml[::-1]:
            if m.idx < slot.idx:
                break
        for pos, m in enumerate(ml):
            if m.idx >= slot.idx:
                return pos + 1
        slot.mark = len(ml) + 1
        ml.append(slot)
        sem = self.sem[e]
        idx = slot.idx
        def attach(eng, e=e, idx=idx):
            self.insts[e][idx].then_inc(sem, 1)
        self._push(e, attach)
        return slot.mark

    def _wait(self, e, src, val):
        w = self.waited[e]
        if w.get(src, 0) >= val:
            return
        w[src] = val
        h = self.sem[src[1]] if src[0] == "eng" else self.dsem[src[1]][src[2]]
        self._push(e, lambda eng: eng.wait_ge(h, val))

    def _deps(self, e, r, w):
        need = {}
        cur = self.nemit[e]
        def add_dep(d):
            if d is None:
                return
            if d[0] == "eng":
                _, e2, slot = d
                if e2 == e:
                    if e2 != "tensor" and cur - slot.idx < 8:
                        need.setdefault(("eng", e2), []).append(slot)
                    return
                need.setdefault(("eng", e2), []).append(slot)
            else:
                _, qn, si, val = d
                key = ("dma", qn, si)
                need[key] = [max(need.get(key, [0])[0], val)]
        for k in r:
            rec = self.res.get(k)
            if rec:
                add_dep(rec["w"])
        for k in w:
            rec = self.res.get(k)
            if rec:
                add_dep(rec["w"])
                for rd in rec["r"]:
                    add_dep(rd)
        for src, items in need.items():
            if src[0] == "eng":
                val = max(self._mark(src[1], s) for s in items)
                self._wait(e, ("eng", src[1]), val)
            else:
                self._wait(e, src, items[0])

    def _record(self, owner, r, w):
        for k in r:
            rec = self.res.setdefault(k, {"w": None, "r": []})
            rec["r"].append(owner)
        for k in w:
            self.res[k] = {"w": owner, "r": []}

    def emit(self, e, fn, r=(), w=()):
        self._deps(e, r, w)
        idx = self.nemit[e]
        self.nemit[e] += 1
        slot = _Slot(idx)
        self.last_slot = getattr(self, "last_slot", {})
        self.last_slot[e] = slot
        def run(eng, e=e, fn=fn, idx=idx):
            inst = fn(eng)
            assert inst is not None, f"emit fn on {e} returned None"
            self.insts[e][idx] = inst
        self._push(e, run)
        self._record(("eng", e, slot), r, w)
        return slot

    def t(self, fn, r=(), w=()): return self.emit("tensor", fn, r, w)
    def v(self, fn, r=(), w=()): return self.emit("vector", fn, r, w)
    def s(self, fn, r=(), w=()): return self.emit("scalar", fn, r, w)
    def g(self, fn, r=(), w=()): return self.emit("gpsimd", fn, r, w)

    def _dma_slot(self, e, qname):
        si = self.drr[qname]
        self.drr[qname] = (si + 1) % NDMAS
        if self.dcnt[qname][si] > 0:
            self._wait(e, ("dma", qname, si), self.dcnt[qname][si])
        self.dcnt[qname][si] += 16
        return si, self.dcnt[qname][si]

    def dma(self, qname, out_ap, in_ap, r=(), w=(), engine=None, **kw):
        e = engine or {"dq_g": "gpsimd", "dq_y": "sync", "dq_a": "scalar"}[qname]
        self._deps(e, r, w)
        si, val = self._dma_slot(e, qname)
        h = self.dsem[qname][si]
        def run(eng):
            eng.dma_start(out=out_ap, in_=in_ap, **kw).then_inc(h, 16)
        self._push(e, run)
        self._record(("dma", qname, si, val), r, w)
        return (qname, si, val)

    def dma_wait(self, e, ref):
        qname, si, val = ref
        self._wait(e, ("dma", qname, si), val)

    def dma_wait_all(self, e, qname):
        for si in range(NDMAS):
            if self.dcnt[qname][si] > 0:
                self._wait(e, ("dma", qname, si), self.dcnt[qname][si])

    def indirect(self, out_ap, out_off, in_ap, in_off, r=(), w=(), **kw):
        e = "gpsimd"
        self._deps(e, r, w)
        si, val = self._dma_slot(e, "dq_g")
        h = self.dsem["dq_g"][si]
        def run(eng):
            eng.indirect_dma_start(out=out_ap, out_offset=out_off, in_=in_ap,
                                   in_offset=in_off, **kw).then_inc(h, 16)
        self._push(e, run)
        self._record(("dma", "dq_g", si, val), r, w)
        return ("dq_g", si, val)

    def barrier_all(self):
        ls = getattr(self, "last_slot", {})
        marks = {e: self._mark(e, sl) for e, sl in ls.items()}
        for e in ENG:
            for e2, v in marks.items():
                if e2 != e:
                    self._wait(e, ("eng", e2), v)
            for qn in DMAQ:
                for si in range(NDMAS):
                    if self.dcnt[qn][si] > 0:
                        self._wait(e, ("dma", qn, si), self.dcnt[qn][si])
        self.res = {}

    def finalize(self):
        nc = self.nc
        with nc.Block() as blk:
            @blk.tensor
            def _(eng):
                for f in self.q["tensor"]:
                    f(eng)
            @blk.vector
            def _(eng):
                for f in self.q["vector"]:
                    f(eng)
            @blk.scalar
            def _(eng):
                for f in self.q["scalar"]:
                    f(eng)
            @blk.gpsimd
            def _(eng):
                for f in self.q["gpsimd"]:
                    f(eng)
            @blk.sync
            def _(eng):
                for f in self.q["sync"]:
                    f(eng)
        return nc


def build_consts():
    c = {}
    c["c_ident"] = np.eye(128, dtype=BF)
    c["c_identf"] = np.eye(128, dtype=np.float32)
    c["c_ones512"] = np.ones((1, 512), dtype=BF)
    c["c_onesrowf"] = np.ones((1, 128), dtype=np.float32)
    c["c_onescol"] = np.ones((128, 1), dtype=BF)
    sel = np.zeros((8, 8 * 128), dtype=BF)
    for e in range(8):
        sel[e, e * 128:(e + 1) * 128] = 1.0
    c["c_sel8"] = sel
    for off in range(4):
        jj = np.arange(128)[:, None] + off * 128
        qq = np.arange(512)[None, :]
        c[f"c_cmask{off}"] = (qq >= jj).astype(BF)
    jj = np.arange(128)[:, None]
    qq = np.arange(512)[None, :]
    for i in range(8):
        dlt = qq - ((i - 4) * 128 + jj)
        c[f"c_wm{i}"] = ((dlt >= 0) & (dlt <= 511)).astype(BF)
    inv = 1.0 / (10000.0 ** (np.arange(0, D, 2, dtype=np.float64) / D))
    ang = np.arange(T, dtype=np.float64)[:, None] * inv[None, :]
    cT = np.empty((D, T), np.float32); sT = np.empty((D, T), np.float32)
    cT[0::2] = np.cos(ang).T; cT[1::2] = np.cos(ang).T
    sT[0::2] = -np.sin(ang).T; sT[1::2] = np.sin(ang).T
    c["c_ropeq_c"] = cT
    c["c_ropeq_s"] = sT
    angk = ang[:, :32]
    cTk = np.empty((64, T), np.float32); sTk = np.empty((64, T), np.float32)
    cTk[0::2] = np.cos(angk).T; cTk[1::2] = np.cos(angk).T
    sTk[0::2] = -np.sin(angk).T; sTk[1::2] = np.sin(angk).T
    c["c_ropek_c"] = np.tile(cTk, (2, 1)).copy()
    c["c_ropek_s"] = np.tile(sTk, (2, 1)).copy()
    return c


CONSTS = build_consts()


class St:
    fused = False
    sub = 0


def salloc(st, name, shape, dtype):
    nc = st.nc
    if not st.fused:
        return nc.alloc_sbuf_tensor(name, shape, dtype)
    esz = {F32: 4, BF16: 2, I32: 4}[dtype]
    nbytes = shape[1] * esz
    nbytes = (nbytes + 31) // 32 * 32
    off = st.arena_base + st.arena_pos
    st.arena_pos += nbytes
    assert off + nbytes <= 229344, f"arena overflow at {name}: {off + nbytes}"
    return nc.alloc_sbuf_tensor_at(f"{name}_s{st.sub}", shape, dtype, offset=off)


def alloc_common(nc):
    st = St()
    st.nc = nc
    st.x = nc.alloc_sbuf_tensor("x_sb", [128, NT * D], F32)
    st.ht = nc.alloc_sbuf_tensor("ht_sb", [128, ND * T], BF16)
    st.htok = nc.alloc_sbuf_tensor("htok_sb", [128, 2 * D], BF16)
    st.stats = nc.alloc_sbuf_tensor("st_sb", [128, 8 * NT], F32)
    st.sq = st.htok[:, 0:D]
    st.eps = nc.alloc_sbuf_tensor("eps_sb", [128, 1], F32)
    st.ident = nc.alloc_sbuf_tensor("ident_sb", [128, 128], BF16)
    st.identf = nc.alloc_sbuf_tensor("identf_sb", [128, 128], F32)
    st.ones512 = nc.alloc_sbuf_tensor("ones512_sb", [1, 512], BF16)
    st.onesrow_f = nc.alloc_sbuf_tensor("onesrf_sb", [1, 128], F32)
    st.onescol = nc.alloc_sbuf_tensor("onesc_sb", [128, 1], BF16)
    st.zslice = nc.alloc_sbuf_tensor("zsl_sb", [128, 2 * 512], F32)
    st.psA = nc.alloc_psum_tensor("psA", [128, 1024], F32)
    st.psB = nc.alloc_psum_tensor("psB", [128, 1024], F32)
    _pb4 = nc.alloc_psum_tensor("pb4", [128, 512], F32)
    _pb5 = nc.alloc_psum_tensor("pb5", [128, 512], F32)
    st.pb = [st.psA[:, 0:512], st.psA[:, 512:1024],
             st.psB[:, 0:512], st.psB[:, 512:1024], _pb4[:], _pb5[:]]
    st.pt = [nc.alloc_psum_tensor(f"pt{i}", [128, 512], BF16) for i in range(2)]
    st.hbm = {}
    return st


def x_t(st, tc):
    return st.x[:, tc * D:(tc + 1) * D]


def ht_c(st, dc):
    return st.ht[:, dc * T:(dc + 1) * T]


def lane(st, i, tc=None):
    if tc is None:
        return st.stats[:, i * NT:(i + 1) * NT]
    return st.stats[:, i * NT + tc:i * NT + tc + 1]


XK = [f"x{tc}" for tc in range(NT)]
HTK = [f"ht{dc}" for dc in range(ND)]


def emit_load_x(p, st, x_hbm):
    p.dma("dq_y", st.x[:].rearrange("p (a m) -> p a m", a=NT),
          x_hbm.rearrange("(a p) m -> p a m", p=128), r=(), w=XK)


def emit_store_x(p, st, x_hbm_out):
    ref = p.dma("dq_y", x_hbm_out.rearrange("(a p) m -> p a m", p=128),
                st.x[:].rearrange("p (a m) -> p a m", a=NT), r=XK, w=())
    p.dma_wait("sync", ref)


def emit_load_consts(p, st):
    hb = st.hbm
    p.dma("dq_y", st.ident[:], hb["c_ident"], w=["ident"])
    p.dma("dq_y", st.identf[:], hb["c_identf"], w=["identf"])
    p.dma("dq_y", st.ones512[:], hb["c_ones512"], w=["ones512"])
    p.dma("dq_y", st.onesrow_f[:], hb["c_onesrowf"], w=["onesrowf"])
    p.dma("dq_y", st.onescol[:], hb["c_onescol"], w=["onescol"])
    p.v(lambda e: e.memset(st.eps[:], 1e-5), w=["eps"])


def emit_ln_stats(p, st, src_t=None, src_k=None):
    src_t = src_t or (lambda tc: x_t(st, tc))
    src_k = src_k or (lambda tc: f"x{tc}")
    p.v(lambda e: e.memset(lane(st, 1), 0.0), w=["st1"])
    for tc in range(NT):
        p.s(lambda e, tc=tc: e.activation(out=st.sq[:], in_=src_t(tc),
                                          func=AF.Square, accum_out=lane(st, 1, tc)),
            r=[src_k(tc), "st1"], w=["htok0", "st1"])
        p.v(lambda e, tc=tc: e.reduce_sum(lane(st, 0, tc), src_t(tc), axis=AX.X),
            r=[src_k(tc)], w=["st0"])
    p.v(lambda e: e.tensor_scalar_mul(lane(st, 4), lane(st, 0), 1.0 / D),
        r=["st0"], w=["st4"])
    p.v(lambda e: e.tensor_tensor(lane(st, 5), lane(st, 4), lane(st, 4), ALU.mult),
        r=["st4"], w=["st5"])
    p.v(lambda e: e.scalar_tensor_tensor(lane(st, 6), lane(st, 1), 1.0 / D,
                                         lane(st, 5), ALU.mult, ALU.subtract),
        r=["st1", "st5"], w=["st6"])
    p.s(lambda e: e.activation(out=lane(st, 5), in_=lane(st, 6), func=AF.Sqrt,
                               bias=st.eps[:, 0:1], scale=1.0),
        r=["st6", "eps"], w=["st5"])
    p.v(lambda e: e.reciprocal(lane(st, 2), lane(st, 5)), r=["st5"], w=["st2"])
    p.v(lambda e: e.scalar_tensor_tensor(lane(st, 3), lane(st, 4), -1.0, lane(st, 2),
                                         ALU.mult, ALU.mult),
        r=["st4", "st2"], w=["st3"])


def emit_ln_to_ht(p, st, xupdate=False, rlane=2, blane=3,
                  gamma_col=None, beta_col=None, gb_res=(), dram_out=None):
    for tc in range(NT):
        bi = tc % 2
        buf = st.htok[:, bi * D:bi * D + D]
        p.s(lambda e, tc=tc, buf=buf: e.activation(
            out=buf, in_=x_t(st, tc), func=AF.Identity,
            scale=lane(st, rlane, tc), bias=lane(st, blane, tc)),
            r=[f"x{tc}", f"st{rlane}", f"st{blane}"], w=[f"htok{bi}"])
        for dc in range(ND):
            pti = (tc * ND + dc) % 2
            p.t(lambda e, buf=buf, dc=dc, pti=pti: e.transpose(
                st.pt[pti][:, 0:128], buf[:, dc * 128:dc * 128 + 128], st.ident[:]),
                r=[f"htok{bi}", "ident"], w=[f"pt{pti}"])
            dst = ht_c(st, dc)[:, tc * 128:tc * 128 + 128]
            if gamma_col is None:
                p.v(lambda e, dst=dst, pti=pti: e.tensor_copy(dst, st.pt[pti][:, 0:128]),
                    r=[f"pt{pti}"], w=[f"ht{dc}"])
            else:
                p.v(lambda e, dst=dst, pti=pti, dc=dc: e.tensor_scalar(
                    dst, st.pt[pti][:, 0:128], gamma_col(dc), beta_col(dc),
                    ALU.mult, ALU.add),
                    r=[f"pt{pti}"] + list(gb_res), w=[f"ht{dc}"])
        if dram_out is not None:
            p.dma("dq_y", dram_out[tc * 128:(tc + 1) * 128, :], buf,
                  r=[f"htok{bi}"], w=[f"hd{tc}"])
        if xupdate:
            p.v(lambda e, tc=tc: e.tensor_scalar(
                x_t(st, tc), x_t(st, tc), lane(st, 7, tc), lane(st, 3, tc),
                ALU.mult, ALU.add),
                r=[f"x{tc}", "st7", "st3", f"htok{bi}"], w=[f"x{tc}"])


def load_w(p, st, name, hbm_ap, K, M, dst=None, dst_off=0, key=None):
    nc = st.nc
    nch = K // 128
    if dst is None:
        dst = salloc(st, f"w_{name}", [128, nch * M], BF16)
        dst_off = 0
    key = key or f"w_{name}"
    p.dma("dq_g",
          dst[:, dst_off:dst_off + nch * M].rearrange("p (c m) -> p c m", c=nch),
          hbm_ap.rearrange("(c p) m -> p c m", p=128), w=[key])
    return dst, key


def load_row(p, st, name, hbm_ap_1d, n, dtype=BF16):
    nc = st.nc
    dst = salloc(st, f"r_{name}", [1, n], dtype)
    p.dma("dq_g", dst[:], hbm_ap_1d.rearrange("(o n) -> o n", o=1), w=[f"r_{name}"])
    return dst, f"r_{name}"


_back_i = [0]


def emit_back_tile(p, st, pb_key, psrc, mi, qb, scale=1.0, ptok=None,
                   ptok_res=(), dstk="x", dst_t=None, add=True):
    dst_t = dst_t or (lambda tc: x_t(st, tc))
    zi = _back_i[0] % 2
    _back_i[0] += 1
    zbuf = st.zslice[:, zi * 512:zi * 512 + 512]
    p.s(lambda e, psrc=psrc, zbuf=zbuf: e.copy(zbuf, psrc),
        r=[pb_key], w=[f"zs{zi}"])
    for j in range(4):
        tc = qb * 4 + j
        pbi = 4 + (zi * 4 + j) % 2
        p.t(lambda e, zbuf=zbuf, j=j, pbi=pbi: e.transpose(
            st.pb[pbi][:, 0:128], zbuf[:, j * 128:j * 128 + 128], st.identf[:]),
            r=[f"zs{zi}", "identf"], w=[f"pb{pbi}"])
        dst = dst_t(tc)[:, mi * 128:mi * 128 + 128]
        dk = f"{dstk}{tc}"
        if add:
            if ptok is None:
                p.v(lambda e, dst=dst, pbi=pbi, s=scale: e.scalar_tensor_tensor(
                    dst, st.pb[pbi][:, 0:128], s, dst, ALU.mult, ALU.add),
                    r=[f"pb{pbi}", dk], w=[dk])
            else:
                p.v(lambda e, dst=dst, pbi=pbi, tc=tc: e.scalar_tensor_tensor(
                    dst, st.pb[pbi][:, 0:128], ptok(tc), dst, ALU.mult, ALU.add),
                    r=[f"pb{pbi}", dk] + list(ptok_res), w=[dk])
        else:
            if ptok is None:
                p.v(lambda e, dst=dst, pbi=pbi, s=scale: e.tensor_scalar_mul(
                    dst, st.pb[pbi][:, 0:128], s),
                    r=[f"pb{pbi}"], w=[dk])
            else:
                p.v(lambda e, dst=dst, pbi=pbi, tc=tc: e.tensor_scalar(
                    dst, st.pb[pbi][:, 0:128], ptok(tc), None, ALU.mult),
                    r=[f"pb{pbi}"] + list(ptok_res), w=[dk])


def emit_l0_attn(p, st):
    nc, hb = st.nc, st.hbm
    emit_ln_stats(p, st)
    emit_ln_to_ht(p, st)

    NHG = 4
    qT = salloc(st, "l0_qT", [128, 2 * T], BF16)
    kT = salloc(st, "l0_kT", [128, 2 * T], BF16)
    VH = salloc(st, "l0_VH", [128, NT * NHG * 65], BF16)
    atT = salloc(st, "l0_atT", [128, 2 * 2048], BF16)
    vtmp = atT[:, 0:2048]
    aoT = salloc(st, "l0_aoT", [128, NQB * ND * 512], BF16)
    wq = salloc(st, "l0_wq", [128, 2 * 2 * ND * 128], BF16)
    wk = salloc(st, "l0_wk", [128, 2 * 2 * ND * 128], BF16)
    wv = salloc(st, "l0_wv", [128, 2 * ND * 128], BF16)
    denrow = salloc(st, "l0_dr", [65, T], F32)
    rb = salloc(st, "l0_rb", [128, 512], F32)
    ddram = nc.dram_tensor("l0_ddram", [16, T], F32, kind="Internal").ap()
    nrm = salloc(st, "l0_nrm", [1, 64], F32)
    negM = salloc(st, "l0_negM", [128, NHG], F32)

    wqkv = hb["L0_attn_W_qkv"]
    bqkv = salloc(st, "l0_bqkv", [1, 3 * 16 * 64], BF16)
    for wi, (c0, c1) in enumerate(((0, 64), (64, 128), (128, 192))):
        p.dma("dq_g", bqkv[0:1, wi * 1024:(wi + 1) * 1024]
              .rearrange("o (h m) -> o h m", h=16),
              hb["L0_attn_b_qkv"][:, c0:c1], w=["r_l0b"])
    bo_row = bqkv[0:1, 0:1024]

    def load_grp(grp):
        gs = (grp % 2) * 2 * ND * 128
        for j in range(NHG // 2):
            ha = grp * NHG + 2 * j
            for (wt, wkey, c0, c1) in ((wq, f"wq{grp % 2}_{j}", 0, 64),
                                       (wk, f"wk{grp % 2}_{j}", 64, 128)):
                for hio in range(2):
                    go = 0 if wt is wv else gs
                    p.dma("dq_g",
                          wt[:, go + j * ND * 128:go + (j + 1) * ND * 128]
                          .rearrange("p (c h m) -> p c h m", c=ND, h=2)[:, :, hio, :],
                          wqkv[ha + hio, :, c0:c1]
                          .rearrange("(c p) m -> p c m", p=128),
                          w=[wkey])

    def load_grp_v(grp):
        for j in range(NHG // 2):
            ha = grp * NHG + 2 * j
            for hio in range(2):
                p.dma("dq_g",
                      wv[:, j * ND * 128:(j + 1) * ND * 128]
                      .rearrange("p (c h m) -> p c h m", c=ND, h=2)[:, :, hio, :],
                      wqkv[ha + hio, :, 128:192]
                      .rearrange("(c p) m -> p c m", p=128),
                      w=[f"wv0_{j}"])

    load_grp(0)
    for grp in range(16 // NHG):
        h0 = grp * NHG
        gs = (grp % 2) * 2 * ND * 128
        load_grp_v(grp)

        for j in range(NHG // 2):
            ha = h0 + 2 * j
            for wi, (wt, wkey, c0, c1) in enumerate(
                    ((wq, f"wq{grp % 2}_{j}", 0, 64), (wk, f"wk{grp % 2}_{j}", 64, 128),
                     (wv, f"wv0_{j}", 128, 192))):
                go = 0 if wi == 2 else gs
                for qb in range(NQB):
                    pbm = (wi * NQB + qb) % 2
                    for kc in range(ND):
                        p.t(lambda e, wt=wt, go=go, j=j, kc=kc, qb=qb, pbm=pbm: e.matmul(
                            st.pb[pbm][:, 0:512],
                            wt[:, go + j * ND * 128 + kc * 128: go + j * ND * 128 + kc * 128 + 128],
                            ht_c(st, kc)[:, qb * QB:qb * QB + QB],
                            start=(kc == 0), stop=False),
                            r=[wkey, f"ht{kc}"], w=[f"pb{pbm}"])
                    p.t(lambda e, ha=ha, wi=wi, pbm=pbm: e.matmul(
                        st.pb[pbm][:, 0:512],
                        bqkv[0:1, wi * 1024 + ha * 64: wi * 1024 + ha * 64 + 128],
                        st.ones512[0:1, 0:512], start=False, stop=True),
                        r=["r_l0b", "ones512"], w=[f"pb{pbm}"])
                    if wi == 0:
                        dst, dk = qT[:, j * T + qb * QB: j * T + qb * QB + QB], f"qT{j}"
                    elif wi == 1:
                        dst, dk = kT[:, j * T + qb * QB: j * T + qb * QB + QB], f"kT{j}"
                    else:
                        dst, dk = vtmp[:, qb * QB: qb * QB + QB], f"atTc0_{qb // 2}"
                    p.s(lambda e, dst=dst, pbm=pbm: e.copy(dst, st.pb[pbm][:, 0:512]),
                        r=[f"pb{pbm}"], w=[dk])
            for kc in range(NT):
                pti = kc % 2
                p.t(lambda e, kc=kc, pti=pti: e.transpose(
                    st.pt[pti][:, 0:128], vtmp[:, kc * 128:kc * 128 + 128],
                    st.ident[:]),
                    r=[f"atTc0_{kc // 8}", "ident"], w=[f"pt{pti}"])
                base = kc * NHG * 65 + (2 * j) * 65
                dstv = VH[:, base:base + 130].rearrange("p (h m) -> p h m", h=2)[:, :, 0:64]
                p.v(lambda e, dstv=dstv, pti=pti: e.tensor_copy(
                    dstv, st.pt[pti][:, 0:128].rearrange("p (h m) -> p h m", h=2)),
                    r=[f"pt{pti}"], w=[f"VH{kc}_{j}"])
        if grp == 0:
            for kc in range(NT):
                on = VH[:, kc * NHG * 65:(kc + 1) * NHG * 65].rearrange(
                    "p (h m) -> p h m", h=NHG)[:, :, 64:65]
                p.v(lambda e, on=on: e.memset(on, 1.0), w=[f"VHo{kc}"])

        if grp + 1 < 16 // NHG:
            load_grp(grp + 1)
        p.v(lambda e: e.memset(nrm[:, 0:32], 0.0), w=["nrm"])
        for j in range(NHG // 2):
            for wi, (src, srck) in enumerate(((qT, "qT"), (kT, "kT"))):
                for qb in range(NQB):
                    p.s(lambda e, src=src, j=j, qb=qb: e.activation(
                        out=st.sq[:, 0:512],
                        in_=src[:, j * T + qb * QB: j * T + qb * QB + QB],
                        func=AF.Square),
                        r=[f"{srck}{j}"], w=["htok0"])
                    for hio in range(2):
                        base = hio * 64
                        idx = wi * 16 + 2 * j + hio
                        p.t(lambda e, base=base: e.matmul(
                            st.pb[2][0:1, 0:512], st.onescol[base:base + 64, 0:1],
                            st.sq[base:base + 64, 0:512], start=True, stop=True),
                            r=["htok0", "onescol"], w=["pb2"])
                        p.v(lambda e: e.reduce_max(nrm[0:1, 40:41],
                                                   st.pb[2][0:1, 0:512], axis=AX.X),
                            r=["pb2"], w=["nrms"])
                        p.v(lambda e, idx=idx: e.tensor_max(
                            nrm[0:1, idx:idx + 1], nrm[0:1, idx:idx + 1],
                            nrm[0:1, 40:41]),
                            r=["nrms", "nrm"], w=["nrm"])
        p.v(lambda e: e.tensor_tensor(nrm[0:1, 32:32 + NHG], nrm[0:1, 0:NHG],
                                      nrm[0:1, 16:16 + NHG], ALU.mult),
            r=["nrm"], w=["nrmm"])
        p.s(lambda e: e.activation(out=nrm[0:1, 48:48 + NHG], in_=nrm[0:1, 32:32 + NHG],
                                   func=AF.Sqrt),
            r=["nrmm"], w=["nrmsq"])
        p.v(lambda e: e.tensor_scalar_mul(nrm[0:1, 56:56 + NHG], nrm[0:1, 48:48 + NHG],
                                          -0.125),
            r=["nrmsq"], w=["nrmneg"])
        p.t(lambda e: e.matmul(st.pb[2][0:128, 0:NHG], st.onesrow_f[0:1, 0:128],
                               nrm[0:1, 56:56 + NHG], start=True, stop=True),
            r=["nrmneg", "onesrowf"], w=["pb2"])
        p.v(lambda e: e.tensor_copy(negM[:, 0:NHG], st.pb[2][0:128, 0:NHG]),
            r=["pb2"], w=["negM"])

        def l0_hotpair(t):
            hA, hB = 2 * t, 2 * t + 1
            j = t
            for qb in range(NQB):
                def sc_pair(ci, pr):
                    hidx, ps, b0 = ((hA, st.psA, 0), (hB, st.psB, 2))[ci]
                    base = (hidx % 2) * 64
                    for half in range(2):
                        kc = 2 * pr + half
                        p.t(lambda e, j=j, base=base, kc=kc, qb=qb, b0=b0: e.matmul(
                            st.pb[b0 + (kc % 2)][:, 0:512],
                            kT[base:base + 64, j * T + kc * 128: j * T + kc * 128 + 128],
                            qT[base:base + 64, j * T + qb * QB: j * T + qb * QB + QB],
                            start=True, stop=True),
                            r=[f"kT{j}", f"qT{j}"], w=[f"pb{b0 + (kc % 2)}"])
                def ex_pair(ci, pr):
                    hidx, ps, b0 = ((hA, st.psA, 0), (hB, st.psB, 2))[ci]
                    cb = ci * 2048 + (pr % 2) * 1024
                    p.s(lambda e, ps=ps, cb=cb, hidx=hidx: e.activation(
                        out=atT[:, cb:cb + 1024], in_=ps[:, 0:1024],
                        func=AF.Exp, scale=0.125, bias=negM[:, hidx:hidx + 1]),
                        r=[f"pb{b0}", f"pb{b0 + 1}", "negM"],
                        w=[f"atTc{ci}_{pr % 2}"])
                def av_pair(ci, pr):
                    hidx = (hA, hB)[ci]
                    avb = (4, 5)[ci]
                    for half in range(2):
                        kc = 2 * pr + half
                        sl = ci * 2048 + (pr % 2) * 1024 + half * 512
                        p.t(lambda e, kc=kc, sl=sl, hidx=hidx, avb=avb,
                            s0=(pr == 0 and half == 0),
                            s1=(pr == 7 and half == 1): e.matmul(
                            st.pb[avb][0:65, 0:512],
                            VH[:, kc * NHG * 65 + hidx * 65: kc * NHG * 65 + hidx * 65 + 65],
                            atT[:, sl:sl + 512],
                            start=s0, stop=s1),
                            r=[f"atTc{ci}_{pr % 2}", f"VH{kc}_{hidx // 2}",
                               f"VHo{kc}"],
                            w=[f"pb{avb}"])
                for pr in range(8):
                    sc_pair(0, pr)
                    sc_pair(1, pr)
                    ex_pair(0, pr)
                    ex_pair(1, pr)
                    if pr > 0:
                        av_pair(0, pr - 1)
                        av_pair(1, pr - 1)
                av_pair(0, 7)
                av_pair(1, 7)
                for ci, (hidx, avb) in enumerate(((hA, 4), (hB, 5))):
                    h = h0 + hidx
                    dc, rrow_off = h // 2, (h % 2) * 64
                    ro2 = ci * 64
                    p.v(lambda e, ro2=ro2, qb=qb, avb=avb: e.tensor_copy(
                        denrow[ro2:ro2 + 1, qb * QB: qb * QB + QB],
                        st.pb[avb][64:65, 0:512]),
                        r=[f"pb{avb}"], w=[f"den{ci}"])
                    if qb == NQB - 1:
                        h2 = h0 + hidx
                        p.dma("dq_y", ddram[h2:h2 + 1, :], denrow[ro2:ro2 + 1, :],
                              r=[f"den{ci}"], w=[f"dd{h2}"])
                    dst = aoT[rrow_off:rrow_off + 64,
                              (qb * ND + dc) * 512:(qb * ND + dc) * 512 + 512]
                    p.v(lambda e, dst=dst, avb=avb: e.tensor_copy(
                        dst, st.pb[avb][0:64, 0:512]),
                        r=[f"pb{avb}"], w=[f"aoT{qb}_{dc}_{rrow_off}"])

        def l0_norm(t):
            for ci in range(2):
                hidx = 2 * t + ci
                h = h0 + hidx
                dc, rrow_off = h // 2, (h % 2) * 64
                for qb in range(NQB):
                    p.dma("dq_a", rb[:, 0:512],
                          ddram[h:h + 1, qb * QB:(qb + 1) * QB]
                          .broadcast_to((128, 512)),
                          r=[f"dd{h}"], w=["rb0"])
                    p.v(lambda e: e.reciprocal(rb[:, 0:512], rb[:, 0:512]),
                        r=["rb0"], w=["rb0"])
                    dst = aoT[rrow_off:rrow_off + 64,
                              (qb * ND + dc) * 512:(qb * ND + dc) * 512 + 512]
                    p.v(lambda e, dst=dst, ro=rrow_off: e.tensor_tensor(
                        dst, dst, rb[ro:ro + 64, 0:512], ALU.mult),
                        r=[f"aoT{qb}_{dc}_{rrow_off}", "rb0"],
                        w=[f"aoT{qb}_{dc}_{rrow_off}"])

        for t in range(NHG // 2):
            l0_hotpair(t)
            if t > 0:
                l0_norm(t - 1)
        l0_norm(NHG // 2 - 1)

    p.dma("dq_g", bo_row, hb["L0_attn_b_o"].rearrange("(o n) -> o n", o=1),
          r=["r_l0b"], w=["r_l0b"])
    for mi in range(ND):
        sl = mi % 2
        p.dma("dq_g", wq[:, sl * 1024:sl * 1024 + 1024]
              .rearrange("p (c m) -> p c m", c=ND),
              hb["L0_attn_W_o"][:, mi * 128:(mi + 1) * 128]
              .rearrange("(c p) m -> p c m", p=128),
              w=[f"wq{sl}"])
        for qb in range(NQB):
            pbm = (mi * NQB + qb) % 2
            for kc in range(ND):
                p.t(lambda e, sl=sl, qb=qb, kc=kc, pbm=pbm: e.matmul(
                    st.pb[pbm][:, 0:512],
                    wq[:, sl * 1024 + kc * 128: sl * 1024 + kc * 128 + 128],
                    aoT[:, (qb * ND + kc) * 512: (qb * ND + kc) * 512 + 512],
                    start=(kc == 0), stop=False),
                    r=[f"wq{sl}", f"aoT{qb}_{kc}_0", f"aoT{qb}_{kc}_64"], w=[f"pb{pbm}"])
            p.t(lambda e, mi=mi, pbm=pbm: e.matmul(
                st.pb[pbm][:, 0:512], bo_row[0:1, mi * 128:mi * 128 + 128],
                st.ones512[0:1, 0:512], start=False, stop=True),
                r=["r_l0b", "ones512"], w=[f"pb{pbm}"])
            emit_back_tile(p, st, f"pb{pbm}", st.pb[pbm][:, 0:512], mi, qb)


def emit_l0_ffn(p, st):
    nc, hb = st.nc, st.hbm
    F = 4096
    emit_ln_stats(p, st)
    p.v(lambda e: e.tensor_scalar(lane(st, 5), lane(st, 6), 1e-5, None, ALU.add),
        r=["st6"], w=["st5"])
    p.v(lambda e: e.reciprocal(lane(st, 5), lane(st, 5)), r=["st5"], w=["st5"])
    p.v(lambda e: e.tensor_tensor(lane(st, 5), lane(st, 6), lane(st, 5), ALU.mult),
        r=["st5", "st6"], w=["st5"])
    p.s(lambda e: e.activation(out=lane(st, 5), in_=lane(st, 5), func=AF.Sqrt,
                               bias=st.eps[:, 0:1], scale=1.0),
        r=["st5", "eps"], w=["st5"])
    p.v(lambda e: e.reciprocal(lane(st, 6), lane(st, 5)), r=["st5"], w=["st6"])
    p.v(lambda e: e.tensor_tensor(lane(st, 5), lane(st, 2), lane(st, 6), ALU.mult),
        r=["st2", "st6"], w=["st5"])
    p.v(lambda e: e.scalar_tensor_tensor(lane(st, 6), lane(st, 4), -1.0, lane(st, 5),
                                         ALU.mult, ALU.mult),
        r=["st4", "st5"], w=["st6"])
    p.v(lambda e: e.tensor_scalar(lane(st, 7), lane(st, 2), 1.0, None, ALU.add),
        r=["st2"], w=["st7"])

    gcol = salloc(st, "l0f_g", [128, 16], F32)
    for i, gk in enumerate(("L0_ffn_ln_gamma", "L0_ffn_ln_beta")):
        p.dma("dq_y", st.zslice[0:1, 0:D], hb[gk].rearrange("(o n) -> o n", o=1),
              w=["zs0", "zs1"])
        for dcg in range(ND):
            pti = dcg % 2
            p.t(lambda e, dcg=dcg, pti=pti: e.transpose(
                st.pb[4 + pti][0:128, 0:1],
                st.zslice[0:1, dcg * 128:dcg * 128 + 128], st.identf[0:1, 0:1]),
                r=["zs0", "zs1", "identf"], w=[f"pb{4 + pti}"])
            p.v(lambda e, i=i, dcg=dcg, pti=pti: e.tensor_copy(
                gcol[:, i * 8 + dcg:i * 8 + dcg + 1], st.pb[4 + pti][0:128, 0:1]),
                r=[f"pb{4 + pti}"], w=["gcol"])

    emit_ln_to_ht(p, st, xupdate=True, rlane=5, blane=6,
                  gamma_col=lambda dc: gcol[:, dc:dc + 1],
                  beta_col=lambda dc: gcol[:, 8 + dc:8 + dc + 1], gb_res=("gcol",))

    b2row, _ = load_row(p, st, "l0fb2", hb["L0_ffn_fc2_bias"], D)
    b1row, _ = load_row(p, st, "l0fb1", hb["L0_ffn_fc1_bias"], F)

    ybuf = salloc(st, "l0f_y", [128, 32 * 512], BF16)
    w1q = salloc(st, "l0f_w1q", [128, 2 * 4 * D], BF16)
    s1b = salloc(st, "l0f_s1b", [128, 4 * D], BF16)
    w2q = salloc(st, "l0f_w2q", [128, 2 * 4 * D], BF16)
    s2b = salloc(st, "l0f_s2b", [128, 4 * D], BF16)
    w1d, w2d = w1q, w2q
    B4 = 4 * D

    for qb in range(NQB):
        for oc in range(32):
            sl = (oc // 4) % 2
            if oc % 4 == 0:
                p.dma("dq_g", w1q[:, sl * B4:sl * B4 + B4]
                      .rearrange("p (c m) -> p c m", c=8),
                      hb["L0_ffn_fc1_q"][:, oc * 128:(oc + 4) * 128]
                      .rearrange("(c p) m -> p c m", p=128),
                      w=[f"w1q{sl}"])
                p.dma("dq_g", s1b[:, 0:B4]
                      .rearrange("p (g m) -> p g m", g=8),
                      hb["L0_ffn_fc1_scales"][:, oc * 128:(oc + 4) * 128]
                      .rearrange("(o g) m -> o g m", o=1).broadcast_to((128, 8, 512)),
                      w=["s1b"])
                p.v(lambda e, sl=sl: e.scalar_tensor_tensor(
                    w1d[:, sl * B4:sl * B4 + B4], w1q[:, sl * B4:sl * B4 + B4], -128.0,
                    s1b[:, 0:B4], ALU.add, ALU.mult),
                    r=[f"w1q{sl}", "s1b"], w=[f"w1q{sl}"])
            ob = oc % 4
            pbm = oc % 2
            for kc in range(ND):
                p.t(lambda e, sl=sl, ob=ob, kc=kc, qb=qb, pbm=pbm: e.matmul(
                    st.pb[pbm][:, 0:512],
                    w1d[:, sl * B4 + kc * 512 + ob * 128:sl * B4 + kc * 512 + ob * 128 + 128],
                    ht_c(st, kc)[:, qb * QB:qb * QB + QB],
                    start=(kc == 0), stop=False),
                    r=[f"w1q{sl}", f"ht{kc}"], w=[f"pb{pbm}"])
            p.t(lambda e, oc=oc, pbm=pbm: e.matmul(
                st.pb[pbm][:, 0:512], b1row[0:1, oc * 128:oc * 128 + 128],
                st.ones512[0:1, 0:512], start=False, stop=True),
                r=["r_l0fb1", "ones512"], w=[f"pb{pbm}"])
            p.s(lambda e, oc=oc, pbm=pbm: e.activation(
                out=ybuf[:, oc * 512:oc * 512 + 512], in_=st.pb[pbm][:, 0:512],
                func=AF.Silu),
                r=[f"pb{pbm}"], w=[f"yb{oc}"])
        pairs = [(tc, dh) for tc in range(4) for dh in range(2)]
        for wv in range(2):
            wave = pairs[wv * 4:(wv + 1) * 4]
            for kc in range(32):
                sl = (kc // 4) % 2
                if kc % 4 == 0:
                    p.dma("dq_g", w2q[:, sl * B4:sl * B4 + B4]
                          .rearrange("p (c m) -> p c m", c=4),
                          hb["L0_ffn_fc2_q"][kc * 128:(kc + 4) * 128, :]
                          .rearrange("(c p) m -> p c m", p=128),
                          w=[f"w2q{sl}"])
                    p.dma("dq_g", s2b[:, 0:B4]
                          .rearrange("p (g m) -> p g m", g=4),
                          hb["L0_ffn_fc2_scales"][kc:kc + 4, :]
                          .rearrange("(o g) m -> o g m", o=1)
                          .broadcast_to((128, 4, 1024)),
                          w=["s2b"])
                    p.v(lambda e, sl=sl: e.scalar_tensor_tensor(
                        w2d[:, sl * B4:sl * B4 + B4], w2q[:, sl * B4:sl * B4 + B4],
                        -128.0, s2b[:, 0:B4], ALU.add, ALU.mult),
                        r=[f"w2q{sl}", "s2b"], w=[f"w2q{sl}"])
                kb = kc % 4
                for i, (tc, dh) in enumerate(wave):
                    p.t(lambda e, kc=kc, sl=sl, kb=kb, tc=tc, dh=dh, i=i: e.matmul(
                        st.pb[i][:, 0:512],
                        ybuf[:, kc * 512 + tc * 128: kc * 512 + tc * 128 + 128],
                        w2d[:, sl * B4 + kb * D + dh * 512: sl * B4 + kb * D + dh * 512 + 512],
                        start=(kc == 0), stop=False),
                        r=[f"w2q{sl}", f"yb{kc}"], w=[f"pb{i}"])
            for i, (tc, dh) in enumerate(wave):
                p.t(lambda e, dh=dh, i=i: e.matmul(
                    st.pb[i][:, 0:512], st.ones512[0:1, 0:128],
                    b2row[0:1, dh * 512:dh * 512 + 512], start=False, stop=True),
                    r=["r_l0fb2", "ones512"], w=[f"pb{i}"])
                tt = qb * 4 + tc
                dst = x_t(st, tt)[:, dh * 512:dh * 512 + 512]
                p.v(lambda e, dst=dst, i=i: e.scalar_tensor_tensor(
                    dst, st.pb[i][:, 0:512], 0.5, dst, ALU.mult, ALU.add),
                    r=[f"pb{i}", f"x{tt}"], w=[f"x{tt}"])


def emit_l1_attn(p, st):
    nc, hb = st.nc, st.hbm
    emit_ln_stats(p, st)
    emit_ln_to_ht(p, st)

    qT = salloc(st, "l1_qT", [128, ND * T], BF16)
    kT = salloc(st, "l1_kT", [128, ND * T], BF16)
    VH = st.ht
    atT = salloc(st, "l1_atT", [128, 16 * 512], BF16)
    wbuf = salloc(st, "l1_wb", [128, 2 * ND * 128], BF16)
    cm = [salloc(st, f"l1_cm{i}", [128, 512], BF16) for i in range(4)]
    for i in range(4):
        p.dma("dq_y", cm[i][:], hb[f"c_cmask{i}"], w=[f"cm{i}"])
    vdram = nc.dram_tensor("l1_vdram", [D, T], BF16, kind="Internal").ap()
    rrow = salloc(st, "l1_rr", [1, 512], F32)
    rcol = salloc(st, "l1_rc", [128, 4], F32)
    nrm = salloc(st, "l1_nrm", [1, 64], F32)
    negM = salloc(st, "l1_negM", [128, 1], F32)

    for wi, (wname, dstT, dk) in enumerate((("L1_attn_Wq", qT, "l1q"),
                                            ("L1_attn_Wk", kT, "l1k"))):
        for mi in range(ND):
            sl = mi % 2
            p.dma("dq_g", wbuf[:, sl * ND * 128:(sl + 1) * ND * 128]
                  .rearrange("p (c m) -> p c m", c=ND),
                  hb[wname][:, mi * 128:(mi + 1) * 128]
                  .rearrange("(c p) m -> p c m", p=128),
                  w=[f"l1wb{sl}"])
            for qb in range(NQB):
                pbm = (mi * NQB + qb) % 2
                for kc in range(ND):
                    p.t(lambda e, sl=sl, kc=kc, qb=qb, pbm=pbm: e.matmul(
                        st.pb[pbm][:, 0:512],
                        wbuf[:, sl * ND * 128 + kc * 128: sl * ND * 128 + kc * 128 + 128],
                        ht_c(st, kc)[:, qb * QB:qb * QB + QB],
                        start=(kc == 0), stop=(kc == ND - 1)),
                        r=[f"l1wb{sl}", f"ht{kc}"], w=[f"pb{pbm}"])
                p.s(lambda e, dstT=dstT, mi=mi, qb=qb, pbm=pbm: e.copy(
                    dstT[:, mi * T + qb * QB: mi * T + qb * QB + QB],
                    st.pb[pbm][:, 0:512]),
                    r=[f"pb{pbm}"], w=[f"{dk}{mi}"])
    for mi in range(ND):
        sl = mi % 2
        p.dma("dq_g", wbuf[:, sl * ND * 128:(sl + 1) * ND * 128]
              .rearrange("p (c m) -> p c m", c=ND),
              hb["L1_attn_Wv"][:, mi * 128:(mi + 1) * 128]
              .rearrange("(c p) m -> p c m", p=128),
              w=[f"l1wb{sl}"])
        for qb in range(NQB):
            pbm = (mi * NQB + qb) % 2
            for kc in range(ND):
                p.t(lambda e, sl=sl, kc=kc, qb=qb, pbm=pbm: e.matmul(
                    st.pb[pbm][:, 0:512],
                    wbuf[:, sl * ND * 128 + kc * 128: sl * ND * 128 + kc * 128 + 128],
                    ht_c(st, kc)[:, qb * QB:qb * QB + QB],
                    start=(kc == 0), stop=(kc == ND - 1)),
                    r=[f"l1wb{sl}", f"ht{kc}"], w=[f"pb{pbm}"])
            vsl = (mi * NQB + qb) % 2
            p.s(lambda e, vsl=vsl, pbm=pbm: e.copy(
                st.zslice[:, vsl * 512:vsl * 512 + 512].bitcast(BF16)[:, 0:512]
                if False else atT[:, vsl * 512:vsl * 512 + 512],
                st.pb[pbm][:, 0:512]),
                r=[f"pb{pbm}"], w=[f"atT{vsl}"])
            p.dma("dq_y", vdram[mi * 128:(mi + 1) * 128, qb * QB:qb * QB + QB],
                  atT[:, vsl * 512:vsl * 512 + 512], r=[f"atT{vsl}"],
                  w=[f"vd{mi}_{qb}"])
    for kc in range(NT):
        p.dma("dq_y", st.ht[:, kc * D:(kc + 1) * D],
              vdram[:, kc * 128:(kc + 1) * 128],
              r=[f"vd{mi}_{qb}" for mi in range(ND) for qb in range(NQB)],
              w=[f"vT{kc}", f"ht{kc // 2}"], transpose=True)

    p.v(lambda e: e.memset(nrm[:, 0:2], 0.0), w=["nrm"])
    for wi, (src, srck) in enumerate(((qT, "l1q"), (kT, "l1k"))):
        for qb in range(NQB):
            for dc in range(ND):
                p.s(lambda e, src=src, dc=dc, qb=qb: e.activation(
                    out=st.sq[:, 0:512],
                    in_=src[:, dc * T + qb * QB: dc * T + qb * QB + QB],
                    func=AF.Square),
                    r=[f"{srck}{dc}"], w=["htok0"])
                p.t(lambda e, dc=dc: e.matmul(
                    st.pb[2][0:1, 0:512], st.onescol[0:128, 0:1],
                    st.sq[:, 0:512], start=(dc == 0), stop=(dc == ND - 1)),
                    r=["htok0", "onescol"], w=["pb2"])
            p.v(lambda e: e.reduce_max(nrm[0:1, 40:41], st.pb[2][0:1, 0:512],
                                       axis=AX.X),
                r=["pb2"], w=["nrms"])
            p.v(lambda e, wi=wi: e.tensor_max(nrm[0:1, wi:wi + 1],
                                              nrm[0:1, wi:wi + 1], nrm[0:1, 40:41]),
                r=["nrms", "nrm"], w=["nrm"])
    p.v(lambda e: e.tensor_tensor(nrm[0:1, 2:3], nrm[0:1, 0:1], nrm[0:1, 1:2],
                                  ALU.mult), r=["nrm"], w=["nrmm"])
    p.s(lambda e: e.activation(out=nrm[0:1, 3:4], in_=nrm[0:1, 2:3], func=AF.Sqrt),
        r=["nrmm"], w=["nrmsq"])
    p.v(lambda e: e.tensor_scalar_mul(nrm[0:1, 4:5], nrm[0:1, 3:4], -1.0 / 32.0),
        r=["nrmsq"], w=["nrmneg"])
    p.t(lambda e: e.matmul(st.pb[2][0:128, 0:1], st.onesrow_f[0:1, 0:128],
                           nrm[0:1, 4:5], start=True, stop=True),
        r=["nrmneg", "onesrowf"], w=["pb2"])
    p.v(lambda e: e.tensor_copy(negM[:, 0:1], st.pb[2][0:128, 0:1]),
        r=["pb2"], w=["negM"])

    for qb in range(NQB):
        nkc = qb * 4 + 4
        for kc in range(nkc):
            pbs = kc % 2
            for dc in range(ND):
                p.t(lambda e, dc=dc, kc=kc, qb=qb, pbs=pbs: e.matmul(
                    st.pb[pbs][:, 0:512],
                    kT[:, dc * T + kc * 128: dc * T + kc * 128 + 128],
                    qT[:, dc * T + qb * QB: dc * T + qb * QB + QB],
                    start=(dc == 0), stop=(dc == ND - 1)),
                    r=[f"l1k{dc}", f"l1q{dc}"], w=[f"pb{pbs}"])
            p.s(lambda e, kc=kc, pbs=pbs: e.activation(
                out=atT[:, kc * 512:kc * 512 + 512], in_=st.pb[pbs][:, 0:512],
                func=AF.Exp, scale=1.0 / 32.0, bias=negM[:, 0:1]),
                r=[f"pb{pbs}", "negM"], w=[f"atT{kc}"])
            if kc >= qb * 4:
                off = kc - qb * 4
                p.v(lambda e, kc=kc, off=off: e.tensor_tensor(
                    atT[:, kc * 512:kc * 512 + 512], atT[:, kc * 512:kc * 512 + 512],
                    cm[off][:], ALU.mult),
                    r=[f"atT{kc}", f"cm{off}"], w=[f"atT{kc}"])
        for kc in range(nkc):
            p.t(lambda e, kc=kc, s0=(kc == 0), s1=(kc == nkc - 1): e.matmul(
                st.pb[2][0:1, 0:512], st.onescol[0:128, 0:1],
                atT[:, kc * 512:kc * 512 + 512], start=s0, stop=s1),
                r=[f"atT{kc}", "onescol"], w=["pb2"])
        p.v(lambda e: e.reciprocal(rrow[0:1, 0:512], st.pb[2][0:1, 0:512]),
            r=["pb2"], w=["rrow"])
        for j in range(4):
            p.t(lambda e, j=j: e.transpose(
                st.pb[3][0:128, 0:1], rrow[0:1, j * 128:j * 128 + 128],
                st.identf[0:1, 0:1]),
                r=["rrow", "identf"], w=["pb3"])
            p.v(lambda e, j=j: e.tensor_copy(rcol[:, j:j + 1], st.pb[3][0:128, 0:1]),
                r=["pb3"], w=["rcol"])
        for mi in range(ND):
            pbm = mi % 2
            for kc in range(nkc):
                p.t(lambda e, kc=kc, mi=mi, s0=(kc == 0), s1=(kc == nkc - 1),
                    pbm=pbm: e.matmul(
                    st.pb[pbm][:, 0:512],
                    st.ht[:, kc * D + mi * 128: kc * D + mi * 128 + 128],
                    atT[:, kc * 512:kc * 512 + 512], start=s0, stop=s1),
                    r=[f"vT{kc}", f"atT{kc}"], w=[f"pb{pbm}"])
            emit_back_tile(p, st, f"pb{pbm}", st.pb[pbm][:, 0:512], mi, qb,
                           ptok=lambda tc, qb=qb: rcol[:, tc - qb * 4:tc - qb * 4 + 1],
                           ptok_res=("rcol",))


def emit_l1_ffn(p, st):
    nc, hb = st.nc, st.hbm
    E, HID = 8, 512
    emit_ln_stats(p, st)
    emit_ln_to_ht(p, st)

    rwn = None
    rwT = salloc(st, "l1f_rwT", [128, ND * 8], BF16)
    hs = salloc(st, "l1f_hs", [128, 4 * 512], BF16)
    rwn = hs[0:8, 0:D]
    p.dma("dq_g", rwn, hb["L1_ffn_routing_W"], w=["rwn", "hs0", "hs1"])
    for dc in range(ND):
        pti = dc % 2
        p.t(lambda e, dc=dc, pti=pti: e.transpose(
            st.pt[pti][0:128, 0:8], rwn[:, dc * 128:(dc + 1) * 128], st.ident[0:8, 0:8]),
            r=["rwn", "hs0", "hs1", "ident"], w=[f"pt{pti}"])
        p.v(lambda e, dc=dc, pti=pti: e.tensor_copy(
            rwT[:, dc * 8:(dc + 1) * 8], st.pt[pti][0:128, 0:8]),
            r=[f"pt{pti}"], w=["rwT"])
    ebrow, _ = load_row(p, st, "l1feb", hb["L1_ffn_expert_bias"], E)
    logT = salloc(st, "l1f_logT", [8, T], BF16)
    actT = salloc(st, "l1f_actT", [8, T], BF16)
    acts = salloc(st, "l1f_acts", [128, NT * 8], F32)
    cft = salloc(st, "l1f_cft", [128, NT * 8], BF16)
    cfT = actT
    ones8 = salloc(st, "l1f_o8", [8, 1], F32)
    p.v(lambda e: e.memset(ones8[:], 1.0), w=["ones8"])
    sel8 = salloc(st, "l1f_sel8", [8, 8 * 128], BF16)
    p.dma("dq_y", sel8[:], hb["c_sel8"], w=["sel8"])

    for qb in range(NQB):
        pbm = qb % 2
        for dc in range(ND):
            p.t(lambda e, dc=dc, qb=qb, pbm=pbm: e.matmul(
                st.pb[pbm][0:8, 0:512], rwT[:, dc * 8:(dc + 1) * 8],
                ht_c(st, dc)[:, qb * QB:qb * QB + QB],
                start=(dc == 0), stop=False),
                r=["rwT", f"ht{dc}"], w=[f"pb{pbm}"])
        p.t(lambda e, qb=qb, pbm=pbm: e.matmul(
            st.pb[pbm][0:8, 0:512], ebrow[0:1, 0:8], st.ones512[0:1, 0:512],
            start=False, stop=True),
            r=["r_l1feb", "ones512"], w=[f"pb{pbm}"])
        p.s(lambda e, qb=qb, pbm=pbm: e.copy(
            logT[:, qb * QB:qb * QB + QB], st.pb[pbm][0:8, 0:512]),
            r=[f"pb{pbm}"], w=["logT"])
    p.s(lambda e: e.activation(out=actT[:], in_=logT[:], func=AF.Abs),
        r=["logT"], w=["actT"])
    p.s(lambda e: e.activation(out=actT[:], in_=actT[:], func=AF.Exp, scale=-1.0),
        r=["actT"], w=["actT"])
    p.s(lambda e: e.activation(out=actT[:], in_=actT[:], func=AF.Ln,
                               bias=ones8[:, 0:1], scale=1.0),
        r=["actT", "ones8"], w=["actT"])
    p.v(lambda e: e.tensor_scalar_max(logT[:], logT[:], 0.0),
        r=["logT"], w=["logT"])
    p.v(lambda e: e.tensor_add(actT[:], actT[:], logT[:]),
        r=["actT", "logT"], w=["actT"])
    p.s(lambda e: e.activation(out=actT[:], in_=actT[:], func=AF.Sqrt),
        r=["actT"], w=["actT"])
    for tc in range(NT):
        pti = tc % 2
        p.t(lambda e, tc=tc, pti=pti: e.transpose(
            st.pt[pti][0:128, 0:8], actT[:, tc * 128:(tc + 1) * 128],
            st.ident[0:8, 0:8]),
            r=["actT", "ident"], w=[f"pt{pti}"])
        p.v(lambda e, tc=tc, pti=pti: e.tensor_copy(
            acts[:, tc * 8:(tc + 1) * 8], st.pt[pti][0:128, 0:8]),
            r=[f"pt{pti}"], w=["acts"])
    for tc in range(NT):
        a = acts[:, tc * 8:(tc + 1) * 8]
        c = cft[:, tc * 8:(tc + 1) * 8]
        m1 = lane(st, 5, tc)
        m2 = lane(st, 6, tc)
        p.v(lambda e, a=a, m1=m1: e.reduce_max(m1, a, axis=AX.X),
            r=["acts"], w=["st5"])
        p.v(lambda e, a=a, c=c, m1=m1: e.tensor_scalar(c, a, m1, None, ALU.is_ge),
            r=["acts", "st5"], w=["cft"])
        p.v(lambda e, a=a, c=c: e.scalar_tensor_tensor(c, c, -30000.0, a,
                                                       ALU.mult, ALU.add),
            r=["cft", "acts"], w=["cft"])
        p.v(lambda e, c=c, m2=m2: e.reduce_max(m2, c, axis=AX.X),
            r=["cft"], w=["st6"])
        p.v(lambda e, a=a, c=c, m2=m2: e.tensor_scalar(c, a, m2, None, ALU.is_ge),
            r=["acts", "st6"], w=["cft"])
        p.v(lambda e, a=a, c=c: e.tensor_tensor(c, c, a, ALU.mult),
            r=["cft", "acts"], w=["cft"])
    for tc in range(NT):
        pti = tc % 2
        p.t(lambda e, tc=tc, pti=pti: e.transpose(
            st.pt[pti][0:8, 0:128], cft[:, tc * 8:(tc + 1) * 8], st.ident[:]),
            r=["cft", "ident"], w=[f"pt{pti}"])
        p.v(lambda e, tc=tc, pti=pti: e.tensor_copy(
            cfT[:, tc * 128:(tc + 1) * 128], st.pt[pti][0:8, 0:128]),
            r=[f"pt{pti}"], w=["cfT", "actT"])

    sw1 = salloc(st, "l1f_sw1", [128, ND * HID], BF16)
    sw2 = salloc(st, "l1f_sw2", [128, 4 * D], BF16)
    p.dma("dq_g", sw1[:].rearrange("p (c m) -> p c m", c=ND),
          hb["L1_ffn_sW1"].rearrange("(c p) m -> p c m", p=128), w=["sw1"])
    p.dma("dq_g", sw2[:].rearrange("p (c m) -> p c m", c=4),
          hb["L1_ffn_sW2"].rearrange("(c p) m -> p c m", p=128), w=["sw2"])
    sb1row, _ = load_row(p, st, "l1fsb1", hb["L1_ffn_sb1"], HID)
    sb2row, _ = load_row(p, st, "l1fsb2", hb["L1_ffn_sb2"], D)
    rb1row, _ = load_row(p, st, "l1frb1", hb["L1_ffn_rb1"].rearrange("e h -> (e h)"),
                         E * HID)
    rb2n = salloc(st, "l1f_rb2", [8, D], BF16)
    p.dma("dq_g", rb2n[:], hb["L1_ffn_rb2"], w=["rb2n"])
    rw1 = salloc(st, "l1f_rw1", [128, 2 * ND * HID], BF16)
    rw2 = salloc(st, "l1f_rw2", [128, 8 * 512], BF16)

    hdn = salloc(st, "l1f_hdn", [128, E * 4 * 512], BF16)
    cfb = salloc(st, "l1f_cfb", [128, 512], BF16)

    for qb in range(NQB):
        for hc in range(4):
            pbm = hc % 2
            for kc in range(ND):
                p.t(lambda e, hc=hc, kc=kc, qb=qb, pbm=pbm: e.matmul(
                    st.pb[pbm][:, 0:512], sw1[:, kc * HID + hc * 128: kc * HID + hc * 128 + 128],
                    ht_c(st, kc)[:, qb * QB:qb * QB + QB],
                    start=(kc == 0), stop=False),
                    r=["sw1", f"ht{kc}"], w=[f"pb{pbm}"])
            p.t(lambda e, hc=hc, pbm=pbm: e.matmul(
                st.pb[pbm][:, 0:512], sb1row[0:1, hc * 128:hc * 128 + 128],
                st.ones512[0:1, 0:512], start=False, stop=True),
                r=["r_l1fsb1", "ones512"], w=[f"pb{pbm}"])
            p.s(lambda e, hc=hc, pbm=pbm: e.activation(
                out=hs[:, hc * 512:hc * 512 + 512], in_=st.pb[pbm][:, 0:512],
                func=AF.Silu),
                r=[f"pb{pbm}"], w=[f"hs{hc}"])
        for ei in range(E):
            sl = ei % 2
            p.dma("dq_g", rw1[:, sl * ND * HID:(sl + 1) * ND * HID]
                  .rearrange("p (c m) -> p c m", c=ND),
                  hb["L1_ffn_rW1"][ei].rearrange("(c p) m -> p c m", p=128),
                  w=[f"rw1{sl}"])
            p.t(lambda e, ei=ei, qb=qb: e.matmul(
                st.pb[4][0:128, 0:512], sel8[:, ei * 128:(ei + 1) * 128],
                cfT[0:8, qb * QB:qb * QB + QB],
                start=True, stop=True),
                r=["cfT", "sel8"], w=["pb4"])
            p.v(lambda e: e.tensor_copy(cfb[:], st.pb[4][0:128, 0:512]),
                r=["pb4"], w=["cfb"])
            for hc in range(4):
                pbm = 2 + hc % 2
                for kc in range(ND):
                    p.t(lambda e, sl=sl, hc=hc, kc=kc, qb=qb, pbm=pbm: e.matmul(
                        st.pb[pbm][:, 0:512],
                        rw1[:, sl * ND * HID + kc * HID + hc * 128:
                            sl * ND * HID + kc * HID + hc * 128 + 128],
                        ht_c(st, kc)[:, qb * QB:qb * QB + QB],
                        start=(kc == 0), stop=False),
                        r=[f"rw1{sl}", f"ht{kc}"], w=[f"pb{pbm}"])
                p.t(lambda e, ei=ei, hc=hc, pbm=pbm: e.matmul(
                    st.pb[pbm][:, 0:512],
                    rb1row[0:1, ei * HID + hc * 128: ei * HID + hc * 128 + 128],
                    st.ones512[0:1, 0:512], start=False, stop=True),
                    r=["r_l1frb1", "ones512"], w=[f"pb{pbm}"])
                p.s(lambda e, ei=ei, hc=hc, pbm=pbm: e.activation(
                    out=hdn[:, (ei * 4 + hc) * 512:(ei * 4 + hc) * 512 + 512],
                    in_=st.pb[pbm][:, 0:512], func=AF.Silu),
                    r=[f"pb{pbm}"], w=[f"hdn{ei}_{hc}"])
                p.v(lambda e, ei=ei, hc=hc: e.tensor_tensor(
                    hdn[:, (ei * 4 + hc) * 512:(ei * 4 + hc) * 512 + 512],
                    hdn[:, (ei * 4 + hc) * 512:(ei * 4 + hc) * 512 + 512],
                    cfb[:], ALU.mult),
                    r=[f"hdn{ei}_{hc}", "cfb"], w=[f"hdn{ei}_{hc}"])
        for mi in range(ND):
            pbm = mi % 2
            for kc in range(4):
                p.t(lambda e, kc=kc, mi=mi, qb=qb, pbm=pbm: e.matmul(
                    st.pb[pbm][:, 0:512], sw2[:, kc * D + mi * 128: kc * D + mi * 128 + 128],
                    hs[:, kc * 512:kc * 512 + 512],
                    start=(kc == 0), stop=False),
                    r=["sw2", f"hs{kc}"], w=[f"pb{pbm}"])
            p.t(lambda e, mi=mi, pbm=pbm: e.matmul(
                st.pb[pbm][:, 0:512], sb2row[0:1, mi * 128:mi * 128 + 128],
                st.ones512[0:1, 0:512], start=False, stop=False),
                r=["r_l1fsb2", "ones512"], w=[f"pb{pbm}"])
            p.t(lambda e, mi=mi, qb=qb, pbm=pbm: e.matmul(
                st.pb[pbm][:, 0:512], rb2n[:, mi * 128:mi * 128 + 128],
                cfT[:, qb * QB:qb * QB + QB], start=False, stop=False),
                r=["rb2n", "cfT"], w=[f"pb{pbm}"])
            sl2 = 0
            p.dma("dq_g", rw2[:, sl2 * E * 512:(sl2 + 1) * E * 512]
                  .rearrange("p (ee c m) -> p ee c m", ee=E, c=4),
                  hb["L1_ffn_rW2"][:, :, mi * 128:(mi + 1) * 128]
                  .rearrange("ee (c p) m -> p ee c m", p=128),
                  w=[f"rw2{sl2}"])
            for ei in range(E):
                for kc in range(4):
                    p.t(lambda e, sl2=sl2, kc=kc, ei=ei, s1=(ei == E - 1 and kc == 3),
                        pbm=pbm: e.matmul(
                        st.pb[pbm][:, 0:512],
                        rw2[:, sl2 * E * 512 + (ei * 4 + kc) * 128:
                            sl2 * E * 512 + (ei * 4 + kc) * 128 + 128],
                        hdn[:, (ei * 4 + kc) * 512:(ei * 4 + kc) * 512 + 512],
                        start=False, stop=s1),
                        r=[f"rw2{sl2}", f"hdn{ei}_{kc}"], w=[f"pb{pbm}"])
            emit_back_tile(p, st, f"pb{pbm}", st.pb[pbm][:, 0:512], mi, qb)


def emit_l2_attn(p, st):
    nc, hb = st.nc, st.hbm
    emit_ln_stats(p, st)
    emit_ln_to_ht(p, st)

    qT = salloc(st, "l2_qT", [128, ND * T], BF16)
    kTd = salloc(st, "l2_kTd", [128, 4 * T], BF16)
    VH = st.ht[:, 0:NT * 4 * 65]
    atT = st.ht[:, 8192:8192 + 4 * 512]
    aoT = salloc(st, "l2_aoT", [128, NQB * ND * 512], BF16)
    wbuf = salloc(st, "l2_wb", [128, 2 * ND * 128], BF16)
    ropc = salloc(st, "l2_ropc", [128, 512], F32)
    rops = salloc(st, "l2_rops", [128, 512], F32)
    rsc = salloc(st, "l2_rsc", [128, 2 * 512], BF16)
    rb2 = salloc(st, "l2_rb2", [128, 512], F32)
    nrm = salloc(st, "l2_nrm", [1, 64], F32)
    negM = salloc(st, "l2_negM", [128, 16], F32)
    SCALE = 1.0 / 32.0

    def vh_keys(kc):
        return [f"vT{kc}", f"ht{(kc * 4 * 65) // T}", f"ht{((kc + 1) * 4 * 65 - 1) // T}"]

    for mi in range(ND):
        sl = mi % 2
        p.dma("dq_g", wbuf[:, sl * ND * 128:(sl + 1) * ND * 128]
              .rearrange("p (c m) -> p c m", c=ND),
              hb["L2_attn_Wq"][:, mi * 128:(mi + 1) * 128]
              .rearrange("(c p) m -> p c m", p=128),
              w=[f"l2wb{sl}"])
        for qb in range(NQB):
            pbm = (mi * NQB + qb) % 2
            for kc in range(ND):
                p.t(lambda e, sl=sl, kc=kc, qb=qb, pbm=pbm: e.matmul(
                    st.pb[pbm][:, 0:512],
                    wbuf[:, sl * ND * 128 + kc * 128: sl * ND * 128 + kc * 128 + 128],
                    ht_c(st, kc)[:, qb * QB:qb * QB + QB],
                    start=(kc == 0), stop=(kc == ND - 1)),
                    r=[f"l2wb{sl}", f"ht{kc}"], w=[f"pb{pbm}"])
            p.s(lambda e, mi=mi, qb=qb, pbm=pbm: e.copy(
                qT[:, mi * T + qb * QB: mi * T + qb * QB + QB], st.pb[pbm][:, 0:512]),
                r=[f"pb{pbm}"], w=[f"l2q{mi}"])
    for kv in range(4):
        sl = kv % 2
        for half in range(2):
            p.dma("dq_g", wbuf[:, sl * ND * 128:(sl + 1) * ND * 128]
                  .rearrange("p (c m) -> p c m", c=ND)[:, :, half * 64:half * 64 + 64],
                  hb["L2_attn_Wk"][:, kv * 64:(kv + 1) * 64]
                  .rearrange("(c p) m -> p c m", p=128),
                  w=[f"l2wb{sl}"])
        for qb in range(NQB):
            pbm = (kv * NQB + qb) % 2
            for kc in range(ND):
                p.t(lambda e, sl=sl, kc=kc, qb=qb, pbm=pbm: e.matmul(
                    st.pb[pbm][:, 0:512],
                    wbuf[:, sl * ND * 128 + kc * 128: sl * ND * 128 + kc * 128 + 128],
                    ht_c(st, kc)[:, qb * QB:qb * QB + QB],
                    start=(kc == 0), stop=(kc == ND - 1)),
                    r=[f"l2wb{sl}", f"ht{kc}"], w=[f"pb{pbm}"])
            p.s(lambda e, kv=kv, qb=qb, pbm=pbm: e.copy(
                kTd[:, kv * T + qb * QB: kv * T + qb * QB + QB], st.pb[pbm][:, 0:512]),
                r=[f"pb{pbm}"], w=[f"l2k{kv}"])
    for vi in range(2):
        sl = vi % 2
        p.dma("dq_g", wbuf[:, sl * ND * 128:(sl + 1) * ND * 128]
              .rearrange("p (c m) -> p c m", c=ND),
              hb["L2_attn_Wv"][:, vi * 128:(vi + 1) * 128]
              .rearrange("(c p) m -> p c m", p=128),
              w=[f"l2wb{sl}"])
        for qb in range(NQB):
            pbm = (vi * NQB + qb) % 2
            for kc in range(ND):
                p.t(lambda e, sl=sl, kc=kc, qb=qb, pbm=pbm: e.matmul(
                    st.pb[pbm][:, 0:512],
                    wbuf[:, sl * ND * 128 + kc * 128: sl * ND * 128 + kc * 128 + 128],
                    ht_c(st, kc)[:, qb * QB:qb * QB + QB],
                    start=(kc == 0), stop=(kc == ND - 1)),
                    r=[f"l2wb{sl}", f"ht{kc}"], w=[f"pb{pbm}"])
            vsl = (vi * NQB + qb) % 4
            p.s(lambda e, vsl=vsl, pbm=pbm: e.copy(
                atT[:, vsl * 512:vsl * 512 + 512], st.pb[pbm][:, 0:512]),
                r=[f"pb{pbm}"], w=[f"atT{vsl}", "ht4"])
            for j in range(4):
                kc = qb * 4 + j
                pti = j % 2
                p.t(lambda e, vsl=vsl, j=j, pti=pti: e.transpose(
                    st.pt[pti][:, 0:128],
                    atT[:, vsl * 512 + j * 128: vsl * 512 + j * 128 + 128],
                    st.ident[:]),
                    r=[f"atT{vsl}", "ht4", "ident"], w=[f"pt{pti}"])
                base = kc * 4 * 65 + (2 * vi) * 65
                dstv = st.ht[:, base:base + 130].rearrange("p (h m) -> p h m", h=2)[:, :, 0:64]
                p.v(lambda e, dstv=dstv, pti=pti: e.tensor_copy(
                    dstv, st.pt[pti][:, 0:128].rearrange("p (h m) -> p h m", h=2)),
                    r=[f"pt{pti}"], w=vh_keys(kc))
    for kc in range(NT):
        on = st.ht[:, kc * 4 * 65:(kc + 1) * 4 * 65].rearrange(
            "p (h m) -> p h m", h=4)[:, :, 64:65]
        p.v(lambda e, on=on: e.memset(on, 1.0), w=vh_keys(kc))

    swapmask = [pp ^ 1 for pp in range(32)]
    for dc in range(ND):
        for qb in range(NQB):
            sl = (dc * NQB + qb) % 2
            p.dma("dq_y", ropc[:, 0:512],
                  hb["c_ropeq_c"][dc * 128:(dc + 1) * 128, qb * QB:qb * QB + QB],
                  w=["ropc0"])
            p.dma("dq_y", rops[:, 0:512],
                  hb["c_ropeq_s"][dc * 128:(dc + 1) * 128, qb * QB:qb * QB + QB],
                  w=["rops0"])
            qsl = qT[:, dc * T + qb * QB: dc * T + qb * QB + QB]
            p.v(lambda e, qsl=qsl, sl=sl: e.stream_shuffle(
                rsc[:, sl * 512:sl * 512 + 512], qsl, swapmask),
                r=[f"l2q{dc}"], w=[f"rsc{sl}"])
            p.v(lambda e, sl=sl: e.tensor_tensor(
                rsc[:, sl * 512:sl * 512 + 512], rsc[:, sl * 512:sl * 512 + 512],
                rops[:, 0:512], ALU.mult),
                r=[f"rsc{sl}", "rops0"], w=[f"rsc{sl}"])
            p.v(lambda e, qsl=qsl, sl=sl: e.scalar_tensor_tensor(
                rsc[:, sl * 512:sl * 512 + 512], qsl, 1.0,
                rsc[:, sl * 512:sl * 512 + 512], ALU.bypass, ALU.bypass),
                r=[f"l2q{dc}", f"rsc{sl}"], w=[f"rsc{sl}"]) if False else None
            p.v(lambda e, qsl=qsl: e.tensor_tensor(
                qsl, qsl, ropc[:, 0:512], ALU.mult),
                r=[f"l2q{dc}", "ropc0"], w=[f"l2q{dc}"])
            p.v(lambda e, qsl=qsl, sl=sl: e.tensor_tensor(
                qsl, qsl, rsc[:, sl * 512:sl * 512 + 512], ALU.add),
                r=[f"l2q{dc}", f"rsc{sl}"], w=[f"l2q{dc}"])
    for kv in range(4):
        for qb in range(NQB):
            sl = (kv * NQB + qb) % 2
            p.dma("dq_y", ropc[:, 0:512],
                  hb["c_ropek_c"][:, qb * QB:qb * QB + QB], w=["ropc0"])
            p.dma("dq_y", rops[:, 0:512],
                  hb["c_ropek_s"][:, qb * QB:qb * QB + QB], w=["rops0"])
            ksl = kTd[:, kv * T + qb * QB: kv * T + qb * QB + QB]
            p.v(lambda e, ksl=ksl, sl=sl: e.stream_shuffle(
                rsc[:, sl * 512:sl * 512 + 512], ksl, swapmask),
                r=[f"l2k{kv}"], w=[f"rsc{sl}"])
            p.v(lambda e, sl=sl: e.tensor_tensor(
                rsc[:, sl * 512:sl * 512 + 512], rsc[:, sl * 512:sl * 512 + 512],
                rops[:, 0:512], ALU.mult),
                r=[f"rsc{sl}", "rops0"], w=[f"rsc{sl}"])
            p.v(lambda e, ksl=ksl: e.tensor_tensor(
                ksl, ksl, ropc[:, 0:512], ALU.mult),
                r=[f"l2k{kv}", "ropc0"], w=[f"l2k{kv}"])
            p.v(lambda e, ksl=ksl, sl=sl: e.tensor_tensor(
                ksl, ksl, rsc[:, sl * 512:sl * 512 + 512], ALU.add),
                r=[f"l2k{kv}", f"rsc{sl}"], w=[f"l2k{kv}"])

    p.v(lambda e: e.memset(nrm[:, 0:32], 0.0), w=["nrm"])
    for wi, (src, srck, nh, cs) in enumerate(((qT, "l2q", 16, 0), (kTd, "l2k", 4, 16))):
        for hh in range(nh):
            dc = hh // 2 if wi == 0 else hh
            base = (hh % 2) * 64 if wi == 0 else 0
            for qb in range(NQB):
                p.s(lambda e, src=src, dc=dc, qb=qb: e.activation(
                    out=st.sq[:, 0:512],
                    in_=src[:, dc * T + qb * QB: dc * T + qb * QB + QB],
                    func=AF.Square),
                    r=[f"{srck}{dc}"], w=["htok0"])
                p.t(lambda e, base=base: e.matmul(
                    st.pb[2][0:1, 0:512], st.onescol[base:base + 64, 0:1],
                    st.sq[base:base + 64, 0:512], start=True, stop=True),
                    r=["htok0", "onescol"], w=["pb2"])
                p.v(lambda e: e.reduce_max(nrm[0:1, 40:41], st.pb[2][0:1, 0:512],
                                           axis=AX.X),
                    r=["pb2"], w=["nrms"])
                idx = cs + hh
                p.v(lambda e, idx=idx: e.tensor_max(
                    nrm[0:1, idx:idx + 1], nrm[0:1, idx:idx + 1], nrm[0:1, 40:41]),
                    r=["nrms", "nrm"], w=["nrm"])
    for h in range(16):
        p.v(lambda e, h=h: e.tensor_tensor(
            nrm[0:1, 32 + h:33 + h], nrm[0:1, h:h + 1], nrm[0:1, 16 + h // 4:17 + h // 4],
            ALU.mult), r=["nrm"], w=["nrmm"])
    p.s(lambda e: e.activation(out=nrm[0:1, 32:48], in_=nrm[0:1, 32:48], func=AF.Sqrt),
        r=["nrmm"], w=["nrmm"])
    p.v(lambda e: e.tensor_scalar_mul(nrm[0:1, 32:48], nrm[0:1, 32:48], -SCALE),
        r=["nrmm"], w=["nrmm"])
    p.t(lambda e: e.matmul(st.pb[2][0:128, 0:16], st.onesrow_f[0:1, 0:128],
                           nrm[0:1, 32:48], start=True, stop=True),
        r=["nrmm", "onesrowf"], w=["pb2"])
    p.v(lambda e: e.tensor_copy(negM[:, 0:16], st.pb[2][0:128, 0:16]),
        r=["pb2"], w=["negM"])

    ddram = nc.dram_tensor("l2_ddram", [16, T], F32, kind="Internal").ap()
    denrow = salloc(st, "l2_dr", [65, T], F32)
    for t in range(8):
        hA, hB = 2 * t, 2 * t + 1
        for qb in range(NQB):
            def sc_pair(ci, pr):
                hh = (hA, hB)[ci]
                b0 = 2 * ci
                base = (hh % 2) * 64
                dcq = hh // 2
                kv = hh // 4
                for half in range(2):
                    kc = 2 * pr + half
                    p.t(lambda e, kv=kv, base=base, dcq=dcq, kc=kc, qb=qb,
                        b0=b0: e.matmul(
                        st.pb[b0 + (kc % 2)][:, 0:512],
                        kTd[base:base + 64, kv * T + kc * 128: kv * T + kc * 128 + 128],
                        qT[base:base + 64, dcq * T + qb * QB: dcq * T + qb * QB + QB],
                        start=True, stop=True),
                        r=[f"l2k{kv}", f"l2q{dcq}"], w=[f"pb{b0 + (kc % 2)}"])
            def ex_pair(ci, pr):
                hh = (hA, hB)[ci]
                ps = (st.psA, st.psB)[ci]
                b0 = 2 * ci
                cb = 8192 + ci * 2048 + (pr % 2) * 1024
                p.s(lambda e, ps=ps, cb=cb, hh=hh: e.activation(
                    out=st.ht[:, cb:cb + 1024], in_=ps[:, 0:1024],
                    func=AF.Exp, scale=SCALE, bias=negM[:, hh:hh + 1]),
                    r=[f"pb{b0}", f"pb{b0 + 1}", "negM"],
                    w=[f"atTc{ci}_{pr % 2}", "ht4", "ht5"])
            def av_pair(ci, pr):
                hh = (hA, hB)[ci]
                avb = (4, 5)[ci]
                kv = hh // 4
                for half in range(2):
                    kc = 2 * pr + half
                    sl = 8192 + ci * 2048 + (pr % 2) * 1024 + half * 512
                    p.t(lambda e, kc=kc, sl=sl, kv=kv, avb=avb,
                        s0=(pr == 0 and half == 0),
                        s1=(pr == 7 and half == 1): e.matmul(
                        st.pb[avb][0:65, 0:512],
                        st.ht[:, kc * 4 * 65 + kv * 65: kc * 4 * 65 + kv * 65 + 65],
                        st.ht[:, sl:sl + 512],
                        start=s0, stop=s1),
                        r=[f"atTc{ci}_{pr % 2}"] + vh_keys(kc), w=[f"pb{avb}"])
            for pr in range(8):
                sc_pair(0, pr)
                sc_pair(1, pr)
                ex_pair(0, pr)
                ex_pair(1, pr)
                if pr > 0:
                    av_pair(0, pr - 1)
                    av_pair(1, pr - 1)
            av_pair(0, 7)
            av_pair(1, 7)
            for ci in range(2):
                hh = (hA, hB)[ci]
                avb = (4, 5)[ci]
                dcq, rro = hh // 2, (hh % 2) * 64
                ro2 = ci * 64
                p.v(lambda e, ro2=ro2, qb=qb, avb=avb: e.tensor_copy(
                    denrow[ro2:ro2 + 1, qb * QB: qb * QB + QB],
                    st.pb[avb][64:65, 0:512]),
                    r=[f"pb{avb}"], w=[f"den{ci}"])
                if qb == NQB - 1:
                    p.dma("dq_y", ddram[hh:hh + 1, :], denrow[ro2:ro2 + 1, :],
                          r=[f"den{ci}"], w=[f"dd{hh}"])
                dst = aoT[rro:rro + 64,
                          (qb * ND + dcq) * 512:(qb * ND + dcq) * 512 + 512]
                p.v(lambda e, dst=dst, avb=avb: e.tensor_copy(
                    dst, st.pb[avb][0:64, 0:512]),
                    r=[f"pb{avb}"], w=[f"aoT{qb}_{dcq}_{rro}"])
        def l2_norm(tt):
            for ci in range(2):
                hh = 2 * tt + ci
                dcq, rro = hh // 2, (hh % 2) * 64
                for qb in range(NQB):
                    p.dma("dq_a", rb[0:128, 0:512] if False else rb2[:, 0:512],
                          ddram[hh:hh + 1, qb * QB:(qb + 1) * QB]
                          .broadcast_to((128, 512)),
                          r=[f"dd{hh}"], w=["rb2s"])
                    p.v(lambda e: e.reciprocal(rb2[:, 0:512], rb2[:, 0:512]),
                        r=["rb2s"], w=["rb2s"])
                    dst = aoT[rro:rro + 64,
                              (qb * ND + dcq) * 512:(qb * ND + dcq) * 512 + 512]
                    p.v(lambda e, dst=dst, rro=rro: e.tensor_tensor(
                        dst, dst, rb2[rro:rro + 64, 0:512], ALU.mult),
                        r=[f"aoT{qb}_{dcq}_{rro}", "rb2s"],
                        w=[f"aoT{qb}_{dcq}_{rro}"])
        if t > 0:
            l2_norm(t - 1)
    l2_norm(7)

    for mi in range(ND):
        sl = mi % 2
        p.dma("dq_g", wbuf[:, sl * ND * 128:(sl + 1) * ND * 128]
              .rearrange("p (c m) -> p c m", c=ND),
              hb["L2_attn_Wo"][:, mi * 128:(mi + 1) * 128]
              .rearrange("(c p) m -> p c m", p=128),
              w=[f"l2wb{sl}"])
        for qb in range(NQB):
            pbm = (mi * NQB + qb) % 2
            for kc in range(ND):
                p.t(lambda e, sl=sl, mi=mi, qb=qb, kc=kc, pbm=pbm: e.matmul(
                    st.pb[pbm][:, 0:512],
                    wbuf[:, sl * ND * 128 + kc * 128: sl * ND * 128 + kc * 128 + 128],
                    aoT[:, (qb * ND + kc) * 512: (qb * ND + kc) * 512 + 512],
                    start=(kc == 0), stop=(kc == ND - 1)),
                    r=[f"l2wb{sl}", f"aoT{qb}_{kc}_0", f"aoT{qb}_{kc}_64"], w=[f"pb{pbm}"])
            emit_back_tile(p, st, f"pb{pbm}", st.pb[pbm][:, 0:512], mi, qb)


def emit_l2_ffn(p, st):
    nc, hb = st.nc, st.hbm
    HID, NH = 3072, 24
    emit_ln_stats(p, st)
    emit_ln_to_ht(p, st)

    wgu = salloc(st, "l2f_wgu", [128, 2 * D], BF16)
    wT = salloc(st, "l2f_wT", [128, 2 * ND * 128], BF16)
    hbuf = salloc(st, "l2f_h", [128, NH * 512], BF16)
    ubuf = salloc(st, "l2f_u", [128, 512], BF16)
    wdT = salloc(st, "l2f_wdT", [128, NH * D], BF16)
    wgut = salloc(st, "l2f_wgut", [128, 2 * 2048], BF16)

    gut = nc.dram_tensor("l2f_gut", [2 * NH * ND * 128, 128], BF16,
                         kind="Internal").ap()
    for which, wname in enumerate(("L2_ffn_w_gate", "L2_ffn_w_up")):
        for hc in range(NH):
            sl = hc % 2
            p.dma("dq_g", wgu[:, sl * D:sl * D + D],
                  hb[wname][hc * 128:(hc + 1) * 128, :],
                  w=[f"l2wgu{sl}"])
            for dc in range(ND):
                pti = dc % 2
                p.t(lambda e, sl=sl, dc=dc, pti=pti: e.transpose(
                    st.pt[pti][:, 0:128],
                    wgu[:, sl * D + dc * 128: sl * D + dc * 128 + 128],
                    st.ident[:]),
                    r=[f"l2wgu{sl}", "ident"], w=[f"pt{pti}"])
                p.v(lambda e, sl=sl, dc=dc, pti=pti: e.tensor_copy(
                    wT[:, sl * 1024 + dc * 128: sl * 1024 + dc * 128 + 128],
                    st.pt[pti][:, 0:128]),
                    r=[f"pt{pti}"], w=[f"l2wT{sl}"])
            p.dma("dq_y", gut[(which * NH + hc) * ND * 128:
                              (which * NH + hc + 1) * ND * 128, :]
                  .rearrange("(c p) m -> p c m", p=128),
                  wT[:, sl * 1024:(sl + 1) * 1024]
                  .rearrange("p (c m) -> p c m", c=ND),
                  r=[f"l2wT{sl}"], w=[f"gut{which}_{hc}"])

    for dmi in range(ND):
        sl = dmi % 2
        p.dma("dq_g", wgu[:, sl * D:sl * D + D],
              hb["L2_ffn_w_down"][dmi * 128:(dmi + 1) * 128, 0:1024],
              w=[f"l2wgu{sl}"])
        p.dma("dq_g", wT[:, sl * 1024:sl * 1024 + 1024],
              hb["L2_ffn_w_down"][dmi * 128:(dmi + 1) * 128, 1024:2048],
              w=[f"l2wT{sl}"])
        p.dma("dq_g", hbuf[:, 23 * 512:23 * 512 + 512],
              hb["L2_ffn_w_down"][dmi * 128:(dmi + 1) * 128, 2048:2560],
              w=["yb23"])
        p.dma("dq_g", hbuf[:, 22 * 512:22 * 512 + 512],
              hb["L2_ffn_w_down"][dmi * 128:(dmi + 1) * 128, 2560:3072],
              w=["yb22"])
        for hc in range(NH):
            if hc < 8:
                src, srk = wgu[:, sl * D + hc * 128: sl * D + hc * 128 + 128], f"l2wgu{sl}"
            elif hc < 16:
                src, srk = wT[:, sl * 1024 + (hc - 8) * 128: sl * 1024 + (hc - 8) * 128 + 128], f"l2wT{sl}"
            elif hc < 20:
                src, srk = hbuf[:, 23 * 512 + (hc - 16) * 128: 23 * 512 + (hc - 16) * 128 + 128], "yb23"
            else:
                src, srk = hbuf[:, 22 * 512 + (hc - 20) * 128: 22 * 512 + (hc - 20) * 128 + 128], "yb22"
            pti = hc % 2
            p.t(lambda e, src=src, pti=pti: e.transpose(
                st.pt[pti][:, 0:128], src, st.ident[:]),
                r=[srk, "ident"], w=[f"pt{pti}"])
            p.v(lambda e, hc=hc, dmi=dmi, pti=pti: e.tensor_copy(
                wdT[:, hc * D + dmi * 128: hc * D + dmi * 128 + 128],
                st.pt[pti][:, 0:128]),
                r=[f"pt{pti}"], w=[f"wdT{hc}"])

    for qb in range(NQB):
        for hc in range(NH):
            sl = hc % 2
            if hc % 2 == 0:
                p.dma("dq_y", wgut[:, 0:2048]
                      .rearrange("p (a c m) -> p a c m", a=2, c=ND),
                      gut[hc * ND * 128:(hc + 2) * ND * 128, :]
                      .rearrange("(a c p) m -> p a c m", p=128, c=ND),
                      r=[f"gut0_{hc}", f"gut0_{hc + 1}"], w=["wgt0"])
                p.dma("dq_a", wgut[:, 2048:4096]
                      .rearrange("p (a c m) -> p a c m", a=2, c=ND),
                      gut[(NH + hc) * ND * 128:(NH + hc + 2) * ND * 128, :]
                      .rearrange("(a c p) m -> p a c m", p=128, c=ND),
                      r=[f"gut1_{hc}", f"gut1_{hc + 1}"], w=["wgt1"])
            for which in range(2):
                base = which * 2048 + (hc % 2) * 1024
                pbm = 2 + which
                for dc in range(ND):
                    p.t(lambda e, base=base, dc=dc, qb=qb, pbm=pbm, which=which: e.matmul(
                        st.pb[pbm][:, 0:512],
                        wgut[:, base + dc * 128: base + dc * 128 + 128],
                        ht_c(st, dc)[:, qb * QB:qb * QB + QB],
                        start=(dc == 0), stop=(dc == ND - 1)),
                        r=[f"wgt{which}", f"ht{dc}"], w=[f"pb{pbm}"])
                if which == 0:
                    p.s(lambda e, hc=hc, pbm=pbm: e.activation(
                        out=hbuf[:, hc * 512:hc * 512 + 512], in_=st.pb[pbm][:, 0:512],
                        func=AF.Silu, scale=1.0 / 127.0),
                        r=[f"pb{pbm}"], w=[f"yb{hc}"])
                else:
                    p.s(lambda e, pbm=pbm: e.mul(ubuf[:, 0:512], st.pb[pbm][:, 0:512],
                                                 1.0 / 127.0),
                        r=[f"pb{pbm}"], w=["ubuf"])
                    p.v(lambda e, hc=hc: e.tensor_tensor(
                        hbuf[:, hc * 512:hc * 512 + 512],
                        hbuf[:, hc * 512:hc * 512 + 512], ubuf[:, 0:512], ALU.mult),
                        r=[f"yb{hc}", "ubuf"], w=[f"yb{hc}"])
        for mi in range(ND):
            pbm = mi % 2
            for hc in range(NH):
                p.t(lambda e, hc=hc, mi=mi, pbm=pbm, s0=(hc == 0), s1=(hc == NH - 1): e.matmul(
                    st.pb[pbm][:, 0:512],
                    wdT[:, hc * D + mi * 128: hc * D + mi * 128 + 128],
                    hbuf[:, hc * 512:hc * 512 + 512],
                    start=s0, stop=s1),
                    r=[f"wdT{hc}", f"yb{hc}"], w=[f"pb{pbm}"])
            emit_back_tile(p, st, f"pb{pbm}", st.pb[pbm][:, 0:512], mi, qb,
                           scale=1.0 / 127.0)


def emit_l3_attn(p, st):
    nc, hb = st.nc, st.hbm
    NHG = 4
    xold = nc.dram_tensor("l3_xold", [T, D], F32, kind="Internal").ap()
    vdram = nc.dram_tensor("l3_vdram", [16 * 68, T], BF16, kind="Internal").ap()
    w1tile = nc.dram_tensor("l3_w1t", [32 * ND * 128, 128], BF16, kind="Internal").ap()
    w2td = nc.dram_tensor("l3_w2t", [4096, D], BF16, kind="Internal").ap()

    p.dma("dq_y", xold.rearrange("(a p) m -> p a m", p=128),
          st.x[:].rearrange("p (a m) -> p a m", a=NT), r=XK, w=["xold"])

    emit_ln_stats(p, st)
    p.v(lambda e: e.tensor_scalar(lane(st, 5), lane(st, 6), 1e-5, None, ALU.add),
        r=["st6"], w=["st5"])
    p.v(lambda e: e.reciprocal(lane(st, 5), lane(st, 5)), r=["st5"], w=["st5"])
    p.v(lambda e: e.tensor_tensor(lane(st, 5), lane(st, 6), lane(st, 5), ALU.mult),
        r=["st5", "st6"], w=["st5"])
    p.s(lambda e: e.activation(out=lane(st, 5), in_=lane(st, 5), func=AF.Sqrt,
                               bias=st.eps[:, 0:1], scale=1.0),
        r=["st5", "eps"], w=["st5"])
    p.v(lambda e: e.reciprocal(lane(st, 6), lane(st, 5)), r=["st5"], w=["st6"])
    p.v(lambda e: e.tensor_tensor(lane(st, 5), lane(st, 2), lane(st, 6), ALU.mult),
        r=["st2", "st6"], w=["st5"])
    p.v(lambda e: e.scalar_tensor_tensor(lane(st, 6), lane(st, 4), -1.0, lane(st, 5),
                                         ALU.mult, ALU.mult),
        r=["st4", "st5"], w=["st6"])

    gcol = salloc(st, "l3_g", [128, 32], F32)
    for i, gk in enumerate(("L3_attn_ln1_g", "L3_attn_ln1_b",
                            "L3_attn_ln2_g", "L3_attn_ln2_b")):
        p.dma("dq_y", st.zslice[0:1, 0:D], hb[gk].rearrange("(o n) -> o n", o=1),
              w=["zs0", "zs1"])
        for dcg in range(ND):
            pti = dcg % 2
            p.t(lambda e, dcg=dcg, pti=pti: e.transpose(
                st.pb[4 + pti][0:128, 0:1],
                st.zslice[0:1, dcg * 128:dcg * 128 + 128], st.identf[0:1, 0:1]),
                r=["zs0", "zs1", "identf"], w=[f"pb{4 + pti}"])
            p.v(lambda e, i=i, dcg=dcg, pti=pti: e.tensor_copy(
                gcol[:, i * 8 + dcg:i * 8 + dcg + 1], st.pb[4 + pti][0:128, 0:1]),
                r=[f"pb{4 + pti}"], w=["gcol"])

    emit_ln_to_ht(p, st, rlane=5, blane=6,
                  gamma_col=lambda dc: gcol[:, dc:dc + 1],
                  beta_col=lambda dc: gcol[:, 8 + dc:8 + dc + 1], gb_res=("gcol",))
    for tc in range(NT):
        p.v(lambda e, tc=tc: e.tensor_scalar(
            x_t(st, tc), x_t(st, tc), lane(st, 2, tc), lane(st, 3, tc),
            ALU.mult, ALU.add),
            r=[f"x{tc}", "st2", "st3"], w=[f"x{tc}"])

    qT = salloc(st, "l3_qT", [128, 2 * T], BF16)
    kT = salloc(st, "l3_kT", [128, 2 * T], BF16)
    aoT = salloc(st, "l3_aoT", [128, NQB * ND * 512], BF16)
    wpr = salloc(st, "l3_wpr", [128, 2 * D], BF16)
    wpT = salloc(st, "l3_wpT", [128, 2 * ND * 128], BF16)
    masks = salloc(st, "l3_mask", [128, 8 * 512], BF16)
    rrow = salloc(st, "l3_rr", [1, 512], F32)
    rb = salloc(st, "l3_rb", [64, 512], F32)
    nrm = salloc(st, "l3_nrm", [1, 64], F32)
    negM = salloc(st, "l3_negM", [128, NHG], F32)
    VH = st.ht[:, 0:NT * NHG * 68]
    atT = st.ht[:, 8192:8192 + 8 * 512]
    ones16 = st.zslice[:, 0:512]

    for i in range(8):
        p.dma("dq_y", masks[:, i * 512:(i + 1) * 512], hb[f"c_wm{i}"],
              w=[f"wm{i}"])

    bqkv_missing = None
    for grp in range(4):
        h0 = grp * NHG
        for j in range(2):
            ha = h0 + 2 * j
            for wi, (dstT, dk, roff) in enumerate(((qT, "l3q", 0), (kT, "l3k", D))):
                sl = (grp * 4 + j * 2 + wi) % 2
                p.dma("dq_g", wpr[:, sl * D:sl * D + D],
                      hb["L3_attn_w_qkv"][roff + ha * 64: roff + ha * 64 + 128, :],
                      w=[f"l3wpr{sl}"])
                for dc in range(ND):
                    pti = dc % 2
                    p.t(lambda e, sl=sl, dc=dc, pti=pti: e.transpose(
                        st.pt[pti][:, 0:128],
                        wpr[:, sl * D + dc * 128: sl * D + dc * 128 + 128],
                        st.ident[:]),
                        r=[f"l3wpr{sl}", "ident"], w=[f"pt{pti}"])
                    p.v(lambda e, sl=sl, dc=dc, pti=pti: e.tensor_copy(
                        wpT[:, sl * ND * 128 + dc * 128: sl * ND * 128 + dc * 128 + 128],
                        st.pt[pti][:, 0:128]),
                        r=[f"pt{pti}"], w=[f"l3wpT{sl}"])
                for qb in range(NQB):
                    pbm = qb % 2
                    for kc in range(ND):
                        p.t(lambda e, sl=sl, kc=kc, qb=qb, pbm=pbm: e.matmul(
                            st.pb[pbm][:, 0:512],
                            wpT[:, sl * ND * 128 + kc * 128: sl * ND * 128 + kc * 128 + 128],
                            ht_c(st, kc)[:, qb * QB:qb * QB + QB],
                            start=(kc == 0), stop=(kc == ND - 1)),
                            r=[f"l3wpT{sl}", f"ht{kc}"], w=[f"pb{pbm}"])
                    p.s(lambda e, dstT=dstT, j=j, qb=qb, pbm=pbm: e.copy(
                        dstT[:, j * T + qb * QB: j * T + qb * QB + QB],
                        st.pb[pbm][:, 0:512]),
                        r=[f"pb{pbm}"], w=[f"{dk}{j}_{grp}"])
            sl = (j) % 2
            p.dma("dq_g", wpr[:, sl * D:sl * D + D],
                  hb["L3_attn_w_qkv"][2 * D + ha * 64: 2 * D + ha * 64 + 128, :],
                  w=[f"l3wpr{sl}"])
            for dc in range(ND):
                pti = dc % 2
                p.t(lambda e, sl=sl, dc=dc, pti=pti: e.transpose(
                    st.pt[pti][:, 0:128],
                    wpr[:, sl * D + dc * 128: sl * D + dc * 128 + 128],
                    st.ident[:]),
                    r=[f"l3wpr{sl}", "ident"], w=[f"pt{pti}"])
                p.v(lambda e, sl=sl, dc=dc, pti=pti: e.tensor_copy(
                    wpT[:, sl * ND * 128 + dc * 128: sl * ND * 128 + dc * 128 + 128],
                    st.pt[pti][:, 0:128]),
                    r=[f"pt{pti}"], w=[f"l3wpT{sl}"])
            for qb in range(NQB):
                pbm = qb % 2
                for kc in range(ND):
                    p.t(lambda e, sl=sl, kc=kc, qb=qb, pbm=pbm: e.matmul(
                        st.pb[pbm][:, 0:512],
                        wpT[:, sl * ND * 128 + kc * 128: sl * ND * 128 + kc * 128 + 128],
                        ht_c(st, kc)[:, qb * QB:qb * QB + QB],
                        start=(kc == 0), stop=(kc == ND - 1)),
                        r=[f"l3wpT{sl}", f"ht{kc}"], w=[f"pb{pbm}"])
                vsl = qb % 2
                p.s(lambda e, vsl=vsl, pbm=pbm: e.copy(
                    st.zslice[:, vsl * 512:vsl * 512 + 512].bitcast(BF16)[:, 0:512]
                    if False else st.htok[:, vsl * 512:vsl * 512 + 512],
                    st.pb[pbm][:, 0:512]),
                    r=[f"pb{pbm}"], w=[f"htok{vsl}"])
                p.dma("dq_y", vdram[(h0 + 2 * j) * 68:(h0 + 2 * j) * 68 + 2 * 68,
                                    qb * QB:qb * QB + QB]
                      .rearrange("(h m) t -> h m t", m=68)[:, 0:64, :],
                      st.htok[:, vsl * 512:vsl * 512 + 512],
                      r=[f"htok{vsl}"], w=[f"vd{grp}_{j}_{qb}"])
    p.v(lambda e: e.memset(st.htok[0:16, 0:512], 1.0), w=["htok0", "htok1"])
    for h in range(16):
        for qb in range(NQB):
            p.dma("dq_y", vdram[h * 68 + 64:h * 68 + 68, qb * QB:qb * QB + QB],
                  st.htok[0:4, 0:512], r=["htok0"], w=[f"vdo{h}_{qb}"])

    SCALE = 0.125
    for grp in range(4):
        h0 = grp * NHG
        for kc in range(NT):
            deps = ([f"vd{grp}_{j}_{qb}" for j in range(2) for qb in range(NQB)]
                    + [f"vdo{h}_{qb}" for h in range(h0, h0 + 4) for qb in range(NQB)])
            if grp == 0:
                deps += HTK
            p.dma("dq_y", st.ht[:, kc * NHG * 68:(kc + 1) * NHG * 68],
                  vdram[h0 * 68:(h0 + 4) * 68, kc * 128:(kc + 1) * 128],
                  r=deps, w=[f"VH{kc}_{grp}", f"ht{(kc * NHG * 68) // T}",
                             f"ht{((kc + 1) * NHG * 68 - 1) // T}"],
                  transpose=True)
        p.v(lambda e: e.memset(nrm[:, 0:32], 0.0), w=["nrm"])
        for j in range(2):
            for wi, (src, srck) in enumerate(((qT, "l3q"), (kT, "l3k"))):
                for qb in range(NQB):
                    p.s(lambda e, src=src, j=j, qb=qb: e.activation(
                        out=st.sq[:, 0:512],
                        in_=src[:, j * T + qb * QB: j * T + qb * QB + QB],
                        func=AF.Square),
                        r=[f"{srck}{j}_{grp}"], w=["htok0"])
                    for hio in range(2):
                        base = hio * 64
                        idx = wi * 16 + 2 * j + hio
                        p.t(lambda e, base=base: e.matmul(
                            st.pb[2][0:1, 0:512], st.onescol[base:base + 64, 0:1],
                            st.sq[base:base + 64, 0:512], start=True, stop=True),
                            r=["htok0", "onescol"], w=["pb2"])
                        p.v(lambda e: e.reduce_max(nrm[0:1, 40:41],
                                                   st.pb[2][0:1, 0:512], axis=AX.X),
                            r=["pb2"], w=["nrms"])
                        p.v(lambda e, idx=idx: e.tensor_max(
                            nrm[0:1, idx:idx + 1], nrm[0:1, idx:idx + 1],
                            nrm[0:1, 40:41]),
                            r=["nrms", "nrm"], w=["nrm"])
        p.v(lambda e: e.tensor_tensor(nrm[0:1, 32:32 + NHG], nrm[0:1, 0:NHG],
                                      nrm[0:1, 16:16 + NHG], ALU.mult),
            r=["nrm"], w=["nrmm"])
        p.s(lambda e: e.activation(out=nrm[0:1, 48:48 + NHG],
                                   in_=nrm[0:1, 32:32 + NHG], func=AF.Sqrt),
            r=["nrmm"], w=["nrmsq"])
        p.v(lambda e: e.tensor_scalar_mul(nrm[0:1, 56:56 + NHG],
                                          nrm[0:1, 48:48 + NHG], -SCALE),
            r=["nrmsq"], w=["nrmneg"])
        p.t(lambda e: e.matmul(st.pb[2][0:128, 0:NHG], st.onesrow_f[0:1, 0:128],
                               nrm[0:1, 56:56 + NHG], start=True, stop=True),
            r=["nrmneg", "onesrowf"], w=["pb2"])
        p.v(lambda e: e.tensor_copy(negM[:, 0:NHG], st.pb[2][0:128, 0:NHG]),
            r=["pb2"], w=["negM"])

        for hidx in range(NHG):
            j, hio = hidx // 2, hidx % 2
            base = hio * 64
            h = h0 + hidx
            dc_o, rrow_off = h // 2, (h % 2) * 64
            for qc in range(NQB):
                first = True
                nvalid = [i for i in range(8) if 0 <= qc * 4 - 4 + i < NT]
                for i in nvalid:
                    kc = qc * 4 - 4 + i
                    pbs = i % 2
                    p.t(lambda e, j=j, base=base, kc=kc, qc=qc, pbs=pbs: e.matmul(
                        st.pb[pbs][:, 0:512],
                        kT[base:base + 64, j * T + kc * 128: j * T + kc * 128 + 128],
                        qT[base:base + 64, j * T + qc * QB: j * T + qc * QB + QB],
                        start=True, stop=True),
                        r=[f"l3k{j}_{grp}", f"l3q{j}_{grp}"], w=[f"pb{pbs}"])
                    p.s(lambda e, i=i, pbs=pbs, hidx=hidx: e.activation(
                        out=atT[:, i * 512:i * 512 + 512], in_=st.pb[pbs][:, 0:512],
                        func=AF.Exp, scale=SCALE, bias=negM[:, hidx:hidx + 1]),
                        r=[f"pb{pbs}", "negM"], w=[f"atT{i}", "ht4", "ht5"])
                    p.v(lambda e, i=i: e.tensor_tensor(
                        atT[:, i * 512:i * 512 + 512], atT[:, i * 512:i * 512 + 512],
                        masks[:, i * 512:(i + 1) * 512], ALU.mult),
                        r=[f"atT{i}", f"wm{i}", "ht5"], w=[f"atT{i}", "ht4", "ht5"])
                for i in nvalid:
                    kc = qc * 4 - 4 + i
                    p.t(lambda e, kc=kc, i=i, hidx=hidx,
                        s0=(i == nvalid[0]), s1=(i == nvalid[-1]): e.matmul(
                        st.pb[3][0:68, 0:512],
                        st.ht[:, kc * NHG * 68 + hidx * 68: kc * NHG * 68 + hidx * 68 + 68],
                        atT[:, i * 512:i * 512 + 512],
                        start=s0, stop=s1),
                        r=[f"atT{i}", f"VH{kc}_{grp}",
                           f"ht{(kc * NHG * 68) // T}"], w=["pb3"])
                p.v(lambda e: e.reciprocal(rrow[0:1, 0:512], st.pb[3][64:65, 0:512]),
                    r=["pb3"], w=["rrow"])
                p.t(lambda e: e.matmul(st.pb[2][0:64, 0:512], st.onesrow_f[0:1, 0:64],
                                       rrow[0:1, 0:512], start=True, stop=True),
                    r=["rrow", "onesrowf"], w=["pb2"])
                p.s(lambda e: e.copy(rb[0:64, 0:512], st.pb[2][0:64, 0:512]),
                    r=["pb2"], w=["rb"])
                p.v(lambda e, rrow_off=rrow_off, qc=qc, dc_o=dc_o: e.tensor_tensor(
                    aoT[rrow_off:rrow_off + 64,
                        (qc * ND + dc_o) * 512:(qc * ND + dc_o) * 512 + 512],
                    st.pb[3][0:64, 0:512], rb[0:64, 0:512], ALU.mult),
                    r=["pb3", "rb"], w=[f"aoT{qc}_{dc_o}_{rrow_off}"])

    for mi in range(ND):
        sl = mi % 2
        p.dma("dq_g", wpr[:, sl * D:sl * D + D],
              hb["L3_attn_w_o"][mi * 128:(mi + 1) * 128, :],
              w=[f"l3wpr{sl}"])
        for dc in range(ND):
            pti = dc % 2
            p.t(lambda e, sl=sl, dc=dc, pti=pti: e.transpose(
                st.pt[pti][:, 0:128],
                wpr[:, sl * D + dc * 128: sl * D + dc * 128 + 128], st.ident[:]),
                r=[f"l3wpr{sl}", "ident"], w=[f"pt{pti}"])
            p.v(lambda e, sl=sl, dc=dc, pti=pti: e.tensor_copy(
                wpT[:, sl * ND * 128 + dc * 128: sl * ND * 128 + dc * 128 + 128],
                st.pt[pti][:, 0:128]),
                r=[f"pt{pti}"], w=[f"l3wpT{sl}"])
        for qb in range(NQB):
            pbm = qb % 2
            for kc in range(ND):
                p.t(lambda e, sl=sl, kc=kc, qb=qb, pbm=pbm: e.matmul(
                    st.pb[pbm][:, 0:512],
                    wpT[:, sl * ND * 128 + kc * 128: sl * ND * 128 + kc * 128 + 128],
                    aoT[:, (qb * ND + kc) * 512: (qb * ND + kc) * 512 + 512],
                    start=(kc == 0), stop=(kc == ND - 1)),
                    r=[f"l3wpT{sl}", f"aoT{qb}_{kc}_0", f"aoT{qb}_{kc}_64"],
                    w=[f"pb{pbm}"])
            emit_back_tile(p, st, f"pb{pbm}", st.pb[pbm][:, 0:512], mi, qb)

    emit_ln_stats(p, st)
    emit_ln_to_ht(p, st, gamma_col=lambda dc: gcol[:, 16 + dc:16 + dc + 1],
                  beta_col=lambda dc: gcol[:, 24 + dc:24 + dc + 1], gb_res=("gcol",))

    b1row, _ = load_row(p, st, "l3b1", hb["L3_attn_mlp_b1"], 4096)
    b2row, _ = load_row(p, st, "l3b2", hb["L3_attn_mlp_b2"], D)
    for oc in range(32):
        sl = oc % 2
        p.dma("dq_g", wpr[:, sl * D:sl * D + D],
              hb["L3_attn_mlp_w1"][oc * 128:(oc + 1) * 128, :],
              w=[f"l3wpr{sl}"])
        for dc in range(ND):
            pti = dc % 2
            p.t(lambda e, sl=sl, dc=dc, pti=pti: e.transpose(
                st.pt[pti][:, 0:128],
                wpr[:, sl * D + dc * 128: sl * D + dc * 128 + 128], st.ident[:]),
                r=[f"l3wpr{sl}", "ident"], w=[f"pt{pti}"])
            p.v(lambda e, sl=sl, dc=dc, pti=pti: e.tensor_copy(
                wpT[:, sl * ND * 128 + dc * 128: sl * ND * 128 + dc * 128 + 128],
                st.pt[pti][:, 0:128]),
                r=[f"pt{pti}"], w=[f"l3wpT{sl}"])
        p.dma("dq_y", w1tile[oc * ND * 128:(oc + 1) * ND * 128, :]
              .rearrange("(c p) m -> p c m", p=128),
              wpT[:, sl * ND * 128:(sl + 1) * ND * 128]
              .rearrange("p (c m) -> p c m", c=ND),
              r=[f"l3wpT{sl}"], w=[f"w1t{oc}"])
    for dmi in range(ND):
        sl = dmi % 2
        p.dma("dq_g", wpr[:, sl * D:sl * D + D]
              .rearrange("p (a m) -> p a m", a=2)
              if False else wpr[:, sl * D:sl * D + D],
              hb["L3_attn_mlp_w2"][dmi * 128:(dmi + 1) * 128, 0:1024],
              w=[f"l3wpr{sl}"])
        p.dma("dq_g", wpT[:, sl * 1024:sl * 1024 + 1024],
              hb["L3_attn_mlp_w2"][dmi * 128:(dmi + 1) * 128, 1024:2048],
              w=[f"l3wpT{sl}"])
        p.dma("dq_g", st.htok[:, 0:1024],
              hb["L3_attn_mlp_w2"][dmi * 128:(dmi + 1) * 128, 2048:3072],
              w=["htok0"])
        p.dma("dq_g", st.htok[:, 1024:2048],
              hb["L3_attn_mlp_w2"][dmi * 128:(dmi + 1) * 128, 3072:4096],
              w=["htok1"])
        for hc in range(32):
            if hc < 8:
                src, srk = wpr[:, sl * D + hc * 128: sl * D + hc * 128 + 128], f"l3wpr{sl}"
            elif hc < 16:
                src, srk = wpT[:, sl * 1024 + (hc - 8) * 128: sl * 1024 + (hc - 8) * 128 + 128], f"l3wpT{sl}"
            elif hc < 24:
                src, srk = st.htok[:, (hc - 16) * 128:(hc - 16) * 128 + 128], "htok0"
            else:
                src, srk = st.htok[:, 1024 + (hc - 24) * 128:1024 + (hc - 24) * 128 + 128], "htok1"
            pti = hc % 2
            p.t(lambda e, src=src, pti=pti: e.transpose(
                st.pt[pti][:, 0:128], src, st.ident[:]),
                r=[srk, "ident"], w=[f"pt{pti}"])
            p.v(lambda e, hc=hc, pti=pti: e.tensor_copy(
                st.zslice[:, 0:512].bitcast(BF16)[:, 0:128] if False else
                st.sq[:, 0:128], st.pt[pti][:, 0:128]),
                r=[f"pt{pti}"], w=["htok0"]) if False else None
            p.s(lambda e, hc=hc, pti=pti: e.copy(st.sq[:, 0:128], st.pt[pti][:, 0:128]),
                r=[f"pt{pti}"], w=["sqx"])
            p.dma("dq_y", w2td[hc * 128:(hc + 1) * 128, dmi * 128:(dmi + 1) * 128],
                  st.sq[:, 0:128], r=["sqx"], w=[f"w2t{hc}_{dmi}"])

    ybuf = aoT
    w2s = salloc(st, "l3_w2s", [128, 2 * 2 * D], BF16)
    w1s = salloc(st, "l3_w1s", [128, 2 * 2 * ND * 128], BF16)
    for qb in range(NQB):
        for oc in range(32):
            sl = (oc // 2) % 2
            if oc % 2 == 0:
                p.dma("dq_y", w1s[:, sl * 2048:(sl + 1) * 2048]
                      .rearrange("p (c m) -> p c m", c=2 * ND),
                      w1tile[oc * ND * 128:(oc + 2) * ND * 128, :]
                      .rearrange("(c p) m -> p c m", p=128),
                      r=[f"w1t{oc}", f"w1t{oc + 1}"],
                      w=[f"w1s{sl}"])
            ob = oc % 2
            pbm = oc % 2
            for kc in range(ND):
                p.t(lambda e, sl=sl, ob=ob, kc=kc, qb=qb, pbm=pbm: e.matmul(
                    st.pb[pbm][:, 0:512],
                    w1s[:, sl * 2048 + (ob * ND + kc) * 128:
                        sl * 2048 + (ob * ND + kc) * 128 + 128],
                    ht_c(st, kc)[:, qb * QB:qb * QB + QB],
                    start=(kc == 0), stop=False),
                    r=[f"w1s{sl}", f"ht{kc}"], w=[f"pb{pbm}"])
            p.t(lambda e, oc=oc, pbm=pbm: e.matmul(
                st.pb[pbm][:, 0:512], b1row[0:1, oc * 128:oc * 128 + 128],
                st.ones512[0:1, 0:512], start=False, stop=True),
                r=["r_l3b1", "ones512"], w=[f"pb{pbm}"])
            p.s(lambda e, oc=oc, pbm=pbm: e.activation(
                out=ybuf[:, oc * 512:oc * 512 + 512], in_=st.pb[pbm][:, 0:512],
                func=AF.Gelu),
                r=[f"pb{pbm}"],
                w=[f"yb{oc}", f"aoT{oc // 8}_{oc % 8}_0", f"aoT{oc // 8}_{oc % 8}_64"])
        pairs = [(tc, dh) for tc in range(4) for dh in range(2)]
        for wv in range(2):
            wave = pairs[wv * 4:(wv + 1) * 4]
            for kc in range(32):
                sl = (kc // 2) % 2
                if kc % 2 == 0:
                    p.dma("dq_y", w2s[:, sl * 2 * D:(sl + 1) * 2 * D]
                          .rearrange("p (c m) -> p c m", c=2),
                          w2td[kc * 128:(kc + 2) * 128, :]
                          .rearrange("(c p) m -> p c m", p=128),
                          r=[f"w2t{kc + q}_{dmi}" for q in range(2)
                             for dmi in range(ND)],
                          w=[f"w2s{sl}"])
                kb = kc % 2
                for i, (tc, dh) in enumerate(wave):
                    p.t(lambda e, kc=kc, sl=sl, kb=kb, tc=tc, dh=dh, i=i: e.matmul(
                        st.pb[i][:, 0:512],
                        ybuf[:, kc * 512 + tc * 128: kc * 512 + tc * 128 + 128],
                        w2s[:, sl * 2 * D + kb * D + dh * 512:
                            sl * 2 * D + kb * D + dh * 512 + 512],
                        start=(kc == 0), stop=False),
                        r=[f"w2s{sl}", f"yb{kc}"], w=[f"pb{i}"])
            for i, (tc, dh) in enumerate(wave):
                p.t(lambda e, dh=dh, i=i: e.matmul(
                    st.pb[i][:, 0:512], st.ones512[0:1, 0:128],
                    b2row[0:1, dh * 512:dh * 512 + 512], start=False, stop=True),
                    r=["r_l3b2", "ones512"], w=[f"pb{i}"])
                tt = qb * 4 + tc
                dst = x_t(st, tt)[:, dh * 512:dh * 512 + 512]
                p.v(lambda e, dst=dst, i=i: e.scalar_tensor_tensor(
                    dst, st.pb[i][:, 0:512], 1.0, dst, ALU.mult, ALU.add),
                    r=[f"pb{i}", f"x{tt}"], w=[f"x{tt}"])

    for tc in range(NT):
        sl = tc % 2
        p.dma("dq_y", st.zslice[:, sl * 512:sl * 512 + 512]
              .rearrange("p (a m) -> p a m", a=1) if False else
              st.zslice[:, sl * 512:sl * 512 + 512],
              xold[tc * 128:(tc + 1) * 128, 0:512], r=["xold"], w=[f"zs{sl}"])
        p.v(lambda e, tc=tc, sl=sl: e.tensor_tensor(
            x_t(st, tc)[:, 0:512], x_t(st, tc)[:, 0:512],
            st.zslice[:, sl * 512:sl * 512 + 512], ALU.add),
            r=[f"x{tc}", f"zs{sl}"], w=[f"x{tc}"])
        p.dma("dq_y", st.zslice[:, sl * 512:sl * 512 + 512],
              xold[tc * 128:(tc + 1) * 128, 512:1024], r=["xold"], w=[f"zs{sl}"])
        p.v(lambda e, tc=tc, sl=sl: e.tensor_tensor(
            x_t(st, tc)[:, 512:1024], x_t(st, tc)[:, 512:1024],
            st.zslice[:, sl * 512:sl * 512 + 512], ALU.add),
            r=[f"x{tc}", f"zs{sl}"], w=[f"x{tc}"])


def emit_l3_ffn(p, st):
    nc, hb = st.nc, st.hbm
    E, CAP, F = 8, 512, 4096
    hdram = nc.dram_tensor("l3f_h", [T, D], BF16, kind="Internal").ap()
    ydram = nc.dram_tensor("l3f_y", [T, D], F32, kind="Internal").ap()

    emit_ln_stats(p, st)
    emit_ln_to_ht(p, st, dram_out=hdram)

    p.v(lambda e: e.memset(st.zslice[:], 0.0), w=["zs0", "zs1"])
    for tc in range(NT):
        p.dma("dq_y", ydram[tc * 128:(tc + 1) * 128, 0:256], st.zslice[:, 0:256],
              r=["zs0"], w=[f"yd{tc}"])
        p.dma("dq_y", ydram[tc * 128:(tc + 1) * 128, 256:512], st.zslice[:, 256:512],
              r=["zs0"], w=[f"yd{tc}"])
        p.dma("dq_y", ydram[tc * 128:(tc + 1) * 128, 512:768], st.zslice[:, 512:768],
              r=["zs1"], w=[f"yd{tc}"])
        p.dma("dq_y", ydram[tc * 128:(tc + 1) * 128, 768:1024], st.zslice[:, 768:1024],
              r=["zs1"], w=[f"yd{tc}"])

    gwT = salloc(st, "l3f_gwT", [128, ND * 8], BF16)
    gwn = salloc(st, "l3f_gwn", [8, D], BF16)
    p.dma("dq_g", gwn[:], hb["L3_ffn_gate_w"], w=["gwn"])
    for dc in range(ND):
        pti = dc % 2
        p.t(lambda e, dc=dc, pti=pti: e.transpose(
            st.pt[pti][0:128, 0:8], gwn[:, dc * 128:(dc + 1) * 128], st.ident[0:8, 0:8]),
            r=["gwn", "ident"], w=[f"pt{pti}"])
        p.v(lambda e, dc=dc, pti=pti: e.tensor_copy(
            gwT[:, dc * 8:(dc + 1) * 8], st.pt[pti][0:128, 0:8]),
            r=[f"pt{pti}"], w=["gwT"])
    logT = salloc(st, "l3f_logT", [8, T], F32)
    probsT = salloc(st, "l3f_probsT", [8, T], F32)
    probs = salloc(st, "l3f_probs", [128, NT * 8], F32)
    for qb in range(NQB):
        pbm = qb % 2
        for dc in range(ND):
            p.t(lambda e, dc=dc, qb=qb, pbm=pbm: e.matmul(
                st.pb[pbm][0:8, 0:512], gwT[:, dc * 8:(dc + 1) * 8],
                ht_c(st, dc)[:, qb * QB:qb * QB + QB],
                start=(dc == 0), stop=(dc == ND - 1)),
                r=["gwT", f"ht{dc}"], w=[f"pb{pbm}"])
        p.s(lambda e, qb=qb, pbm=pbm: e.copy(
            logT[:, qb * QB:qb * QB + QB], st.pb[pbm][0:8, 0:512]),
            r=[f"pb{pbm}"], w=["logT"])
    p.v(lambda e: e.memset(lane(st, 6), 0.0), w=["st6"])
    for tc in range(NT):
        pti = tc % 2
        p.t(lambda e, tc=tc, pti=pti: e.transpose(
            st.pb[4 + pti][0:128, 0:8], logT[:, tc * 128:(tc + 1) * 128],
            st.identf[0:8, 0:8]),
            r=["logT", "identf"], w=[f"pb{4 + pti}"])
        pr = probs[:, tc * 8:(tc + 1) * 8]
        m1 = lane(st, 5, tc)
        p.v(lambda e, pr=pr, pti=pti: e.tensor_copy(pr, st.pb[4 + pti][0:128, 0:8]),
            r=[f"pb{4 + pti}"], w=["probs"])
        p.v(lambda e, pr=pr, m1=m1: e.reduce_max(m1, pr, axis=AX.X),
            r=["probs"], w=["st5"])
        p.v(lambda e, m1=m1: e.tensor_scalar_mul(m1, m1, -1.0), r=["st5"], w=["st5"])
        p.s(lambda e, pr=pr, m1=m1, tc=tc: e.activation(
            out=pr, in_=pr, func=AF.Exp, bias=m1, scale=1.0,
            accum_out=lane(st, 6, tc)),
            r=["probs", "st5", "st6"], w=["probs", "st6"])
        p.v(lambda e, tc=tc: e.reciprocal(lane(st, 6, tc), lane(st, 6, tc)),
            r=["st6"], w=["st6"])
        p.v(lambda e, pr=pr, tc=tc: e.tensor_scalar(pr, pr, lane(st, 6, tc), None,
                                                    ALU.mult),
            r=["probs", "st6"], w=["probs"])
    for tc in range(NT):
        pti = tc % 2
        p.t(lambda e, tc=tc, pti=pti: e.transpose(
            st.pb[4 + pti][0:8, 0:128], probs[:, tc * 8:(tc + 1) * 8], st.identf[:]),
            r=["probs", "identf"], w=[f"pb{4 + pti}"])
        p.v(lambda e, tc=tc, pti=pti: e.tensor_copy(
            probsT[:, tc * 128:(tc + 1) * 128], st.pb[4 + pti][0:8, 0:128]),
            r=[f"pb{4 + pti}"], w=["probsT"])

    thr = salloc(st, "l3f_thr", [8, 8], F32)
    ge = logT
    p.v(lambda e: e.memset(thr[:, 0:1], 0.5), w=["thr"])
    step = 0.25
    for it in range(24):
        p.v(lambda e: e.tensor_scalar(ge[:], probsT[:], thr[:, 0:1], None, ALU.is_ge),
            r=["probsT", "thr"], w=["logT"])
        p.v(lambda e: e.reduce_sum(thr[:, 1:2], ge[:], axis=AX.X),
            r=["logT"], w=["thrc"])
        p.v(lambda e: e.tensor_scalar(thr[:, 2:3], thr[:, 1:2], 512.0, None, ALU.is_gt),
            r=["thrc"], w=["thrd"])
        p.v(lambda e: e.tensor_scalar(thr[:, 2:3], thr[:, 2:3], 2.0, -1.0,
                                      ALU.mult, ALU.add),
            r=["thrd"], w=["thrd"])
        p.v(lambda e, stp=step: e.scalar_tensor_tensor(
            thr[:, 0:1], thr[:, 2:3], stp, thr[:, 0:1], ALU.mult, ALU.add),
            r=["thrd", "thr"], w=["thr"])
        step *= 0.5
    for rnd in range(4):
        p.v(lambda e: e.tensor_scalar(ge[:], probsT[:], thr[:, 0:1], None, ALU.is_ge),
            r=["probsT", "thr"], w=["logT"])
        p.v(lambda e: e.reduce_sum(thr[:, 1:2], ge[:], axis=AX.X),
            r=["logT"], w=["thrc"])
        p.v(lambda e: e.scalar_tensor_tensor(probs[:, 0:1] if False else ge[:],
                                             ge[:], 1.0, probsT[:], ALU.bypass,
                                             ALU.mult),
            r=["logT", "probsT"], w=["logT"]) if False else None
        p.v(lambda e: e.tensor_scalar(probsT[:], probsT[:], -2.0, None, ALU.add),
            r=["probsT"], w=["probsT"])
        p.v(lambda e: e.tensor_tensor(ge[:], ge[:], probsT[:], ALU.mult),
            r=["logT", "probsT"], w=["logT"])
        p.v(lambda e: e.tensor_scalar(probsT[:], probsT[:], 2.0, None, ALU.add),
            r=["probsT"], w=["probsT"])
        p.v(lambda e: e.tensor_scalar(ge[:], ge[:], 2.0, None, ALU.add),
            r=["logT"], w=["logT"])
        p.v(lambda e: e.tensor_reduce(thr[:, 3:4], ge[:], axis=AX.X, op=ALU.min),
            r=["logT"], w=["thrmin"])
        p.v(lambda e: e.tensor_scalar(ge[:], probsT[:], thr[:, 0:1], None, ALU.is_lt),
            r=["probsT", "thr"], w=["logT"])
        p.v(lambda e: e.tensor_tensor(ge[:], ge[:], probsT[:], ALU.mult),
            r=["logT", "probsT"], w=["logT"])
        p.v(lambda e: e.tensor_reduce(thr[:, 4:5], ge[:], axis=AX.X, op=ALU.max),
            r=["logT"], w=["thrmax"])
        p.v(lambda e: e.tensor_scalar(thr[:, 5:6], thr[:, 1:2], 512.0, None, ALU.is_gt),
            r=["thrc"], w=["thrp"])
        p.v(lambda e: e.tensor_scalar(thr[:, 6:7], thr[:, 1:2], 512.0, None, ALU.is_lt),
            r=["thrc"], w=["thrn"])
        p.v(lambda e: e.tensor_scalar(thr[:, 3:4], thr[:, 3:4], 1.0 + 2.0 ** -22,
                                      None, ALU.mult),
            r=["thrmin"], w=["thrmin"])
        p.v(lambda e: e.tensor_tensor(thr[:, 3:4], thr[:, 3:4], thr[:, 0:1],
                                      ALU.subtract), r=["thrmin", "thr"], w=["thrmin"])
        p.v(lambda e: e.tensor_tensor(thr[:, 3:4], thr[:, 3:4], thr[:, 5:6],
                                      ALU.mult), r=["thrmin", "thrp"], w=["thrmin"])
        p.v(lambda e: e.tensor_tensor(thr[:, 0:1], thr[:, 0:1], thr[:, 3:4],
                                      ALU.add), r=["thrmin", "thr"], w=["thr"])
        p.v(lambda e: e.tensor_tensor(thr[:, 4:5], thr[:, 4:5], thr[:, 0:1],
                                      ALU.subtract), r=["thrmax", "thr"], w=["thrmax"])
        p.v(lambda e: e.tensor_tensor(thr[:, 4:5], thr[:, 4:5], thr[:, 6:7],
                                      ALU.mult), r=["thrmax", "thrn"], w=["thrmax"])
        p.v(lambda e: e.tensor_tensor(thr[:, 0:1], thr[:, 0:1], thr[:, 4:5],
                                      ALU.add), r=["thrmax", "thr"], w=["thr"])

    slotT = salloc(st, "l3f_slotT", [8, T], F32)
    zer8 = salloc(st, "l3f_z8", [8, T], F32)
    p.v(lambda e: e.memset(zer8[:], 0.0), w=["zer8"])
    p.v(lambda e: e.tensor_scalar(ge[:], probsT[:], thr[:, 0:1], None, ALU.is_ge),
        r=["probsT", "thr"], w=["logT"])
    p.v(lambda e: e.tensor_tensor_scan(slotT[:], ge[:], zer8[:], 0.0,
                                       ALU.add, ALU.bypass),
        r=["logT", "zer8"], w=["slotT"])
    p.v(lambda e: e.tensor_tensor(slotT[:], slotT[:], ge[:], ALU.mult),
        r=["slotT", "logT"], w=["slotT"])
    p.v(lambda e: e.tensor_scalar(slotT[:], slotT[:], -1.0, None, ALU.add),
        r=["slotT"], w=["slotT"])
    slottok = salloc(st, "l3f_slott", [128, NT * 8], F32)
    for tc in range(NT):
        pti = tc % 2
        p.t(lambda e, tc=tc, pti=pti: e.transpose(
            st.pb[4 + pti][0:128, 0:8], slotT[:, tc * 128:(tc + 1) * 128],
            st.identf[0:8, 0:8]),
            r=["slotT", "identf"], w=[f"pb{4 + pti}"])
        p.v(lambda e, tc=tc, pti=pti: e.tensor_copy(
            slottok[:, tc * 8:(tc + 1) * 8], st.pb[4 + pti][0:128, 0:8]),
            r=[f"pb{4 + pti}"], w=["slottok"])
    iot = salloc(st, "l3f_iot", [128, 512], F32)
    tokcol = salloc(st, "l3f_tokc", [128, 16], F32)
    p.g(lambda e: e.iota(iot[:], pattern=[[1, 512]], base=0, channel_multiplier=0,
                         allow_small_or_imprecise_dtypes=True), w=["iot"])
    p.g(lambda e: e.iota(tokcol[:], pattern=[[128, 16]], base=0, channel_multiplier=1,
                         allow_small_or_imprecise_dtypes=True), w=["tokcol"])
    PT = salloc(st, "l3f_PT", [128, 512], BF16)
    rhs2 = salloc(st, "l3f_rhs2", [128, NT * 2], BF16)
    mg = salloc(st, "l3f_mg", [128, 2 * E * 4], F32)
    mint = salloc(st, "l3f_mint", [128, E * 4], I32)
    grow = salloc(st, "l3f_grow", [128, 4], F32)
    p.v(lambda e: e.tensor_copy(rhs2[:].rearrange("p (t two) -> p t two", two=2)[:, :, 0:1],
                                tokcol[:].rearrange("p (t o) -> p t o", o=1)),
        r=["tokcol"], w=["rhs2"])
    for e_i in range(E):
        p.v(lambda e, e_i=e_i: e.tensor_copy(
            rhs2[:].rearrange("p (t two) -> p t two", two=2)[:, :, 1:2],
            probs[:].rearrange("p (t ee) -> p t ee", ee=8)[:, :, e_i:e_i + 1]),
            r=["probs", "rhs2"], w=["rhs2"])
        for tc in range(NT):
            p.v(lambda e, e_i=e_i, tc=tc: e.tensor_scalar(
                PT[:], iot[:], slottok[:, tc * 8 + e_i:tc * 8 + e_i + 1], None,
                ALU.is_equal),
                r=["iot", "slottok"], w=["PT"])
            for sc in range(4):
                p.t(lambda e, tc=tc, sc=sc, s0=(tc == 0): e.matmul(
                    st.pb[sc][0:128, 0:2], PT[:, sc * 128:(sc + 1) * 128],
                    rhs2[:, tc * 2:tc * 2 + 2], start=s0, stop=(tc == NT - 1)),
                    r=["PT", "rhs2"], w=[f"pb{sc}"])
        for sc in range(4):
            p.v(lambda e, e_i=e_i, sc=sc: e.tensor_copy(
                mg[:, e_i * 4 + sc:e_i * 4 + sc + 1], st.pb[sc][0:128, 0:1]),
                r=[f"pb{sc}"], w=["mg"])
            p.v(lambda e, e_i=e_i, sc=sc: e.tensor_copy(
                mg[:, 32 + e_i * 4 + sc:32 + e_i * 4 + sc + 1], st.pb[sc][0:128, 1:2]),
                r=[f"pb{sc}"], w=["mg"])
    p.v(lambda e: e.tensor_copy(mint[:], mg[:, 0:32]), r=["mg"], w=["mint"])

    xg = salloc(st, "l3f_xg", [128, 4 * D], BF16)
    xgT = salloc(st, "l3f_xgT", [128, ND * 512], BF16)
    w1b = salloc(st, "l3f_w1b", [128, 2 * 2048], BF16)
    yb1 = st.ht
    w2b = salloc(st, "l3f_w2b", [128, 2 * 32 * 128], BF16)
    y2t = salloc(st, "l3f_y2t", [128, 4 * D], F32)

    for e_i in range(E):
        for sc in range(4):
            p.indirect(xg[:, sc * D:(sc + 1) * D], None,
                       hdram, IndirectOffsetOnAxis(mint[:, e_i * 4 + sc:e_i * 4 + sc + 1], 0),
                       r=["mint"] + [f"hd{tc}" for tc in range(NT)],
                       w=[f"xg{sc}"])
            for dc in range(ND):
                pti = dc % 2
                p.t(lambda e, sc=sc, dc=dc, pti=pti: e.transpose(
                    st.pt[pti][:, 0:128], xg[:, sc * D + dc * 128: sc * D + dc * 128 + 128],
                    st.ident[:]),
                    r=[f"xg{sc}", "ident"], w=[f"pt{pti}"])
                p.v(lambda e, sc=sc, dc=dc, pti=pti: e.tensor_copy(
                    xgT[:, dc * 512 + sc * 128: dc * 512 + sc * 128 + 128],
                    st.pt[pti][:, 0:128]),
                    r=[f"pt{pti}"], w=[f"xgT{dc}"])
        for oc in range(32):
            sl = (oc // 2) % 2
            if oc % 2 == 0:
                p.dma("dq_g", w1b[:, sl * 2048:(sl + 1) * 2048]
                      .rearrange("p (c m) -> p c m", c=ND),
                      hb["L3_ffn_w1"][e_i][:, oc * 128:(oc + 2) * 128]
                      .rearrange("(c p) m -> p c m", p=128),
                      w=[f"w1b{sl}"])
            ob = oc % 2
            pbm = oc % 2
            for kc in range(ND):
                p.t(lambda e, sl=sl, ob=ob, kc=kc, pbm=pbm: e.matmul(
                    st.pb[pbm][:, 0:512],
                    w1b[:, sl * 2048 + kc * 256 + ob * 128:
                        sl * 2048 + kc * 256 + ob * 128 + 128],
                    xgT[:, kc * 512:(kc + 1) * 512],
                    start=(kc == 0), stop=(kc == ND - 1)),
                    r=[f"w1b{sl}", f"xgT{kc}"], w=[f"pb{pbm}"])
            p.s(lambda e, oc=oc, pbm=pbm: e.activation(
                out=yb1[:, oc * 512:oc * 512 + 512], in_=st.pb[pbm][:, 0:512],
                func=AF.Gelu),
                r=[f"pb{pbm}"], w=[f"yb{oc}", f"ht{oc // 4}"])
        for mi in range(ND):
            sl = mi % 2
            p.dma("dq_g", w2b[:, sl * 32 * 128:(sl + 1) * 32 * 128]
                  .rearrange("p (c m) -> p c m", c=32),
                  hb["L3_ffn_w2"][e_i][:, mi * 128:(mi + 1) * 128]
                  .rearrange("(c p) m -> p c m", p=128),
                  w=[f"w2b{sl}"])
            pbm = mi % 2
            for kc in range(32):
                p.t(lambda e, sl=sl, kc=kc, pbm=pbm: e.matmul(
                    st.pb[pbm][:, 0:512],
                    w2b[:, sl * 32 * 128 + kc * 128: sl * 32 * 128 + kc * 128 + 128],
                    yb1[:, kc * 512:(kc + 1) * 512],
                    start=(kc == 0), stop=(kc == 31)),
                    r=[f"w2b{sl}", f"yb{kc}", f"ht{kc // 4}"], w=[f"pb{pbm}"])
            zi = mi % 2
            p.s(lambda e, pbm=pbm, zi=zi: e.copy(
                st.zslice[:, zi * 512:zi * 512 + 512], st.pb[pbm][:, 0:512]),
                r=[f"pb{pbm}"], w=[f"zs{zi}"])
            for sc in range(4):
                pbi = 4 + sc % 2
                p.t(lambda e, zi=zi, sc=sc, pbi=pbi: e.transpose(
                    st.pb[pbi][:, 0:128],
                    st.zslice[:, zi * 512 + sc * 128: zi * 512 + sc * 128 + 128],
                    st.identf[:]),
                    r=[f"zs{zi}", "identf"], w=[f"pb{pbi}"])
                p.v(lambda e, e_i=e_i, sc=sc, mi=mi, pbi=pbi: e.tensor_scalar(
                    y2t[:, sc * D + mi * 128: sc * D + mi * 128 + 128],
                    st.pb[pbi][:, 0:128],
                    mg[:, 32 + e_i * 4 + sc:32 + e_i * 4 + sc + 1], None, ALU.mult),
                    r=[f"pb{pbi}", "mg"], w=[f"y2t{sc}"])
        for sc in range(4):
            p.indirect(ydram, IndirectOffsetOnAxis(mint[:, e_i * 4 + sc:e_i * 4 + sc + 1], 0),
                       y2t[:, sc * D:(sc + 1) * D], None,
                       compute_op=ALU.add,
                       r=["mint", f"y2t{sc}"] + [f"yd{tc}" for tc in range(NT)],
                       w=[f"yd{tc}" for tc in range(NT)])

    for tc in range(NT):
        for half in range(2):
            sl = (tc * 2 + half) % 2
            p.dma("dq_y", st.zslice[:, sl * 512:sl * 512 + 512],
                  ydram[tc * 128:(tc + 1) * 128, half * 512:half * 512 + 512],
                  r=[f"yd{tcc}" for tcc in range(NT)], w=[f"zs{sl}"])
            p.v(lambda e, tc=tc, half=half, sl=sl: e.tensor_tensor(
                x_t(st, tc)[:, half * 512:half * 512 + 512],
                x_t(st, tc)[:, half * 512:half * 512 + 512],
                st.zslice[:, sl * 512:sl * 512 + 512], ALU.add),
                r=[f"x{tc}", f"zs{sl}"], w=[f"x{tc}"])


SUBLAYERS = {
    0: ("l0_attn", emit_l0_attn,
        ["L0_attn_W_qkv", "L0_attn_b_qkv", "L0_attn_W_o", "L0_attn_b_o"]),
    1: ("l0_ffn", emit_l0_ffn,
        ["L0_ffn_ln_gamma", "L0_ffn_ln_beta", "L0_ffn_fc1_q", "L0_ffn_fc1_scales",
         "L0_ffn_fc1_bias", "L0_ffn_fc2_q", "L0_ffn_fc2_scales", "L0_ffn_fc2_bias"]),
    2: ("l1_attn", emit_l1_attn, ["L1_attn_Wk", "L1_attn_Wq", "L1_attn_Wv"]),
    3: ("l1_ffn", emit_l1_ffn,
        ["L1_ffn_routing_W", "L1_ffn_expert_bias", "L1_ffn_sW1", "L1_ffn_sb1",
         "L1_ffn_sW2", "L1_ffn_sb2", "L1_ffn_rW1", "L1_ffn_rb1", "L1_ffn_rW2",
         "L1_ffn_rb2"]),
    4: ("l2_attn", emit_l2_attn,
        ["L2_attn_Wq", "L2_attn_Wk", "L2_attn_Wv", "L2_attn_Wo"]),
    5: ("l2_ffn", emit_l2_ffn, ["L2_ffn_w_gate", "L2_ffn_w_up", "L2_ffn_w_down"]),
    6: ("l3_attn", emit_l3_attn,
        ["L3_attn_w_qkv", "L3_attn_w_o", "L3_attn_ln1_g", "L3_attn_ln1_b",
         "L3_attn_ln2_g", "L3_attn_ln2_b", "L3_attn_mlp_w1", "L3_attn_mlp_b1",
         "L3_attn_mlp_w2", "L3_attn_mlp_b2"]),
    7: ("l3_ffn", emit_l3_ffn, ["L3_ffn_gate_w", "L3_ffn_w1", "L3_ffn_w2"]),
}

CONST_KEYS = list(CONSTS.keys())


def build_sublayer_program(k, weight_shapes):
    name, emitter, wkeys = SUBLAYERS[k]
    nc = bass.Bass("TRN2", target_bir_lowering=False)
    st = alloc_common(nc)
    x_in = nc.dram_tensor("x_in", [T, D], F32, kind="ExternalInput").ap()
    x_out = nc.dram_tensor("x_out", [T, D], F32, kind="ExternalOutput").ap()
    for ck in CONST_KEYS:
        a = CONSTS[ck]
        dt = BF16 if a.dtype == BF else (F32 if a.dtype == np.float32 else I32)
        st.hbm[ck] = nc.dram_tensor(ck, list(a.shape), dt, kind="ExternalInput").ap()
    for wk in wkeys:
        shape, dtype = weight_shapes[wk]
        dt = {"float32": F32, "int32": I32}[dtype]
        st.hbm[wk] = nc.dram_tensor(wk, list(shape), dt, kind="ExternalInput").ap()
    p = P(nc)
    emit_load_consts(p, st)
    emit_load_x(p, st, x_in)
    emitter(p, st)
    emit_store_x(p, st, x_out)
    return p.finalize()


def build_fused_program(weight_shapes):
    nc = bass.Bass("TRN2", target_bir_lowering=False)
    st = alloc_common(nc)
    st.fused = True
    st.arena_base = nc.sbuf_base
    x_in = nc.dram_tensor("x_in", [T, D], F32, kind="ExternalInput").ap()
    x_out = nc.dram_tensor("x_out", [T, D], F32, kind="ExternalOutput").ap()
    for ck in CONST_KEYS:
        a = CONSTS[ck]
        dt = BF16 if a.dtype == BF else (F32 if a.dtype == np.float32 else I32)
        st.hbm[ck] = nc.dram_tensor(ck, list(a.shape), dt, kind="ExternalInput").ap()
    for k in sorted(SUBLAYERS):
        for wk in SUBLAYERS[k][2]:
            shape, dtype = weight_shapes[wk]
            dt = {"float32": F32, "int32": I32}[dtype]
            st.hbm[wk] = nc.dram_tensor(wk, list(shape), dt, kind="ExternalInput").ap()
    p = P(nc)
    emit_load_consts(p, st)
    emit_load_x(p, st, x_in)
    for k in sorted(SUBLAYERS):
        st.sub = k
        st.arena_pos = 0
        SUBLAYERS[k][1](p, st)
        p.barrier_all()
    emit_store_x(p, st, x_out)
    return p.finalize()


def kernel(**inputs):
    weight_shapes = {k: (v.shape, str(v.dtype)) for k, v in inputs.items()}
    x = np.ascontiguousarray(inputs["L0_attn_residual_stream"], np.float32)
    B = x.shape[0]
    nc = build_fused_program(weight_shapes)
    wall = [wk for k in sorted(SUBLAYERS) for wk in SUBLAYERS[k][2]]
    in_maps = []
    for b in range(B):
        m = {"x_in": np.ascontiguousarray(x[b])}
        for ck in CONST_KEYS:
            m[ck] = CONSTS[ck]
        for wk in wall:
            m[wk] = np.ascontiguousarray(inputs[wk])
        in_maps.append(m)
    res = run_bass_kernel_spmd(nc, in_maps, core_ids=list(range(B)))
    return np.stack([res.results[b]["x_out"] for b in range(B)], 0)
